# version 06
# baseline (speedup 1.0000x reference)
import numpy as np
import ml_dtypes
from contextlib import ExitStack

import concourse.bass as bass
import concourse.mybir as mybir
from concourse.bass_core_rust import IndirectOffsetOnAxis
from concourse.bass_utils import run_bass_kernel_spmd

F32 = mybir.dt.float32
BF16 = mybir.dt.bfloat16
U32 = mybir.dt.uint32
AF = mybir.ActivationFunctionType
ALU = mybir.AluOpType
AX = mybir.AxisListType
BF = ml_dtypes.bfloat16

SCALE = 1.0 / float(np.sqrt(128.0))
LN_EPS = 1e-5
SENT_BASE = 1 << 20


class Cfg:
    def __init__(self, S=8192, D=2048, H=16, DL=512, E=8, DF=1408, CAP=640,
                 R=4, G=4, NCORES=8):
        self.S, self.D, self.H, self.DL, self.E, self.DF = S, D, H, DL, E, DF
        self.DH = 128
        self.CAP = CAP
        self.ES = 2 * CAP
        self.R, self.G, self.NCORES = R, G, NCORES
        self.B = NCORES // R
        self.TOK = S // R
        self.NT = self.TOK // 128
        self.NG = self.NT // G
        self.GQ = 128 * G
        self.FC = D // 128
        self.LC = DL // 128
        self.KCH = S // 128
        self.TC = self.TOK // 128
        self.FO = DF // 128
        self.CC = CAP // 128
        self.FS = D // 512
        assert self.NT % G == 0 and D % 512 == 0 and DL % 128 == 0
        assert DF % 128 == 0 and S % 512 == 0 and CAP % 128 == 0
        assert self.TOK % 512 == 0

    def kc_count(self, a):
        return self.R * (a * self.G + self.G)

    def col0(self, a, kc):
        jmin = 0
        for j in range(self.G):
            if kc > self.R * (a * self.G + j) + self.R - 1:
                jmin = j + 1
        return 128 * jmin

    def masked_j(self, a, kc):
        d = kc - self.R * a * self.G
        if d < 0:
            return None
        j = d // self.R
        return j if j < self.G else None


class TA:

    def __init__(self, handle, n, inner):
        self.t = handle
        self.n = n
        self.inner = inner

    def __getitem__(self, i):
        assert 0 <= i < self.n
        return self.t[:, self.inner * i:self.inner * (i + 1)]

    @property
    def full(self):
        return self.t[:, :]

    @property
    def r3(self):
        return self.t[:, :].rearrange("p (n i) -> p n i", n=self.n)


class _Op:
    __slots__ = ("eng", "fn", "deps", "chan", "cnt", "phase", "quantum")

    def __init__(self, eng, fn, deps, chan, quantum, phase):
        self.eng, self.fn, self.deps, self.chan = eng, fn, deps, chan
        self.quantum, self.phase = quantum, phase
        self.cnt = None


class Builder:
    ENGS = ("tensor", "vector", "scalar", "gpsimd", "sync")

    @staticmethod
    def q_eng(q):
        return "sync" if q.startswith("hw") else "gpsimd"

    def __init__(self, nc):
        self.nc = nc
        self.ops = []
        self.phase = 0
        self.chans = {}
        self.chan_count = {}

    def _chan(self, name):
        if name not in self.chans:
            self.chans[name] = self.nc.alloc_semaphore(name)
            self.chan_count[name] = 0
        return self.chans[name]

    def op(self, eng, fn, deps=()):
        deps = [d for d in deps if d is not None]
        o = _Op(eng, fn, deps, "c_" + eng, 1, self.phase)
        self.ops.append(o)
        return o

    def dma(self, q, fn, deps=()):
        eng = self.q_eng(q)
        deps = [d for d in deps if d is not None]
        o = _Op(eng, fn, deps, q, 16, self.phase)
        self.ops.append(o)
        return o

    def flush_phase(self, name=None):
        ops, self.ops = self.ops, []
        cur_phase = self.phase
        self.phase += 1
        if not ops:
            return
        need_inc = set()
        dma_chans = set()
        for o in ops:
            for d in o.deps:
                if d.phase == cur_phase:
                    need_inc.add(id(d))
            if o.quantum == 16:
                need_inc.add(id(o))
                dma_chans.add(o.chan)
        for o in ops:
            if id(o) in need_inc:
                self._chan(o.chan)
                self.chan_count[o.chan] += o.quantum
                o.cnt = self.chan_count[o.chan]
        drains = {}
        for ch in dma_chans:
            drains.setdefault(self.q_eng(ch), []).append(
                (ch, self.chan_count[ch]))
        by_eng = {e: [] for e in self.ENGS}
        for o in ops:
            by_eng[o.eng].append(o)
        builder = self

        def emit_for(eng_name):
            def run(eng):
                waited = {}
                for o in by_eng[eng_name]:
                    need = {}
                    for d in o.deps:
                        if d.phase != cur_phase:
                            continue
                        assert d.cnt is not None, "dep op was never counted"
                        if need.get(d.chan, -1) < d.cnt:
                            need[d.chan] = d.cnt
                    for ch, cnt in need.items():
                        if waited.get(ch, -1) >= cnt:
                            continue
                        eng.wait_ge(builder.chans[ch], cnt)
                        waited[ch] = cnt
                    instr = o.fn(eng)
                    if o.cnt is not None:
                        instr.then_inc(builder.chans[o.chan], o.quantum)
                for ch, tot in drains.get(eng_name, []):
                    if waited.get(ch, -1) < tot:
                        eng.wait_ge(builder.chans[ch], tot)
            return run

        import contextlib
        scope = self.nc.named_scope(name) if name else contextlib.nullcontext()
        with scope:
            with self.nc.Block(name) as blk:
                for e in self.ENGS:
                    if by_eng[e] or drains.get(e):
                        getattr(blk, e)(emit_for(e))


def build_program(c: Cfg, dbg_names=(), stop_after=None):
    import os
    p5skip = set(os.environ.get("P5SKIP", "").split(",")) - {""}

    nc = bass.Bass("TRN2", target_bir_lowering=False)
    P = Builder(nc)

    def din(name, shape):
        return nc.dram_tensor(name, list(shape), F32, kind="ExternalInput").ap()

    x_batch = din("x_batch", (c.S, c.D))
    x_own = din("x_own", (c.TOK, c.D))
    qg_row = din("qg_row", (1, c.TOK))
    wq_w = din("wq_w", (c.D, c.D)); wq_b = din("wq_b", (1, c.D))
    wk_w = din("wk_w", (c.D, c.D)); wk_b = din("wk_b", (1, c.D))
    wkl_w = din("wkl_w", (c.D, c.DL)); wkl_b = din("wkl_b", (1, c.DL))
    wv_w = din("wv_w", (c.D, c.D)); wv_b = din("wv_b", (1, c.D))
    wvl_w = din("wvl_w", (c.D, c.DL)); wvl_b = din("wvl_b", (1, c.DL))
    upk_w = din("upk_w", (c.H, c.DL, c.DH)); upk_b = din("upk_b", (c.H, c.DH))
    upv_w = din("upv_w", (c.H, c.DL, c.DH)); upv_b = din("upv_b", (c.H, c.DH))
    wo_w = din("wo_w", (c.D, c.D)); wo_b = din("wo_b", (1, c.D))
    gate_w = din("gate_w", (c.E, c.D))
    w_gate = din("w_gate", (c.E, c.D, c.DF))
    w_up = din("w_up", (c.E, c.D, c.DF))
    w_down = din("w_down", (c.E, c.DF, c.D))
    sw_gate = din("sw_gate", (c.D, c.DF))
    sw_up = din("sw_up", (c.D, c.DF))
    sw_down = din("sw_down", (c.DF, c.D))

    y_out = nc.dram_tensor("y", [c.TOK, c.D], F32, kind="ExternalOutput").ap()

    def dscr(name, shape, dt):
        return nc.dram_tensor(name, list(shape), dt, kind="Internal").ap()

    klat_dram = dscr("klat_dram", (c.LC, 128, c.S), BF16)
    vlat_dram = dscr("vlat_dram", (c.LC, 128, c.S), BF16)
    K_dram = dscr("K_dram", (c.H, 128, c.S), BF16)
    V_dram = dscr("V_dram", (c.H, c.KCH, 128, 128), BF16)
    Q_dram = dscr("Q_dram", (c.H, 128, c.TOK), BF16)
    O_dram = dscr("O_dram", (c.H, 128, c.TOK), BF16)
    resid_dram = dscr("resid_dram", (c.TOK, c.D), F32)
    xln2_dram = dscr("xln2_dram", (c.FC, 128, c.TOK), BF16)
    xln2tm_dram = dscr("xln2tm_dram", (c.TOK, c.D), BF16)
    xg_dram = dscr("xg_dram", (c.E * c.ES, c.D), BF16)
    backmap_dram = dscr("backmap_dram", (c.E * c.ES, 1), U32)
    gatew_dram = dscr("gatew_dram", (c.E * c.ES, 1), F32)
    rank_dram = dscr("rank_dram", (2 * c.TOK, c.D), F32)

    dbg_map = {
        "klat": klat_dram, "vlat": vlat_dram, "K": K_dram, "V": V_dram,
        "Q": Q_dram, "O": O_dram, "resid": resid_dram, "xln2": xln2_dram,
        "xg": xg_dram, "backmap": backmap_dram, "gatew": gatew_dram,
        "rank": rank_dram,
    }

    ident_bf = nc.inline_tensor(np.eye(128, dtype=BF), "ident_bf")
    ut_bf = nc.inline_tensor(np.triu(np.ones((128, 128), dtype=BF)), "ut_bf")
    eoff_dram = nc.inline_tensor(
        (np.arange(c.E, dtype=np.float32) * c.ES).reshape(1, c.E), "eoff")

    ctx = ExitStack()

    def sb(name, shape, dt):
        return ctx.enter_context(nc.sbuf_tensor(name, list(shape), dt))

    def sba(name, n, inner, dt, stack=None):
        st = stack or ctx
        h = st.enter_context(nc.sbuf_tensor(name, [128, n * inner], dt))
        return TA(h, n, inner)

    def psa(name, n, inner, dt=F32, stack=None):
        st = stack or ctx
        return [st.enter_context(nc.psum_tensor(f"{name}{i}", [128, inner], dt))[:, :]
                for i in range(n)]

    ones_col = sb("ones_col", (128, 1), BF16)
    ones_row = sb("ones_row", (1, 512), BF16)
    onesrow_f32 = sb("onesrow_f32", (1, 128), F32)
    ident_sb = sb("ident_sb", (128, 128), BF16)
    onef_sb = sb("onef_sb", (1, 1), BF16)
    kg_col = sb("kg_col", (128, 1), F32)
    ut_sb = sb("ut_sb", (128, 128), BF16)
    allones_sb = sb("allones_sb", (128, 128), BF16)
    eoffb_sb = sb("eoffb_sb", (128, c.E), F32)
    tokid_sb = sb("tokid_sb", (128, c.TC), U32)
    tokid2_sb = sb("tokid2_sb", (128, c.TC), U32)

    biasq_sb = sb("biasq_sb", (1, c.D), BF16)
    biask_sb = sb("biask_sb", (1, c.D), BF16)
    biasv_sb = sb("biasv_sb", (1, c.D), BF16)
    biaskl_sb = sb("biaskl_sb", (1, c.DL), F32)
    biasvl_sb = sb("biasvl_sb", (1, c.DL), F32)
    biasupk_sb = sb("biasupk_sb", (1, c.H * c.DH), BF16)
    biasupv_sb = sb("biasupv_sb", (1, c.H * c.DH), BF16)
    biaswo_sb = sb("biaswo_sb", (1, c.D), BF16)
    bkm_sb = sb("bkm_sb", (1, c.DL), BF16)
    bvm_sb = sb("bvm_sb", (1, c.DL), BF16)

    gwT_sb = sba("gwT_sb", c.FC, c.E, BF16)

    const_vals = [float(LN_EPS), float(SCALE), -1.0]
    const_sb = sba("const_sb", len(const_vals), 1, F32)
    for ci, cv in enumerate(const_vals):
        nc.const_aps.register(F32, cv, const_sb[ci])

    attn_ctx = ExitStack()
    nmask = c.NG * c.R * c.G
    mask_sb = sba("mask_sb", nmask, 128, BF16, attn_ctx)

    bc_disp_reg = nc.gpsimd.alloc_register("bc_disp")
    bc_rank_reg = nc.gpsimd.alloc_register("bc_rank")

    def mask_tile(a, kc):
        kcrel = kc - c.R * a * c.G
        return mask_sb[a * (c.R * c.G) + kcrel]

    with ExitStack() as pc:
        qg_sb = pc.enter_context(nc.sbuf_tensor("qg_sb", [1, c.TOK], F32))
        qgb_sb = pc.enter_context(nc.sbuf_tensor("qgb_sb", [128, c.TOK], F32))
        gaterow_sb = pc.enter_context(
            nc.sbuf_tensor("gaterow_sb", [c.E, c.D], BF16))
        bmi_sb = pc.enter_context(
            nc.sbuf_tensor("bmi_sb", [128, c.E * c.ES // 128], U32))
        qgb_ps = pc.enter_context(nc.psum_tensor("qgb_ps", [128, 512], F32))
        tp_ps = psa("p0tp_ps", 2, 128, BF16, pc)
        eob_ps = pc.enter_context(nc.psum_tensor("eob_ps", [128, c.E], F32))
        eoffr_sb = pc.enter_context(nc.sbuf_tensor("eoffr_sb", [1, c.E], F32))

        d_id = P.dma("hw", lambda e: e.dma_start(out=ident_sb[:], in_=ident_bf.ap()))
        d_ut = P.dma("hw", lambda e: e.dma_start(out=ut_sb[:], in_=ut_bf.ap()))
        d_eo = P.dma("hw", lambda e: e.dma_start(out=eoffr_sb[:], in_=eoff_dram.ap()))
        d_qg = P.dma("hw", lambda e: e.dma_start(out=qg_sb[:], in_=qg_row))
        hw_last = d_qg

        m_oc = P.op("gpsimd", lambda e: e.memset(ones_col[:], 1.0))
        m_or = P.op("gpsimd", lambda e: e.memset(ones_row[:], 1.0))
        m_of = P.op("gpsimd", lambda e: e.memset(onesrow_f32[:], 1.0))
        m_o1 = P.op("gpsimd", lambda e: e.memset(onef_sb[:], 1.0))
        m_ao = P.op("gpsimd", lambda e: e.memset(allones_sb[:], 1.0))
        P.op("gpsimd", lambda e: e.reg_mov(bc_disp_reg, c.E * c.ES - 1))
        P.op("gpsimd", lambda e: e.reg_mov(bc_rank_reg, 2 * c.TOK - 1))
        for ci, cv in enumerate(const_vals):
            P.op("gpsimd", (lambda ci, cv: lambda e: e.memset(const_sb[ci], cv))(ci, cv))
        m_kg = P.op("gpsimd", lambda e: e.iota(
            kg_col[:], [[1, 1]], base=0, channel_multiplier=1,
            allow_small_or_imprecise_dtypes=True))
        m_t1 = P.op("gpsimd", lambda e: e.iota(
            tokid_sb[:], [[128, c.TC]], base=0, channel_multiplier=1))
        m_t2 = P.op("gpsimd", lambda e: e.iota(
            tokid2_sb[:], [[128, c.TC]], base=c.TOK, channel_multiplier=1))
        m_bm = P.op("gpsimd", lambda e: e.iota(
            bmi_sb[:], [[1, c.E * c.ES // 128]], base=SENT_BASE,
            channel_multiplier=c.E * c.ES // 128))
        P.dma("sw", lambda e: e.dma_start(
            out=backmap_dram.flatten().rearrange("(p f) -> p f", p=128),
            in_=bmi_sb[:]), deps=[m_bm])

        bias_loads = {}
        for nm, dst, src in (("q", biasq_sb, wq_b), ("k", biask_sb, wk_b),
                             ("v", biasv_sb, wv_b), ("kl", biaskl_sb, wkl_b),
                             ("vl", biasvl_sb, wvl_b), ("wo", biaswo_sb, wo_b)):
            bias_loads[nm] = P.dma(
                "sw",
                (lambda d, s: lambda e: e.dma_start(out=d[:], in_=s))(dst, src))
        d_buk = P.dma("sw", lambda e: e.dma_start(
            out=biasupk_sb[:], in_=upk_b.flatten()[None, :]))
        d_buv = P.dma("sw", lambda e: e.dma_start(
            out=biasupv_sb[:], in_=upv_b.flatten()[None, :]))
        d_gw = P.dma("sw", lambda e: e.dma_start(out=gaterow_sb[:], in_=gate_w))
        sw_last = d_gw

        qgb_cp = []
        for gi in range(c.TOK // 512):
            mm = P.op("tensor", (lambda g: lambda e: e.matmul(
                qgb_ps[:], onesrow_f32[:], qg_sb[:, 512 * g:512 * (g + 1)],
                start=True, stop=True))(gi),
                deps=[hw_last, m_of] + ([qgb_cp[-1]] if qgb_cp else []))
            cp = P.op("vector", (lambda g: lambda e: e.tensor_copy(
                out=qgb_sb[:, 512 * g:512 * (g + 1)], in_=qgb_ps[:]))(gi),
                deps=[mm])
            qgb_cp.append(cp)

        mm_eo = P.op("tensor", lambda e: e.matmul(
            eob_ps[:], onesrow_f32[:], eoffr_sb[:], start=True, stop=True),
            deps=[hw_last, m_of, qgb_cp[-1]])
        P.op("vector", lambda e: e.tensor_copy(out=eoffb_sb[:], in_=eob_ps[:]),
             deps=[mm_eo])

        for a in range(c.NG):
            for kcrel in range(c.R * c.G):
                kc = c.R * a * c.G + kcrel
                j = kcrel // c.R
                colbase = a * c.GQ + 128 * j
                mi = a * (c.R * c.G) + kcrel
                P.op("vector", (lambda mi, cb, kc: lambda e: e.tensor_scalar(
                    out=mask_sb[mi], in0=qgb_sb[:, cb:cb + 128],
                    scalar1=kg_col[:], scalar2=float(128 * kc),
                    op0=ALU.subtract, op1=ALU.is_ge))(mi, colbase, kc),
                    deps=[qgb_cp[colbase // 512], m_kg])

        prev = None
        for f in range(c.FC):
            mmt = P.op("tensor", (lambda f: lambda e: e.transpose(
                tp_ps[f % 2][0:128, 0:c.E],
                gaterow_sb[:, 128 * f:128 * (f + 1)],
                ident_sb[0:c.E, 0:c.E]))(f),
                deps=[sw_last, hw_last, prev])
            prev = P.op("vector", (lambda f, mmt: lambda e: e.tensor_copy(
                out=gwT_sb[f],
                in_=tp_ps[f % 2][0:128, 0:c.E]))(f, mmt),
                deps=[mmt])

        P.flush_phase("p0")
    if stop_after == "p0":
        return nc

    def emit_ln(src_tile, dst_tile, lnscr, stats4, deps, scr_guard=None):
        s_sum = stats4[:, 0:1]
        s_sq = stats4[:, 1:2]
        s_negm = stats4[:, 2:3]
        s_rstd = stats4[:, 3:4]
        o1 = P.op("vector", lambda e: e.tensor_reduce(
            out=s_sum, in_=src_tile, axis=AX.X, op=ALU.add), deps=deps)
        o2 = P.op("scalar", lambda e: e.activation(
            out=lnscr, in_=src_tile, func=AF.Square, accum_out=s_sq),
            deps=list(deps) + [scr_guard])
        o3 = P.op("vector", lambda e: e.tensor_scalar(
            out=s_negm, in0=s_sum, scalar1=-1.0 / c.D, scalar2=None,
            op0=ALU.mult), deps=[o1])
        o4 = P.op("vector", lambda e: e.scalar_tensor_tensor(
            out=s_sum, in0=s_negm, scalar=-1.0, in1=s_negm,
            op0=ALU.mult, op1=ALU.mult), deps=[o3])
        o5 = P.op("vector", lambda e: e.tensor_scalar(
            out=s_sq, in0=s_sq, scalar1=1.0 / c.D, scalar2=None,
            op0=ALU.mult), deps=[o2])
        o6 = P.op("vector", lambda e: e.tensor_tensor(
            out=s_sq, in0=s_sq, in1=s_sum, op=ALU.add), deps=[o4, o5])
        o7 = P.op("scalar", lambda e: e.activation(
            out=s_sq, in_=s_sq, func=AF.Sqrt, bias=float(LN_EPS)), deps=[o6])
        o8 = P.op("vector", lambda e: e.reciprocal(out=s_rstd, in_=s_sq),
                  deps=[o7])
        o9 = P.op("vector", lambda e: e.tensor_scalar(
            out=dst_tile, in0=src_tile, scalar1=s_negm, scalar2=s_rstd,
            op0=ALU.add, op1=ALU.mult), deps=[o8, o3] + list(deps))
        return o9, o2

    p1_ctx = ExitStack()
    wkm_sb = sba("wkm_sb", c.FC, c.DL, BF16, p1_ctx)
    wvm_sb = sba("wvm_sb", c.FC, c.DL, BF16, p1_ctx)

    with ExitStack() as pc:
        wl_sb = [sba("wkl_sb", c.FC, c.DL, BF16, pc),
                 sba("wvl_sb", c.FC, c.DL, BF16, pc)]
        wrow_sb = sba("wrow_sb", 2, c.D, BF16, pc)
        wT_sb = sba("wT_sb", c.FC, 128, BF16, pc)
        bcol_sb = [sba("bcolk_sb", c.FC, 1, BF16, pc),
                   sba("bcolv_sb", c.FC, 1, BF16, pc)]
        tp_ps = psa("p1tp_ps", 2, 128, BF16, pc)
        bc_ps = psa("p1bc_ps", 2, 1, F32, pc)
        mm_ps = psa("p1mm_ps", 2, 512, F32, pc)
        bm_ps = pc.enter_context(nc.psum_tensor("p1bm_ps", [1, 512], F32))

        d_wl0 = P.dma("sw", lambda e: e.dma_start(
            out=wl_sb[0].r3,
            in_=wkl_w.rearrange("(cc p) l -> p cc l", p=128)))
        d_wl1 = P.dma("sw", lambda e: e.dma_start(
            out=wl_sb[1].r3,
            in_=wvl_w.rearrange("(cc p) l -> p cc l", p=128)))
        d_wl = [[d_wl0, d_wl1], [d_wl0, d_wl1]]
        bcol_ops = {}
        prev_ev = None
        for kv in range(2):
            brow = biask_sb if kv == 0 else biasv_sb
            for i in range(c.FC):
                n = kv * c.FC + i
                mm = P.op("tensor", (lambda brow, i, n: lambda e: e.matmul(
                    bc_ps[n % 2], brow[:, 128 * i:128 * (i + 1)],
                    onef_sb[:], start=True, stop=True))(brow, i, n),
                    deps=[bias_loads["k" if kv == 0 else "v"], m_o1, prev_ev])
                ev = P.op("vector", (lambda kv, i, n, mm: lambda e: e.tensor_copy(
                    out=bcol_sb[kv][i], in_=bc_ps[n % 2]))(kv, i, n, mm),
                    deps=[mm])
                bcol_ops[(kv, i)] = ev
                prev_ev = ev

        nlo = (c.DL + 511) // 512
        hist = {}
        for kv in range(2):
            w_src = wk_w if kv == 0 else wv_w
            dstw = wkm_sb if kv == 0 else wvm_sb
            for i in range(c.FC):
                it = kv * c.FC + i
                d_row = P.dma("sw_a" if it % 2 == 0 else "sw_b",
                    (lambda w_src, i, it: lambda e: e.dma_start(
                        out=wrow_sb[it % 2], in_=w_src[128 * i:128 * (i + 1), :]))(w_src, i, it),
                    deps=[hist.get(("tpdone", it - 2))])
                tevs = []
                for j in range(c.FC):
                    ntp = it * c.FC + j
                    mmt = P.op("tensor", (lambda it, j, ntp: lambda e: e.transpose(
                        tp_ps[ntp % 2], wrow_sb[it % 2][:, 128 * j:128 * (j + 1)],
                        ident_sb[:]))(it, j, ntp),
                        deps=[d_row, hist.get(("tpev", ntp - 2)),
                              hist.get(("wTuse", it - 1)) if j == 0 else None])
                    ev = P.op("vector", (lambda j, ntp, mmt: lambda e: e.tensor_copy(
                        out=wT_sb[j], in_=tp_ps[ntp % 2]))(j, ntp, mmt), deps=[mmt])
                    hist[("tpev", ntp)] = ev
                    tevs.append(ev)
                hist[("tpdone", it)] = tevs[-1]
                for lo in range(nlo):
                    lw = min(512, c.DL - 512 * lo)
                    nn = it * nlo + lo
                    mm = None
                    for j in range(c.FC):
                        mm = P.op("tensor",
                                  (lambda nn, j, lo, lw, kv: lambda e: e.matmul(
                                      mm_ps[nn % 2][:, 0:lw],
                                      wT_sb[j],
                                      wl_sb[kv][j][:, 512 * lo:512 * lo + lw],
                                      start=(j == 0), stop=(j == c.FC - 1)))(nn, j, lo, lw, kv),
                                  deps=(([tevs[-1], hist.get(("meev", nn - 2))] + d_wl[kv]) if j == 0 else []))
                    ev = P.op("vector", (lambda dstw, i, lo, lw, nn, mm: lambda e:
                              e.tensor_copy(
                                  out=dstw[i][:, 512 * lo:512 * lo + lw],
                                  in_=mm_ps[nn % 2][:, 0:lw]))(
                                      dstw, i, lo, lw, nn, mm),
                              deps=[mm])
                    hist[("meev", nn)] = ev
                    hist[("wTuse", it)] = mm

        for kv in range(2):
            browl = biaskl_sb if kv == 0 else biasvl_sb
            dst = bkm_sb if kv == 0 else bvm_sb
            for lo in range(nlo):
                lw = min(512, c.DL - 512 * lo)
                mm = None
                for j in range(c.FC):
                    mm = P.op("tensor", (lambda kv, j, lo, lw: lambda e: e.matmul(
                        bm_ps[0:1, 0:lw], bcol_sb[kv][j],
                        wl_sb[kv][j][:, 512 * lo:512 * lo + lw],
                        start=(j == 0), stop=(j == c.FC - 1)))(kv, j, lo, lw),
                        deps=[bcol_ops[(kv, j)],
                              hist.get(("bmev", kv, lo - 1)) if j == 0 else None,
                              hist.get(("bmev", kv - 1, nlo - 1)) if (j == 0 and lo == 0) else None]
                             + (d_wl[kv] if j == 0 else []))
                ev = P.op("vector", (lambda dst, browl, lo, lw, mm: lambda e:
                          e.tensor_tensor(out=dst[:, 512 * lo:512 * lo + lw],
                                          in0=bm_ps[0:1, 0:lw],
                                          in1=browl[:, 512 * lo:512 * lo + lw],
                                          op=ALU.add))(dst, browl, lo, lw, mm),
                          deps=[mm, bias_loads["kl" if kv == 0 else "vl"]])
                hist[("bmev", kv, lo)] = ev

        P.flush_phase("p1")
    if stop_after == "p1":
        return nc

    with ExitStack() as pc:
        xb_sb = [sba("xb0_sb", 4, c.D, F32, pc), sba("xb1_sb", 4, c.D, F32, pc)]
        lnscr = pc.enter_context(nc.sbuf_tensor("lnscr", [128, c.D], F32))
        lnst = pc.enter_context(nc.sbuf_tensor("lnst", [128, 16], F32))
        xlt_sb = sba("xlt_sb", 4, c.D, BF16, pc)
        xlf_sb = sba("xlf_sb", c.FC, 512, BF16, pc)
        lat_sb = sba("lat_sb", 2, 512, BF16, pc)
        tp_ps = psa("pbtp_ps", 2, 128, BF16, pc)
        kl_ps = psa("pbkl_ps", 2, 512, F32, pc)

        hist = {}
        prev_sq = None
        for g in range(c.S // 512):
            par = g % 2
            guard = [hist.get(("xbuse", g - 2))]
            d_xh = []
            for hf in range(2):
                d_xh.append(P.dma("hw_a" if hf == 0 else "hw_b",
                    (lambda g, hf, par: lambda e: e.dma_start(
                        out=xb_sb[par].t[:, c.D * 2 * hf:c.D * 2 * (hf + 1)].rearrange(
                            "p (t d) -> p t d", t=2),
                        in_=x_batch[512 * g + 256 * hf:512 * g + 256 * (hf + 1), :].rearrange(
                            "(t p) d -> p t d", p=128)))(g, hf, par),
                    deps=guard))
            tp_last = None
            ln_ops = []
            for t in range(4):
                ln, prev_sq = emit_ln(
                    xb_sb[par][t], xlt_sb[t], lnscr[:, :], lnst[:, 4 * t:4 * t + 4],
                    deps=[d_xh[t // 2],
                          hist.get(("xltuse", g - 1)) if t == 0 else None],
                    scr_guard=prev_sq)
                ln_ops.append(ln)
                if t == 3:
                    hist[("xbuse", g)] = ln
                for f in range(c.FC):
                    n = (g * 4 + t) * c.FC + f
                    mmt = P.op("tensor", (lambda t, f, n: lambda e: e.transpose(
                        tp_ps[n % 2], xlt_sb[t][:, 128 * f:128 * (f + 1)],
                        ident_sb[:]))(t, f, n),
                        deps=[ln, hist.get(("tpev", n - 2)),
                              hist.get(("xlfuse", g - 1)) if (t == 0 and f == 0) else None])
                    ev = P.op("vector", (lambda f, t, n, mmt: lambda e: e.tensor_copy(
                        out=xlf_sb[f][:, 128 * t:128 * (t + 1)], in_=tp_ps[n % 2]))(f, t, n, mmt),
                        deps=[mmt])
                    hist[("tpev", n)] = ev
                    tp_last = ev
            hist[("xltuse", g)] = tp_last
            last_mm = None
            for kv in range(2):
                wsrc = wkm_sb if kv == 0 else wvm_sb
                bsrc = bkm_sb if kv == 0 else bvm_sb
                ddst = klat_dram if kv == 0 else vlat_dram
                for lc in range(c.LC):
                    gn = g * 2 * c.LC + kv * c.LC + lc
                    pp = gn % 2
                    mm = None
                    for f in range(c.FC):
                        mm = P.op("tensor",
                                  (lambda pp, f, lc, wsrc: lambda e: e.matmul(
                                      kl_ps[pp], wsrc[f][:, 128 * lc:128 * (lc + 1)],
                                      xlf_sb[f], start=(f == 0), stop=False))(pp, f, lc, wsrc),
                                  deps=([tp_last, hist.get(("klev", gn - 2))]
                                        if f == 0 else []))
                    mmb = P.op("tensor", (lambda pp, lc, bsrc: lambda e: e.matmul(
                        kl_ps[pp], bsrc[:, 128 * lc:128 * (lc + 1)], ones_row[:],
                        start=False, stop=True))(pp, lc, bsrc), deps=[mm])
                    ev = P.op("vector", (lambda pp, mmb, gn: lambda e: e.tensor_copy(
                        out=lat_sb[gn % 2], in_=kl_ps[pp]))(pp, mmb, gn),
                        deps=[mmb, hist.get(("latst", gn - 2))])
                    st = P.dma("hw_s0" if gn % 2 == 0 else "hw_s1",
                        (lambda gn, ddst, lc, g: lambda e: e.dma_start(
                            out=ddst[lc][:, 512 * g:512 * (g + 1)], in_=lat_sb[gn % 2]))(gn, ddst, lc, g),
                        deps=[ev])
                    hist[("klev", gn)] = ev
                    hist[("latst", gn)] = st
                    last_mm = mm
            hist[("xlfuse", g)] = last_mm

        P.flush_phase("p1b")
    p1_ctx.close()
    if stop_after == "p1b":
        return nc

    with ExitStack() as pc:
        wq_sb = sba("wq_sb", c.FC, c.D, BF16, pc)
        xo_sb = [sba("xo0_sb", 4, c.D, F32, pc)]
        lnscr = pc.enter_context(nc.sbuf_tensor("lnscr_c", [128, c.D], F32))
        lnst = pc.enter_context(nc.sbuf_tensor("lnst_c", [128, 16], F32))
        xlt_sb = sba("xlt_c", 4, c.D, BF16, pc)
        xlf_sb = sba("xlf_c", c.FC, 512, BF16, pc)
        q_sb = sba("q_sb", 2, 512, BF16, pc)
        tp_ps = psa("pctp_ps", 2, 128, BF16, pc)
        q_ps = psa("pcq_ps", 2, 512, F32, pc)

        d_wq = P.dma("sw", lambda e: e.dma_start(
            out=wq_sb.r3, in_=wq_w.rearrange("(cc p) d -> p cc d", p=128)))

        hist = {}
        prev_sq = None
        for a in range(c.TOK // 512):
            par = 0
            guard = [hist.get(("xouse", a - 1))]
            d_xh = []
            for hf in range(2):
                d_xh.append(P.dma("hw_a" if hf == 0 else "hw_b",
                    (lambda a, hf, par: lambda e: e.dma_start(
                        out=xo_sb[par].t[:, c.D * 2 * hf:c.D * 2 * (hf + 1)].rearrange(
                            "p (t d) -> p t d", t=2),
                        in_=x_own[512 * a + 256 * hf:512 * a + 256 * (hf + 1), :].rearrange(
                            "(t p) d -> p t d", p=128)))(a, hf, par),
                    deps=guard))
            tp_last = None
            for t in range(4):
                ln, prev_sq = emit_ln(
                    xo_sb[par][t], xlt_sb[t], lnscr[:, :], lnst[:, 4 * t:4 * t + 4],
                    deps=[d_xh[t // 2],
                          hist.get(("xltuse", a - 1)) if t == 0 else None],
                    scr_guard=prev_sq)
                if t == 3:
                    hist[("xouse", a)] = ln
                for f in range(c.FC):
                    n = (a * 4 + t) * c.FC + f
                    mmt = P.op("tensor", (lambda t, f, n: lambda e: e.transpose(
                        tp_ps[n % 2], xlt_sb[t][:, 128 * f:128 * (f + 1)],
                        ident_sb[:]))(t, f, n),
                        deps=[ln, hist.get(("tpev", n - 2)),
                              hist.get(("xlfuse", a - 1)) if (t == 0 and f == 0) else None])
                    ev = P.op("vector", (lambda f, t, n, mmt: lambda e: e.tensor_copy(
                        out=xlf_sb[f][:, 128 * t:128 * (t + 1)], in_=tp_ps[n % 2]))(f, t, n, mmt),
                        deps=[mmt])
                    hist[("tpev", n)] = ev
                    tp_last = ev
            hist[("xltuse", a)] = tp_last
            last_mm = None
            for h in range(c.H):
                qn = a * c.H + h
                pp = qn % 2
                mm = None
                for f in range(c.FC):
                    mm = P.op("tensor", (lambda pp, f, h: lambda e: e.matmul(
                        q_ps[pp], wq_sb[f][:, 128 * h:128 * (h + 1)],
                        xlf_sb[f], start=(f == 0), stop=False))(pp, f, h),
                        deps=([tp_last, d_wq, hist.get(("qev", qn - 2))]
                              if f == 0 else []))
                mmb = P.op("tensor", (lambda pp, h: lambda e: e.matmul(
                    q_ps[pp], biasq_sb[:, 128 * h:128 * (h + 1)], ones_row[:],
                    start=False, stop=True))(pp, h), deps=[mm])
                ev = P.op("vector", (lambda pp, mmb, qn: lambda e: e.tensor_copy(
                    out=q_sb[qn % 2], in_=q_ps[pp]))(pp, mmb, qn),
                    deps=[mmb, hist.get(("qst", qn - 2))])
                st = P.dma("hw_s0" if qn % 2 == 0 else "hw_s1",
                    (lambda qn, h, a: lambda e: e.dma_start(
                        out=Q_dram[h][:, 512 * a:512 * (a + 1)], in_=q_sb[qn % 2]))(qn, h, a),
                    deps=[ev])
                hist[("qev", qn)] = ev
                hist[("qst", qn)] = st
                last_mm = mm
            hist[("xlfuse", a)] = last_mm

        P.flush_phase("p1c")
    if stop_after == "p1c":
        return nc

    with ExitStack() as pc:
        upk_sb = sba("upk_sb", c.H * c.LC, 128, BF16, pc)
        upv_sb = sba("upv_sb", c.H * c.LC, 128, BF16, pc)
        kl_sb = [sba("p2kl0", c.LC, 512, BF16, pc), sba("p2kl1", c.LC, 512, BF16, pc)]
        vl_sb = [sba("p2vl0", c.LC, 512, BF16, pc), sba("p2vl1", c.LC, 512, BF16, pc)]
        kp_sb = sba("kp_sb", 2, 512, BF16, pc)
        vp_sb = sba("vp_sb", 2, 4 * 128, BF16, pc)
        K_ps = psa("p2K_ps", 2, 512, F32, pc)
        V_ps = psa("p2V_ps", 2, 512, F32, pc)

        d_upk = P.dma("sw", lambda e: e.dma_start(
            out=upk_sb.t[:, :].rearrange("p (h cc d) -> p h cc d", h=c.H, cc=c.LC),
            in_=upk_w.rearrange("h (cc p) d -> p h cc d", p=128)))
        d_upv = P.dma("sw", lambda e: e.dma_start(
            out=upv_sb.t[:, :].rearrange("p (h cc d) -> p h cc d", h=c.H, cc=c.LC),
            in_=upv_w.rearrange("h (cc p) d -> p h cc d", p=128)))

        hist = {}
        for ks in range(c.S // 512):
            par = ks % 2
            d_kl = P.dma("hw_a" if par == 0 else "hw_b", (lambda ks, par: lambda e: e.dma_start(
                out=kl_sb[par].r3,
                in_=klat_dram[:, :, 512 * ks:512 * (ks + 1)].rearrange(
                    "cc p s -> p cc s")))(ks, par),
                deps=[hist.get(("kluse", ks - 2))])
            d_vl = P.dma("hw_a" if par == 0 else "hw_b", (lambda ks, par: lambda e: e.dma_start(
                out=vl_sb[par].r3,
                in_=vlat_dram[:, :, 512 * ks:512 * (ks + 1)].rearrange(
                    "cc p s -> p cc s")))(ks, par),
                deps=[hist.get(("vluse", ks - 2))])
            for h in range(c.H):
                kn = ks * c.H + h
                pk = kn % 2
                mm = None
                for lc in range(c.LC):
                    mm = P.op("tensor", (lambda pk, par, h, lc: lambda e: e.matmul(
                        K_ps[pk], upk_sb[h * c.LC + lc], kl_sb[par][lc],
                        start=(lc == 0), stop=False))(pk, par, h, lc),
                        deps=([d_kl, d_vl, d_upk, d_upv, hist.get(("kev", kn - 2))]
                              if lc == 0 else []))
                mmb = P.op("tensor", (lambda pk, h: lambda e: e.matmul(
                    K_ps[pk], biasupk_sb[:, 128 * h:128 * (h + 1)], ones_row[:],
                    start=False, stop=True))(pk, h), deps=[mm])
                ev = P.op("scalar", (lambda pk, mmb: lambda e: e.copy(
                    out=kp_sb[pk], in_=K_ps[pk]))(pk, mmb),
                    deps=[mmb, hist.get(("kst", kn - 2))])
                st = P.dma("hw_sk0" if pk == 0 else "hw_sk1",
                    (lambda pk, h, ks: lambda e: e.dma_start(
                        out=K_dram[h][:, 512 * ks:512 * (ks + 1)], in_=kp_sb[pk]))(pk, h, ks),
                    deps=[ev])
                hist[("kev", kn)] = ev
                hist[("kst", kn)] = st
                if h == c.H - 1:
                    hist[("kluse", ks)] = mm
                vevs = []
                mmv = None
                for t4 in range(4):
                    vn = kn * 4 + t4
                    for lc in range(c.LC):
                        mmv = P.op("tensor", (lambda pk, par, h, lc, t4: lambda e: e.matmul(
                            V_ps[pk][:, 128 * t4:128 * (t4 + 1)],
                            vl_sb[par][lc][:, 128 * t4:128 * (t4 + 1)],
                            upv_sb[h * c.LC + lc],
                            start=(lc == 0), stop=(lc == c.LC - 1)))(pk, par, h, lc, t4),
                            deps=([d_kl, d_vl, d_upk, d_upv, hist.get(("vev", kn - 2))]
                                  if (lc == 0 and t4 == 0) else []))
                    vevs.append(mmv)
                evv = P.op("vector", (lambda pk, mmv: lambda e: e.tensor_copy(
                    out=vp_sb[pk], in_=V_ps[pk]))(pk, mmv),
                    deps=vevs + [hist.get(("vst", kn - 2))])
                hist[("vev", kn)] = evv
                vevs = [evv]
                stv = P.dma("hw_sv0" if pk == 0 else "hw_sv1",
                    (lambda pk, h, ks: lambda e: e.dma_start(
                        out=V_dram[h, 4 * ks:4 * ks + 4].rearrange("t p d -> p t d"),
                        in_=vp_sb[pk].rearrange("p (t d) -> p t d", t=4)))(pk, h, ks),
                    deps=vevs)
                hist[("vst", kn)] = stv
                if h == c.H - 1:
                    hist[("vluse", ks)] = mmv

        P.flush_phase("p2")
    if stop_after == "p2":
        return nc

    with ExitStack() as pc:
        K_sb = [pc.enter_context(nc.sbuf_tensor(f"K{i}_sb", [128, c.S], BF16))
                for i in range(2)]
        V_sb = [pc.enter_context(nc.sbuf_tensor(f"V{i}_sb", [128, c.KCH * 128], BF16))
                for i in range(2)]
        Qh_sb = [pc.enter_context(nc.sbuf_tensor(f"Q{i}_sb", [128, c.TOK], BF16))
                 for i in range(2)]
        SPD = 4
        LAG = 3
        pT_sb = sba("pT_sb", SPD, 512, BF16, pc)
        l_bf = pc.enter_context(nc.sbuf_tensor("l_bf", [1, 512], BF16))
        rec_sb = pc.enter_context(nc.sbuf_tensor("rec_sb", [1, 512], F32))
        bc_sb = pc.enter_context(nc.sbuf_tensor("bc_sb", [128, 512], F32))
        O_sb = sba("O_sb", 2, 512, BF16, pc)
        s_ps = psa("p3s_ps", SPD, 512, F32, pc)
        o_ps = pc.enter_context(nc.psum_tensor("p3o_ps", [128, 512], F32))
        l_ps = pc.enter_context(nc.psum_tensor("p3l_ps", [1, 512], F32))
        bc_ps = pc.enter_context(nc.psum_tensor("p3bc_ps", [128, 512], F32))

        last_sps_rd = [None] * SPD
        last_pT_rd = [None] * SPD
        last_l_rd = None
        last_o_rd = None
        last_bc_rd = None
        hist = {}
        for h in range(c.H):
            par = h % 2
            hd = hist.get(("hdone", h - 2))
            hq = "hw_a" if par == 0 else "hw_b"
            d_K = P.dma(hq, (lambda h, par: lambda e: e.dma_start(
                out=K_sb[par][:, :], in_=K_dram[h]))(h, par), deps=[hd])
            d_V = P.dma(hq, (lambda h, par: lambda e: e.dma_start(
                out=V_sb[par][:, :].rearrange("p (k d) -> p k d", k=c.KCH),
                in_=V_dram[h].rearrange("k p d -> p k d")))(h, par),
                deps=[hd])
            d_Q = P.dma(hq, (lambda h, par: lambda e: e.dma_start(
                out=Qh_sb[par][:, :], in_=Q_dram[h]))(h, par), deps=[hd])
            d_KVQ = [d_K, d_V, d_Q]
            for a in range(c.NG):
                KC = c.kc_count(a)
                sc_ops = [None] * KC
                ex_ops = [None] * KC
                mk_ops = [None] * KC
                av_ops = [None] * KC
                sm_ops = [None] * KC
                sm = None

                def emit_scores(kc):
                    col0 = c.col0(a, kc)
                    pp = kc % SPD
                    sc_ops[kc] = P.op("tensor",
                        (lambda pp, kc, col0, a, par: lambda e: e.matmul(
                            s_ps[pp][:, col0:512],
                            K_sb[par][:, 128 * kc:128 * (kc + 1)],
                            Qh_sb[par][:, c.GQ * a + col0:c.GQ * (a + 1)],
                            start=True, stop=True))(pp, kc, col0, a, par),
                        deps=d_KVQ + [last_sps_rd[pp]])
                    jm = c.masked_j(a, kc)
                    ex_ops[kc] = P.op("scalar",
                        (lambda pp, col0: lambda e: e.activation(
                            out=pT_sb.t[:, 512 * pp + col0:512 * (pp + 1)],
                            in_=s_ps[pp][:, col0:512],
                            func=AF.Exp, scale=SCALE))(pp, col0),
                        deps=[sc_ops[kc], last_pT_rd[pp]])
                    last_sps_rd[pp] = ex_ops[kc]
                    if jm is not None:
                        mk_ops[kc] = P.op("vector",
                            (lambda pp, a, kc, jm: lambda e: e.tensor_tensor(
                                out=pT_sb.t[:, 512 * pp + 128 * jm:512 * pp + 128 * (jm + 1)],
                                in0=pT_sb.t[:, 512 * pp + 128 * jm:512 * pp + 128 * (jm + 1)],
                                in1=mask_tile(a, kc), op=ALU.mult))(pp, a, kc, jm),
                            deps=[ex_ops[kc]])

                def emit_smav(kc):
                    nonlocal sm, last_o_rd
                    col0 = c.col0(a, kc)
                    pp = kc % SPD
                    pe_dep = mk_ops[kc] or ex_ops[kc]
                    sm_ops[kc] = sm = P.op("tensor",
                        (lambda pp, kc, col0, KC: lambda e: e.matmul(
                            l_ps[0:1, col0:512], ones_col[:],
                            pT_sb.t[:, 512 * pp + col0:512 * (pp + 1)],
                            start=(kc == 0), stop=(kc == KC - 1)))(pp, kc, col0, KC),
                        deps=[pe_dep] + ([last_l_rd] if kc == 0 else []))
                    av_ops[kc] = P.op("tensor",
                        (lambda pp, kc, col0, par: lambda e: e.matmul(
                            o_ps[:, col0:512], V_sb[par][:, 128 * kc:128 * (kc + 1)],
                            pT_sb.t[:, 512 * pp + col0:512 * (pp + 1)],
                            start=(kc == 0), stop=False))(pp, kc, col0, par),
                        deps=[pe_dep] + (d_KVQ + [last_o_rd] if kc == 0 else []))
                    last_pT_rd[pp] = av_ops[kc]

                for i in range(KC + LAG):
                    if i < KC:
                        emit_scores(i)
                    if i - LAG >= 0:
                        emit_smav(i - LAG)
                evl = P.op("vector", lambda e: e.tensor_copy(out=l_bf[:], in_=l_ps[:, :]),
                           deps=[sm])
                rc = P.op("vector", lambda e: e.reciprocal(out=rec_sb[:], in_=l_ps[:, :]),
                          deps=[sm, evl])
                last_l_rd = rc
                vb = P.op("tensor", (lambda h: lambda e: e.matmul(
                    o_ps[:, :], biasupv_sb[:, 128 * h:128 * (h + 1)], l_bf[:],
                    start=False, stop=True))(h), deps=[evl, av_ops[KC - 1]])
                bc = P.op("tensor", lambda e: e.matmul(
                    bc_ps[:, :], onesrow_f32[:], rec_sb[:], start=True, stop=True),
                    deps=[rc, last_bc_rd])
                bce = P.op("scalar", lambda e: e.copy(out=bc_sb[:], in_=bc_ps[:, :]),
                           deps=[bc])
                last_bc_rd = bce
                po = (h * c.NG + a) % 2
                onorm = P.op("vector", (lambda po: lambda e: e.tensor_tensor(
                    out=O_sb[po], in0=o_ps[:, :], in1=bc_sb[:], op=ALU.mult))(po),
                    deps=[vb, bce, hist.get(("ost", h * c.NG + a - 2))])
                last_o_rd = onorm
                ost = P.dma("hw_s0" if po == 0 else "hw_s1",
                    (lambda po, h, a: lambda e: e.dma_start(
                        out=O_dram[h][:, c.GQ * a:c.GQ * (a + 1)], in_=O_sb[po]))(po, h, a),
                    deps=[onorm])
                hist[("ost", h * c.NG + a)] = ost
                if a == c.NG - 1:
                    hist[("hdone", h)] = onorm

        P.flush_phase("p3")
    attn_ctx.close()
    if stop_after == "p3":
        return nc

    with ExitStack() as pc:
        wo_sb = sba("wo_sb", c.FC, c.D, BF16, pc)
        Otc_sb = [sba(f"Otc{i}_sb", c.H, 128, BF16, pc) for i in range(2)]
        xot_sb = sba("xot_sb", 2, c.D, F32, pc)
        resr_sb = [pc.enter_context(nc.sbuf_tensor(f"resr{i}_sb", [128, c.D], F32))
                   for i in range(2)]
        lnscr = pc.enter_context(nc.sbuf_tensor("lnscr4", [128, c.D], F32))
        lnst = pc.enter_context(nc.sbuf_tensor("lnst4", [128, 8], F32))
        xl2_sb = [pc.enter_context(nc.sbuf_tensor(f"xl2{i}_sb", [128, c.D], BF16))
                  for i in range(2)]
        zero_sb = pc.enter_context(nc.sbuf_tensor("zero_sb", [128, 512], F32))
        wo_ps = psa("p4wo_ps", 2, 512, F32, pc)

        d_wo = P.dma("sw", lambda e: e.dma_start(
            out=wo_sb.r3, in_=wo_w.rearrange("(cc p) d -> p cc d", p=128)))
        z_zero = P.op("gpsimd", lambda e: e.memset(zero_sb[:], 0.0))
        for i in range(2 * c.TOK // 128):
            for j in range(c.FS):
                P.dma("hw2", (lambda i, j: lambda e: e.dma_start(
                    out=rank_dram[128 * i:128 * (i + 1), 512 * j:512 * (j + 1)],
                    in_=zero_sb[:]))(i, j),
                    deps=[z_zero])

        hist = {}
        prev_sq = None
        for tc in range(c.TC):
            par = tc % 2
            d_O = P.dma("hw_a" if par == 0 else "hw_b", (lambda tc, par: lambda e: e.dma_start(
                out=Otc_sb[par].r3,
                in_=O_dram[:, :, 128 * tc:128 * (tc + 1)].rearrange(
                    "h p t -> p h t")))(tc, par),
                deps=[hist.get(("owomm", tc - 2))])
            d_xo = P.dma("hw_c" if par == 0 else "hw_d", (lambda tc, par: lambda e: e.dma_start(
                out=xot_sb.t[:, c.D * par:c.D * (par + 1)],
                in_=x_own[128 * tc:128 * (tc + 1), :]))(tc, par),
                deps=[hist.get(("resev", tc - 2))])
            res_evs = []
            mm = None
            for fs in range(c.FS):
                wn = tc * c.FS + fs
                pp = wn % 2
                mm = None
                for hh in range(c.H):
                    mm = P.op("tensor", (lambda pp, par, hh, fs: lambda e: e.matmul(
                        wo_ps[pp], Otc_sb[par][hh],
                        wo_sb[hh][:, 512 * fs:512 * (fs + 1)],
                        start=(hh == 0), stop=False))(pp, par, hh, fs),
                        deps=([d_O, d_wo, hist.get(("woev", wn - 2))]
                              if hh == 0 else []))
                mmb = P.op("tensor", (lambda pp, fs: lambda e: e.matmul(
                    wo_ps[pp], ones_row[:, 0:128],
                    biaswo_sb[:, 512 * fs:512 * (fs + 1)],
                    start=False, stop=True))(pp, fs), deps=[mm])
                ev = P.op("vector", (lambda pp, par, fs: lambda e: e.tensor_tensor(
                    out=resr_sb[par][:, 512 * fs:512 * (fs + 1)],
                    in0=wo_ps[pp], in1=xot_sb.t[:, c.D * par + 512 * fs:c.D * par + 512 * (fs + 1)],
                    op=ALU.add))(pp, par, fs),
                    deps=[mmb, d_xo,
                          hist.get(("resst", tc - 2)) if fs == 0 else None,
                          hist.get(("resuse", tc - 2)) if fs == 0 else None])
                hist[("woev", wn)] = ev
                res_evs.append(ev)
            hist[("owomm", tc)] = mm
            hist[("resev", tc)] = res_evs[-1]
            ln, prev_sq = emit_ln(
                resr_sb[par][:, :], xl2_sb[par][:, :], lnscr[:, :],
                lnst[:, 4 * par:4 * par + 4],
                deps=[res_evs[-1], hist.get(("xl2st", tc - 2))],
                scr_guard=prev_sq)
            hist[("resuse", tc)] = ln
            rst = P.dma("hw_s0" if par == 0 else "hw_s1",
                (lambda par, tc: lambda e: e.dma_start(
                    out=resid_dram[128 * tc:128 * (tc + 1), :], in_=resr_sb[par][:, :]))(par, tc),
                deps=[res_evs[-1]])
            hist[("resst", tc)] = rst
            x2st = P.dma("hw_s2" if par == 0 else "hw_s3",
                (lambda par, tc: lambda e: e.dma_start(
                    out=xln2tm_dram[128 * tc:128 * (tc + 1), :], in_=xl2_sb[par][:, :]))(par, tc),
                deps=[ln])
            hist[("xl2st", tc)] = x2st

        P.flush_phase("p4a")
    if stop_after == "p4a":
        return nc

    with ExitStack() as pc:
        xl2_sb = [pc.enter_context(nc.sbuf_tensor(f"xl2b{i}_sb", [128, c.D], BF16))
                  for i in range(2)]
        xlf2_sb = sba("xlf2_sb", c.FC, c.TOK, BF16, pc)
        lg_sb = pc.enter_context(nc.sbuf_tensor("lg_sb", [128, c.E], F32))
        m1_sb = pc.enter_context(nc.sbuf_tensor("m1_sb", [128, 1], F32))
        m2_sb = pc.enter_context(nc.sbuf_tensor("m2_sb", [128, 1], F32))
        epick_sb = pc.enter_context(nc.sbuf_tensor("epick_sb", [128, 1], F32))
        msk1_sb = pc.enter_context(nc.sbuf_tensor("msk1_sb", [128, c.E], F32))
        msk12_sb = pc.enter_context(nc.sbuf_tensor("msk12_sb", [128, c.E], F32))
        mskr2_sb = pc.enter_context(nc.sbuf_tensor("mskr2_sb", [128, c.E], F32))
        lgm_sb = pc.enter_context(nc.sbuf_tensor("lgm_sb", [128, c.E], F32))
        mb_sb = sba("mb_sb", c.TC, c.E, BF16, pc)
        cum_sb = pc.enter_context(nc.sbuf_tensor("cum_sb", [128, c.E], F32))
        pick_sb = pc.enter_context(nc.sbuf_tensor("pick_sb", [128, c.E], F32))
        dest_f = pc.enter_context(nc.sbuf_tensor("dest_f", [128, 2], F32))
        dest_u = [pc.enter_context(nc.sbuf_tensor(f"dest_u{i}", [128, c.TC], U32))
                  for i in range(2)]
        diffs_sb = pc.enter_context(nc.sbuf_tensor("diffs_sb", [128, c.TC], F32))
        w1_sb = pc.enter_context(nc.sbuf_tensor("w1_sb", [128, c.TC], F32))
        w2_sb = pc.enter_context(nc.sbuf_tensor("w2_sb", [128, c.TC], F32))
        tp_ps = psa("p4tp_ps", 2, 128, BF16, pc)
        r_ps = pc.enter_context(nc.psum_tensor("p4r_ps", [128, c.E], F32))
        cum_ps = pc.enter_context(nc.psum_tensor("p4cum_ps", [128, c.E], F32))

        hist = {}
        o_df = None
        for tc in range(c.TC):
            par = tc % 2
            d_x2 = P.dma("hw_a" if par == 0 else "hw_b", (lambda tc, par: lambda e: e.dma_start(
                out=xl2_sb[par][:, :],
                in_=xln2tm_dram[128 * tc:128 * (tc + 1), :]))(tc, par),
                deps=[hist.get(("xl2use", tc - 2))])
            tp_evs = []
            for f in range(c.FC):
                n = tc * c.FC + f
                mmt = P.op("tensor", (lambda par, f, n: lambda e: e.transpose(
                    tp_ps[n % 2], xl2_sb[par][:, 128 * f:128 * (f + 1)],
                    ident_sb[:]))(par, f, n),
                    deps=[d_x2, hist.get(("tpev", n - 2))])
                evt = P.op("vector", (lambda f, tc, n, mmt: lambda e: e.tensor_copy(
                    out=xlf2_sb[f][:, 128 * tc:128 * (tc + 1)], in_=tp_ps[n % 2]))(f, tc, n, mmt),
                    deps=[mmt])
                hist[("tpev", n)] = evt
                tp_evs.append(evt)
            mmr = None
            for f in range(c.FC):
                mmr = P.op("tensor", (lambda f, tc: lambda e: e.matmul(
                    r_ps[:, :], xlf2_sb[f][:, 128 * tc:128 * (tc + 1)], gwT_sb[f],
                    start=(f == 0), stop=(f == c.FC - 1)))(f, tc),
                    deps=[tp_evs[f]] + ([hist.get(("lgev", tc - 1))] if f == 0 else []))
            lgev = P.op("vector", lambda e: e.tensor_copy(out=lg_sb[:], in_=r_ps[:, :]),
                        deps=[mmr])
            hist[("lgev", tc)] = lgev
            o_m1 = P.op("vector", lambda e: e.tensor_reduce(
                out=m1_sb[:], in_=lg_sb[:], axis=AX.X, op=ALU.max), deps=[lgev])
            o_k1 = P.op("vector", lambda e: e.tensor_scalar(
                out=msk1_sb[:], in0=lg_sb[:], scalar1=m1_sb[:], scalar2=None,
                op0=ALU.is_ge), deps=[o_m1])
            o_lgm = P.op("vector", lambda e: e.scalar_tensor_tensor(
                out=lgm_sb[:], in0=msk1_sb[:], scalar=-1e30, in1=lg_sb[:],
                op0=ALU.mult, op1=ALU.add), deps=[o_k1])
            o_m2 = P.op("vector", lambda e: e.tensor_reduce(
                out=m2_sb[:], in_=lgm_sb[:], axis=AX.X, op=ALU.max), deps=[o_lgm])
            o_k12 = P.op("vector", lambda e: e.tensor_scalar(
                out=msk12_sb[:], in0=lg_sb[:], scalar1=m2_sb[:], scalar2=None,
                op0=ALU.is_ge), deps=[o_m2])
            o_kr2 = P.op("vector", lambda e: e.tensor_tensor(
                out=mskr2_sb[:], in0=msk12_sb[:], in1=msk1_sb[:],
                op=ALU.subtract), deps=[o_k12])
            o_df = P.op("vector", (lambda tc: lambda e: e.tensor_tensor(
                out=diffs_sb[:, tc:tc + 1], in0=m1_sb[:], in1=m2_sb[:],
                op=ALU.subtract))(tc), deps=[o_m2, o_kr2])
            o_mb = P.op("vector", (lambda tc: lambda e: e.tensor_copy(
                out=mb_sb[tc], in_=msk12_sb[:]))(tc), deps=[o_k12])
            hist[("mb", tc)] = o_mb
            mcs = P.op("tensor", (lambda tc: lambda e: e.matmul(
                cum_ps[:, :], ut_sb[:], mb_sb[tc], start=True,
                stop=(tc == 0)))(tc),
                deps=[o_mb, d_ut, hist.get(("cumread", tc - 1))])
            for tcp in range(tc):
                mcs = P.op("tensor", (lambda tcp, tc: lambda e: e.matmul(
                    cum_ps[:, :], allones_sb[:], mb_sb[tcp], start=False,
                    stop=(tcp == tc - 1)))(tcp, tc),
                    deps=[mcs, hist[("mb", tcp)]])
            cumev = P.op("vector", lambda e: e.tensor_copy(out=cum_sb[:], in_=cum_ps[:, :]),
                         deps=[mcs])
            hist[("cumread", tc)] = cumev
            dsts = []
            prev_chain = cumev
            for rk, mm_sel in ((0, msk1_sb), (1, mskr2_sb)):
                o_p = P.op("vector", (lambda mm_sel: lambda e: e.tensor_tensor(
                    out=pick_sb[:], in0=cum_sb[:], in1=mm_sel[:], op=ALU.mult))(mm_sel),
                    deps=[prev_chain, o_kr2])
                o_s = P.op("vector", (lambda rk: lambda e: e.tensor_reduce(
                    out=dest_f[:, rk:rk + 1], in_=pick_sb[:], axis=AX.X,
                    op=ALU.add))(rk), deps=[o_p])
                o_c = P.op("vector", (lambda rk: lambda e: e.tensor_scalar(
                    out=dest_f[:, rk:rk + 1], in0=dest_f[:, rk:rk + 1],
                    scalar1=-1.0, scalar2=float(c.ES - 1),
                    op0=ALU.add, op1=ALU.min))(rk), deps=[o_s])
                o_pe = P.op("vector", (lambda mm_sel: lambda e: e.tensor_tensor(
                    out=pick_sb[:], in0=eoffb_sb[:], in1=mm_sel[:], op=ALU.mult))(mm_sel),
                    deps=[o_c])
                o_se = P.op("vector", lambda e: e.tensor_reduce(
                    out=epick_sb[:], in_=pick_sb[:], axis=AX.X, op=ALU.add),
                    deps=[o_pe])
                o_ad = P.op("vector", (lambda rk: lambda e: e.tensor_tensor(
                    out=dest_f[:, rk:rk + 1], in0=dest_f[:, rk:rk + 1],
                    in1=epick_sb[:], op=ALU.add))(rk), deps=[o_se])
                o_u = P.op("vector", (lambda rk, tc: lambda e: e.tensor_copy(
                    out=dest_u[rk][:, tc:tc + 1], in_=dest_f[:, rk:rk + 1]))(rk, tc),
                    deps=[o_ad])
                dsts.append(o_u)
                prev_chain = o_u
            sc_last = None
            for rk, tsrc in ((0, tokid_sb), (1, tokid2_sb)):
                s1 = P.dma("sw_a" if par == 0 else "sw_b",
                    (lambda rk, par, tc: lambda e: e.indirect_dma_start(
                    out=xg_dram,
                    out_offset=IndirectOffsetOnAxis(dest_u[rk][:, tc:tc + 1], 0),
                    in_=xl2_sb[par][:, :], in_offset=None,
                    bounds_check=bc_disp_reg, oob_is_err=False))(rk, par, tc),
                    deps=[dsts[rk], d_x2])
                s2 = P.dma("sw_a" if par == 0 else "sw_b",
                    (lambda rk, tsrc, tc: lambda e: e.indirect_dma_start(
                    out=backmap_dram,
                    out_offset=IndirectOffsetOnAxis(dest_u[rk][:, tc:tc + 1], 0),
                    in_=tsrc[:, tc:tc + 1], in_offset=None,
                    bounds_check=bc_disp_reg, oob_is_err=False))(rk, tsrc, tc),
                    deps=[dsts[rk]])
                sc_last = s2
            hist[("dispdone", tc)] = sc_last
            hist[("xl2use", tc)] = sc_last

        sg1 = P.op("scalar", lambda e: e.activation(
            out=w1_sb[:], in_=diffs_sb[:], func=AF.Sigmoid), deps=[o_df])
        sg2 = P.op("scalar", lambda e: e.activation(
            out=w2_sb[:], in_=diffs_sb[:], func=AF.Sigmoid, scale=-1.0),
            deps=[o_df])
        for tc in range(c.TC):
            for rk, wsrc, sdep in ((0, w1_sb, sg1), (1, w2_sb, sg2)):
                P.dma("sw_g0" if tc % 2 == 0 else "sw_g1",
                    (lambda rk, wsrc, tc: lambda e: e.indirect_dma_start(
                    out=gatew_dram,
                    out_offset=IndirectOffsetOnAxis(dest_u[rk][:, tc:tc + 1], 0),
                    in_=wsrc[:, tc:tc + 1], in_offset=None,
                    bounds_check=bc_disp_reg, oob_is_err=False))(rk, wsrc, tc),
                    deps=[sdep, hist[("dispdone", tc)]])
        P.dma("hw", lambda e: e.dma_start(
            out=xln2_dram.rearrange("cc p t -> p cc t"), in_=xlf2_sb.r3),
            deps=[hist[("tpev", c.TC * c.FC - 1)]])

        P.flush_phase("p4b")
    if stop_after == "p4b":
        return nc

    with ExitStack() as pc:
        xg_sb = [sba("xg0_sb", c.CC, c.D, BF16, pc)]
        xgf_sb = [sba("xgf0_sb", c.FC, c.CAP, BF16, pc),
                  sba("xgf1_sb", c.FC, c.CAP, BF16, pc)]
        bmc_sb = [sba(f"bmc{i}_sb", c.CC, 1, U32, pc) for i in range(2)]
        gwc_sb = [sba(f"gwc{i}_sb", c.CC, 1, F32, pc) for i in range(2)]
        wg_sb = [sba(f"wg{i}_sb", c.FC, 128, BF16, pc) for i in range(2)]
        wu_sb = [sba(f"wu{i}_sb", c.FC, 128, BF16, pc) for i in range(2)]
        wd_sb = [sba("wd0_sb", c.FO, c.D, BF16, pc)]
        gs_sb = pc.enter_context(nc.sbuf_tensor("gs_sb", [128, c.CAP], BF16))
        hfm_sb = [sba(f"hfm{i}_sb", c.FO, c.CAP, BF16, pc) for i in range(2)]
        dn_sb = sba("dn_sb", 2, c.D, F32, pc)
        tp_ps = psa("p5tp_ps", 2, 128, BF16, pc)
        g_ps = pc.enter_context(nc.psum_tensor("p5g_ps", [128, 512], F32))
        g2_ps = pc.enter_context(nc.psum_tensor("p5g2_ps", [128, 128], F32))
        u_ps = pc.enter_context(nc.psum_tensor("p5u_ps", [128, 512], F32))
        u2_ps = pc.enter_context(nc.psum_tensor("p5u2_ps", [128, 128], F32))
        d_ps = psa("p5d_ps", 2, 512, F32, pc)

        ngrp = [(0, min(512, c.CAP))]
        if c.CAP > 512:
            ngrp.append((512, c.CAP - 512))

        hist = {}
        for ei in range(c.E):
            par = ei % 2
            hq5 = "hw_a" if par == 0 else "hw_b"
            d_xg = P.dma(hq5, (lambda ei: lambda e: e.dma_start(
                out=xg_sb[0].r3,
                in_=xg_dram[c.ES * ei:c.ES * ei + c.CAP, :].rearrange(
                    "(t p) d -> p t d", p=128)))(ei),
                deps=[hist.get(("xguse", ei - 1))])
            d_bm = None if "bmgw" in p5skip else P.dma(hq5, (lambda ei, par: lambda e: e.dma_start(
                out=bmc_sb[par].r3,
                in_=backmap_dram[c.ES * ei:c.ES * ei + c.CAP].rearrange(
                    "(t p) one -> p t one", p=128),
                allow_slow_non_contiguous=True))(ei, par),
                deps=(hist.get(("bmuse", ei - 2)) or []))
            d_gw = None if "bmgw" in p5skip else P.dma(hq5, (lambda ei, par: lambda e: e.dma_start(
                out=gwc_sb[par].r3,
                in_=gatew_dram[c.ES * ei:c.ES * ei + c.CAP].rearrange(
                    "(t p) one -> p t one", p=128),
                allow_slow_non_contiguous=True))(ei, par),
                deps=[hist.get(("gwuse", ei - 2))])
            ld3 = [d_xg, d_bm, d_gw]
            tp_last = None
            for t in range(c.CC):
                for f in range(c.FC):
                    n = (ei * c.CC + t) * c.FC + f
                    mmt = P.op("tensor", (lambda t, f, n: lambda e: e.transpose(
                        tp_ps[n % 2], xg_sb[0][t][:, 128 * f:128 * (f + 1)],
                        ident_sb[:]))(t, f, n),
                        deps=ld3 + [hist.get(("p5tp", n - 2)),
                              hist.get(("xgfuse", ei - 2)) if (t == 0 and f == 0) else None])
                    evt = P.op("vector", (lambda par, t, f, n, mmt: lambda e: e.tensor_copy(
                        out=xgf_sb[par][f][:, 128 * t:128 * (t + 1)], in_=tp_ps[n % 2]))(par, t, f, n, mmt),
                        deps=[mmt])
                    hist[("p5tp", n)] = evt
                    tp_last = evt
            hist[("xguse", ei)] = tp_last
            for fo in range(c.FO):
                gfo = ei * c.FO + fo
                pw = gfo % 2
                d_wg = None if "wld" in p5skip else P.dma("sw_a" if pw == 0 else "sw_b",
                    (lambda ei, fo, pw: lambda e: e.dma_start(
                    out=wg_sb[pw].r3,
                    in_=w_gate[ei][:, 128 * fo:128 * (fo + 1)].rearrange(
                        "(cc p) k -> p cc k", p=128)))(ei, fo, pw),
                    deps=[hist.get(("wguse", gfo - 2))])
                d_wu = None if "wld" in p5skip else P.dma("sw_a" if pw == 0 else "sw_b",
                    (lambda ei, fo, pw: lambda e: e.dma_start(
                    out=wu_sb[pw].r3,
                    in_=w_up[ei][:, 128 * fo:128 * (fo + 1)].rearrange(
                        "(cc p) k -> p cc k", p=128)))(ei, fo, pw),
                    deps=[hist.get(("wguse", gfo - 2))])
                hlast = None
                mmu = None
                for (cg0, cgw) in ngrp:
                    gp = g_ps if cgw > 128 else g2_ps
                    up = u_ps if cgw > 128 else u2_ps
                    mmg = mmu = None
                    for f in range(c.FC):
                        mmg = None if "gu" in p5skip else P.op("tensor", (lambda pw, f, cg0, cgw, par, gp: lambda e: e.matmul(
                            gp[:, 0:cgw], wg_sb[pw][f],
                            xgf_sb[par][f][:, cg0:cg0 + cgw],
                            start=(f == 0), stop=(f == c.FC - 1)))(pw, f, cg0, cgw, par, gp),
                            deps=([d_wg, d_wu, tp_last, hist.get(("gev", gfo - 1, cg0))]
                                  if f == 0 else []))
                        mmu = None if "gu" in p5skip else P.op("tensor", (lambda pw, f, cg0, cgw, par, up: lambda e: e.matmul(
                            up[:, 0:cgw], wu_sb[pw][f],
                            xgf_sb[par][f][:, cg0:cg0 + cgw],
                            start=(f == 0), stop=(f == c.FC - 1)))(pw, f, cg0, cgw, par, up),
                            deps=([d_wg, d_wu, tp_last, hist.get(("uev", gfo - 1, cg0))]
                                  if f == 0 else []))
                    sil = None if "gu" in p5skip else P.op("scalar", (lambda cg0, cgw, gp: lambda e: e.activation(
                        out=gs_sb[:, cg0:cg0 + cgw], in_=gp[:, 0:cgw],
                        func=AF.Silu))(cg0, cgw, gp),
                        deps=[mmg, hist.get(("huse", gfo - 1, cg0))])
                    hev = None if "gu" in p5skip else P.op("vector", (lambda par, fo, cg0, cgw, up, sil: lambda e: e.tensor_tensor(
                        out=hfm_sb[par][fo][:, cg0:cg0 + cgw],
                        in0=gs_sb[:, cg0:cg0 + cgw], in1=up[:, 0:cgw],
                        op=ALU.mult))(par, fo, cg0, cgw, up, sil),
                        deps=[sil, mmu,
                              hist.get(("hread", ei - 2)) if (fo == 0 and cg0 == 0) else None])
                    hist[("gev", gfo, cg0)] = sil
                    hist[("uev", gfo, cg0)] = hev
                    hist[("huse", gfo, cg0)] = hev
                    hlast = hev
                hist[("wguse", gfo)] = mmu
                hist[("hev", ei, fo)] = hlast
            d_wd = None if "wld" in p5skip else P.dma("sw_c", (lambda ei: lambda e: e.dma_start(
                out=wd_sb[0].r3,
                in_=w_down[ei].rearrange("(cc p) d -> p cc d", p=128)))(ei),
                deps=[hist.get(("wduse", ei - 1))])
            hdeps = [hist[("hev", ei, fo)] for fo in range(c.FO)]
            last_mmd = None
            sts = []
            for t in range(c.CC):
                gt = ei * c.CC + t
                dnb = gt % 2
                evs = []
                for fs in range(c.FS):
                    dnn = gt * c.FS + fs
                    pd = dnn % 2
                    mmd = None
                    for fo in range(c.FO):
                        mmd = None if "down" in p5skip else P.op("tensor", (lambda pd, par, t, fo, fs: lambda e: e.matmul(
                            d_ps[pd], hfm_sb[par][fo][:, 128 * t:128 * (t + 1)],
                            wd_sb[0][fo][:, 512 * fs:512 * (fs + 1)],
                            start=(fo == 0), stop=(fo == c.FO - 1)))(pd, par, t, fo, fs),
                            deps=(([d_wd] + hdeps +
                                   [hist.get(("dnevn", dnn - 2))]) if fo == 0 else []))
                    evfn = (lambda pd, dnb, par, t, fs: lambda e: e.tensor_scalar(
                        out=dn_sb[dnb][:, 512 * fs:512 * (fs + 1)], in0=d_ps[pd],
                        scalar1=gwc_sb[par][t], scalar2=None, op0=ALU.mult))(pd, dnb, par, t, fs)
                    ev = None if ("down" in p5skip or "dnev" in p5skip) else P.op(
                        "vector", evfn,
                        deps=[mmd, hist.get(("dnst", gt - 2))] +
                             ([] if "dnnold3" in p5skip else ld3))
                    hist[("dnevn", dnn)] = ev
                    evs.append(ev)
                    last_mmd = mmd
                st = None if "scat" in p5skip else P.dma("sw_e" if dnb == 0 else "sw_f",
                    (lambda dnb, par, t: lambda e: e.indirect_dma_start(
                    out=rank_dram,
                    out_offset=IndirectOffsetOnAxis(bmc_sb[par][t], 0),
                    in_=dn_sb[dnb], in_offset=None,
                    bounds_check=bc_rank_reg, oob_is_err=False))(dnb, par, t),
                    deps=evs + ld3)
                hist[("dnst", gt)] = st
                sts.append(st)
            hist[("bmuse", ei)] = sts[-2:]
            hist[("gwuse", ei)] = hist[("dnevn", (ei * c.CC + c.CC) * c.FS - 1)]
            hist[("xgfuse", ei)] = last_mmd
            hist[("hread", ei)] = last_mmd
            hist[("wduse", ei)] = last_mmd
            if stop_after == f"p5e{ei}":
                break

        P.flush_phase("p5")
    if stop_after == "p5":
        return nc

    with ExitStack() as pc:
        TH = c.TOK // 2
        GW = min(512, TH)
        NCG = TH // GW
        TCH = TH // 128
        xlf2_sb = sba("xlf2b_sb", c.FC, TH, BF16, pc)
        wg_sb = [sba(f"swg{i}_sb", c.FC, 128, BF16, pc) for i in range(2)]
        wu_sb = [sba(f"swu{i}_sb", c.FC, 128, BF16, pc) for i in range(2)]
        gs_sb = pc.enter_context(nc.sbuf_tensor("bgs_sb", [128, TH], BF16))
        hfm_sb = sba("bhfm_sb", c.FO, TH, BF16, pc)
        wd_sb = sba("bwd_sb", c.FO, c.D, BF16, pc)
        res_sb = sba("bres_sb", 2, c.D, F32, pc)
        rk0_sb = sba("brk0_sb", 2, c.D, F32, pc)
        rk1_sb = sba("brk1_sb", 2, c.D, F32, pc)
        g_ps = psa("bg_ps", 2, 512, F32, pc)
        u_ps = psa("bu_ps", 2, 512, F32, pc)
        d_ps = psa("bd_ps", 2, 512, F32, pc)

        d_wd = P.dma("sw", lambda e: e.dma_start(
            out=wd_sb.r3, in_=sw_down.rearrange("(cc p) d -> p cc d", p=128)))

        hist = {}
        out_sts = []
        for half in range(2):
            hoff = half * TH
            d_x2 = P.dma("hw_x0" if half == 0 else "hw_x1",
                (lambda hoff: lambda e: e.dma_start(
                    out=xlf2_sb.r3,
                    in_=xln2_dram[:, :, hoff:hoff + TH].rearrange(
                        "cc p t -> p cc t")))(hoff),
                deps=[hist.get(("x2use", half - 1))])
            for fo in range(c.FO):
                gfo = half * c.FO + fo
                pw = gfo % 2
                d_wg = P.dma("sw_a" if pw == 0 else "sw_b",
                    (lambda fo, pw: lambda e: e.dma_start(
                        out=wg_sb[pw].r3,
                        in_=sw_gate[:, 128 * fo:128 * (fo + 1)].rearrange(
                            "(cc p) k -> p cc k", p=128)))(fo, pw),
                    deps=[hist.get(("wguse", gfo - 2))])
                d_wu = P.dma("sw_a" if pw == 0 else "sw_b",
                    (lambda fo, pw: lambda e: e.dma_start(
                        out=wu_sb[pw].r3,
                        in_=sw_up[:, 128 * fo:128 * (fo + 1)].rearrange(
                            "(cc p) k -> p cc k", p=128)))(fo, pw),
                    deps=[hist.get(("wguse", gfo - 2))])
                mmu = None
                hev = None
                for cg in range(NCG):
                    gn = gfo * NCG + cg
                    pg = gn % 2
                    mmg = mmu = None
                    for f in range(c.FC):
                        mmg = P.op("tensor", (lambda pw, pg, f, cg: lambda e: e.matmul(
                            g_ps[pg][:, 0:GW], wg_sb[pw][f],
                            xlf2_sb[f][:, GW * cg:GW * (cg + 1)],
                            start=(f == 0), stop=(f == c.FC - 1)))(pw, pg, f, cg),
                            deps=([d_wg, d_wu, d_x2, hist.get(("gev", gn - 2))]
                                  if f == 0 else []))
                        mmu = P.op("tensor", (lambda pw, pg, f, cg: lambda e: e.matmul(
                            u_ps[pg][:, 0:GW], wu_sb[pw][f],
                            xlf2_sb[f][:, GW * cg:GW * (cg + 1)],
                            start=(f == 0), stop=(f == c.FC - 1)))(pw, pg, f, cg),
                            deps=([d_wg, d_wu, d_x2, hist.get(("uev", gn - 2))]
                                  if f == 0 else []))
                    sil = P.op("scalar", (lambda pg, cg: lambda e: e.activation(
                        out=gs_sb[:, GW * cg:GW * (cg + 1)], in_=g_ps[pg][:, 0:GW],
                        func=AF.Silu))(pg, cg),
                        deps=[mmg, hist.get(("huse", gn - NCG))])
                    hev = P.op("vector", (lambda pg, fo, cg, sil: lambda e: e.tensor_tensor(
                        out=hfm_sb[fo][:, GW * cg:GW * (cg + 1)],
                        in0=gs_sb[:, GW * cg:GW * (cg + 1)], in1=u_ps[pg][:, 0:GW],
                        op=ALU.mult))(pg, fo, cg, sil),
                        deps=[sil, mmu,
                              hist.get(("hread", half - 1)) if cg == 0 else None])
                    hist[("gev", gn)] = sil
                    hist[("uev", gn)] = hev
                    hist[("huse", gn)] = hev
                hist[("wguse", gfo)] = mmu
                hist[("hev", half, fo)] = hev

            hdeps = [hist[("hev", half, fo)] for fo in range(c.FO)]
            last_mmd = None
            for tch in range(TCH):
                tc = half * TCH + tch
                par = tc % 2
                gd = hist.get(("outst", tc - 2))
                hqb = "hw_a" if par == 0 else "hw_b"
                d_rs = P.dma(hqb, (lambda tc, par: lambda e: e.dma_start(
                    out=res_sb[par], in_=resid_dram[128 * tc:128 * (tc + 1), :]))(tc, par),
                    deps=[gd])
                d_r0 = P.dma(hqb, (lambda tc, par: lambda e: e.dma_start(
                    out=rk0_sb[par], in_=rank_dram[128 * tc:128 * (tc + 1), :]))(tc, par),
                    deps=[gd])
                d_r1 = P.dma(hqb, (lambda tc, par: lambda e: e.dma_start(
                    out=rk1_sb[par],
                    in_=rank_dram[c.TOK + 128 * tc:c.TOK + 128 * (tc + 1), :]))(tc, par),
                    deps=[gd])
                d_rr = [d_rs, d_r0, d_r1]
                cmb_ops = []
                for fs in range(c.FS):
                    dnn = tc * c.FS + fs
                    pd = dnn % 2
                    mmd = None
                    for fo in range(c.FO):
                        mmd = P.op("tensor", (lambda pd, tch, fo, fs: lambda e: e.matmul(
                            d_ps[pd], hfm_sb[fo][:, 128 * tch:128 * (tch + 1)],
                            wd_sb[fo][:, 512 * fs:512 * (fs + 1)],
                            start=(fo == 0), stop=(fo == c.FO - 1)))(pd, tch, fo, fs),
                            deps=(([d_wd] + hdeps + [hist.get(("cmbev", dnn - 2))])
                                  if fo == 0 else []))
                    c1 = P.op("vector", (lambda pd, par, fs, mmd: lambda e: e.tensor_tensor(
                        out=res_sb[par][:, 512 * fs:512 * (fs + 1)], in0=d_ps[pd],
                        in1=res_sb[par][:, 512 * fs:512 * (fs + 1)], op=ALU.add))(pd, par, fs, mmd),
                        deps=[mmd, hist.get(("outst", tc - 2)) if fs == 0 else None] + d_rr)
                    c2 = P.op("vector", (lambda par, fs, c1: lambda e: e.tensor_tensor(
                        out=res_sb[par][:, 512 * fs:512 * (fs + 1)],
                        in0=res_sb[par][:, 512 * fs:512 * (fs + 1)],
                        in1=rk0_sb[par][:, 512 * fs:512 * (fs + 1)], op=ALU.add))(par, fs, c1),
                        deps=[c1])
                    c3 = P.op("vector", (lambda par, fs, c2: lambda e: e.tensor_tensor(
                        out=res_sb[par][:, 512 * fs:512 * (fs + 1)],
                        in0=res_sb[par][:, 512 * fs:512 * (fs + 1)],
                        in1=rk1_sb[par][:, 512 * fs:512 * (fs + 1)], op=ALU.add))(par, fs, c2),
                        deps=[c2])
                    hist[("cmbev", dnn)] = c3
                    cmb_ops.append(c3)
                    last_mmd = mmd
                hist[("cmb", tc)] = cmb_ops[-1]
                st = P.dma("hw_s0" if par == 0 else "hw_s1",
                    (lambda par, tc: lambda e: e.dma_start(
                        out=y_out[128 * tc:128 * (tc + 1), :], in_=res_sb[par]))(par, tc),
                    deps=cmb_ops)
                hist[("outst", tc)] = st
                out_sts.append(st)
            hist[("x2use", half)] = last_mmd
            hist[("hread", half)] = last_mmd

        for nm in dbg_names:
            dsrc = dbg_map[nm]
            t = nc.dram_tensor("dbg_" + nm, list(dsrc.shape), dsrc.dtype,
                               kind="ExternalOutput").ap()
            out_sts.append(P.dma("hw", (lambda t, dsrc: lambda e: e.dma_start(
                out=t.flatten()[None, :], in_=dsrc.flatten()[None, :]))(t, dsrc)))

        P.flush_phase("p5b")

    ctx.close()
    return nc



def _shard(cfg, x_full):
    c = cfg
    per_core = []
    for cid in range(c.NCORES):
        b = cid // c.R
        r = cid % c.R
        tiles = [r + c.R * v for v in range(c.NT)]
        rows = np.concatenate([np.arange(128 * t, 128 * (t + 1)) for t in tiles])
        xb = np.ascontiguousarray(x_full[b])
        xo = np.ascontiguousarray(x_full[b][rows])
        qg = np.ascontiguousarray(rows.astype(np.float32).reshape(1, -1))
        per_core.append((b, rows, xb, xo, qg))
    return per_core


def _wmap(inputs):
    w = {k: np.ascontiguousarray(np.asarray(v, np.float32))
         for k, v in inputs.items() if k != "L0_attn_x"}
    return {
        "wq_w": w["L0_attn_wq_w"], "wq_b": w["L0_attn_wq_b"].reshape(1, -1),
        "wk_w": w["L0_attn_wk_w"], "wk_b": w["L0_attn_wk_b"].reshape(1, -1),
        "wkl_w": w["L0_attn_wkl_w"], "wkl_b": w["L0_attn_wkl_b"].reshape(1, -1),
        "wv_w": w["L0_attn_wv_w"], "wv_b": w["L0_attn_wv_b"].reshape(1, -1),
        "wvl_w": w["L0_attn_wvl_w"], "wvl_b": w["L0_attn_wvl_b"].reshape(1, -1),
        "upk_w": w["L0_attn_upk_w"], "upk_b": w["L0_attn_upk_b"],
        "upv_w": w["L0_attn_upv_w"], "upv_b": w["L0_attn_upv_b"],
        "wo_w": w["L0_attn_wo_w"], "wo_b": w["L0_attn_wo_b"].reshape(1, -1),
        "gate_w": w["L0_ffn_gate_w"],
        "w_gate": w["L0_ffn_w_gate"], "w_up": w["L0_ffn_w_up"],
        "w_down": w["L0_ffn_w_down"],
        "sw_gate": w["L0_ffn_sw_gate"], "sw_up": w["L0_ffn_sw_up"],
        "sw_down": w["L0_ffn_sw_down"],
    }


def kernel(**inputs):
    c = Cfg()
    x = np.asarray(inputs["L0_attn_x"], np.float32)
    assert x.shape == (c.B, c.S, c.D)
    wmap = _wmap(inputs)
    nc = build_program(c)
    shards = _shard(c, x)
    in_maps = []
    for (b, rows, xb, xo, qg) in shards:
        m = dict(wmap)
        m["x_batch"] = xb
        m["x_own"] = xo
        m["qg_row"] = qg
        in_maps.append(m)
    res = run_bass_kernel_spmd(nc, in_maps, core_ids=list(range(c.NCORES)))
    out = np.empty((c.B, c.S, c.D), np.float32)
    for cid, (b, rows, _, _, _) in enumerate(shards):
        out[b][rows] = res.results[cid]["y"]
    return out


# version 07
# speedup vs baseline: 1.1206x; 1.1206x over previous
import numpy as np
import ml_dtypes
from contextlib import ExitStack

import concourse.bass as bass
import concourse.mybir as mybir
from concourse.bass_core_rust import IndirectOffsetOnAxis
from concourse.bass_utils import run_bass_kernel_spmd

F32 = mybir.dt.float32
BF16 = mybir.dt.bfloat16
U32 = mybir.dt.uint32
AF = mybir.ActivationFunctionType
ALU = mybir.AluOpType
AX = mybir.AxisListType
BF = ml_dtypes.bfloat16

SCALE = 1.0 / float(np.sqrt(128.0))
LN_EPS = 1e-5
SENT_BASE = 1 << 20


class Cfg:
    def __init__(self, S=8192, D=2048, H=16, DL=512, E=8, DF=1408, CAP=640,
                 R=4, G=4, NCORES=8):
        self.S, self.D, self.H, self.DL, self.E, self.DF = S, D, H, DL, E, DF
        self.DH = 128
        self.CAP = CAP
        self.ES = 2 * CAP
        self.R, self.G, self.NCORES = R, G, NCORES
        self.B = NCORES // R
        self.TOK = S // R
        self.NT = self.TOK // 128
        self.NG = self.NT // G
        self.GQ = 128 * G
        self.FC = D // 128
        self.LC = DL // 128
        self.KCH = S // 128
        self.TC = self.TOK // 128
        self.FO = DF // 128
        self.CC = CAP // 128
        self.FS = D // 512
        assert self.NT % G == 0 and D % 512 == 0 and DL % 128 == 0
        assert DF % 128 == 0 and S % 512 == 0 and CAP % 128 == 0
        assert self.TOK % 512 == 0

    def kc_count(self, a):
        return self.R * (a * self.G + self.G)

    def col0(self, a, kc):
        jmin = 0
        for j in range(self.G):
            if kc > self.R * (a * self.G + j) + self.R - 1:
                jmin = j + 1
        return 128 * jmin

    def masked_j(self, a, kc):
        d = kc - self.R * a * self.G
        if d < 0:
            return None
        j = d // self.R
        return j if j < self.G else None


class TA:

    def __init__(self, handle, n, inner):
        self.t = handle
        self.n = n
        self.inner = inner

    def __getitem__(self, i):
        assert 0 <= i < self.n
        return self.t[:, self.inner * i:self.inner * (i + 1)]

    @property
    def full(self):
        return self.t[:, :]

    @property
    def r3(self):
        return self.t[:, :].rearrange("p (n i) -> p n i", n=self.n)


class _Op:
    __slots__ = ("eng", "fn", "deps", "chan", "cnt", "phase", "quantum")

    def __init__(self, eng, fn, deps, chan, quantum, phase):
        self.eng, self.fn, self.deps, self.chan = eng, fn, deps, chan
        self.quantum, self.phase = quantum, phase
        self.cnt = None


class Builder:
    ENGS = ("tensor", "vector", "scalar", "gpsimd", "sync")

    @staticmethod
    def q_eng(q):
        return "sync" if q.startswith("hw") else "gpsimd"

    def __init__(self, nc):
        self.nc = nc
        self.ops = []
        self.phase = 0
        self.chans = {}
        self.chan_count = {}

    def _chan(self, name):
        if name not in self.chans:
            self.chans[name] = self.nc.alloc_semaphore(name)
            self.chan_count[name] = 0
        return self.chans[name]

    def op(self, eng, fn, deps=()):
        deps = [d for d in deps if d is not None]
        o = _Op(eng, fn, deps, "c_" + eng, 1, self.phase)
        self.ops.append(o)
        return o

    def dma(self, q, fn, deps=()):
        eng = self.q_eng(q)
        deps = [d for d in deps if d is not None]
        o = _Op(eng, fn, deps, q, 16, self.phase)
        self.ops.append(o)
        return o

    def flush_phase(self, name=None):
        ops, self.ops = self.ops, []
        cur_phase = self.phase
        self.phase += 1
        if not ops:
            return
        need_inc = set()
        dma_chans = set()
        for o in ops:
            for d in o.deps:
                if d.phase == cur_phase:
                    need_inc.add(id(d))
            if o.quantum == 16:
                need_inc.add(id(o))
                dma_chans.add(o.chan)
        for o in ops:
            if id(o) in need_inc:
                self._chan(o.chan)
                self.chan_count[o.chan] += o.quantum
                o.cnt = self.chan_count[o.chan]
        drains = {}
        for ch in dma_chans:
            drains.setdefault(self.q_eng(ch), []).append(
                (ch, self.chan_count[ch]))
        by_eng = {e: [] for e in self.ENGS}
        for o in ops:
            by_eng[o.eng].append(o)
        builder = self

        def emit_for(eng_name):
            def run(eng):
                waited = {}
                for o in by_eng[eng_name]:
                    need = {}
                    for d in o.deps:
                        if d.phase != cur_phase:
                            continue
                        assert d.cnt is not None, "dep op was never counted"
                        if need.get(d.chan, -1) < d.cnt:
                            need[d.chan] = d.cnt
                    for ch, cnt in need.items():
                        if waited.get(ch, -1) >= cnt:
                            continue
                        eng.wait_ge(builder.chans[ch], cnt)
                        waited[ch] = cnt
                    instr = o.fn(eng)
                    if o.cnt is not None:
                        instr.then_inc(builder.chans[o.chan], o.quantum)
                for ch, tot in drains.get(eng_name, []):
                    if waited.get(ch, -1) < tot:
                        eng.wait_ge(builder.chans[ch], tot)
            return run

        import contextlib
        scope = self.nc.named_scope(name) if name else contextlib.nullcontext()
        with scope:
            with self.nc.Block(name) as blk:
                for e in self.ENGS:
                    if by_eng[e] or drains.get(e):
                        getattr(blk, e)(emit_for(e))


def build_program(c: Cfg, dbg_names=(), stop_after=None):
    import os
    p5skip = set(os.environ.get("P5SKIP", "").split(",")) - {""}

    nc = bass.Bass("TRN2", target_bir_lowering=False)
    P = Builder(nc)

    def din(name, shape):
        return nc.dram_tensor(name, list(shape), F32, kind="ExternalInput").ap()

    x_batch = din("x_batch", (c.S, c.D))
    x_own = din("x_own", (c.TOK, c.D))
    qg_row = din("qg_row", (1, c.TOK))
    wq_w = din("wq_w", (c.D, c.D)); wq_b = din("wq_b", (1, c.D))
    wk_w = din("wk_w", (c.D, c.D)); wk_b = din("wk_b", (1, c.D))
    wkl_w = din("wkl_w", (c.D, c.DL)); wkl_b = din("wkl_b", (1, c.DL))
    wv_w = din("wv_w", (c.D, c.D)); wv_b = din("wv_b", (1, c.D))
    wvl_w = din("wvl_w", (c.D, c.DL)); wvl_b = din("wvl_b", (1, c.DL))
    upk_w = din("upk_w", (c.H, c.DL, c.DH)); upk_b = din("upk_b", (c.H, c.DH))
    upv_w = din("upv_w", (c.H, c.DL, c.DH)); upv_b = din("upv_b", (c.H, c.DH))
    wo_w = din("wo_w", (c.D, c.D)); wo_b = din("wo_b", (1, c.D))
    gate_w = din("gate_w", (c.E, c.D))
    w_gate = din("w_gate", (c.E, c.D, c.DF))
    w_up = din("w_up", (c.E, c.D, c.DF))
    w_down = din("w_down", (c.E, c.DF, c.D))
    sw_gate = din("sw_gate", (c.D, c.DF))
    sw_up = din("sw_up", (c.D, c.DF))
    sw_down = din("sw_down", (c.DF, c.D))

    y_out = nc.dram_tensor("y", [c.TOK, c.D], F32, kind="ExternalOutput").ap()

    def dscr(name, shape, dt):
        return nc.dram_tensor(name, list(shape), dt, kind="Internal").ap()

    klat_dram = dscr("klat_dram", (c.LC, 128, c.S), BF16)
    vlat_dram = dscr("vlat_dram", (c.LC, 128, c.S), BF16)
    K_dram = dscr("K_dram", (c.H, 128, c.S), BF16)
    V_dram = dscr("V_dram", (c.H, c.KCH, 128, 128), BF16)
    Q_dram = dscr("Q_dram", (c.H, 128, c.TOK), BF16)
    O_dram = dscr("O_dram", (c.H, 128, c.TOK), BF16)
    resid_dram = dscr("resid_dram", (c.TOK, c.D), F32)
    xln2_dram = dscr("xln2_dram", (c.FC, 128, c.TOK), BF16)
    xln2tm_dram = dscr("xln2tm_dram", (c.TOK, c.D), BF16)
    xg_dram = dscr("xg_dram", (c.E * c.ES, c.D), BF16)
    backmap_dram = dscr("backmap_dram", (c.E * c.ES, 1), U32)
    gatew_dram = dscr("gatew_dram", (c.E * c.ES, 1), F32)
    rank_dram = dscr("rank_dram", (2 * c.TOK, c.D), F32)

    dbg_map = {
        "klat": klat_dram, "vlat": vlat_dram, "K": K_dram, "V": V_dram,
        "Q": Q_dram, "O": O_dram, "resid": resid_dram, "xln2": xln2_dram,
        "xg": xg_dram, "backmap": backmap_dram, "gatew": gatew_dram,
        "rank": rank_dram,
    }

    ident_bf = nc.inline_tensor(np.eye(128, dtype=BF), "ident_bf")
    ut_bf = nc.inline_tensor(np.triu(np.ones((128, 128), dtype=BF)), "ut_bf")
    eoff_dram = nc.inline_tensor(
        (np.arange(c.E, dtype=np.float32) * c.ES).reshape(1, c.E), "eoff")

    ctx = ExitStack()

    def sb(name, shape, dt):
        return ctx.enter_context(nc.sbuf_tensor(name, list(shape), dt))

    def sba(name, n, inner, dt, stack=None):
        st = stack or ctx
        h = st.enter_context(nc.sbuf_tensor(name, [128, n * inner], dt))
        return TA(h, n, inner)

    def psa(name, n, inner, dt=F32, stack=None):
        st = stack or ctx
        return [st.enter_context(nc.psum_tensor(f"{name}{i}", [128, inner], dt))[:, :]
                for i in range(n)]

    ones_col = sb("ones_col", (128, 1), BF16)
    ones_row = sb("ones_row", (1, 512), BF16)
    onesrow_f32 = sb("onesrow_f32", (1, 128), F32)
    ident_sb = sb("ident_sb", (128, 128), BF16)
    onef_sb = sb("onef_sb", (1, 1), BF16)
    kg_col = sb("kg_col", (128, 1), F32)
    ut_sb = sb("ut_sb", (128, 128), BF16)
    allones_sb = sb("allones_sb", (128, 128), BF16)
    eoffb_sb = sb("eoffb_sb", (128, c.E), F32)
    tokid_sb = sb("tokid_sb", (128, c.TC), U32)
    tokid2_sb = sb("tokid2_sb", (128, c.TC), U32)

    biasq_sb = sb("biasq_sb", (1, c.D), BF16)
    biask_sb = sb("biask_sb", (1, c.D), BF16)
    biasv_sb = sb("biasv_sb", (1, c.D), BF16)
    biaskl_sb = sb("biaskl_sb", (1, c.DL), F32)
    biasvl_sb = sb("biasvl_sb", (1, c.DL), F32)
    biasupk_sb = sb("biasupk_sb", (1, c.H * c.DH), BF16)
    biasupv_sb = sb("biasupv_sb", (1, c.H * c.DH), BF16)
    biaswo_sb = sb("biaswo_sb", (1, c.D), BF16)
    bkm_sb = sb("bkm_sb", (1, c.DL), BF16)
    bvm_sb = sb("bvm_sb", (1, c.DL), BF16)

    gwT_sb = sba("gwT_sb", c.FC, c.E, BF16)

    const_vals = [float(LN_EPS), float(SCALE), -1.0]
    const_sb = sba("const_sb", len(const_vals), 1, F32)
    for ci, cv in enumerate(const_vals):
        nc.const_aps.register(F32, cv, const_sb[ci])

    attn_ctx = ExitStack()
    nmask = c.NG * c.R * c.G
    mask_sb = sba("mask_sb", nmask, 128, BF16, attn_ctx)

    bc_disp_reg = nc.gpsimd.alloc_register("bc_disp")
    bc_rank_reg = nc.gpsimd.alloc_register("bc_rank")

    def mask_tile(a, kc):
        kcrel = kc - c.R * a * c.G
        return mask_sb[a * (c.R * c.G) + kcrel]

    with ExitStack() as pc:
        qg_sb = pc.enter_context(nc.sbuf_tensor("qg_sb", [1, c.TOK], F32))
        qgb_sb = pc.enter_context(nc.sbuf_tensor("qgb_sb", [128, c.TOK], F32))
        gaterow_sb = pc.enter_context(
            nc.sbuf_tensor("gaterow_sb", [c.E, c.D], BF16))
        bmi_sb = pc.enter_context(
            nc.sbuf_tensor("bmi_sb", [128, c.E * c.ES // 128], U32))
        qgb_ps = pc.enter_context(nc.psum_tensor("qgb_ps", [128, 512], F32))
        tp_ps = psa("p0tp_ps", 2, 128, BF16, pc)
        eob_ps = pc.enter_context(nc.psum_tensor("eob_ps", [128, c.E], F32))
        eoffr_sb = pc.enter_context(nc.sbuf_tensor("eoffr_sb", [1, c.E], F32))

        d_id = P.dma("hw", lambda e: e.dma_start(out=ident_sb[:], in_=ident_bf.ap()))
        d_ut = P.dma("hw", lambda e: e.dma_start(out=ut_sb[:], in_=ut_bf.ap()))
        d_eo = P.dma("hw", lambda e: e.dma_start(out=eoffr_sb[:], in_=eoff_dram.ap()))
        d_qg = P.dma("hw", lambda e: e.dma_start(out=qg_sb[:], in_=qg_row))
        hw_last = d_qg

        m_oc = P.op("gpsimd", lambda e: e.memset(ones_col[:], 1.0))
        m_or = P.op("gpsimd", lambda e: e.memset(ones_row[:], 1.0))
        m_of = P.op("gpsimd", lambda e: e.memset(onesrow_f32[:], 1.0))
        m_o1 = P.op("gpsimd", lambda e: e.memset(onef_sb[:], 1.0))
        m_ao = P.op("gpsimd", lambda e: e.memset(allones_sb[:], 1.0))
        P.op("gpsimd", lambda e: e.reg_mov(bc_disp_reg, c.E * c.ES - 1))
        P.op("gpsimd", lambda e: e.reg_mov(bc_rank_reg, 2 * c.TOK - 1))
        for ci, cv in enumerate(const_vals):
            P.op("gpsimd", (lambda ci, cv: lambda e: e.memset(const_sb[ci], cv))(ci, cv))
        m_kg = P.op("gpsimd", lambda e: e.iota(
            kg_col[:], [[1, 1]], base=0, channel_multiplier=1,
            allow_small_or_imprecise_dtypes=True))
        m_t1 = P.op("gpsimd", lambda e: e.iota(
            tokid_sb[:], [[128, c.TC]], base=0, channel_multiplier=1))
        m_t2 = P.op("gpsimd", lambda e: e.iota(
            tokid2_sb[:], [[128, c.TC]], base=c.TOK, channel_multiplier=1))
        m_bm = P.op("gpsimd", lambda e: e.iota(
            bmi_sb[:], [[1, c.E * c.ES // 128]], base=SENT_BASE,
            channel_multiplier=c.E * c.ES // 128))
        P.dma("sw", lambda e: e.dma_start(
            out=backmap_dram.flatten().rearrange("(p f) -> p f", p=128),
            in_=bmi_sb[:]), deps=[m_bm])

        bias_loads = {}
        for nm, dst, src in (("q", biasq_sb, wq_b), ("k", biask_sb, wk_b),
                             ("v", biasv_sb, wv_b), ("kl", biaskl_sb, wkl_b),
                             ("vl", biasvl_sb, wvl_b), ("wo", biaswo_sb, wo_b)):
            bias_loads[nm] = P.dma(
                "sw",
                (lambda d, s: lambda e: e.dma_start(out=d[:], in_=s))(dst, src))
        d_buk = P.dma("sw", lambda e: e.dma_start(
            out=biasupk_sb[:], in_=upk_b.flatten()[None, :]))
        d_buv = P.dma("sw", lambda e: e.dma_start(
            out=biasupv_sb[:], in_=upv_b.flatten()[None, :]))
        d_gw = P.dma("sw", lambda e: e.dma_start(out=gaterow_sb[:], in_=gate_w))
        sw_last = d_gw

        qgb_cp = []
        for gi in range(c.TOK // 512):
            mm = P.op("tensor", (lambda g: lambda e: e.matmul(
                qgb_ps[:], onesrow_f32[:], qg_sb[:, 512 * g:512 * (g + 1)],
                start=True, stop=True))(gi),
                deps=[hw_last, m_of] + ([qgb_cp[-1]] if qgb_cp else []))
            cp = P.op("vector", (lambda g: lambda e: e.tensor_copy(
                out=qgb_sb[:, 512 * g:512 * (g + 1)], in_=qgb_ps[:]))(gi),
                deps=[mm])
            qgb_cp.append(cp)

        mm_eo = P.op("tensor", lambda e: e.matmul(
            eob_ps[:], onesrow_f32[:], eoffr_sb[:], start=True, stop=True),
            deps=[hw_last, m_of, qgb_cp[-1]])
        P.op("vector", lambda e: e.tensor_copy(out=eoffb_sb[:], in_=eob_ps[:]),
             deps=[mm_eo])

        for a in range(c.NG):
            for kcrel in range(c.R * c.G):
                kc = c.R * a * c.G + kcrel
                j = kcrel // c.R
                colbase = a * c.GQ + 128 * j
                mi = a * (c.R * c.G) + kcrel
                P.op("vector", (lambda mi, cb, kc: lambda e: e.tensor_scalar(
                    out=mask_sb[mi], in0=qgb_sb[:, cb:cb + 128],
                    scalar1=kg_col[:], scalar2=float(128 * kc),
                    op0=ALU.subtract, op1=ALU.is_ge))(mi, colbase, kc),
                    deps=[qgb_cp[colbase // 512], m_kg])

        prev = None
        for f in range(c.FC):
            mmt = P.op("tensor", (lambda f: lambda e: e.transpose(
                tp_ps[f % 2][0:128, 0:c.E],
                gaterow_sb[:, 128 * f:128 * (f + 1)],
                ident_sb[0:c.E, 0:c.E]))(f),
                deps=[sw_last, hw_last, prev])
            prev = P.op("vector", (lambda f, mmt: lambda e: e.tensor_copy(
                out=gwT_sb[f],
                in_=tp_ps[f % 2][0:128, 0:c.E]))(f, mmt),
                deps=[mmt])

        P.flush_phase("p0")
    if stop_after == "p0":
        return nc

    def emit_ln(src_tile, dst_tile, lnscr, stats4, deps, scr_guard=None):
        s_sum = stats4[:, 0:1]
        s_sq = stats4[:, 1:2]
        s_negm = stats4[:, 2:3]
        s_rstd = stats4[:, 3:4]
        o1 = P.op("vector", lambda e: e.tensor_reduce(
            out=s_sum, in_=src_tile, axis=AX.X, op=ALU.add), deps=deps)
        o2 = P.op("scalar", lambda e: e.activation(
            out=lnscr, in_=src_tile, func=AF.Square, accum_out=s_sq),
            deps=list(deps) + [scr_guard])
        o3 = P.op("vector", lambda e: e.tensor_scalar(
            out=s_negm, in0=s_sum, scalar1=-1.0 / c.D, scalar2=None,
            op0=ALU.mult), deps=[o1])
        o4 = P.op("vector", lambda e: e.scalar_tensor_tensor(
            out=s_sum, in0=s_negm, scalar=-1.0, in1=s_negm,
            op0=ALU.mult, op1=ALU.mult), deps=[o3])
        o5 = P.op("vector", lambda e: e.tensor_scalar(
            out=s_sq, in0=s_sq, scalar1=1.0 / c.D, scalar2=None,
            op0=ALU.mult), deps=[o2])
        o6 = P.op("vector", lambda e: e.tensor_tensor(
            out=s_sq, in0=s_sq, in1=s_sum, op=ALU.add), deps=[o4, o5])
        o7 = P.op("scalar", lambda e: e.activation(
            out=s_sq, in_=s_sq, func=AF.Sqrt, bias=float(LN_EPS)), deps=[o6])
        o8 = P.op("vector", lambda e: e.reciprocal(out=s_rstd, in_=s_sq),
                  deps=[o7])
        o9 = P.op("vector", lambda e: e.tensor_scalar(
            out=dst_tile, in0=src_tile, scalar1=s_negm, scalar2=s_rstd,
            op0=ALU.add, op1=ALU.mult), deps=[o8, o3] + list(deps))
        return o9, o2

    p1_ctx = ExitStack()
    wkm_sb = sba("wkm_sb", c.FC, c.DL, BF16, p1_ctx)
    wvm_sb = sba("wvm_sb", c.FC, c.DL, BF16, p1_ctx)

    with ExitStack() as pc:
        wl_sb = [sba("wkl_sb", c.FC, c.DL, BF16, pc),
                 sba("wvl_sb", c.FC, c.DL, BF16, pc)]
        wrow_sb = sba("wrow_sb", 2, c.D, BF16, pc)
        wT_sb = sba("wT_sb", c.FC, 128, BF16, pc)
        bcol_sb = [sba("bcolk_sb", c.FC, 1, BF16, pc),
                   sba("bcolv_sb", c.FC, 1, BF16, pc)]
        tp_ps = psa("p1tp_ps", 2, 128, BF16, pc)
        bc_ps = psa("p1bc_ps", 2, 1, F32, pc)
        mm_ps = psa("p1mm_ps", 2, 512, F32, pc)
        bm_ps = pc.enter_context(nc.psum_tensor("p1bm_ps", [1, 512], F32))

        d_wl0 = P.dma("sw", lambda e: e.dma_start(
            out=wl_sb[0].r3,
            in_=wkl_w.rearrange("(cc p) l -> p cc l", p=128)))
        d_wl1 = P.dma("sw", lambda e: e.dma_start(
            out=wl_sb[1].r3,
            in_=wvl_w.rearrange("(cc p) l -> p cc l", p=128)))
        d_wl = [[d_wl0, d_wl1], [d_wl0, d_wl1]]
        bcol_ops = {}
        prev_ev = None
        for kv in range(2):
            brow = biask_sb if kv == 0 else biasv_sb
            for i in range(c.FC):
                n = kv * c.FC + i
                mm = P.op("tensor", (lambda brow, i, n: lambda e: e.matmul(
                    bc_ps[n % 2], brow[:, 128 * i:128 * (i + 1)],
                    onef_sb[:], start=True, stop=True))(brow, i, n),
                    deps=[bias_loads["k" if kv == 0 else "v"], m_o1, prev_ev])
                ev = P.op("vector", (lambda kv, i, n, mm: lambda e: e.tensor_copy(
                    out=bcol_sb[kv][i], in_=bc_ps[n % 2]))(kv, i, n, mm),
                    deps=[mm])
                bcol_ops[(kv, i)] = ev
                prev_ev = ev

        nlo = (c.DL + 511) // 512
        hist = {}
        for kv in range(2):
            w_src = wk_w if kv == 0 else wv_w
            dstw = wkm_sb if kv == 0 else wvm_sb
            for i in range(c.FC):
                it = kv * c.FC + i
                d_row = P.dma("sw_a" if it % 2 == 0 else "sw_b",
                    (lambda w_src, i, it: lambda e: e.dma_start(
                        out=wrow_sb[it % 2], in_=w_src[128 * i:128 * (i + 1), :]))(w_src, i, it),
                    deps=[hist.get(("tpdone", it - 2))])
                tevs = []
                for j in range(c.FC):
                    ntp = it * c.FC + j
                    mmt = P.op("tensor", (lambda it, j, ntp: lambda e: e.transpose(
                        tp_ps[ntp % 2], wrow_sb[it % 2][:, 128 * j:128 * (j + 1)],
                        ident_sb[:]))(it, j, ntp),
                        deps=[d_row, hist.get(("tpev", ntp - 2)),
                              hist.get(("wTuse", it - 1)) if j == 0 else None])
                    ev = P.op("vector", (lambda j, ntp, mmt: lambda e: e.tensor_copy(
                        out=wT_sb[j], in_=tp_ps[ntp % 2]))(j, ntp, mmt), deps=[mmt])
                    hist[("tpev", ntp)] = ev
                    tevs.append(ev)
                hist[("tpdone", it)] = tevs[-1]
                for lo in range(nlo):
                    lw = min(512, c.DL - 512 * lo)
                    nn = it * nlo + lo
                    mm = None
                    for j in range(c.FC):
                        mm = P.op("tensor",
                                  (lambda nn, j, lo, lw, kv: lambda e: e.matmul(
                                      mm_ps[nn % 2][:, 0:lw],
                                      wT_sb[j],
                                      wl_sb[kv][j][:, 512 * lo:512 * lo + lw],
                                      start=(j == 0), stop=(j == c.FC - 1)))(nn, j, lo, lw, kv),
                                  deps=(([tevs[-1], hist.get(("meev", nn - 2))] + d_wl[kv]) if j == 0 else []))
                    ev = P.op("vector", (lambda dstw, i, lo, lw, nn, mm: lambda e:
                              e.tensor_copy(
                                  out=dstw[i][:, 512 * lo:512 * lo + lw],
                                  in_=mm_ps[nn % 2][:, 0:lw]))(
                                      dstw, i, lo, lw, nn, mm),
                              deps=[mm])
                    hist[("meev", nn)] = ev
                    hist[("wTuse", it)] = mm

        for kv in range(2):
            browl = biaskl_sb if kv == 0 else biasvl_sb
            dst = bkm_sb if kv == 0 else bvm_sb
            for lo in range(nlo):
                lw = min(512, c.DL - 512 * lo)
                mm = None
                for j in range(c.FC):
                    mm = P.op("tensor", (lambda kv, j, lo, lw: lambda e: e.matmul(
                        bm_ps[0:1, 0:lw], bcol_sb[kv][j],
                        wl_sb[kv][j][:, 512 * lo:512 * lo + lw],
                        start=(j == 0), stop=(j == c.FC - 1)))(kv, j, lo, lw),
                        deps=[bcol_ops[(kv, j)],
                              hist.get(("bmev", kv, lo - 1)) if j == 0 else None,
                              hist.get(("bmev", kv - 1, nlo - 1)) if (j == 0 and lo == 0) else None]
                             + (d_wl[kv] if j == 0 else []))
                ev = P.op("vector", (lambda dst, browl, lo, lw, mm: lambda e:
                          e.tensor_tensor(out=dst[:, 512 * lo:512 * lo + lw],
                                          in0=bm_ps[0:1, 0:lw],
                                          in1=browl[:, 512 * lo:512 * lo + lw],
                                          op=ALU.add))(dst, browl, lo, lw, mm),
                          deps=[mm, bias_loads["kl" if kv == 0 else "vl"]])
                hist[("bmev", kv, lo)] = ev

        P.flush_phase("p1")
    if stop_after == "p1":
        return nc

    with ExitStack() as pc:
        xb_sb = [sba("xb0_sb", 4, c.D, F32, pc), sba("xb1_sb", 4, c.D, F32, pc)]
        lnscr = pc.enter_context(nc.sbuf_tensor("lnscr", [128, c.D], F32))
        lnst = pc.enter_context(nc.sbuf_tensor("lnst", [128, 16], F32))
        xlt_sb = sba("xlt_sb", 4, c.D, BF16, pc)
        xlf_sb = sba("xlf_sb", c.FC, 512, BF16, pc)
        lat_sb = sba("lat_sb", 2, 512, BF16, pc)
        tp_ps = psa("pbtp_ps", 2, 128, BF16, pc)
        kl_ps = psa("pbkl_ps", 2, 512, F32, pc)

        hist = {}
        prev_sq = None
        for g in range(c.S // 512):
            par = g % 2
            guard = [hist.get(("xbuse", g - 2))]
            d_xh = []
            for hf in range(2):
                d_xh.append(P.dma("hw_a" if hf == 0 else "hw_b",
                    (lambda g, hf, par: lambda e: e.dma_start(
                        out=xb_sb[par].t[:, c.D * 2 * hf:c.D * 2 * (hf + 1)].rearrange(
                            "p (t d) -> p t d", t=2),
                        in_=x_batch[512 * g + 256 * hf:512 * g + 256 * (hf + 1), :].rearrange(
                            "(t p) d -> p t d", p=128)))(g, hf, par),
                    deps=guard))
            tp_last = None
            ln_ops = []
            for t in range(4):
                ln, prev_sq = emit_ln(
                    xb_sb[par][t], xlt_sb[t], lnscr[:, :], lnst[:, 4 * t:4 * t + 4],
                    deps=[d_xh[t // 2],
                          hist.get(("xltuse", g - 1)) if t == 0 else None],
                    scr_guard=prev_sq)
                ln_ops.append(ln)
                if t == 3:
                    hist[("xbuse", g)] = ln
                for f in range(c.FC):
                    n = (g * 4 + t) * c.FC + f
                    mmt = P.op("tensor", (lambda t, f, n: lambda e: e.transpose(
                        tp_ps[n % 2], xlt_sb[t][:, 128 * f:128 * (f + 1)],
                        ident_sb[:]))(t, f, n),
                        deps=[ln, hist.get(("tpev", n - 2)),
                              hist.get(("xlfuse", g - 1)) if (t == 0 and f == 0) else None])
                    ev = P.op("vector", (lambda f, t, n, mmt: lambda e: e.tensor_copy(
                        out=xlf_sb[f][:, 128 * t:128 * (t + 1)], in_=tp_ps[n % 2]))(f, t, n, mmt),
                        deps=[mmt])
                    hist[("tpev", n)] = ev
                    tp_last = ev
            hist[("xltuse", g)] = tp_last
            last_mm = None
            for kv in range(2):
                wsrc = wkm_sb if kv == 0 else wvm_sb
                bsrc = bkm_sb if kv == 0 else bvm_sb
                ddst = klat_dram if kv == 0 else vlat_dram
                for lc in range(c.LC):
                    gn = g * 2 * c.LC + kv * c.LC + lc
                    pp = gn % 2
                    mm = None
                    for f in range(c.FC):
                        mm = P.op("tensor",
                                  (lambda pp, f, lc, wsrc: lambda e: e.matmul(
                                      kl_ps[pp], wsrc[f][:, 128 * lc:128 * (lc + 1)],
                                      xlf_sb[f], start=(f == 0), stop=False))(pp, f, lc, wsrc),
                                  deps=([tp_last, hist.get(("klev", gn - 2))]
                                        if f == 0 else []))
                    mmb = P.op("tensor", (lambda pp, lc, bsrc: lambda e: e.matmul(
                        kl_ps[pp], bsrc[:, 128 * lc:128 * (lc + 1)], ones_row[:],
                        start=False, stop=True))(pp, lc, bsrc), deps=[mm])
                    ev = P.op("vector", (lambda pp, mmb, gn: lambda e: e.tensor_copy(
                        out=lat_sb[gn % 2], in_=kl_ps[pp]))(pp, mmb, gn),
                        deps=[mmb, hist.get(("latst", gn - 2))])
                    st = P.dma("hw_s0" if gn % 2 == 0 else "hw_s1",
                        (lambda gn, ddst, lc, g: lambda e: e.dma_start(
                            out=ddst[lc][:, 512 * g:512 * (g + 1)], in_=lat_sb[gn % 2]))(gn, ddst, lc, g),
                        deps=[ev])
                    hist[("klev", gn)] = ev
                    hist[("latst", gn)] = st
                    last_mm = mm
            hist[("xlfuse", g)] = last_mm

        P.flush_phase("p1b")
    p1_ctx.close()
    if stop_after == "p1b":
        return nc

    with ExitStack() as pc:
        wq_sb = sba("wq_sb", c.FC, c.D, BF16, pc)
        xo_sb = [sba("xo0_sb", 4, c.D, F32, pc)]
        lnscr = pc.enter_context(nc.sbuf_tensor("lnscr_c", [128, c.D], F32))
        lnst = pc.enter_context(nc.sbuf_tensor("lnst_c", [128, 16], F32))
        xlt_sb = sba("xlt_c", 4, c.D, BF16, pc)
        xlf_sb = sba("xlf_c", c.FC, 512, BF16, pc)
        q_sb = sba("q_sb", 2, 512, BF16, pc)
        tp_ps = psa("pctp_ps", 2, 128, BF16, pc)
        q_ps = psa("pcq_ps", 2, 512, F32, pc)

        d_wq = P.dma("sw", lambda e: e.dma_start(
            out=wq_sb.r3, in_=wq_w.rearrange("(cc p) d -> p cc d", p=128)))

        hist = {}
        prev_sq = None
        for a in range(c.TOK // 512):
            par = 0
            guard = [hist.get(("xouse", a - 1))]
            d_xh = []
            for hf in range(2):
                d_xh.append(P.dma("hw_a" if hf == 0 else "hw_b",
                    (lambda a, hf, par: lambda e: e.dma_start(
                        out=xo_sb[par].t[:, c.D * 2 * hf:c.D * 2 * (hf + 1)].rearrange(
                            "p (t d) -> p t d", t=2),
                        in_=x_own[512 * a + 256 * hf:512 * a + 256 * (hf + 1), :].rearrange(
                            "(t p) d -> p t d", p=128)))(a, hf, par),
                    deps=guard))
            tp_last = None
            for t in range(4):
                ln, prev_sq = emit_ln(
                    xo_sb[par][t], xlt_sb[t], lnscr[:, :], lnst[:, 4 * t:4 * t + 4],
                    deps=[d_xh[t // 2],
                          hist.get(("xltuse", a - 1)) if t == 0 else None],
                    scr_guard=prev_sq)
                if t == 3:
                    hist[("xouse", a)] = ln
                for f in range(c.FC):
                    n = (a * 4 + t) * c.FC + f
                    mmt = P.op("tensor", (lambda t, f, n: lambda e: e.transpose(
                        tp_ps[n % 2], xlt_sb[t][:, 128 * f:128 * (f + 1)],
                        ident_sb[:]))(t, f, n),
                        deps=[ln, hist.get(("tpev", n - 2)),
                              hist.get(("xlfuse", a - 1)) if (t == 0 and f == 0) else None])
                    ev = P.op("vector", (lambda f, t, n, mmt: lambda e: e.tensor_copy(
                        out=xlf_sb[f][:, 128 * t:128 * (t + 1)], in_=tp_ps[n % 2]))(f, t, n, mmt),
                        deps=[mmt])
                    hist[("tpev", n)] = ev
                    tp_last = ev
            hist[("xltuse", a)] = tp_last
            last_mm = None
            for h in range(c.H):
                qn = a * c.H + h
                pp = qn % 2
                mm = None
                for f in range(c.FC):
                    mm = P.op("tensor", (lambda pp, f, h: lambda e: e.matmul(
                        q_ps[pp], wq_sb[f][:, 128 * h:128 * (h + 1)],
                        xlf_sb[f], start=(f == 0), stop=False))(pp, f, h),
                        deps=([tp_last, d_wq, hist.get(("qev", qn - 2))]
                              if f == 0 else []))
                mmb = P.op("tensor", (lambda pp, h: lambda e: e.matmul(
                    q_ps[pp], biasq_sb[:, 128 * h:128 * (h + 1)], ones_row[:],
                    start=False, stop=True))(pp, h), deps=[mm])
                ev = P.op("vector", (lambda pp, mmb, qn: lambda e: e.tensor_copy(
                    out=q_sb[qn % 2], in_=q_ps[pp]))(pp, mmb, qn),
                    deps=[mmb, hist.get(("qst", qn - 2))])
                st = P.dma("hw_s0" if qn % 2 == 0 else "hw_s1",
                    (lambda qn, h, a: lambda e: e.dma_start(
                        out=Q_dram[h][:, 512 * a:512 * (a + 1)], in_=q_sb[qn % 2]))(qn, h, a),
                    deps=[ev])
                hist[("qev", qn)] = ev
                hist[("qst", qn)] = st
                last_mm = mm
            hist[("xlfuse", a)] = last_mm

        P.flush_phase("p1c")
    if stop_after == "p1c":
        return nc

    with ExitStack() as pc:
        upk_sb = sba("upk_sb", c.H * c.LC, 128, BF16, pc)
        upv_sb = sba("upv_sb", c.H * c.LC, 128, BF16, pc)
        kl_sb = [sba("p2kl0", c.LC, 512, BF16, pc), sba("p2kl1", c.LC, 512, BF16, pc)]
        vl_sb = [sba("p2vl0", c.LC, 512, BF16, pc), sba("p2vl1", c.LC, 512, BF16, pc)]
        kp_sb = sba("kp_sb", 2, 512, BF16, pc)
        vp_sb = sba("vp_sb", 2, 4 * 128, BF16, pc)
        K_ps = psa("p2K_ps", 2, 512, F32, pc)
        V_ps = psa("p2V_ps", 2, 512, F32, pc)

        d_upk = P.dma("sw", lambda e: e.dma_start(
            out=upk_sb.t[:, :].rearrange("p (h cc d) -> p h cc d", h=c.H, cc=c.LC),
            in_=upk_w.rearrange("h (cc p) d -> p h cc d", p=128)))
        d_upv = P.dma("sw", lambda e: e.dma_start(
            out=upv_sb.t[:, :].rearrange("p (h cc d) -> p h cc d", h=c.H, cc=c.LC),
            in_=upv_w.rearrange("h (cc p) d -> p h cc d", p=128)))

        hist = {}
        for ks in range(c.S // 512):
            par = ks % 2
            d_kl = P.dma("hw_a" if par == 0 else "hw_b", (lambda ks, par: lambda e: e.dma_start(
                out=kl_sb[par].r3,
                in_=klat_dram[:, :, 512 * ks:512 * (ks + 1)].rearrange(
                    "cc p s -> p cc s")))(ks, par),
                deps=[hist.get(("kluse", ks - 2))])
            d_vl = P.dma("hw_a" if par == 0 else "hw_b", (lambda ks, par: lambda e: e.dma_start(
                out=vl_sb[par].r3,
                in_=vlat_dram[:, :, 512 * ks:512 * (ks + 1)].rearrange(
                    "cc p s -> p cc s")))(ks, par),
                deps=[hist.get(("vluse", ks - 2))])
            for h in range(c.H):
                kn = ks * c.H + h
                pk = kn % 2
                mm = None
                for lc in range(c.LC):
                    mm = P.op("tensor", (lambda pk, par, h, lc: lambda e: e.matmul(
                        K_ps[pk], upk_sb[h * c.LC + lc], kl_sb[par][lc],
                        start=(lc == 0), stop=False))(pk, par, h, lc),
                        deps=([d_kl, d_vl, d_upk, d_upv, hist.get(("kev", kn - 2))]
                              if lc == 0 else []))
                mmb = P.op("tensor", (lambda pk, h: lambda e: e.matmul(
                    K_ps[pk], biasupk_sb[:, 128 * h:128 * (h + 1)], ones_row[:],
                    start=False, stop=True))(pk, h), deps=[mm])
                ev = P.op("scalar", (lambda pk, mmb: lambda e: e.copy(
                    out=kp_sb[pk], in_=K_ps[pk]))(pk, mmb),
                    deps=[mmb, hist.get(("kst", kn - 2))])
                st = P.dma("hw_sk0" if pk == 0 else "hw_sk1",
                    (lambda pk, h, ks: lambda e: e.dma_start(
                        out=K_dram[h][:, 512 * ks:512 * (ks + 1)], in_=kp_sb[pk]))(pk, h, ks),
                    deps=[ev])
                hist[("kev", kn)] = ev
                hist[("kst", kn)] = st
                if h == c.H - 1:
                    hist[("kluse", ks)] = mm
                vevs = []
                mmv = None
                for t4 in range(4):
                    vn = kn * 4 + t4
                    for lc in range(c.LC):
                        mmv = P.op("tensor", (lambda pk, par, h, lc, t4: lambda e: e.matmul(
                            V_ps[pk][:, 128 * t4:128 * (t4 + 1)],
                            vl_sb[par][lc][:, 128 * t4:128 * (t4 + 1)],
                            upv_sb[h * c.LC + lc],
                            start=(lc == 0), stop=(lc == c.LC - 1)))(pk, par, h, lc, t4),
                            deps=([d_kl, d_vl, d_upk, d_upv, hist.get(("vev", kn - 2))]
                                  if (lc == 0 and t4 == 0) else []))
                    vevs.append(mmv)
                evv = P.op("vector", (lambda pk, mmv: lambda e: e.tensor_copy(
                    out=vp_sb[pk], in_=V_ps[pk]))(pk, mmv),
                    deps=vevs + [hist.get(("vst", kn - 2))])
                hist[("vev", kn)] = evv
                vevs = [evv]
                stv = P.dma("hw_sv0" if pk == 0 else "hw_sv1",
                    (lambda pk, h, ks: lambda e: e.dma_start(
                        out=V_dram[h, 4 * ks:4 * ks + 4].rearrange("t p d -> p t d"),
                        in_=vp_sb[pk].rearrange("p (t d) -> p t d", t=4)))(pk, h, ks),
                    deps=vevs)
                hist[("vst", kn)] = stv
                if h == c.H - 1:
                    hist[("vluse", ks)] = mmv

        P.flush_phase("p2")
    if stop_after == "p2":
        return nc

    with ExitStack() as pc:
        K_sb = [pc.enter_context(nc.sbuf_tensor(f"K{i}_sb", [128, c.S], BF16))
                for i in range(2)]
        V_sb = [pc.enter_context(nc.sbuf_tensor(f"V{i}_sb", [128, c.KCH * 128], BF16))
                for i in range(2)]
        Qh_sb = [pc.enter_context(nc.sbuf_tensor(f"Q{i}_sb", [128, c.TOK], BF16))
                 for i in range(2)]
        SPD = 2
        LAG = 4
        pT_sb = sba("pT_sb", SPD, 1024, BF16, pc)
        l_bf = pc.enter_context(nc.sbuf_tensor("l_bf", [1, 512], BF16))
        rec_sb = pc.enter_context(nc.sbuf_tensor("rec_sb", [1, 512], F32))
        bc_sb = pc.enter_context(nc.sbuf_tensor("bc_sb", [128, 512], F32))
        O_sb = sba("O_sb", 2, 512, BF16, pc)
        s_ps = psa("p3s_ps", SPD, 1024, F32, pc)
        o_ps = pc.enter_context(nc.psum_tensor("p3o_ps", [128, 512], F32))
        l_ps = pc.enter_context(nc.psum_tensor("p3l_ps", [1, 512], F32))
        bc_ps = pc.enter_context(nc.psum_tensor("p3bc_ps", [128, 512], F32))

        last_sps_rd = [None] * SPD
        last_pT_rd = [None] * SPD
        last_l_rd = None
        last_o_rd = None
        last_bc_rd = None
        hist = {}
        for h in range(c.H):
            par = h % 2
            hd = hist.get(("hdone", h - 2))
            hq = "hw_a" if par == 0 else "hw_b"
            d_K = P.dma(hq, (lambda h, par: lambda e: e.dma_start(
                out=K_sb[par][:, :], in_=K_dram[h]))(h, par), deps=[hd])
            d_V = P.dma(hq, (lambda h, par: lambda e: e.dma_start(
                out=V_sb[par][:, :].rearrange("p (k d) -> p k d", k=c.KCH),
                in_=V_dram[h].rearrange("k p d -> p k d")))(h, par),
                deps=[hd])
            d_Q = P.dma(hq, (lambda h, par: lambda e: e.dma_start(
                out=Qh_sb[par][:, :], in_=Q_dram[h]))(h, par), deps=[hd])
            d_KVQ = [d_K, d_V, d_Q]
            for a in range(c.NG):
                KC = c.kc_count(a)
                assert KC % 2 == 0
                sc_ops = [None] * KC
                ex_ops = [None] * (KC // 2)
                mk_ops = [None] * KC
                av_ops = [None] * KC
                sm = None

                def emit_scores(kc):
                    col0 = c.col0(a, kc)
                    pi = (kc // 2) % SPD
                    off = 512 * (kc % 2)
                    sc_ops[kc] = P.op("tensor",
                        (lambda pi, off, kc, col0, a, par: lambda e: e.matmul(
                            s_ps[pi][:, off + col0:off + 512],
                            K_sb[par][:, 128 * kc:128 * (kc + 1)],
                            Qh_sb[par][:, c.GQ * a + col0:c.GQ * (a + 1)],
                            start=True, stop=True))(pi, off, kc, col0, a, par),
                        deps=(d_KVQ + [last_sps_rd[pi]]) if kc % 2 == 0 else [])
                    if kc % 2 == 1:
                        col0p = c.col0(a, kc - 1)
                        ex_ops[kc // 2] = P.op("scalar",
                            (lambda pi, col0p: lambda e: e.activation(
                                out=pT_sb[pi][:, col0p:1024],
                                in_=s_ps[pi][:, col0p:1024],
                                func=AF.Exp, scale=SCALE))(pi, col0p),
                            deps=[sc_ops[kc], last_pT_rd[pi]])
                        last_sps_rd[pi] = ex_ops[kc // 2]
                        for k2 in (kc - 1, kc):
                            jm = c.masked_j(a, k2)
                            if jm is not None:
                                off2 = 512 * (k2 % 2)
                                mk_ops[k2] = P.op("vector",
                                    (lambda pi, a, k2, jm, off2: lambda e: e.tensor_tensor(
                                        out=pT_sb[pi][:, off2 + 128 * jm:off2 + 128 * (jm + 1)],
                                        in0=pT_sb[pi][:, off2 + 128 * jm:off2 + 128 * (jm + 1)],
                                        in1=mask_tile(a, k2), op=ALU.mult))(pi, a, k2, jm, off2),
                                    deps=[ex_ops[kc // 2]])

                def emit_smav(kc):
                    nonlocal sm, last_o_rd
                    col0 = c.col0(a, kc)
                    pi = (kc // 2) % SPD
                    off = 512 * (kc % 2)
                    pe_dep = mk_ops[kc] or ex_ops[kc // 2]
                    sm_ops = sm2 = P.op("tensor",
                        (lambda pi, off, kc, col0, KC: lambda e: e.matmul(
                            l_ps[0:1, col0:512], ones_col[:],
                            pT_sb[pi][:, off + col0:off + 512],
                            start=(kc == 0), stop=(kc == KC - 1)))(pi, off, kc, col0, KC),
                        deps=[pe_dep] + ([last_l_rd] if kc == 0 else []))
                    sm = sm2
                    av_ops[kc] = P.op("tensor",
                        (lambda pi, off, kc, col0, par: lambda e: e.matmul(
                            o_ps[:, col0:512], V_sb[par][:, 128 * kc:128 * (kc + 1)],
                            pT_sb[pi][:, off + col0:off + 512],
                            start=(kc == 0), stop=False))(pi, off, kc, col0, par),
                        deps=[pe_dep] + (d_KVQ + [last_o_rd] if kc == 0 else []))
                    if kc % 2 == 1:
                        last_pT_rd[pi] = av_ops[kc]

                for i in range(KC + LAG):
                    if i - LAG >= 0:
                        emit_smav(i - LAG)
                    if i < KC:
                        emit_scores(i)
                evl = P.op("vector", lambda e: e.tensor_copy(out=l_bf[:], in_=l_ps[:, :]),
                           deps=[sm])
                rc = P.op("vector", lambda e: e.reciprocal(out=rec_sb[:], in_=l_ps[:, :]),
                          deps=[sm, evl])
                last_l_rd = rc
                vb = P.op("tensor", (lambda h: lambda e: e.matmul(
                    o_ps[:, :], biasupv_sb[:, 128 * h:128 * (h + 1)], l_bf[:],
                    start=False, stop=True))(h), deps=[evl, av_ops[KC - 1]])
                bc = P.op("tensor", lambda e: e.matmul(
                    bc_ps[:, :], onesrow_f32[:], rec_sb[:], start=True, stop=True),
                    deps=[rc, last_bc_rd])
                bce = P.op("scalar", lambda e: e.copy(out=bc_sb[:], in_=bc_ps[:, :]),
                           deps=[bc])
                last_bc_rd = bce
                po = (h * c.NG + a) % 2
                onorm = P.op("vector", (lambda po: lambda e: e.tensor_tensor(
                    out=O_sb[po], in0=o_ps[:, :], in1=bc_sb[:], op=ALU.mult))(po),
                    deps=[vb, bce, hist.get(("ost", h * c.NG + a - 2))])
                last_o_rd = onorm
                ost = P.dma("hw_s0" if po == 0 else "hw_s1",
                    (lambda po, h, a: lambda e: e.dma_start(
                        out=O_dram[h][:, c.GQ * a:c.GQ * (a + 1)], in_=O_sb[po]))(po, h, a),
                    deps=[onorm])
                hist[("ost", h * c.NG + a)] = ost
                if a == c.NG - 1:
                    hist[("hdone", h)] = onorm

        P.flush_phase("p3")
    attn_ctx.close()
    if stop_after == "p3":
        return nc

    with ExitStack() as pc:
        wo_sb = sba("wo_sb", c.FC, c.D, BF16, pc)
        Otc_sb = [sba(f"Otc{i}_sb", c.H, 128, BF16, pc) for i in range(2)]
        xot_sb = sba("xot_sb", 2, c.D, F32, pc)
        resr_sb = [pc.enter_context(nc.sbuf_tensor(f"resr{i}_sb", [128, c.D], F32))
                   for i in range(2)]
        lnscr = pc.enter_context(nc.sbuf_tensor("lnscr4", [128, c.D], F32))
        lnst = pc.enter_context(nc.sbuf_tensor("lnst4", [128, 8], F32))
        xl2_sb = [pc.enter_context(nc.sbuf_tensor(f"xl2{i}_sb", [128, c.D], BF16))
                  for i in range(2)]
        zero_sb = pc.enter_context(nc.sbuf_tensor("zero_sb", [128, 512], F32))
        wo_ps = psa("p4wo_ps", 2, 512, F32, pc)

        d_wo = P.dma("sw", lambda e: e.dma_start(
            out=wo_sb.r3, in_=wo_w.rearrange("(cc p) d -> p cc d", p=128)))
        z_zero = P.op("gpsimd", lambda e: e.memset(zero_sb[:], 0.0))
        for i in range(2 * c.TOK // 128):
            for j in range(c.FS):
                P.dma("hw2", (lambda i, j: lambda e: e.dma_start(
                    out=rank_dram[128 * i:128 * (i + 1), 512 * j:512 * (j + 1)],
                    in_=zero_sb[:]))(i, j),
                    deps=[z_zero])

        hist = {}
        prev_sq = None
        for tc in range(c.TC):
            par = tc % 2
            d_O = P.dma("hw_a" if par == 0 else "hw_b", (lambda tc, par: lambda e: e.dma_start(
                out=Otc_sb[par].r3,
                in_=O_dram[:, :, 128 * tc:128 * (tc + 1)].rearrange(
                    "h p t -> p h t")))(tc, par),
                deps=[hist.get(("owomm", tc - 2))])
            d_xo = P.dma("hw_c" if par == 0 else "hw_d", (lambda tc, par: lambda e: e.dma_start(
                out=xot_sb.t[:, c.D * par:c.D * (par + 1)],
                in_=x_own[128 * tc:128 * (tc + 1), :]))(tc, par),
                deps=[hist.get(("resev", tc - 2))])
            res_evs = []
            mm = None
            for fs in range(c.FS):
                wn = tc * c.FS + fs
                pp = wn % 2
                mm = None
                for hh in range(c.H):
                    mm = P.op("tensor", (lambda pp, par, hh, fs: lambda e: e.matmul(
                        wo_ps[pp], Otc_sb[par][hh],
                        wo_sb[hh][:, 512 * fs:512 * (fs + 1)],
                        start=(hh == 0), stop=False))(pp, par, hh, fs),
                        deps=([d_O, d_wo, hist.get(("woev", wn - 2))]
                              if hh == 0 else []))
                mmb = P.op("tensor", (lambda pp, fs: lambda e: e.matmul(
                    wo_ps[pp], ones_row[:, 0:128],
                    biaswo_sb[:, 512 * fs:512 * (fs + 1)],
                    start=False, stop=True))(pp, fs), deps=[mm])
                ev = P.op("vector", (lambda pp, par, fs: lambda e: e.tensor_tensor(
                    out=resr_sb[par][:, 512 * fs:512 * (fs + 1)],
                    in0=wo_ps[pp], in1=xot_sb.t[:, c.D * par + 512 * fs:c.D * par + 512 * (fs + 1)],
                    op=ALU.add))(pp, par, fs),
                    deps=[mmb, d_xo,
                          hist.get(("resst", tc - 2)) if fs == 0 else None,
                          hist.get(("resuse", tc - 2)) if fs == 0 else None])
                hist[("woev", wn)] = ev
                res_evs.append(ev)
            hist[("owomm", tc)] = mm
            hist[("resev", tc)] = res_evs[-1]
            ln, prev_sq = emit_ln(
                resr_sb[par][:, :], xl2_sb[par][:, :], lnscr[:, :],
                lnst[:, 4 * par:4 * par + 4],
                deps=[res_evs[-1], hist.get(("xl2st", tc - 2))],
                scr_guard=prev_sq)
            hist[("resuse", tc)] = ln
            rst = P.dma("hw_s0" if par == 0 else "hw_s1",
                (lambda par, tc: lambda e: e.dma_start(
                    out=resid_dram[128 * tc:128 * (tc + 1), :], in_=resr_sb[par][:, :]))(par, tc),
                deps=[res_evs[-1]])
            hist[("resst", tc)] = rst
            x2st = P.dma("hw_s2" if par == 0 else "hw_s3",
                (lambda par, tc: lambda e: e.dma_start(
                    out=xln2tm_dram[128 * tc:128 * (tc + 1), :], in_=xl2_sb[par][:, :]))(par, tc),
                deps=[ln])
            hist[("xl2st", tc)] = x2st

        P.flush_phase("p4a")
    if stop_after == "p4a":
        return nc

    with ExitStack() as pc:
        xl2_sb = [pc.enter_context(nc.sbuf_tensor(f"xl2b{i}_sb", [128, c.D], BF16))
                  for i in range(2)]
        xlf2_sb = sba("xlf2_sb", c.FC, c.TOK, BF16, pc)
        lg_sb = pc.enter_context(nc.sbuf_tensor("lg_sb", [128, c.E], F32))
        m1_sb = pc.enter_context(nc.sbuf_tensor("m1_sb", [128, 1], F32))
        m2_sb = pc.enter_context(nc.sbuf_tensor("m2_sb", [128, 1], F32))
        epick_sb = pc.enter_context(nc.sbuf_tensor("epick_sb", [128, 1], F32))
        msk1_sb = pc.enter_context(nc.sbuf_tensor("msk1_sb", [128, c.E], F32))
        msk12_sb = pc.enter_context(nc.sbuf_tensor("msk12_sb", [128, c.E], F32))
        mskr2_sb = pc.enter_context(nc.sbuf_tensor("mskr2_sb", [128, c.E], F32))
        lgm_sb = pc.enter_context(nc.sbuf_tensor("lgm_sb", [128, c.E], F32))
        mb_sb = sba("mb_sb", c.TC, c.E, BF16, pc)
        cum_sb = pc.enter_context(nc.sbuf_tensor("cum_sb", [128, c.E], F32))
        pick_sb = pc.enter_context(nc.sbuf_tensor("pick_sb", [128, c.E], F32))
        dest_f = pc.enter_context(nc.sbuf_tensor("dest_f", [128, 2], F32))
        dest_u = [pc.enter_context(nc.sbuf_tensor(f"dest_u{i}", [128, c.TC], U32))
                  for i in range(2)]
        diffs_sb = pc.enter_context(nc.sbuf_tensor("diffs_sb", [128, c.TC], F32))
        w1_sb = pc.enter_context(nc.sbuf_tensor("w1_sb", [128, c.TC], F32))
        w2_sb = pc.enter_context(nc.sbuf_tensor("w2_sb", [128, c.TC], F32))
        tp_ps = psa("p4tp_ps", 2, 128, BF16, pc)
        r_ps = pc.enter_context(nc.psum_tensor("p4r_ps", [128, c.E], F32))
        cum_ps = pc.enter_context(nc.psum_tensor("p4cum_ps", [128, c.E], F32))

        hist = {}
        o_df = None
        for tc in range(c.TC):
            par = tc % 2
            d_x2 = P.dma("hw_a" if par == 0 else "hw_b", (lambda tc, par: lambda e: e.dma_start(
                out=xl2_sb[par][:, :],
                in_=xln2tm_dram[128 * tc:128 * (tc + 1), :]))(tc, par),
                deps=[hist.get(("xl2use", tc - 2))])
            tp_evs = []
            for f in range(c.FC):
                n = tc * c.FC + f
                mmt = P.op("tensor", (lambda par, f, n: lambda e: e.transpose(
                    tp_ps[n % 2], xl2_sb[par][:, 128 * f:128 * (f + 1)],
                    ident_sb[:]))(par, f, n),
                    deps=[d_x2, hist.get(("tpev", n - 2))])
                evt = P.op("vector", (lambda f, tc, n, mmt: lambda e: e.tensor_copy(
                    out=xlf2_sb[f][:, 128 * tc:128 * (tc + 1)], in_=tp_ps[n % 2]))(f, tc, n, mmt),
                    deps=[mmt])
                hist[("tpev", n)] = evt
                tp_evs.append(evt)
            mmr = None
            for f in range(c.FC):
                mmr = P.op("tensor", (lambda f, tc: lambda e: e.matmul(
                    r_ps[:, :], xlf2_sb[f][:, 128 * tc:128 * (tc + 1)], gwT_sb[f],
                    start=(f == 0), stop=(f == c.FC - 1)))(f, tc),
                    deps=[tp_evs[f]] + ([hist.get(("lgev", tc - 1))] if f == 0 else []))
            lgev = P.op("vector", lambda e: e.tensor_copy(out=lg_sb[:], in_=r_ps[:, :]),
                        deps=[mmr])
            hist[("lgev", tc)] = lgev
            o_m1 = P.op("vector", lambda e: e.tensor_reduce(
                out=m1_sb[:], in_=lg_sb[:], axis=AX.X, op=ALU.max), deps=[lgev])
            o_k1 = P.op("vector", lambda e: e.tensor_scalar(
                out=msk1_sb[:], in0=lg_sb[:], scalar1=m1_sb[:], scalar2=None,
                op0=ALU.is_ge), deps=[o_m1])
            o_lgm = P.op("vector", lambda e: e.scalar_tensor_tensor(
                out=lgm_sb[:], in0=msk1_sb[:], scalar=-1e30, in1=lg_sb[:],
                op0=ALU.mult, op1=ALU.add), deps=[o_k1])
            o_m2 = P.op("vector", lambda e: e.tensor_reduce(
                out=m2_sb[:], in_=lgm_sb[:], axis=AX.X, op=ALU.max), deps=[o_lgm])
            o_k12 = P.op("vector", lambda e: e.tensor_scalar(
                out=msk12_sb[:], in0=lg_sb[:], scalar1=m2_sb[:], scalar2=None,
                op0=ALU.is_ge), deps=[o_m2])
            o_kr2 = P.op("vector", lambda e: e.tensor_tensor(
                out=mskr2_sb[:], in0=msk12_sb[:], in1=msk1_sb[:],
                op=ALU.subtract), deps=[o_k12])
            o_df = P.op("vector", (lambda tc: lambda e: e.tensor_tensor(
                out=diffs_sb[:, tc:tc + 1], in0=m1_sb[:], in1=m2_sb[:],
                op=ALU.subtract))(tc), deps=[o_m2, o_kr2])
            o_mb = P.op("vector", (lambda tc: lambda e: e.tensor_copy(
                out=mb_sb[tc], in_=msk12_sb[:]))(tc), deps=[o_k12])
            hist[("mb", tc)] = o_mb
            mcs = P.op("tensor", (lambda tc: lambda e: e.matmul(
                cum_ps[:, :], ut_sb[:], mb_sb[tc], start=True,
                stop=(tc == 0)))(tc),
                deps=[o_mb, d_ut, hist.get(("cumread", tc - 1))])
            for tcp in range(tc):
                mcs = P.op("tensor", (lambda tcp, tc: lambda e: e.matmul(
                    cum_ps[:, :], allones_sb[:], mb_sb[tcp], start=False,
                    stop=(tcp == tc - 1)))(tcp, tc),
                    deps=[mcs, hist[("mb", tcp)]])
            cumev = P.op("vector", lambda e: e.tensor_copy(out=cum_sb[:], in_=cum_ps[:, :]),
                         deps=[mcs])
            hist[("cumread", tc)] = cumev
            dsts = []
            prev_chain = cumev
            for rk, mm_sel in ((0, msk1_sb), (1, mskr2_sb)):
                o_p = P.op("vector", (lambda mm_sel: lambda e: e.tensor_tensor(
                    out=pick_sb[:], in0=cum_sb[:], in1=mm_sel[:], op=ALU.mult))(mm_sel),
                    deps=[prev_chain, o_kr2])
                o_s = P.op("vector", (lambda rk: lambda e: e.tensor_reduce(
                    out=dest_f[:, rk:rk + 1], in_=pick_sb[:], axis=AX.X,
                    op=ALU.add))(rk), deps=[o_p])
                o_c = P.op("vector", (lambda rk: lambda e: e.tensor_scalar(
                    out=dest_f[:, rk:rk + 1], in0=dest_f[:, rk:rk + 1],
                    scalar1=-1.0, scalar2=float(c.ES - 1),
                    op0=ALU.add, op1=ALU.min))(rk), deps=[o_s])
                o_pe = P.op("vector", (lambda mm_sel: lambda e: e.tensor_tensor(
                    out=pick_sb[:], in0=eoffb_sb[:], in1=mm_sel[:], op=ALU.mult))(mm_sel),
                    deps=[o_c])
                o_se = P.op("vector", lambda e: e.tensor_reduce(
                    out=epick_sb[:], in_=pick_sb[:], axis=AX.X, op=ALU.add),
                    deps=[o_pe])
                o_ad = P.op("vector", (lambda rk: lambda e: e.tensor_tensor(
                    out=dest_f[:, rk:rk + 1], in0=dest_f[:, rk:rk + 1],
                    in1=epick_sb[:], op=ALU.add))(rk), deps=[o_se])
                o_u = P.op("vector", (lambda rk, tc: lambda e: e.tensor_copy(
                    out=dest_u[rk][:, tc:tc + 1], in_=dest_f[:, rk:rk + 1]))(rk, tc),
                    deps=[o_ad])
                dsts.append(o_u)
                prev_chain = o_u
            sc_last = None
            for rk, tsrc in ((0, tokid_sb), (1, tokid2_sb)):
                s1 = P.dma("sw_a" if par == 0 else "sw_b",
                    (lambda rk, par, tc: lambda e: e.indirect_dma_start(
                    out=xg_dram,
                    out_offset=IndirectOffsetOnAxis(dest_u[rk][:, tc:tc + 1], 0),
                    in_=xl2_sb[par][:, :], in_offset=None,
                    bounds_check=bc_disp_reg, oob_is_err=False))(rk, par, tc),
                    deps=[dsts[rk], d_x2])
                s2 = P.dma("sw_a" if par == 0 else "sw_b",
                    (lambda rk, tsrc, tc: lambda e: e.indirect_dma_start(
                    out=backmap_dram,
                    out_offset=IndirectOffsetOnAxis(dest_u[rk][:, tc:tc + 1], 0),
                    in_=tsrc[:, tc:tc + 1], in_offset=None,
                    bounds_check=bc_disp_reg, oob_is_err=False))(rk, tsrc, tc),
                    deps=[dsts[rk]])
                sc_last = s2
            hist[("dispdone", tc)] = sc_last
            hist[("xl2use", tc)] = sc_last

        sg1 = P.op("scalar", lambda e: e.activation(
            out=w1_sb[:], in_=diffs_sb[:], func=AF.Sigmoid), deps=[o_df])
        sg2 = P.op("scalar", lambda e: e.activation(
            out=w2_sb[:], in_=diffs_sb[:], func=AF.Sigmoid, scale=-1.0),
            deps=[o_df])
        for tc in range(c.TC):
            for rk, wsrc, sdep in ((0, w1_sb, sg1), (1, w2_sb, sg2)):
                P.dma("sw_g0" if tc % 2 == 0 else "sw_g1",
                    (lambda rk, wsrc, tc: lambda e: e.indirect_dma_start(
                    out=gatew_dram,
                    out_offset=IndirectOffsetOnAxis(dest_u[rk][:, tc:tc + 1], 0),
                    in_=wsrc[:, tc:tc + 1], in_offset=None,
                    bounds_check=bc_disp_reg, oob_is_err=False))(rk, wsrc, tc),
                    deps=[sdep, hist[("dispdone", tc)]])
        P.dma("hw", lambda e: e.dma_start(
            out=xln2_dram.rearrange("cc p t -> p cc t"), in_=xlf2_sb.r3),
            deps=[hist[("tpev", c.TC * c.FC - 1)]])

        P.flush_phase("p4b")
    if stop_after == "p4b":
        return nc

    with ExitStack() as pc:
        xg_sb = [sba("xg0_sb", c.CC, c.D, BF16, pc)]
        xgf_sb = [sba("xgf0_sb", c.FC, c.CAP, BF16, pc),
                  sba("xgf1_sb", c.FC, c.CAP, BF16, pc)]
        bmc_sb = [sba(f"bmc{i}_sb", c.CC, 1, U32, pc) for i in range(2)]
        gwc_sb = [sba(f"gwc{i}_sb", c.CC, 1, F32, pc) for i in range(2)]
        wg_sb = [sba(f"wg{i}_sb", c.FC, 128, BF16, pc) for i in range(2)]
        wu_sb = [sba(f"wu{i}_sb", c.FC, 128, BF16, pc) for i in range(2)]
        wd_sb = [sba("wd0_sb", c.FO, c.D, BF16, pc)]
        gs_sb = pc.enter_context(nc.sbuf_tensor("gs_sb", [128, c.CAP], BF16))
        hfm_sb = [sba(f"hfm{i}_sb", c.FO, c.CAP, BF16, pc) for i in range(2)]
        dn_sb = sba("dn_sb", 2, c.D, F32, pc)
        tp_ps = psa("p5tp_ps", 2, 128, BF16, pc)
        g_ps = pc.enter_context(nc.psum_tensor("p5g_ps", [128, 512], F32))
        g2_ps = pc.enter_context(nc.psum_tensor("p5g2_ps", [128, 128], F32))
        u_ps = pc.enter_context(nc.psum_tensor("p5u_ps", [128, 512], F32))
        u2_ps = pc.enter_context(nc.psum_tensor("p5u2_ps", [128, 128], F32))
        d_ps = psa("p5d_ps", 2, 512, F32, pc)

        ngrp = [(0, min(512, c.CAP))]
        if c.CAP > 512:
            ngrp.append((512, c.CAP - 512))

        hist = {}
        for ei in range(c.E):
            par = ei % 2
            hq5 = "hw_a" if par == 0 else "hw_b"
            d_xg = P.dma(hq5, (lambda ei: lambda e: e.dma_start(
                out=xg_sb[0].r3,
                in_=xg_dram[c.ES * ei:c.ES * ei + c.CAP, :].rearrange(
                    "(t p) d -> p t d", p=128)))(ei),
                deps=[hist.get(("xguse", ei - 1))])
            d_bm = None if "bmgw" in p5skip else P.dma(hq5, (lambda ei, par: lambda e: e.dma_start(
                out=bmc_sb[par].r3,
                in_=backmap_dram[c.ES * ei:c.ES * ei + c.CAP].rearrange(
                    "(t p) one -> p t one", p=128),
                allow_slow_non_contiguous=True))(ei, par),
                deps=(hist.get(("bmuse", ei - 2)) or []))
            d_gw = None if "bmgw" in p5skip else P.dma(hq5, (lambda ei, par: lambda e: e.dma_start(
                out=gwc_sb[par].r3,
                in_=gatew_dram[c.ES * ei:c.ES * ei + c.CAP].rearrange(
                    "(t p) one -> p t one", p=128),
                allow_slow_non_contiguous=True))(ei, par),
                deps=[hist.get(("gwuse", ei - 2))])
            ld3 = [d_xg, d_bm, d_gw]
            tp_last = None
            for t in range(c.CC):
                for f in range(c.FC):
                    n = (ei * c.CC + t) * c.FC + f
                    mmt = P.op("tensor", (lambda t, f, n: lambda e: e.transpose(
                        tp_ps[n % 2], xg_sb[0][t][:, 128 * f:128 * (f + 1)],
                        ident_sb[:]))(t, f, n),
                        deps=ld3 + [hist.get(("p5tp", n - 2)),
                              hist.get(("xgfuse", ei - 2)) if (t == 0 and f == 0) else None])
                    evt = P.op("vector", (lambda par, t, f, n, mmt: lambda e: e.tensor_copy(
                        out=xgf_sb[par][f][:, 128 * t:128 * (t + 1)], in_=tp_ps[n % 2]))(par, t, f, n, mmt),
                        deps=[mmt])
                    hist[("p5tp", n)] = evt
                    tp_last = evt
            hist[("xguse", ei)] = tp_last
            for fo in range(c.FO):
                gfo = ei * c.FO + fo
                pw = gfo % 2
                d_wg = None if "wld" in p5skip else P.dma("sw_a" if pw == 0 else "sw_b",
                    (lambda ei, fo, pw: lambda e: e.dma_start(
                    out=wg_sb[pw].r3,
                    in_=w_gate[ei][:, 128 * fo:128 * (fo + 1)].rearrange(
                        "(cc p) k -> p cc k", p=128)))(ei, fo, pw),
                    deps=[hist.get(("wguse", gfo - 2))])
                d_wu = None if "wld" in p5skip else P.dma("sw_a" if pw == 0 else "sw_b",
                    (lambda ei, fo, pw: lambda e: e.dma_start(
                    out=wu_sb[pw].r3,
                    in_=w_up[ei][:, 128 * fo:128 * (fo + 1)].rearrange(
                        "(cc p) k -> p cc k", p=128)))(ei, fo, pw),
                    deps=[hist.get(("wguse", gfo - 2))])
                hlast = None
                mmu = None
                for (cg0, cgw) in ngrp:
                    gp = g_ps if cgw > 128 else g2_ps
                    up = u_ps if cgw > 128 else u2_ps
                    mmg = mmu = None
                    for f in range(c.FC):
                        mmg = None if "gu" in p5skip else P.op("tensor", (lambda pw, f, cg0, cgw, par, gp: lambda e: e.matmul(
                            gp[:, 0:cgw], wg_sb[pw][f],
                            xgf_sb[par][f][:, cg0:cg0 + cgw],
                            start=(f == 0), stop=(f == c.FC - 1)))(pw, f, cg0, cgw, par, gp),
                            deps=([d_wg, d_wu, tp_last, hist.get(("gev", gfo - 1, cg0))]
                                  if f == 0 else []))
                        mmu = None if "gu" in p5skip else P.op("tensor", (lambda pw, f, cg0, cgw, par, up: lambda e: e.matmul(
                            up[:, 0:cgw], wu_sb[pw][f],
                            xgf_sb[par][f][:, cg0:cg0 + cgw],
                            start=(f == 0), stop=(f == c.FC - 1)))(pw, f, cg0, cgw, par, up),
                            deps=([d_wg, d_wu, tp_last, hist.get(("uev", gfo - 1, cg0))]
                                  if f == 0 else []))
                    sil = None if "gu" in p5skip else P.op("scalar", (lambda cg0, cgw, gp: lambda e: e.activation(
                        out=gs_sb[:, cg0:cg0 + cgw], in_=gp[:, 0:cgw],
                        func=AF.Silu))(cg0, cgw, gp),
                        deps=[mmg, hist.get(("huse", gfo - 1, cg0))])
                    hev = None if "gu" in p5skip else P.op("vector", (lambda par, fo, cg0, cgw, up, sil: lambda e: e.tensor_tensor(
                        out=hfm_sb[par][fo][:, cg0:cg0 + cgw],
                        in0=gs_sb[:, cg0:cg0 + cgw], in1=up[:, 0:cgw],
                        op=ALU.mult))(par, fo, cg0, cgw, up, sil),
                        deps=[sil, mmu,
                              hist.get(("hread", ei - 2)) if (fo == 0 and cg0 == 0) else None])
                    hist[("gev", gfo, cg0)] = sil
                    hist[("uev", gfo, cg0)] = hev
                    hist[("huse", gfo, cg0)] = hev
                    hlast = hev
                hist[("wguse", gfo)] = mmu
                hist[("hev", ei, fo)] = hlast
            d_wd = None if "wld" in p5skip else P.dma("sw_c", (lambda ei: lambda e: e.dma_start(
                out=wd_sb[0].r3,
                in_=w_down[ei].rearrange("(cc p) d -> p cc d", p=128)))(ei),
                deps=[hist.get(("wduse", ei - 1))])
            hdeps = [hist[("hev", ei, fo)] for fo in range(c.FO)]
            last_mmd = None
            sts = []
            for t in range(c.CC):
                gt = ei * c.CC + t
                dnb = gt % 2
                evs = []
                for fs in range(c.FS):
                    dnn = gt * c.FS + fs
                    pd = dnn % 2
                    mmd = None
                    for fo in range(c.FO):
                        mmd = None if "down" in p5skip else P.op("tensor", (lambda pd, par, t, fo, fs: lambda e: e.matmul(
                            d_ps[pd], hfm_sb[par][fo][:, 128 * t:128 * (t + 1)],
                            wd_sb[0][fo][:, 512 * fs:512 * (fs + 1)],
                            start=(fo == 0), stop=(fo == c.FO - 1)))(pd, par, t, fo, fs),
                            deps=(([d_wd] + hdeps +
                                   [hist.get(("dnevn", dnn - 2))]) if fo == 0 else []))
                    evfn = (lambda pd, dnb, par, t, fs: lambda e: e.tensor_scalar(
                        out=dn_sb[dnb][:, 512 * fs:512 * (fs + 1)], in0=d_ps[pd],
                        scalar1=gwc_sb[par][t], scalar2=None, op0=ALU.mult))(pd, dnb, par, t, fs)
                    ev = None if ("down" in p5skip or "dnev" in p5skip) else P.op(
                        "vector", evfn,
                        deps=[mmd, hist.get(("dnst", gt - 2))] +
                             ([] if "dnnold3" in p5skip else ld3))
                    hist[("dnevn", dnn)] = ev
                    evs.append(ev)
                    last_mmd = mmd
                st = None if "scat" in p5skip else P.dma("sw_e" if dnb == 0 else "sw_f",
                    (lambda dnb, par, t: lambda e: e.indirect_dma_start(
                    out=rank_dram,
                    out_offset=IndirectOffsetOnAxis(bmc_sb[par][t], 0),
                    in_=dn_sb[dnb], in_offset=None,
                    bounds_check=bc_rank_reg, oob_is_err=False))(dnb, par, t),
                    deps=evs + ld3)
                hist[("dnst", gt)] = st
                sts.append(st)
            hist[("bmuse", ei)] = sts[-2:]
            hist[("gwuse", ei)] = hist[("dnevn", (ei * c.CC + c.CC) * c.FS - 1)]
            hist[("xgfuse", ei)] = last_mmd
            hist[("hread", ei)] = last_mmd
            hist[("wduse", ei)] = last_mmd
            if stop_after == f"p5e{ei}":
                break

        P.flush_phase("p5")
    if stop_after == "p5":
        return nc

    with ExitStack() as pc:
        TH = c.TOK // 2
        GW = min(512, TH)
        NCG = TH // GW
        TCH = TH // 128
        xlf2_sb = sba("xlf2b_sb", c.FC, TH, BF16, pc)
        wg_sb = [sba(f"swg{i}_sb", c.FC, 128, BF16, pc) for i in range(2)]
        wu_sb = [sba(f"swu{i}_sb", c.FC, 128, BF16, pc) for i in range(2)]
        gs_sb = pc.enter_context(nc.sbuf_tensor("bgs_sb", [128, TH], BF16))
        hfm_sb = sba("bhfm_sb", c.FO, TH, BF16, pc)
        wd_sb = sba("bwd_sb", c.FO, c.D, BF16, pc)
        res_sb = sba("bres_sb", 2, c.D, F32, pc)
        rk0_sb = sba("brk0_sb", 2, c.D, F32, pc)
        rk1_sb = sba("brk1_sb", 2, c.D, F32, pc)
        g_ps = psa("bg_ps", 2, 512, F32, pc)
        u_ps = psa("bu_ps", 2, 512, F32, pc)
        d_ps = psa("bd_ps", 2, 512, F32, pc)

        d_wd = P.dma("sw", lambda e: e.dma_start(
            out=wd_sb.r3, in_=sw_down.rearrange("(cc p) d -> p cc d", p=128)))

        hist = {}
        out_sts = []
        for half in range(2):
            hoff = half * TH
            d_x2 = P.dma("hw_x0" if half == 0 else "hw_x1",
                (lambda hoff: lambda e: e.dma_start(
                    out=xlf2_sb.r3,
                    in_=xln2_dram[:, :, hoff:hoff + TH].rearrange(
                        "cc p t -> p cc t")))(hoff),
                deps=[hist.get(("x2use", half - 1))])
            for fo in range(c.FO):
                gfo = half * c.FO + fo
                pw = gfo % 2
                d_wg = P.dma("sw_a" if pw == 0 else "sw_b",
                    (lambda fo, pw: lambda e: e.dma_start(
                        out=wg_sb[pw].r3,
                        in_=sw_gate[:, 128 * fo:128 * (fo + 1)].rearrange(
                            "(cc p) k -> p cc k", p=128)))(fo, pw),
                    deps=[hist.get(("wguse", gfo - 2))])
                d_wu = P.dma("sw_a" if pw == 0 else "sw_b",
                    (lambda fo, pw: lambda e: e.dma_start(
                        out=wu_sb[pw].r3,
                        in_=sw_up[:, 128 * fo:128 * (fo + 1)].rearrange(
                            "(cc p) k -> p cc k", p=128)))(fo, pw),
                    deps=[hist.get(("wguse", gfo - 2))])
                mmu = None
                hev = None
                for cg in range(NCG):
                    gn = gfo * NCG + cg
                    pg = gn % 2
                    mmg = mmu = None
                    for f in range(c.FC):
                        mmg = P.op("tensor", (lambda pw, pg, f, cg: lambda e: e.matmul(
                            g_ps[pg][:, 0:GW], wg_sb[pw][f],
                            xlf2_sb[f][:, GW * cg:GW * (cg + 1)],
                            start=(f == 0), stop=(f == c.FC - 1)))(pw, pg, f, cg),
                            deps=([d_wg, d_wu, d_x2, hist.get(("gev", gn - 2))]
                                  if f == 0 else []))
                        mmu = P.op("tensor", (lambda pw, pg, f, cg: lambda e: e.matmul(
                            u_ps[pg][:, 0:GW], wu_sb[pw][f],
                            xlf2_sb[f][:, GW * cg:GW * (cg + 1)],
                            start=(f == 0), stop=(f == c.FC - 1)))(pw, pg, f, cg),
                            deps=([d_wg, d_wu, d_x2, hist.get(("uev", gn - 2))]
                                  if f == 0 else []))
                    sil = P.op("scalar", (lambda pg, cg: lambda e: e.activation(
                        out=gs_sb[:, GW * cg:GW * (cg + 1)], in_=g_ps[pg][:, 0:GW],
                        func=AF.Silu))(pg, cg),
                        deps=[mmg, hist.get(("huse", gn - NCG))])
                    hev = P.op("vector", (lambda pg, fo, cg, sil: lambda e: e.tensor_tensor(
                        out=hfm_sb[fo][:, GW * cg:GW * (cg + 1)],
                        in0=gs_sb[:, GW * cg:GW * (cg + 1)], in1=u_ps[pg][:, 0:GW],
                        op=ALU.mult))(pg, fo, cg, sil),
                        deps=[sil, mmu,
                              hist.get(("hread", half - 1)) if cg == 0 else None])
                    hist[("gev", gn)] = sil
                    hist[("uev", gn)] = hev
                    hist[("huse", gn)] = hev
                hist[("wguse", gfo)] = mmu
                hist[("hev", half, fo)] = hev

            hdeps = [hist[("hev", half, fo)] for fo in range(c.FO)]
            last_mmd = None
            for tch in range(TCH):
                tc = half * TCH + tch
                par = tc % 2
                gd = hist.get(("outst", tc - 2))
                hqb = "hw_a" if par == 0 else "hw_b"
                d_rs = P.dma(hqb, (lambda tc, par: lambda e: e.dma_start(
                    out=res_sb[par], in_=resid_dram[128 * tc:128 * (tc + 1), :]))(tc, par),
                    deps=[gd])
                d_r0 = P.dma(hqb, (lambda tc, par: lambda e: e.dma_start(
                    out=rk0_sb[par], in_=rank_dram[128 * tc:128 * (tc + 1), :]))(tc, par),
                    deps=[gd])
                d_r1 = P.dma(hqb, (lambda tc, par: lambda e: e.dma_start(
                    out=rk1_sb[par],
                    in_=rank_dram[c.TOK + 128 * tc:c.TOK + 128 * (tc + 1), :]))(tc, par),
                    deps=[gd])
                d_rr = [d_rs, d_r0, d_r1]
                cmb_ops = []
                for fs in range(c.FS):
                    dnn = tc * c.FS + fs
                    pd = dnn % 2
                    mmd = None
                    for fo in range(c.FO):
                        mmd = P.op("tensor", (lambda pd, tch, fo, fs: lambda e: e.matmul(
                            d_ps[pd], hfm_sb[fo][:, 128 * tch:128 * (tch + 1)],
                            wd_sb[fo][:, 512 * fs:512 * (fs + 1)],
                            start=(fo == 0), stop=(fo == c.FO - 1)))(pd, tch, fo, fs),
                            deps=(([d_wd] + hdeps + [hist.get(("cmbev", dnn - 2))])
                                  if fo == 0 else []))
                    c1 = P.op("vector", (lambda pd, par, fs, mmd: lambda e: e.tensor_tensor(
                        out=res_sb[par][:, 512 * fs:512 * (fs + 1)], in0=d_ps[pd],
                        in1=res_sb[par][:, 512 * fs:512 * (fs + 1)], op=ALU.add))(pd, par, fs, mmd),
                        deps=[mmd, hist.get(("outst", tc - 2)) if fs == 0 else None] + d_rr)
                    c2 = P.op("vector", (lambda par, fs, c1: lambda e: e.tensor_tensor(
                        out=res_sb[par][:, 512 * fs:512 * (fs + 1)],
                        in0=res_sb[par][:, 512 * fs:512 * (fs + 1)],
                        in1=rk0_sb[par][:, 512 * fs:512 * (fs + 1)], op=ALU.add))(par, fs, c1),
                        deps=[c1])
                    c3 = P.op("vector", (lambda par, fs, c2: lambda e: e.tensor_tensor(
                        out=res_sb[par][:, 512 * fs:512 * (fs + 1)],
                        in0=res_sb[par][:, 512 * fs:512 * (fs + 1)],
                        in1=rk1_sb[par][:, 512 * fs:512 * (fs + 1)], op=ALU.add))(par, fs, c2),
                        deps=[c2])
                    hist[("cmbev", dnn)] = c3
                    cmb_ops.append(c3)
                    last_mmd = mmd
                hist[("cmb", tc)] = cmb_ops[-1]
                st = P.dma("hw_s0" if par == 0 else "hw_s1",
                    (lambda par, tc: lambda e: e.dma_start(
                        out=y_out[128 * tc:128 * (tc + 1), :], in_=res_sb[par]))(par, tc),
                    deps=cmb_ops)
                hist[("outst", tc)] = st
                out_sts.append(st)
            hist[("x2use", half)] = last_mmd
            hist[("hread", half)] = last_mmd

        for nm in dbg_names:
            dsrc = dbg_map[nm]
            t = nc.dram_tensor("dbg_" + nm, list(dsrc.shape), dsrc.dtype,
                               kind="ExternalOutput").ap()
            out_sts.append(P.dma("hw", (lambda t, dsrc: lambda e: e.dma_start(
                out=t.flatten()[None, :], in_=dsrc.flatten()[None, :]))(t, dsrc)))

        P.flush_phase("p5b")

    ctx.close()
    return nc



def _shard(cfg, x_full):
    c = cfg
    per_core = []
    for cid in range(c.NCORES):
        b = cid // c.R
        r = cid % c.R
        tiles = [r + c.R * v for v in range(c.NT)]
        rows = np.concatenate([np.arange(128 * t, 128 * (t + 1)) for t in tiles])
        xb = np.ascontiguousarray(x_full[b])
        xo = np.ascontiguousarray(x_full[b][rows])
        qg = np.ascontiguousarray(rows.astype(np.float32).reshape(1, -1))
        per_core.append((b, rows, xb, xo, qg))
    return per_core


def _wmap(inputs):
    w = {k: np.ascontiguousarray(np.asarray(v, np.float32))
         for k, v in inputs.items() if k != "L0_attn_x"}
    return {
        "wq_w": w["L0_attn_wq_w"], "wq_b": w["L0_attn_wq_b"].reshape(1, -1),
        "wk_w": w["L0_attn_wk_w"], "wk_b": w["L0_attn_wk_b"].reshape(1, -1),
        "wkl_w": w["L0_attn_wkl_w"], "wkl_b": w["L0_attn_wkl_b"].reshape(1, -1),
        "wv_w": w["L0_attn_wv_w"], "wv_b": w["L0_attn_wv_b"].reshape(1, -1),
        "wvl_w": w["L0_attn_wvl_w"], "wvl_b": w["L0_attn_wvl_b"].reshape(1, -1),
        "upk_w": w["L0_attn_upk_w"], "upk_b": w["L0_attn_upk_b"],
        "upv_w": w["L0_attn_upv_w"], "upv_b": w["L0_attn_upv_b"],
        "wo_w": w["L0_attn_wo_w"], "wo_b": w["L0_attn_wo_b"].reshape(1, -1),
        "gate_w": w["L0_ffn_gate_w"],
        "w_gate": w["L0_ffn_w_gate"], "w_up": w["L0_ffn_w_up"],
        "w_down": w["L0_ffn_w_down"],
        "sw_gate": w["L0_ffn_sw_gate"], "sw_up": w["L0_ffn_sw_up"],
        "sw_down": w["L0_ffn_sw_down"],
    }


def kernel(**inputs):
    c = Cfg()
    x = np.asarray(inputs["L0_attn_x"], np.float32)
    assert x.shape == (c.B, c.S, c.D)
    wmap = _wmap(inputs)
    nc = build_program(c)
    shards = _shard(c, x)
    in_maps = []
    for (b, rows, xb, xo, qg) in shards:
        m = dict(wmap)
        m["x_batch"] = xb
        m["x_own"] = xo
        m["qg_row"] = qg
        in_maps.append(m)
    res = run_bass_kernel_spmd(nc, in_maps, core_ids=list(range(c.NCORES)))
    out = np.empty((c.B, c.S, c.D), np.float32)
    for cid, (b, rows, _, _, _) in enumerate(shards):
        out[b][rows] = res.results[cid]["y"]
    return out


# version 08
# speedup vs baseline: 1.1406x; 1.0178x over previous
import numpy as np
import ml_dtypes
from contextlib import ExitStack

import concourse.bass as bass
import concourse.mybir as mybir
from concourse.bass_core_rust import IndirectOffsetOnAxis
from concourse.bass_utils import run_bass_kernel_spmd

F32 = mybir.dt.float32
BF16 = mybir.dt.bfloat16
U32 = mybir.dt.uint32
AF = mybir.ActivationFunctionType
ALU = mybir.AluOpType
AX = mybir.AxisListType
BF = ml_dtypes.bfloat16

SCALE = 1.0 / float(np.sqrt(128.0))
LN_EPS = 1e-5
SENT_BASE = 1 << 20


class Cfg:
    def __init__(self, S=8192, D=2048, H=16, DL=512, E=8, DF=1408, CAP=640,
                 R=4, G=4, NCORES=8):
        self.S, self.D, self.H, self.DL, self.E, self.DF = S, D, H, DL, E, DF
        self.DH = 128
        self.CAP = CAP
        self.ES = 2 * CAP
        self.R, self.G, self.NCORES = R, G, NCORES
        self.B = NCORES // R
        self.TOK = S // R
        self.NT = self.TOK // 128
        self.NG = self.NT // G
        self.GQ = 128 * G
        self.FC = D // 128
        self.LC = DL // 128
        self.KCH = S // 128
        self.TC = self.TOK // 128
        self.FO = DF // 128
        self.CC = CAP // 128
        self.FS = D // 512
        assert self.NT % G == 0 and D % 512 == 0 and DL % 128 == 0
        assert DF % 128 == 0 and S % 512 == 0 and CAP % 128 == 0
        assert self.TOK % 512 == 0

    def kc_count(self, a):
        return self.R * (a * self.G + self.G)

    def col0(self, a, kc):
        jmin = 0
        for j in range(self.G):
            if kc > self.R * (a * self.G + j) + self.R - 1:
                jmin = j + 1
        return 128 * jmin

    def masked_j(self, a, kc):
        d = kc - self.R * a * self.G
        if d < 0:
            return None
        j = d // self.R
        return j if j < self.G else None


class TA:

    def __init__(self, handle, n, inner):
        self.t = handle
        self.n = n
        self.inner = inner

    def __getitem__(self, i):
        assert 0 <= i < self.n
        return self.t[:, self.inner * i:self.inner * (i + 1)]

    @property
    def full(self):
        return self.t[:, :]

    @property
    def r3(self):
        return self.t[:, :].rearrange("p (n i) -> p n i", n=self.n)


class _Op:
    __slots__ = ("eng", "fn", "deps", "chan", "cnt", "phase", "quantum")

    def __init__(self, eng, fn, deps, chan, quantum, phase):
        self.eng, self.fn, self.deps, self.chan = eng, fn, deps, chan
        self.quantum, self.phase = quantum, phase
        self.cnt = None


class Builder:
    ENGS = ("tensor", "vector", "scalar", "gpsimd", "sync")

    @staticmethod
    def q_eng(q):
        return "sync" if q.startswith("hw") else "gpsimd"

    def __init__(self, nc):
        self.nc = nc
        self.ops = []
        self.phase = 0
        self.chans = {}
        self.chan_count = {}

    def _chan(self, name):
        if name not in self.chans:
            self.chans[name] = self.nc.alloc_semaphore(name)
            self.chan_count[name] = 0
        return self.chans[name]

    def op(self, eng, fn, deps=()):
        deps = [d for d in deps if d is not None]
        o = _Op(eng, fn, deps, "c_" + eng, 1, self.phase)
        self.ops.append(o)
        return o

    def dma(self, q, fn, deps=()):
        eng = self.q_eng(q)
        deps = [d for d in deps if d is not None]
        o = _Op(eng, fn, deps, q, 16, self.phase)
        self.ops.append(o)
        return o

    def flush_phase(self, name=None):
        ops, self.ops = self.ops, []
        cur_phase = self.phase
        self.phase += 1
        if not ops:
            return
        need_inc = set()
        dma_chans = set()
        for o in ops:
            for d in o.deps:
                if d.phase == cur_phase:
                    need_inc.add(id(d))
            if o.quantum == 16:
                need_inc.add(id(o))
                dma_chans.add(o.chan)
        for o in ops:
            if id(o) in need_inc:
                self._chan(o.chan)
                self.chan_count[o.chan] += o.quantum
                o.cnt = self.chan_count[o.chan]
        drains = {}
        for ch in dma_chans:
            drains.setdefault(self.q_eng(ch), []).append(
                (ch, self.chan_count[ch]))
        by_eng = {e: [] for e in self.ENGS}
        for o in ops:
            by_eng[o.eng].append(o)
        builder = self

        def emit_for(eng_name):
            def run(eng):
                waited = {}
                for o in by_eng[eng_name]:
                    need = {}
                    for d in o.deps:
                        if d.phase != cur_phase:
                            continue
                        assert d.cnt is not None, "dep op was never counted"
                        if need.get(d.chan, -1) < d.cnt:
                            need[d.chan] = d.cnt
                    for ch, cnt in need.items():
                        if waited.get(ch, -1) >= cnt:
                            continue
                        eng.wait_ge(builder.chans[ch], cnt)
                        waited[ch] = cnt
                    instr = o.fn(eng)
                    if o.cnt is not None:
                        instr.then_inc(builder.chans[o.chan], o.quantum)
                for ch, tot in drains.get(eng_name, []):
                    if waited.get(ch, -1) < tot:
                        eng.wait_ge(builder.chans[ch], tot)
            return run

        import contextlib
        scope = self.nc.named_scope(name) if name else contextlib.nullcontext()
        with scope:
            with self.nc.Block(name) as blk:
                for e in self.ENGS:
                    if by_eng[e] or drains.get(e):
                        getattr(blk, e)(emit_for(e))


def build_program(c: Cfg, dbg_names=(), stop_after=None):
    import os
    p5skip = set(os.environ.get("P5SKIP", "").split(",")) - {""}

    nc = bass.Bass("TRN2", target_bir_lowering=False)
    P = Builder(nc)

    def din(name, shape):
        return nc.dram_tensor(name, list(shape), F32, kind="ExternalInput").ap()

    x_batch = din("x_batch", (c.S, c.D))
    x_own = din("x_own", (c.TOK, c.D))
    qg_row = din("qg_row", (1, c.TOK))
    wq_w = din("wq_w", (c.D, c.D)); wq_b = din("wq_b", (1, c.D))
    wk_w = din("wk_w", (c.D, c.D)); wk_b = din("wk_b", (1, c.D))
    wkl_w = din("wkl_w", (c.D, c.DL)); wkl_b = din("wkl_b", (1, c.DL))
    wv_w = din("wv_w", (c.D, c.D)); wv_b = din("wv_b", (1, c.D))
    wvl_w = din("wvl_w", (c.D, c.DL)); wvl_b = din("wvl_b", (1, c.DL))
    upk_w = din("upk_w", (c.H, c.DL, c.DH)); upk_b = din("upk_b", (c.H, c.DH))
    upv_w = din("upv_w", (c.H, c.DL, c.DH)); upv_b = din("upv_b", (c.H, c.DH))
    wo_w = din("wo_w", (c.D, c.D)); wo_b = din("wo_b", (1, c.D))
    gate_w = din("gate_w", (c.E, c.D))
    w_gate = din("w_gate", (c.E, c.D, c.DF))
    w_up = din("w_up", (c.E, c.D, c.DF))
    w_down = din("w_down", (c.E, c.DF, c.D))
    sw_gate = din("sw_gate", (c.D, c.DF))
    sw_up = din("sw_up", (c.D, c.DF))
    sw_down = din("sw_down", (c.DF, c.D))

    y_out = nc.dram_tensor("y", [c.TOK, c.D], F32, kind="ExternalOutput").ap()

    def dscr(name, shape, dt):
        return nc.dram_tensor(name, list(shape), dt, kind="Internal").ap()

    klat_dram = dscr("klat_dram", (c.LC, 128, c.S), BF16)
    vlat_dram = dscr("vlat_dram", (c.LC, 128, c.S), BF16)
    K_dram = dscr("K_dram", (c.H, 128, c.S), BF16)
    V_dram = dscr("V_dram", (c.H, c.KCH, 128, 128), BF16)
    Q_dram = dscr("Q_dram", (c.H, 128, c.TOK), BF16)
    O_dram = dscr("O_dram", (c.H, 128, c.TOK), BF16)
    resid_dram = dscr("resid_dram", (c.TOK, c.D), F32)
    xln2_dram = dscr("xln2_dram", (c.FC, 128, c.TOK), BF16)
    xln2tm_dram = dscr("xln2tm_dram", (c.TOK, c.D), BF16)
    xg_dram = dscr("xg_dram", (c.E * c.ES, c.D), BF16)
    backmap_dram = dscr("backmap_dram", (c.E * c.ES, 1), U32)
    gatew_dram = dscr("gatew_dram", (c.E * c.ES, 1), F32)
    rank_dram = dscr("rank_dram", (2 * c.TOK, c.D), F32)

    dbg_map = {
        "klat": klat_dram, "vlat": vlat_dram, "K": K_dram, "V": V_dram,
        "Q": Q_dram, "O": O_dram, "resid": resid_dram, "xln2": xln2_dram,
        "xg": xg_dram, "backmap": backmap_dram, "gatew": gatew_dram,
        "rank": rank_dram,
    }

    ident_bf = nc.inline_tensor(np.eye(128, dtype=BF), "ident_bf")
    ut_bf = nc.inline_tensor(np.triu(np.ones((128, 128), dtype=BF)), "ut_bf")
    eoff_dram = nc.inline_tensor(
        (np.arange(c.E, dtype=np.float32) * c.ES).reshape(1, c.E), "eoff")

    ctx = ExitStack()

    def sb(name, shape, dt):
        return ctx.enter_context(nc.sbuf_tensor(name, list(shape), dt))

    def sba(name, n, inner, dt, stack=None):
        st = stack or ctx
        h = st.enter_context(nc.sbuf_tensor(name, [128, n * inner], dt))
        return TA(h, n, inner)

    def psa(name, n, inner, dt=F32, stack=None):
        st = stack or ctx
        return [st.enter_context(nc.psum_tensor(f"{name}{i}", [128, inner], dt))[:, :]
                for i in range(n)]

    ones_col = sb("ones_col", (128, 1), BF16)
    ones_row = sb("ones_row", (1, 512), BF16)
    onesrow_f32 = sb("onesrow_f32", (1, 128), F32)
    ident_sb = sb("ident_sb", (128, 128), BF16)
    onef_sb = sb("onef_sb", (1, 1), BF16)
    kg_col = sb("kg_col", (128, 1), F32)
    ut_sb = sb("ut_sb", (128, 128), BF16)
    allones_sb = sb("allones_sb", (128, 128), BF16)
    eoffb_sb = sb("eoffb_sb", (128, c.E), F32)
    tokid_sb = sb("tokid_sb", (128, c.TC), U32)
    tokid2_sb = sb("tokid2_sb", (128, c.TC), U32)

    biasq_sb = sb("biasq_sb", (1, c.D), BF16)
    biask_sb = sb("biask_sb", (1, c.D), BF16)
    biasv_sb = sb("biasv_sb", (1, c.D), BF16)
    biaskl_sb = sb("biaskl_sb", (1, c.DL), F32)
    biasvl_sb = sb("biasvl_sb", (1, c.DL), F32)
    biasupk_sb = sb("biasupk_sb", (1, c.H * c.DH), BF16)
    biasupv_sb = sb("biasupv_sb", (1, c.H * c.DH), BF16)
    biaswo_sb = sb("biaswo_sb", (1, c.D), BF16)
    bkm_sb = sb("bkm_sb", (1, c.DL), BF16)
    bvm_sb = sb("bvm_sb", (1, c.DL), BF16)

    gwT_sb = sba("gwT_sb", c.FC, c.E, BF16)

    const_vals = [float(LN_EPS), float(SCALE), -1.0]
    const_sb = sba("const_sb", len(const_vals), 1, F32)
    for ci, cv in enumerate(const_vals):
        nc.const_aps.register(F32, cv, const_sb[ci])

    attn_ctx = ExitStack()
    nmask = c.NG * c.R * c.G
    mask_sb = sba("mask_sb", nmask, 128, BF16, attn_ctx)

    bc_disp_reg = nc.gpsimd.alloc_register("bc_disp")
    bc_rank_reg = nc.gpsimd.alloc_register("bc_rank")

    def mask_tile(a, kc):
        kcrel = kc - c.R * a * c.G
        return mask_sb[a * (c.R * c.G) + kcrel]

    with ExitStack() as pc:
        qg_sb = pc.enter_context(nc.sbuf_tensor("qg_sb", [1, c.TOK], F32))
        qgb_sb = pc.enter_context(nc.sbuf_tensor("qgb_sb", [128, c.TOK], F32))
        gaterow_sb = pc.enter_context(
            nc.sbuf_tensor("gaterow_sb", [c.E, c.D], BF16))
        bmi_sb = pc.enter_context(
            nc.sbuf_tensor("bmi_sb", [128, c.E * c.ES // 128], U32))
        qgb_ps = pc.enter_context(nc.psum_tensor("qgb_ps", [128, 512], F32))
        tp_ps = psa("p0tp_ps", 2, 128, BF16, pc)
        eob_ps = pc.enter_context(nc.psum_tensor("eob_ps", [128, c.E], F32))
        eoffr_sb = pc.enter_context(nc.sbuf_tensor("eoffr_sb", [1, c.E], F32))

        d_id = P.dma("hw", lambda e: e.dma_start(out=ident_sb[:], in_=ident_bf.ap()))
        d_ut = P.dma("hw", lambda e: e.dma_start(out=ut_sb[:], in_=ut_bf.ap()))
        d_eo = P.dma("hw", lambda e: e.dma_start(out=eoffr_sb[:], in_=eoff_dram.ap()))
        d_qg = P.dma("hw", lambda e: e.dma_start(out=qg_sb[:], in_=qg_row))
        hw_last = d_qg

        m_oc = P.op("gpsimd", lambda e: e.memset(ones_col[:], 1.0))
        m_or = P.op("gpsimd", lambda e: e.memset(ones_row[:], 1.0))
        m_of = P.op("gpsimd", lambda e: e.memset(onesrow_f32[:], 1.0))
        m_o1 = P.op("gpsimd", lambda e: e.memset(onef_sb[:], 1.0))
        m_ao = P.op("gpsimd", lambda e: e.memset(allones_sb[:], 1.0))
        P.op("gpsimd", lambda e: e.reg_mov(bc_disp_reg, c.E * c.ES - 1))
        P.op("gpsimd", lambda e: e.reg_mov(bc_rank_reg, 2 * c.TOK - 1))
        for ci, cv in enumerate(const_vals):
            P.op("gpsimd", (lambda ci, cv: lambda e: e.memset(const_sb[ci], cv))(ci, cv))
        m_kg = P.op("gpsimd", lambda e: e.iota(
            kg_col[:], [[1, 1]], base=0, channel_multiplier=1,
            allow_small_or_imprecise_dtypes=True))
        m_t1 = P.op("gpsimd", lambda e: e.iota(
            tokid_sb[:], [[128, c.TC]], base=0, channel_multiplier=1))
        m_t2 = P.op("gpsimd", lambda e: e.iota(
            tokid2_sb[:], [[128, c.TC]], base=c.TOK, channel_multiplier=1))
        m_bm = P.op("gpsimd", lambda e: e.iota(
            bmi_sb[:], [[1, c.E * c.ES // 128]], base=SENT_BASE,
            channel_multiplier=c.E * c.ES // 128))
        P.dma("sw", lambda e: e.dma_start(
            out=backmap_dram.flatten().rearrange("(p f) -> p f", p=128),
            in_=bmi_sb[:]), deps=[m_bm])

        bias_loads = {}
        for nm, dst, src in (("q", biasq_sb, wq_b), ("k", biask_sb, wk_b),
                             ("v", biasv_sb, wv_b), ("kl", biaskl_sb, wkl_b),
                             ("vl", biasvl_sb, wvl_b), ("wo", biaswo_sb, wo_b)):
            bias_loads[nm] = P.dma(
                "sw",
                (lambda d, s: lambda e: e.dma_start(out=d[:], in_=s))(dst, src))
        d_buk = P.dma("sw", lambda e: e.dma_start(
            out=biasupk_sb[:], in_=upk_b.flatten()[None, :]))
        d_buv = P.dma("sw", lambda e: e.dma_start(
            out=biasupv_sb[:], in_=upv_b.flatten()[None, :]))
        d_gw = P.dma("sw", lambda e: e.dma_start(out=gaterow_sb[:], in_=gate_w))
        sw_last = d_gw

        qgb_cp = []
        for gi in range(c.TOK // 512):
            mm = P.op("tensor", (lambda g: lambda e: e.matmul(
                qgb_ps[:], onesrow_f32[:], qg_sb[:, 512 * g:512 * (g + 1)],
                start=True, stop=True))(gi),
                deps=[hw_last, m_of] + ([qgb_cp[-1]] if qgb_cp else []))
            cp = P.op("vector", (lambda g: lambda e: e.tensor_copy(
                out=qgb_sb[:, 512 * g:512 * (g + 1)], in_=qgb_ps[:]))(gi),
                deps=[mm])
            qgb_cp.append(cp)

        mm_eo = P.op("tensor", lambda e: e.matmul(
            eob_ps[:], onesrow_f32[:], eoffr_sb[:], start=True, stop=True),
            deps=[hw_last, m_of, qgb_cp[-1]])
        P.op("vector", lambda e: e.tensor_copy(out=eoffb_sb[:], in_=eob_ps[:]),
             deps=[mm_eo])

        for a in range(c.NG):
            for kcrel in range(c.R * c.G):
                kc = c.R * a * c.G + kcrel
                j = kcrel // c.R
                colbase = a * c.GQ + 128 * j
                mi = a * (c.R * c.G) + kcrel
                P.op("vector", (lambda mi, cb, kc: lambda e: e.tensor_scalar(
                    out=mask_sb[mi], in0=qgb_sb[:, cb:cb + 128],
                    scalar1=kg_col[:], scalar2=float(128 * kc),
                    op0=ALU.subtract, op1=ALU.is_ge))(mi, colbase, kc),
                    deps=[qgb_cp[colbase // 512], m_kg])

        prev = None
        for f in range(c.FC):
            mmt = P.op("tensor", (lambda f: lambda e: e.transpose(
                tp_ps[f % 2][0:128, 0:c.E],
                gaterow_sb[:, 128 * f:128 * (f + 1)],
                ident_sb[0:c.E, 0:c.E]))(f),
                deps=[sw_last, hw_last, prev])
            prev = P.op("vector", (lambda f, mmt: lambda e: e.tensor_copy(
                out=gwT_sb[f],
                in_=tp_ps[f % 2][0:128, 0:c.E]))(f, mmt),
                deps=[mmt])

        P.flush_phase("p0")
    if stop_after == "p0":
        return nc

    def emit_ln(src_tile, dst_tile, lnscr, stats4, deps, scr_guard=None):
        s_sum = stats4[:, 0:1]
        s_sq = stats4[:, 1:2]
        s_negm = stats4[:, 2:3]
        s_rstd = stats4[:, 3:4]
        o1 = P.op("vector", lambda e: e.tensor_reduce(
            out=s_sum, in_=src_tile, axis=AX.X, op=ALU.add), deps=deps)
        o2 = P.op("scalar", lambda e: e.activation(
            out=lnscr, in_=src_tile, func=AF.Square, accum_out=s_sq),
            deps=list(deps) + [scr_guard])
        o3 = P.op("vector", lambda e: e.tensor_scalar(
            out=s_negm, in0=s_sum, scalar1=-1.0 / c.D, scalar2=None,
            op0=ALU.mult), deps=[o1])
        o4 = P.op("vector", lambda e: e.scalar_tensor_tensor(
            out=s_sum, in0=s_negm, scalar=-1.0, in1=s_negm,
            op0=ALU.mult, op1=ALU.mult), deps=[o3])
        o5 = P.op("vector", lambda e: e.tensor_scalar(
            out=s_sq, in0=s_sq, scalar1=1.0 / c.D, scalar2=None,
            op0=ALU.mult), deps=[o2])
        o6 = P.op("vector", lambda e: e.tensor_tensor(
            out=s_sq, in0=s_sq, in1=s_sum, op=ALU.add), deps=[o4, o5])
        o7 = P.op("scalar", lambda e: e.activation(
            out=s_sq, in_=s_sq, func=AF.Sqrt, bias=float(LN_EPS)), deps=[o6])
        o8 = P.op("vector", lambda e: e.reciprocal(out=s_rstd, in_=s_sq),
                  deps=[o7])
        o9 = P.op("vector", lambda e: e.tensor_scalar(
            out=dst_tile, in0=src_tile, scalar1=s_negm, scalar2=s_rstd,
            op0=ALU.add, op1=ALU.mult), deps=[o8, o3] + list(deps))
        return o9, o2

    p1_ctx = ExitStack()
    wkm_sb = sba("wkm_sb", c.FC, c.DL, BF16, p1_ctx)
    wvm_sb = sba("wvm_sb", c.FC, c.DL, BF16, p1_ctx)

    with ExitStack() as pc:
        wl_sb = [sba("wkl_sb", c.FC, c.DL, BF16, pc),
                 sba("wvl_sb", c.FC, c.DL, BF16, pc)]
        wrow_sb = sba("wrow_sb", 2, c.D, BF16, pc)
        wT_sb = sba("wT_sb", c.FC, 128, BF16, pc)
        bcol_sb = [sba("bcolk_sb", c.FC, 1, BF16, pc),
                   sba("bcolv_sb", c.FC, 1, BF16, pc)]
        tp_ps = psa("p1tp_ps", 2, 128, BF16, pc)
        bc_ps = psa("p1bc_ps", 2, 1, F32, pc)
        mm_ps = psa("p1mm_ps", 2, 512, F32, pc)
        bm_ps = pc.enter_context(nc.psum_tensor("p1bm_ps", [1, 512], F32))

        d_wl0 = P.dma("sw", lambda e: e.dma_start(
            out=wl_sb[0].r3,
            in_=wkl_w.rearrange("(cc p) l -> p cc l", p=128)))
        d_wl1 = P.dma("sw", lambda e: e.dma_start(
            out=wl_sb[1].r3,
            in_=wvl_w.rearrange("(cc p) l -> p cc l", p=128)))
        d_wl = [[d_wl0, d_wl1], [d_wl0, d_wl1]]
        bcol_ops = {}
        prev_ev = None
        for kv in range(2):
            brow = biask_sb if kv == 0 else biasv_sb
            for i in range(c.FC):
                n = kv * c.FC + i
                mm = P.op("tensor", (lambda brow, i, n: lambda e: e.matmul(
                    bc_ps[n % 2], brow[:, 128 * i:128 * (i + 1)],
                    onef_sb[:], start=True, stop=True))(brow, i, n),
                    deps=[bias_loads["k" if kv == 0 else "v"], m_o1, prev_ev])
                ev = P.op("vector", (lambda kv, i, n, mm: lambda e: e.tensor_copy(
                    out=bcol_sb[kv][i], in_=bc_ps[n % 2]))(kv, i, n, mm),
                    deps=[mm])
                bcol_ops[(kv, i)] = ev
                prev_ev = ev

        nlo = (c.DL + 511) // 512
        hist = {}
        for kv in range(2):
            w_src = wk_w if kv == 0 else wv_w
            dstw = wkm_sb if kv == 0 else wvm_sb
            for i in range(c.FC):
                it = kv * c.FC + i
                d_row = P.dma("sw_a" if it % 2 == 0 else "sw_b",
                    (lambda w_src, i, it: lambda e: e.dma_start(
                        out=wrow_sb[it % 2], in_=w_src[128 * i:128 * (i + 1), :]))(w_src, i, it),
                    deps=[hist.get(("tpdone", it - 2))])
                tevs = []
                for j in range(c.FC):
                    ntp = it * c.FC + j
                    mmt = P.op("tensor", (lambda it, j, ntp: lambda e: e.transpose(
                        tp_ps[ntp % 2], wrow_sb[it % 2][:, 128 * j:128 * (j + 1)],
                        ident_sb[:]))(it, j, ntp),
                        deps=[d_row, hist.get(("tpev", ntp - 2)),
                              hist.get(("wTuse", it - 1)) if j == 0 else None])
                    ev = P.op("vector", (lambda j, ntp, mmt: lambda e: e.tensor_copy(
                        out=wT_sb[j], in_=tp_ps[ntp % 2]))(j, ntp, mmt), deps=[mmt])
                    hist[("tpev", ntp)] = ev
                    tevs.append(ev)
                hist[("tpdone", it)] = tevs[-1]
                for lo in range(nlo):
                    lw = min(512, c.DL - 512 * lo)
                    nn = it * nlo + lo
                    mm = None
                    for j in range(c.FC):
                        mm = P.op("tensor",
                                  (lambda nn, j, lo, lw, kv: lambda e: e.matmul(
                                      mm_ps[nn % 2][:, 0:lw],
                                      wT_sb[j],
                                      wl_sb[kv][j][:, 512 * lo:512 * lo + lw],
                                      start=(j == 0), stop=(j == c.FC - 1)))(nn, j, lo, lw, kv),
                                  deps=(([tevs[-1], hist.get(("meev", nn - 2))] + d_wl[kv]) if j == 0 else []))
                    ev = P.op("vector", (lambda dstw, i, lo, lw, nn, mm: lambda e:
                              e.tensor_copy(
                                  out=dstw[i][:, 512 * lo:512 * lo + lw],
                                  in_=mm_ps[nn % 2][:, 0:lw]))(
                                      dstw, i, lo, lw, nn, mm),
                              deps=[mm])
                    hist[("meev", nn)] = ev
                    hist[("wTuse", it)] = mm

        for kv in range(2):
            browl = biaskl_sb if kv == 0 else biasvl_sb
            dst = bkm_sb if kv == 0 else bvm_sb
            for lo in range(nlo):
                lw = min(512, c.DL - 512 * lo)
                mm = None
                for j in range(c.FC):
                    mm = P.op("tensor", (lambda kv, j, lo, lw: lambda e: e.matmul(
                        bm_ps[0:1, 0:lw], bcol_sb[kv][j],
                        wl_sb[kv][j][:, 512 * lo:512 * lo + lw],
                        start=(j == 0), stop=(j == c.FC - 1)))(kv, j, lo, lw),
                        deps=[bcol_ops[(kv, j)],
                              hist.get(("bmev", kv, lo - 1)) if j == 0 else None,
                              hist.get(("bmev", kv - 1, nlo - 1)) if (j == 0 and lo == 0) else None]
                             + (d_wl[kv] if j == 0 else []))
                ev = P.op("vector", (lambda dst, browl, lo, lw, mm: lambda e:
                          e.tensor_tensor(out=dst[:, 512 * lo:512 * lo + lw],
                                          in0=bm_ps[0:1, 0:lw],
                                          in1=browl[:, 512 * lo:512 * lo + lw],
                                          op=ALU.add))(dst, browl, lo, lw, mm),
                          deps=[mm, bias_loads["kl" if kv == 0 else "vl"]])
                hist[("bmev", kv, lo)] = ev

        P.flush_phase("p1")
    if stop_after == "p1":
        return nc

    with ExitStack() as pc:
        xb_sb = [sba("xb0_sb", 4, c.D, F32, pc), sba("xb1_sb", 4, c.D, F32, pc)]
        lnscr = pc.enter_context(nc.sbuf_tensor("lnscr", [128, c.D], F32))
        lnst = pc.enter_context(nc.sbuf_tensor("lnst", [128, 16], F32))
        xlt_sb = sba("xlt_sb", 4, c.D, BF16, pc)
        xlf_sb = sba("xlf_sb", c.FC, 512, BF16, pc)
        lat_sb = sba("lat_sb", 2, 512, BF16, pc)
        tp_ps = psa("pbtp_ps", 2, 128, BF16, pc)
        kl_ps = psa("pbkl_ps", 2, 512, F32, pc)

        hist = {}
        prev_sq = None
        for g in range(c.S // 512):
            par = g % 2
            guard = [hist.get(("xbuse", g - 2))]
            d_xh = []
            for hf in range(2):
                d_xh.append(P.dma("hw_a" if hf == 0 else "hw_b",
                    (lambda g, hf, par: lambda e: e.dma_start(
                        out=xb_sb[par].t[:, c.D * 2 * hf:c.D * 2 * (hf + 1)].rearrange(
                            "p (t d) -> p t d", t=2),
                        in_=x_batch[512 * g + 256 * hf:512 * g + 256 * (hf + 1), :].rearrange(
                            "(t p) d -> p t d", p=128)))(g, hf, par),
                    deps=guard))
            tp_last = None
            ln_ops = []
            for t in range(4):
                ln, prev_sq = emit_ln(
                    xb_sb[par][t], xlt_sb[t], lnscr[:, :], lnst[:, 4 * t:4 * t + 4],
                    deps=[d_xh[t // 2],
                          hist.get(("xltuse", g - 1)) if t == 0 else None],
                    scr_guard=prev_sq)
                ln_ops.append(ln)
                if t == 3:
                    hist[("xbuse", g)] = ln
                for f in range(c.FC):
                    n = (g * 4 + t) * c.FC + f
                    mmt = P.op("tensor", (lambda t, f, n: lambda e: e.transpose(
                        tp_ps[n % 2], xlt_sb[t][:, 128 * f:128 * (f + 1)],
                        ident_sb[:]))(t, f, n),
                        deps=[ln, hist.get(("tpev", n - 2)),
                              hist.get(("xlfuse", g - 1)) if (t == 0 and f == 0) else None])
                    ev = P.op("vector", (lambda f, t, n, mmt: lambda e: e.tensor_copy(
                        out=xlf_sb[f][:, 128 * t:128 * (t + 1)], in_=tp_ps[n % 2]))(f, t, n, mmt),
                        deps=[mmt])
                    hist[("tpev", n)] = ev
                    tp_last = ev
            hist[("xltuse", g)] = tp_last
            last_mm = None
            for kv in range(2):
                wsrc = wkm_sb if kv == 0 else wvm_sb
                bsrc = bkm_sb if kv == 0 else bvm_sb
                ddst = klat_dram if kv == 0 else vlat_dram
                for lc in range(c.LC):
                    gn = g * 2 * c.LC + kv * c.LC + lc
                    pp = gn % 2
                    mm = None
                    for f in range(c.FC):
                        mm = P.op("tensor",
                                  (lambda pp, f, lc, wsrc: lambda e: e.matmul(
                                      kl_ps[pp], wsrc[f][:, 128 * lc:128 * (lc + 1)],
                                      xlf_sb[f], start=(f == 0), stop=False))(pp, f, lc, wsrc),
                                  deps=([tp_last, hist.get(("klev", gn - 2))]
                                        if f == 0 else []))
                    mmb = P.op("tensor", (lambda pp, lc, bsrc: lambda e: e.matmul(
                        kl_ps[pp], bsrc[:, 128 * lc:128 * (lc + 1)], ones_row[:],
                        start=False, stop=True))(pp, lc, bsrc), deps=[mm])
                    ev = P.op("vector", (lambda pp, mmb, gn: lambda e: e.tensor_copy(
                        out=lat_sb[gn % 2], in_=kl_ps[pp]))(pp, mmb, gn),
                        deps=[mmb, hist.get(("latst", gn - 2))])
                    st = P.dma("hw_s0" if gn % 2 == 0 else "hw_s1",
                        (lambda gn, ddst, lc, g: lambda e: e.dma_start(
                            out=ddst[lc][:, 512 * g:512 * (g + 1)], in_=lat_sb[gn % 2]))(gn, ddst, lc, g),
                        deps=[ev])
                    hist[("klev", gn)] = ev
                    hist[("latst", gn)] = st
                    last_mm = mm
            hist[("xlfuse", g)] = last_mm

        P.flush_phase("p1b")
    p1_ctx.close()
    if stop_after == "p1b":
        return nc

    with ExitStack() as pc:
        wq_sb = sba("wq_sb", c.FC, c.D, BF16, pc)
        xo_sb = [sba("xo0_sb", 4, c.D, F32, pc)]
        lnscr = pc.enter_context(nc.sbuf_tensor("lnscr_c", [128, c.D], F32))
        lnst = pc.enter_context(nc.sbuf_tensor("lnst_c", [128, 16], F32))
        xlt_sb = sba("xlt_c", 4, c.D, BF16, pc)
        xlf_sb = sba("xlf_c", c.FC, 512, BF16, pc)
        q_sb = sba("q_sb", 2, 512, BF16, pc)
        tp_ps = psa("pctp_ps", 2, 128, BF16, pc)
        q_ps = psa("pcq_ps", 2, 512, F32, pc)

        d_wq = P.dma("sw", lambda e: e.dma_start(
            out=wq_sb.r3, in_=wq_w.rearrange("(cc p) d -> p cc d", p=128)))

        hist = {}
        prev_sq = None
        for a in range(c.TOK // 512):
            par = 0
            guard = [hist.get(("xouse", a - 1))]
            d_xh = []
            for hf in range(2):
                d_xh.append(P.dma("hw_a" if hf == 0 else "hw_b",
                    (lambda a, hf, par: lambda e: e.dma_start(
                        out=xo_sb[par].t[:, c.D * 2 * hf:c.D * 2 * (hf + 1)].rearrange(
                            "p (t d) -> p t d", t=2),
                        in_=x_own[512 * a + 256 * hf:512 * a + 256 * (hf + 1), :].rearrange(
                            "(t p) d -> p t d", p=128)))(a, hf, par),
                    deps=guard))
            tp_last = None
            for t in range(4):
                ln, prev_sq = emit_ln(
                    xo_sb[par][t], xlt_sb[t], lnscr[:, :], lnst[:, 4 * t:4 * t + 4],
                    deps=[d_xh[t // 2],
                          hist.get(("xltuse", a - 1)) if t == 0 else None],
                    scr_guard=prev_sq)
                if t == 3:
                    hist[("xouse", a)] = ln
                for f in range(c.FC):
                    n = (a * 4 + t) * c.FC + f
                    mmt = P.op("tensor", (lambda t, f, n: lambda e: e.transpose(
                        tp_ps[n % 2], xlt_sb[t][:, 128 * f:128 * (f + 1)],
                        ident_sb[:]))(t, f, n),
                        deps=[ln, hist.get(("tpev", n - 2)),
                              hist.get(("xlfuse", a - 1)) if (t == 0 and f == 0) else None])
                    ev = P.op("vector", (lambda f, t, n, mmt: lambda e: e.tensor_copy(
                        out=xlf_sb[f][:, 128 * t:128 * (t + 1)], in_=tp_ps[n % 2]))(f, t, n, mmt),
                        deps=[mmt])
                    hist[("tpev", n)] = ev
                    tp_last = ev
            hist[("xltuse", a)] = tp_last
            last_mm = None
            for h in range(c.H):
                qn = a * c.H + h
                pp = qn % 2
                mm = None
                for f in range(c.FC):
                    mm = P.op("tensor", (lambda pp, f, h: lambda e: e.matmul(
                        q_ps[pp], wq_sb[f][:, 128 * h:128 * (h + 1)],
                        xlf_sb[f], start=(f == 0), stop=False))(pp, f, h),
                        deps=([tp_last, d_wq, hist.get(("qev", qn - 2))]
                              if f == 0 else []))
                mmb = P.op("tensor", (lambda pp, h: lambda e: e.matmul(
                    q_ps[pp], biasq_sb[:, 128 * h:128 * (h + 1)], ones_row[:],
                    start=False, stop=True))(pp, h), deps=[mm])
                ev = P.op("vector", (lambda pp, mmb, qn: lambda e: e.tensor_copy(
                    out=q_sb[qn % 2], in_=q_ps[pp]))(pp, mmb, qn),
                    deps=[mmb, hist.get(("qst", qn - 2))])
                st = P.dma("hw_s0" if qn % 2 == 0 else "hw_s1",
                    (lambda qn, h, a: lambda e: e.dma_start(
                        out=Q_dram[h][:, 512 * a:512 * (a + 1)], in_=q_sb[qn % 2]))(qn, h, a),
                    deps=[ev])
                hist[("qev", qn)] = ev
                hist[("qst", qn)] = st
                last_mm = mm
            hist[("xlfuse", a)] = last_mm

        P.flush_phase("p1c")
    if stop_after == "p1c":
        return nc

    with ExitStack() as pc:
        upk_sb = sba("upk_sb", c.H * c.LC, 128, BF16, pc)
        upv_sb = sba("upv_sb", c.H * c.LC, 128, BF16, pc)
        kl_sb = [sba("p2kl0", c.LC, 512, BF16, pc), sba("p2kl1", c.LC, 512, BF16, pc)]
        vl_sb = [sba("p2vl0", c.LC, 512, BF16, pc), sba("p2vl1", c.LC, 512, BF16, pc)]
        kp_sb = sba("kp_sb", 2, 512, BF16, pc)
        vp_sb = sba("vp_sb", 2, 4 * 128, BF16, pc)
        K_ps = psa("p2K_ps", 2, 512, F32, pc)
        V_ps = psa("p2V_ps", 2, 512, F32, pc)

        d_upk = P.dma("sw", lambda e: e.dma_start(
            out=upk_sb.t[:, :].rearrange("p (h cc d) -> p h cc d", h=c.H, cc=c.LC),
            in_=upk_w.rearrange("h (cc p) d -> p h cc d", p=128)))
        d_upv = P.dma("sw", lambda e: e.dma_start(
            out=upv_sb.t[:, :].rearrange("p (h cc d) -> p h cc d", h=c.H, cc=c.LC),
            in_=upv_w.rearrange("h (cc p) d -> p h cc d", p=128)))

        hist = {}
        for ks in range(c.S // 512):
            par = ks % 2
            d_kl = P.dma("hw_a" if par == 0 else "hw_b", (lambda ks, par: lambda e: e.dma_start(
                out=kl_sb[par].r3,
                in_=klat_dram[:, :, 512 * ks:512 * (ks + 1)].rearrange(
                    "cc p s -> p cc s")))(ks, par),
                deps=[hist.get(("kluse", ks - 2))])
            d_vl = P.dma("hw_a" if par == 0 else "hw_b", (lambda ks, par: lambda e: e.dma_start(
                out=vl_sb[par].r3,
                in_=vlat_dram[:, :, 512 * ks:512 * (ks + 1)].rearrange(
                    "cc p s -> p cc s")))(ks, par),
                deps=[hist.get(("vluse", ks - 2))])
            for h in range(c.H):
                kn = ks * c.H + h
                pk = kn % 2
                mm = None
                for lc in range(c.LC):
                    mm = P.op("tensor", (lambda pk, par, h, lc: lambda e: e.matmul(
                        K_ps[pk], upk_sb[h * c.LC + lc], kl_sb[par][lc],
                        start=(lc == 0), stop=False))(pk, par, h, lc),
                        deps=([d_kl, d_vl, d_upk, d_upv, hist.get(("kev", kn - 2))]
                              if lc == 0 else []))
                mmb = P.op("tensor", (lambda pk, h: lambda e: e.matmul(
                    K_ps[pk], biasupk_sb[:, 128 * h:128 * (h + 1)], ones_row[:],
                    start=False, stop=True))(pk, h), deps=[mm])
                ev = P.op("scalar", (lambda pk, mmb: lambda e: e.copy(
                    out=kp_sb[pk], in_=K_ps[pk]))(pk, mmb),
                    deps=[mmb, hist.get(("kst", kn - 2))])
                st = P.dma("hw_sk0" if pk == 0 else "hw_sk1",
                    (lambda pk, h, ks: lambda e: e.dma_start(
                        out=K_dram[h][:, 512 * ks:512 * (ks + 1)], in_=kp_sb[pk]))(pk, h, ks),
                    deps=[ev])
                hist[("kev", kn)] = ev
                hist[("kst", kn)] = st
                if h == c.H - 1:
                    hist[("kluse", ks)] = mm
                vevs = []
                mmv = None
                for t4 in range(4):
                    vn = kn * 4 + t4
                    for lc in range(c.LC):
                        mmv = P.op("tensor", (lambda pk, par, h, lc, t4: lambda e: e.matmul(
                            V_ps[pk][:, 128 * t4:128 * (t4 + 1)],
                            vl_sb[par][lc][:, 128 * t4:128 * (t4 + 1)],
                            upv_sb[h * c.LC + lc],
                            start=(lc == 0), stop=(lc == c.LC - 1)))(pk, par, h, lc, t4),
                            deps=([d_kl, d_vl, d_upk, d_upv, hist.get(("vev", kn - 2))]
                                  if (lc == 0 and t4 == 0) else []))
                    vevs.append(mmv)
                evv = P.op("vector", (lambda pk, mmv: lambda e: e.tensor_copy(
                    out=vp_sb[pk], in_=V_ps[pk]))(pk, mmv),
                    deps=vevs + [hist.get(("vst", kn - 2))])
                hist[("vev", kn)] = evv
                vevs = [evv]
                stv = P.dma("hw_sv0" if pk == 0 else "hw_sv1",
                    (lambda pk, h, ks: lambda e: e.dma_start(
                        out=V_dram[h, 4 * ks:4 * ks + 4].rearrange("t p d -> p t d"),
                        in_=vp_sb[pk].rearrange("p (t d) -> p t d", t=4)))(pk, h, ks),
                    deps=vevs)
                hist[("vst", kn)] = stv
                if h == c.H - 1:
                    hist[("vluse", ks)] = mmv

        P.flush_phase("p2")
    if stop_after == "p2":
        return nc

    with ExitStack() as pc:
        K_sb = [pc.enter_context(nc.sbuf_tensor(f"K{i}_sb", [128, c.S], BF16))
                for i in range(2)]
        V_sb = [pc.enter_context(nc.sbuf_tensor(f"V{i}_sb", [128, c.KCH * 128], BF16))
                for i in range(2)]
        Qh_sb = [pc.enter_context(nc.sbuf_tensor(f"Q{i}_sb", [128, c.TOK], BF16))
                 for i in range(2)]
        SPD = 2
        LAG = 4
        pT_sb = sba("pT_sb", SPD, 1024, BF16, pc)
        l_bf = pc.enter_context(nc.sbuf_tensor("l_bf", [1, 512], BF16))
        rec_sb = pc.enter_context(nc.sbuf_tensor("rec_sb", [1, 512], F32))
        bc_sb = pc.enter_context(nc.sbuf_tensor("bc_sb", [128, 512], F32))
        O_sb = sba("O_sb", 2, 512, BF16, pc)
        s_ps = psa("p3s_ps", SPD, 1024, F32, pc)
        o_ps = pc.enter_context(nc.psum_tensor("p3o_ps", [128, 512], F32))
        l_ps = pc.enter_context(nc.psum_tensor("p3l_ps", [1, 512], F32))
        bc_ps = pc.enter_context(nc.psum_tensor("p3bc_ps", [128, 512], F32))

        last_sps_rd = [None] * SPD
        last_pT_rd = [None] * SPD
        epi_state = {"last_l_rd": None, "last_o_rd": None, "last_bc_rd": None}
        pending_epi = [None]
        hist = {}
        for h in range(c.H):
            par = h % 2
            hd = hist.get(("hdone", h - 2))
            hq = "hw_a" if par == 0 else "hw_b"
            d_K = P.dma(hq, (lambda h, par: lambda e: e.dma_start(
                out=K_sb[par][:, :], in_=K_dram[h]))(h, par), deps=[hd])
            d_V = P.dma(hq, (lambda h, par: lambda e: e.dma_start(
                out=V_sb[par][:, :].rearrange("p (k d) -> p k d", k=c.KCH),
                in_=V_dram[h].rearrange("k p d -> p k d")))(h, par),
                deps=[hd])
            d_Q = P.dma(hq, (lambda h, par: lambda e: e.dma_start(
                out=Qh_sb[par][:, :], in_=Q_dram[h]))(h, par), deps=[hd])
            d_KVQ = [d_K, d_V, d_Q]
            for a in range(c.NG):
                KC = c.kc_count(a)
                assert KC % 2 == 0
                sc_ops = [None] * KC
                ex_ops = [None] * (KC // 2)
                mk_ops = [None] * KC
                av_ops = [None] * KC
                sm = None

                def emit_scores(kc):
                    col0 = c.col0(a, kc)
                    pi = (kc // 2) % SPD
                    off = 512 * (kc % 2)
                    sc_ops[kc] = P.op("tensor",
                        (lambda pi, off, kc, col0, a, par: lambda e: e.matmul(
                            s_ps[pi][:, off + col0:off + 512],
                            K_sb[par][:, 128 * kc:128 * (kc + 1)],
                            Qh_sb[par][:, c.GQ * a + col0:c.GQ * (a + 1)],
                            start=True, stop=True))(pi, off, kc, col0, a, par),
                        deps=(d_KVQ + [last_sps_rd[pi]]) if kc % 2 == 0 else [])
                    if kc % 2 == 1:
                        col0p = c.col0(a, kc - 1)
                        ex_ops[kc // 2] = P.op("scalar",
                            (lambda pi, col0p: lambda e: e.activation(
                                out=pT_sb[pi][:, col0p:1024],
                                in_=s_ps[pi][:, col0p:1024],
                                func=AF.Exp, scale=SCALE))(pi, col0p),
                            deps=[sc_ops[kc], last_pT_rd[pi]])
                        last_sps_rd[pi] = ex_ops[kc // 2]
                        for k2 in (kc - 1, kc):
                            jm = c.masked_j(a, k2)
                            if jm is not None:
                                off2 = 512 * (k2 % 2)
                                mk_ops[k2] = P.op("vector",
                                    (lambda pi, a, k2, jm, off2: lambda e: e.tensor_tensor(
                                        out=pT_sb[pi][:, off2 + 128 * jm:off2 + 128 * (jm + 1)],
                                        in0=pT_sb[pi][:, off2 + 128 * jm:off2 + 128 * (jm + 1)],
                                        in1=mask_tile(a, k2), op=ALU.mult))(pi, a, k2, jm, off2),
                                    deps=[ex_ops[kc // 2]])

                def emit_smav(kc):
                    nonlocal sm
                    col0 = c.col0(a, kc)
                    pi = (kc // 2) % SPD
                    off = 512 * (kc % 2)
                    pe_dep = mk_ops[kc] or ex_ops[kc // 2]
                    sm_ops = sm2 = P.op("tensor",
                        (lambda pi, off, kc, col0, KC: lambda e: e.matmul(
                            l_ps[0:1, col0:512], ones_col[:],
                            pT_sb[pi][:, off + col0:off + 512],
                            start=(kc == 0), stop=(kc == KC - 1)))(pi, off, kc, col0, KC),
                        deps=[pe_dep] + ([epi_state["last_l_rd"]] if kc == 0 else []))
                    sm = sm2
                    av_ops[kc] = P.op("tensor",
                        (lambda pi, off, kc, col0, par: lambda e: e.matmul(
                            o_ps[:, col0:512], V_sb[par][:, 128 * kc:128 * (kc + 1)],
                            pT_sb[pi][:, off + col0:off + 512],
                            start=(kc == 0), stop=False))(pi, off, kc, col0, par),
                        deps=[pe_dep] + (d_KVQ + [epi_state["last_o_rd"]] if kc == 0 else []))
                    if kc % 2 == 1:
                        last_pT_rd[pi] = av_ops[kc]

                for i in range(KC + LAG):
                    if i - LAG >= 0:
                        emit_smav(i - LAG)
                    if i < KC:
                        emit_scores(i)
                    if i == LAG - 1 and pending_epi[0] is not None:
                        pending_epi[0]()
                        pending_epi[0] = None

                def make_epi(h, a, sm, av_last):
                    def epi():
                        nonlocal_state = None
                        self_last_l_rd = None
                        evl = P.op("vector", lambda e: e.tensor_copy(
                            out=l_bf[:], in_=l_ps[:, :]), deps=[sm])
                        rc = P.op("vector", lambda e: e.reciprocal(
                            out=rec_sb[:], in_=l_ps[:, :]), deps=[sm, evl])
                        epi_state["last_l_rd"] = rc
                        vb = P.op("tensor", (lambda h: lambda e: e.matmul(
                            o_ps[:, :], biasupv_sb[:, 128 * h:128 * (h + 1)], l_bf[:],
                            start=False, stop=True))(h), deps=[evl, av_last])
                        bc = P.op("tensor", lambda e: e.matmul(
                            bc_ps[:, :], onesrow_f32[:], rec_sb[:], start=True, stop=True),
                            deps=[rc, epi_state["last_bc_rd"]])
                        bce = P.op("scalar", lambda e: e.copy(
                            out=bc_sb[:], in_=bc_ps[:, :]), deps=[bc])
                        epi_state["last_bc_rd"] = bce
                        po = (h * c.NG + a) % 2
                        onorm = P.op("vector", (lambda po: lambda e: e.tensor_tensor(
                            out=O_sb[po], in0=o_ps[:, :], in1=bc_sb[:],
                            op=ALU.mult))(po),
                            deps=[vb, bce, hist.get(("ost", h * c.NG + a - 2))])
                        epi_state["last_o_rd"] = onorm
                        ost = P.dma("hw_s0" if po == 0 else "hw_s1",
                            (lambda po, h, a: lambda e: e.dma_start(
                                out=O_dram[h][:, c.GQ * a:c.GQ * (a + 1)],
                                in_=O_sb[po]))(po, h, a),
                            deps=[onorm])
                        hist[("ost", h * c.NG + a)] = ost
                        if a == c.NG - 1:
                            hist[("hdone", h)] = onorm
                    return epi

                pending_epi[0] = make_epi(h, a, sm, av_ops[KC - 1])

        if pending_epi[0] is not None:
            pending_epi[0]()
            pending_epi[0] = None

        P.flush_phase("p3")
    attn_ctx.close()
    if stop_after == "p3":
        return nc

    with ExitStack() as pc:
        wo_sb = sba("wo_sb", c.FC, c.D, BF16, pc)
        Otc_sb = [sba(f"Otc{i}_sb", c.H, 128, BF16, pc) for i in range(2)]
        xot_sb = sba("xot_sb", 2, c.D, F32, pc)
        resr_sb = [pc.enter_context(nc.sbuf_tensor(f"resr{i}_sb", [128, c.D], F32))
                   for i in range(2)]
        lnscr = pc.enter_context(nc.sbuf_tensor("lnscr4", [128, c.D], F32))
        lnst = pc.enter_context(nc.sbuf_tensor("lnst4", [128, 8], F32))
        xl2_sb = [pc.enter_context(nc.sbuf_tensor(f"xl2{i}_sb", [128, c.D], BF16))
                  for i in range(2)]
        zero_sb = pc.enter_context(nc.sbuf_tensor("zero_sb", [128, 512], F32))
        wo_ps = psa("p4wo_ps", 2, 512, F32, pc)

        d_wo = P.dma("sw", lambda e: e.dma_start(
            out=wo_sb.r3, in_=wo_w.rearrange("(cc p) d -> p cc d", p=128)))
        z_zero = P.op("gpsimd", lambda e: e.memset(zero_sb[:], 0.0))
        for i in range(2 * c.TOK // 128):
            for j in range(c.FS):
                P.dma("hw2", (lambda i, j: lambda e: e.dma_start(
                    out=rank_dram[128 * i:128 * (i + 1), 512 * j:512 * (j + 1)],
                    in_=zero_sb[:]))(i, j),
                    deps=[z_zero])

        hist = {}
        prev_sq = None
        for tc in range(c.TC):
            par = tc % 2
            d_O = P.dma("hw_a" if par == 0 else "hw_b", (lambda tc, par: lambda e: e.dma_start(
                out=Otc_sb[par].r3,
                in_=O_dram[:, :, 128 * tc:128 * (tc + 1)].rearrange(
                    "h p t -> p h t")))(tc, par),
                deps=[hist.get(("owomm", tc - 2))])
            d_xo = P.dma("hw_c" if par == 0 else "hw_d", (lambda tc, par: lambda e: e.dma_start(
                out=xot_sb.t[:, c.D * par:c.D * (par + 1)],
                in_=x_own[128 * tc:128 * (tc + 1), :]))(tc, par),
                deps=[hist.get(("resev", tc - 2))])
            res_evs = []
            mm = None
            for fs in range(c.FS):
                wn = tc * c.FS + fs
                pp = wn % 2
                mm = None
                for hh in range(c.H):
                    mm = P.op("tensor", (lambda pp, par, hh, fs: lambda e: e.matmul(
                        wo_ps[pp], Otc_sb[par][hh],
                        wo_sb[hh][:, 512 * fs:512 * (fs + 1)],
                        start=(hh == 0), stop=False))(pp, par, hh, fs),
                        deps=([d_O, d_wo, hist.get(("woev", wn - 2))]
                              if hh == 0 else []))
                mmb = P.op("tensor", (lambda pp, fs: lambda e: e.matmul(
                    wo_ps[pp], ones_row[:, 0:128],
                    biaswo_sb[:, 512 * fs:512 * (fs + 1)],
                    start=False, stop=True))(pp, fs), deps=[mm])
                ev = P.op("vector", (lambda pp, par, fs: lambda e: e.tensor_tensor(
                    out=resr_sb[par][:, 512 * fs:512 * (fs + 1)],
                    in0=wo_ps[pp], in1=xot_sb.t[:, c.D * par + 512 * fs:c.D * par + 512 * (fs + 1)],
                    op=ALU.add))(pp, par, fs),
                    deps=[mmb, d_xo,
                          hist.get(("resst", tc - 2)) if fs == 0 else None,
                          hist.get(("resuse", tc - 2)) if fs == 0 else None])
                hist[("woev", wn)] = ev
                res_evs.append(ev)
            hist[("owomm", tc)] = mm
            hist[("resev", tc)] = res_evs[-1]
            ln, prev_sq = emit_ln(
                resr_sb[par][:, :], xl2_sb[par][:, :], lnscr[:, :],
                lnst[:, 4 * par:4 * par + 4],
                deps=[res_evs[-1], hist.get(("xl2st", tc - 2))],
                scr_guard=prev_sq)
            hist[("resuse", tc)] = ln
            rst = P.dma("hw_s0" if par == 0 else "hw_s1",
                (lambda par, tc: lambda e: e.dma_start(
                    out=resid_dram[128 * tc:128 * (tc + 1), :], in_=resr_sb[par][:, :]))(par, tc),
                deps=[res_evs[-1]])
            hist[("resst", tc)] = rst
            x2st = P.dma("hw_s2" if par == 0 else "hw_s3",
                (lambda par, tc: lambda e: e.dma_start(
                    out=xln2tm_dram[128 * tc:128 * (tc + 1), :], in_=xl2_sb[par][:, :]))(par, tc),
                deps=[ln])
            hist[("xl2st", tc)] = x2st

        P.flush_phase("p4a")
    if stop_after == "p4a":
        return nc

    with ExitStack() as pc:
        xl2_sb = [pc.enter_context(nc.sbuf_tensor(f"xl2b{i}_sb", [128, c.D], BF16))
                  for i in range(2)]
        xlf2_sb = sba("xlf2_sb", c.FC, c.TOK, BF16, pc)
        lg_sb = pc.enter_context(nc.sbuf_tensor("lg_sb", [128, c.E], F32))
        m1_sb = pc.enter_context(nc.sbuf_tensor("m1_sb", [128, 1], F32))
        m2_sb = pc.enter_context(nc.sbuf_tensor("m2_sb", [128, 1], F32))
        epick_sb = pc.enter_context(nc.sbuf_tensor("epick_sb", [128, 1], F32))
        msk1_sb = pc.enter_context(nc.sbuf_tensor("msk1_sb", [128, c.E], F32))
        msk12_sb = pc.enter_context(nc.sbuf_tensor("msk12_sb", [128, c.E], F32))
        mskr2_sb = pc.enter_context(nc.sbuf_tensor("mskr2_sb", [128, c.E], F32))
        lgm_sb = pc.enter_context(nc.sbuf_tensor("lgm_sb", [128, c.E], F32))
        mb_sb = sba("mb_sb", c.TC, c.E, BF16, pc)
        cum_sb = pc.enter_context(nc.sbuf_tensor("cum_sb", [128, c.E], F32))
        pick_sb = pc.enter_context(nc.sbuf_tensor("pick_sb", [128, c.E], F32))
        dest_f = pc.enter_context(nc.sbuf_tensor("dest_f", [128, 2], F32))
        dest_u = [pc.enter_context(nc.sbuf_tensor(f"dest_u{i}", [128, c.TC], U32))
                  for i in range(2)]
        diffs_sb = pc.enter_context(nc.sbuf_tensor("diffs_sb", [128, c.TC], F32))
        w1_sb = pc.enter_context(nc.sbuf_tensor("w1_sb", [128, c.TC], F32))
        w2_sb = pc.enter_context(nc.sbuf_tensor("w2_sb", [128, c.TC], F32))
        tp_ps = psa("p4tp_ps", 2, 128, BF16, pc)
        r_ps = pc.enter_context(nc.psum_tensor("p4r_ps", [128, c.E], F32))
        cum_ps = pc.enter_context(nc.psum_tensor("p4cum_ps", [128, c.E], F32))

        hist = {}
        o_df = None
        for tc in range(c.TC):
            par = tc % 2
            d_x2 = P.dma("hw_a" if par == 0 else "hw_b", (lambda tc, par: lambda e: e.dma_start(
                out=xl2_sb[par][:, :],
                in_=xln2tm_dram[128 * tc:128 * (tc + 1), :]))(tc, par),
                deps=[hist.get(("xl2use", tc - 2))])
            tp_evs = []
            for f in range(c.FC):
                n = tc * c.FC + f
                mmt = P.op("tensor", (lambda par, f, n: lambda e: e.transpose(
                    tp_ps[n % 2], xl2_sb[par][:, 128 * f:128 * (f + 1)],
                    ident_sb[:]))(par, f, n),
                    deps=[d_x2, hist.get(("tpev", n - 2))])
                evt = P.op("vector", (lambda f, tc, n, mmt: lambda e: e.tensor_copy(
                    out=xlf2_sb[f][:, 128 * tc:128 * (tc + 1)], in_=tp_ps[n % 2]))(f, tc, n, mmt),
                    deps=[mmt])
                hist[("tpev", n)] = evt
                tp_evs.append(evt)
            mmr = None
            for f in range(c.FC):
                mmr = P.op("tensor", (lambda f, tc: lambda e: e.matmul(
                    r_ps[:, :], xlf2_sb[f][:, 128 * tc:128 * (tc + 1)], gwT_sb[f],
                    start=(f == 0), stop=(f == c.FC - 1)))(f, tc),
                    deps=[tp_evs[f]] + ([hist.get(("lgev", tc - 1))] if f == 0 else []))
            lgev = P.op("vector", lambda e: e.tensor_copy(out=lg_sb[:], in_=r_ps[:, :]),
                        deps=[mmr])
            hist[("lgev", tc)] = lgev
            o_m1 = P.op("vector", lambda e: e.tensor_reduce(
                out=m1_sb[:], in_=lg_sb[:], axis=AX.X, op=ALU.max), deps=[lgev])
            o_k1 = P.op("vector", lambda e: e.tensor_scalar(
                out=msk1_sb[:], in0=lg_sb[:], scalar1=m1_sb[:], scalar2=None,
                op0=ALU.is_ge), deps=[o_m1])
            o_lgm = P.op("vector", lambda e: e.scalar_tensor_tensor(
                out=lgm_sb[:], in0=msk1_sb[:], scalar=-1e30, in1=lg_sb[:],
                op0=ALU.mult, op1=ALU.add), deps=[o_k1])
            o_m2 = P.op("vector", lambda e: e.tensor_reduce(
                out=m2_sb[:], in_=lgm_sb[:], axis=AX.X, op=ALU.max), deps=[o_lgm])
            o_k12 = P.op("vector", lambda e: e.tensor_scalar(
                out=msk12_sb[:], in0=lg_sb[:], scalar1=m2_sb[:], scalar2=None,
                op0=ALU.is_ge), deps=[o_m2])
            o_kr2 = P.op("vector", lambda e: e.tensor_tensor(
                out=mskr2_sb[:], in0=msk12_sb[:], in1=msk1_sb[:],
                op=ALU.subtract), deps=[o_k12])
            o_df = P.op("vector", (lambda tc: lambda e: e.tensor_tensor(
                out=diffs_sb[:, tc:tc + 1], in0=m1_sb[:], in1=m2_sb[:],
                op=ALU.subtract))(tc), deps=[o_m2, o_kr2])
            o_mb = P.op("vector", (lambda tc: lambda e: e.tensor_copy(
                out=mb_sb[tc], in_=msk12_sb[:]))(tc), deps=[o_k12])
            hist[("mb", tc)] = o_mb
            mcs = P.op("tensor", (lambda tc: lambda e: e.matmul(
                cum_ps[:, :], ut_sb[:], mb_sb[tc], start=True,
                stop=(tc == 0)))(tc),
                deps=[o_mb, d_ut, hist.get(("cumread", tc - 1))])
            for tcp in range(tc):
                mcs = P.op("tensor", (lambda tcp, tc: lambda e: e.matmul(
                    cum_ps[:, :], allones_sb[:], mb_sb[tcp], start=False,
                    stop=(tcp == tc - 1)))(tcp, tc),
                    deps=[mcs, hist[("mb", tcp)]])
            cumev = P.op("vector", lambda e: e.tensor_copy(out=cum_sb[:], in_=cum_ps[:, :]),
                         deps=[mcs])
            hist[("cumread", tc)] = cumev
            dsts = []
            prev_chain = cumev
            for rk, mm_sel in ((0, msk1_sb), (1, mskr2_sb)):
                o_p = P.op("vector", (lambda mm_sel: lambda e: e.tensor_tensor(
                    out=pick_sb[:], in0=cum_sb[:], in1=mm_sel[:], op=ALU.mult))(mm_sel),
                    deps=[prev_chain, o_kr2])
                o_s = P.op("vector", (lambda rk: lambda e: e.tensor_reduce(
                    out=dest_f[:, rk:rk + 1], in_=pick_sb[:], axis=AX.X,
                    op=ALU.add))(rk), deps=[o_p])
                o_c = P.op("vector", (lambda rk: lambda e: e.tensor_scalar(
                    out=dest_f[:, rk:rk + 1], in0=dest_f[:, rk:rk + 1],
                    scalar1=-1.0, scalar2=float(c.ES - 1),
                    op0=ALU.add, op1=ALU.min))(rk), deps=[o_s])
                o_pe = P.op("vector", (lambda mm_sel: lambda e: e.tensor_tensor(
                    out=pick_sb[:], in0=eoffb_sb[:], in1=mm_sel[:], op=ALU.mult))(mm_sel),
                    deps=[o_c])
                o_se = P.op("vector", lambda e: e.tensor_reduce(
                    out=epick_sb[:], in_=pick_sb[:], axis=AX.X, op=ALU.add),
                    deps=[o_pe])
                o_ad = P.op("vector", (lambda rk: lambda e: e.tensor_tensor(
                    out=dest_f[:, rk:rk + 1], in0=dest_f[:, rk:rk + 1],
                    in1=epick_sb[:], op=ALU.add))(rk), deps=[o_se])
                o_u = P.op("vector", (lambda rk, tc: lambda e: e.tensor_copy(
                    out=dest_u[rk][:, tc:tc + 1], in_=dest_f[:, rk:rk + 1]))(rk, tc),
                    deps=[o_ad])
                dsts.append(o_u)
                prev_chain = o_u
            sc_last = None
            for rk, tsrc in ((0, tokid_sb), (1, tokid2_sb)):
                s1 = P.dma("sw_a" if par == 0 else "sw_b",
                    (lambda rk, par, tc: lambda e: e.indirect_dma_start(
                    out=xg_dram,
                    out_offset=IndirectOffsetOnAxis(dest_u[rk][:, tc:tc + 1], 0),
                    in_=xl2_sb[par][:, :], in_offset=None,
                    bounds_check=bc_disp_reg, oob_is_err=False))(rk, par, tc),
                    deps=[dsts[rk], d_x2])
                s2 = P.dma("sw_a" if par == 0 else "sw_b",
                    (lambda rk, tsrc, tc: lambda e: e.indirect_dma_start(
                    out=backmap_dram,
                    out_offset=IndirectOffsetOnAxis(dest_u[rk][:, tc:tc + 1], 0),
                    in_=tsrc[:, tc:tc + 1], in_offset=None,
                    bounds_check=bc_disp_reg, oob_is_err=False))(rk, tsrc, tc),
                    deps=[dsts[rk]])
                sc_last = s2
            hist[("dispdone", tc)] = sc_last
            hist[("xl2use", tc)] = sc_last

        sg1 = P.op("scalar", lambda e: e.activation(
            out=w1_sb[:], in_=diffs_sb[:], func=AF.Sigmoid), deps=[o_df])
        sg2 = P.op("scalar", lambda e: e.activation(
            out=w2_sb[:], in_=diffs_sb[:], func=AF.Sigmoid, scale=-1.0),
            deps=[o_df])
        for tc in range(c.TC):
            for rk, wsrc, sdep in ((0, w1_sb, sg1), (1, w2_sb, sg2)):
                P.dma("sw_g0" if tc % 2 == 0 else "sw_g1",
                    (lambda rk, wsrc, tc: lambda e: e.indirect_dma_start(
                    out=gatew_dram,
                    out_offset=IndirectOffsetOnAxis(dest_u[rk][:, tc:tc + 1], 0),
                    in_=wsrc[:, tc:tc + 1], in_offset=None,
                    bounds_check=bc_disp_reg, oob_is_err=False))(rk, wsrc, tc),
                    deps=[sdep, hist[("dispdone", tc)]])
        P.dma("hw", lambda e: e.dma_start(
            out=xln2_dram.rearrange("cc p t -> p cc t"), in_=xlf2_sb.r3),
            deps=[hist[("tpev", c.TC * c.FC - 1)]])

        P.flush_phase("p4b")
    if stop_after == "p4b":
        return nc

    with ExitStack() as pc:
        xg_sb = [sba("xg0_sb", c.CC, c.D, BF16, pc)]
        xgf_sb = [sba("xgf0_sb", c.FC, c.CAP, BF16, pc),
                  sba("xgf1_sb", c.FC, c.CAP, BF16, pc)]
        bmc_sb = [sba(f"bmc{i}_sb", c.CC, 1, U32, pc) for i in range(2)]
        gwc_sb = [sba(f"gwc{i}_sb", c.CC, 1, F32, pc) for i in range(2)]
        wg_sb = [sba(f"wg{i}_sb", c.FC, 128, BF16, pc) for i in range(2)]
        wu_sb = [sba(f"wu{i}_sb", c.FC, 128, BF16, pc) for i in range(2)]
        wd_sb = [sba("wd0_sb", c.FO, c.D, BF16, pc)]
        gs_sb = pc.enter_context(nc.sbuf_tensor("gs_sb", [128, c.CAP], BF16))
        hfm_sb = [sba(f"hfm{i}_sb", c.FO, c.CAP, BF16, pc) for i in range(2)]
        dn_sb = sba("dn_sb", 2, c.D, F32, pc)
        tp_ps = psa("p5tp_ps", 2, 128, BF16, pc)
        g_ps = pc.enter_context(nc.psum_tensor("p5g_ps", [128, 512], F32))
        g2_ps = pc.enter_context(nc.psum_tensor("p5g2_ps", [128, 128], F32))
        u_ps = pc.enter_context(nc.psum_tensor("p5u_ps", [128, 512], F32))
        u2_ps = pc.enter_context(nc.psum_tensor("p5u2_ps", [128, 128], F32))
        d_ps = psa("p5d_ps", 2, 512, F32, pc)

        ngrp = [(0, min(512, c.CAP))]
        if c.CAP > 512:
            ngrp.append((512, c.CAP - 512))

        hist = {}
        for ei in range(c.E):
            par = ei % 2
            hq5 = "hw_a" if par == 0 else "hw_b"
            d_xg = P.dma(hq5, (lambda ei: lambda e: e.dma_start(
                out=xg_sb[0].r3,
                in_=xg_dram[c.ES * ei:c.ES * ei + c.CAP, :].rearrange(
                    "(t p) d -> p t d", p=128)))(ei),
                deps=[hist.get(("xguse", ei - 1))])
            d_bm = None if "bmgw" in p5skip else P.dma(hq5, (lambda ei, par: lambda e: e.dma_start(
                out=bmc_sb[par].r3,
                in_=backmap_dram[c.ES * ei:c.ES * ei + c.CAP].rearrange(
                    "(t p) one -> p t one", p=128),
                allow_slow_non_contiguous=True))(ei, par),
                deps=(hist.get(("bmuse", ei - 2)) or []))
            d_gw = None if "bmgw" in p5skip else P.dma(hq5, (lambda ei, par: lambda e: e.dma_start(
                out=gwc_sb[par].r3,
                in_=gatew_dram[c.ES * ei:c.ES * ei + c.CAP].rearrange(
                    "(t p) one -> p t one", p=128),
                allow_slow_non_contiguous=True))(ei, par),
                deps=[hist.get(("gwuse", ei - 2))])
            ld3 = [d_xg, d_bm, d_gw]
            tp_last = None
            for t in range(c.CC):
                for f in range(c.FC):
                    n = (ei * c.CC + t) * c.FC + f
                    mmt = P.op("tensor", (lambda t, f, n: lambda e: e.transpose(
                        tp_ps[n % 2], xg_sb[0][t][:, 128 * f:128 * (f + 1)],
                        ident_sb[:]))(t, f, n),
                        deps=ld3 + [hist.get(("p5tp", n - 2)),
                              hist.get(("xgfuse", ei - 2)) if (t == 0 and f == 0) else None])
                    evt = P.op("vector", (lambda par, t, f, n, mmt: lambda e: e.tensor_copy(
                        out=xgf_sb[par][f][:, 128 * t:128 * (t + 1)], in_=tp_ps[n % 2]))(par, t, f, n, mmt),
                        deps=[mmt])
                    hist[("p5tp", n)] = evt
                    tp_last = evt
            hist[("xguse", ei)] = tp_last
            for fo in range(c.FO):
                gfo = ei * c.FO + fo
                pw = gfo % 2
                d_wg = None if "wld" in p5skip else P.dma("sw_a" if pw == 0 else "sw_b",
                    (lambda ei, fo, pw: lambda e: e.dma_start(
                    out=wg_sb[pw].r3,
                    in_=w_gate[ei][:, 128 * fo:128 * (fo + 1)].rearrange(
                        "(cc p) k -> p cc k", p=128)))(ei, fo, pw),
                    deps=[hist.get(("wguse", gfo - 2))])
                d_wu = None if "wld" in p5skip else P.dma("sw_a" if pw == 0 else "sw_b",
                    (lambda ei, fo, pw: lambda e: e.dma_start(
                    out=wu_sb[pw].r3,
                    in_=w_up[ei][:, 128 * fo:128 * (fo + 1)].rearrange(
                        "(cc p) k -> p cc k", p=128)))(ei, fo, pw),
                    deps=[hist.get(("wguse", gfo - 2))])
                hlast = None
                mmu = None
                for (cg0, cgw) in ngrp:
                    gp = g_ps if cgw > 128 else g2_ps
                    up = u_ps if cgw > 128 else u2_ps
                    mmg = mmu = None
                    for f in range(c.FC):
                        mmg = None if "gu" in p5skip else P.op("tensor", (lambda pw, f, cg0, cgw, par, gp: lambda e: e.matmul(
                            gp[:, 0:cgw], wg_sb[pw][f],
                            xgf_sb[par][f][:, cg0:cg0 + cgw],
                            start=(f == 0), stop=(f == c.FC - 1)))(pw, f, cg0, cgw, par, gp),
                            deps=([d_wg, d_wu, tp_last, hist.get(("gev", gfo - 1, cg0))]
                                  if f == 0 else []))
                        mmu = None if "gu" in p5skip else P.op("tensor", (lambda pw, f, cg0, cgw, par, up: lambda e: e.matmul(
                            up[:, 0:cgw], wu_sb[pw][f],
                            xgf_sb[par][f][:, cg0:cg0 + cgw],
                            start=(f == 0), stop=(f == c.FC - 1)))(pw, f, cg0, cgw, par, up),
                            deps=([d_wg, d_wu, tp_last, hist.get(("uev", gfo - 1, cg0))]
                                  if f == 0 else []))
                    sil = None if "gu" in p5skip else P.op("scalar", (lambda cg0, cgw, gp: lambda e: e.activation(
                        out=gs_sb[:, cg0:cg0 + cgw], in_=gp[:, 0:cgw],
                        func=AF.Silu))(cg0, cgw, gp),
                        deps=[mmg, hist.get(("huse", gfo - 1, cg0))])
                    hev = None if "gu" in p5skip else P.op("vector", (lambda par, fo, cg0, cgw, up, sil: lambda e: e.tensor_tensor(
                        out=hfm_sb[par][fo][:, cg0:cg0 + cgw],
                        in0=gs_sb[:, cg0:cg0 + cgw], in1=up[:, 0:cgw],
                        op=ALU.mult))(par, fo, cg0, cgw, up, sil),
                        deps=[sil, mmu,
                              hist.get(("hread", ei - 2)) if (fo == 0 and cg0 == 0) else None])
                    hist[("gev", gfo, cg0)] = sil
                    hist[("uev", gfo, cg0)] = hev
                    hist[("huse", gfo, cg0)] = hev
                    hlast = hev
                hist[("wguse", gfo)] = mmu
                hist[("hev", ei, fo)] = hlast
            d_wd = None if "wld" in p5skip else P.dma("sw_c", (lambda ei: lambda e: e.dma_start(
                out=wd_sb[0].r3,
                in_=w_down[ei].rearrange("(cc p) d -> p cc d", p=128)))(ei),
                deps=[hist.get(("wduse", ei - 1))])
            hdeps = [hist[("hev", ei, fo)] for fo in range(c.FO)]
            last_mmd = None
            sts = []
            for t in range(c.CC):
                gt = ei * c.CC + t
                dnb = gt % 2
                evs = []
                for fs in range(c.FS):
                    dnn = gt * c.FS + fs
                    pd = dnn % 2
                    mmd = None
                    for fo in range(c.FO):
                        mmd = None if "down" in p5skip else P.op("tensor", (lambda pd, par, t, fo, fs: lambda e: e.matmul(
                            d_ps[pd], hfm_sb[par][fo][:, 128 * t:128 * (t + 1)],
                            wd_sb[0][fo][:, 512 * fs:512 * (fs + 1)],
                            start=(fo == 0), stop=(fo == c.FO - 1)))(pd, par, t, fo, fs),
                            deps=(([d_wd] + hdeps +
                                   [hist.get(("dnevn", dnn - 2))]) if fo == 0 else []))
                    evfn = (lambda pd, dnb, par, t, fs: lambda e: e.tensor_scalar(
                        out=dn_sb[dnb][:, 512 * fs:512 * (fs + 1)], in0=d_ps[pd],
                        scalar1=gwc_sb[par][t], scalar2=None, op0=ALU.mult))(pd, dnb, par, t, fs)
                    ev = None if ("down" in p5skip or "dnev" in p5skip) else P.op(
                        "vector", evfn,
                        deps=[mmd, hist.get(("dnst", gt - 2))] +
                             ([] if "dnnold3" in p5skip else ld3))
                    hist[("dnevn", dnn)] = ev
                    evs.append(ev)
                    last_mmd = mmd
                st = None if "scat" in p5skip else P.dma("sw_e" if dnb == 0 else "sw_f",
                    (lambda dnb, par, t: lambda e: e.indirect_dma_start(
                    out=rank_dram,
                    out_offset=IndirectOffsetOnAxis(bmc_sb[par][t], 0),
                    in_=dn_sb[dnb], in_offset=None,
                    bounds_check=bc_rank_reg, oob_is_err=False))(dnb, par, t),
                    deps=evs + ld3)
                hist[("dnst", gt)] = st
                sts.append(st)
            hist[("bmuse", ei)] = sts[-2:]
            hist[("gwuse", ei)] = hist[("dnevn", (ei * c.CC + c.CC) * c.FS - 1)]
            hist[("xgfuse", ei)] = last_mmd
            hist[("hread", ei)] = last_mmd
            hist[("wduse", ei)] = last_mmd
            if stop_after == f"p5e{ei}":
                break

        P.flush_phase("p5")
    if stop_after == "p5":
        return nc

    with ExitStack() as pc:
        TH = c.TOK // 2
        GW = min(512, TH)
        NCG = TH // GW
        TCH = TH // 128
        xlf2_sb = sba("xlf2b_sb", c.FC, TH, BF16, pc)
        wg_sb = [sba(f"swg{i}_sb", c.FC, 128, BF16, pc) for i in range(2)]
        wu_sb = [sba(f"swu{i}_sb", c.FC, 128, BF16, pc) for i in range(2)]
        gs_sb = pc.enter_context(nc.sbuf_tensor("bgs_sb", [128, TH], BF16))
        hfm_sb = sba("bhfm_sb", c.FO, TH, BF16, pc)
        wd_sb = sba("bwd_sb", c.FO, c.D, BF16, pc)
        res_sb = sba("bres_sb", 2, c.D, F32, pc)
        rk0_sb = sba("brk0_sb", 2, c.D, F32, pc)
        rk1_sb = sba("brk1_sb", 2, c.D, F32, pc)
        g_ps = psa("bg_ps", 2, 512, F32, pc)
        u_ps = psa("bu_ps", 2, 512, F32, pc)
        d_ps = psa("bd_ps", 2, 512, F32, pc)

        d_wd = P.dma("sw", lambda e: e.dma_start(
            out=wd_sb.r3, in_=sw_down.rearrange("(cc p) d -> p cc d", p=128)))

        hist = {}
        out_sts = []
        for half in range(2):
            hoff = half * TH
            d_x2 = P.dma("hw_x0" if half == 0 else "hw_x1",
                (lambda hoff: lambda e: e.dma_start(
                    out=xlf2_sb.r3,
                    in_=xln2_dram[:, :, hoff:hoff + TH].rearrange(
                        "cc p t -> p cc t")))(hoff),
                deps=[hist.get(("x2use", half - 1))])
            for fo in range(c.FO):
                gfo = half * c.FO + fo
                pw = gfo % 2
                d_wg = P.dma("sw_a" if pw == 0 else "sw_b",
                    (lambda fo, pw: lambda e: e.dma_start(
                        out=wg_sb[pw].r3,
                        in_=sw_gate[:, 128 * fo:128 * (fo + 1)].rearrange(
                            "(cc p) k -> p cc k", p=128)))(fo, pw),
                    deps=[hist.get(("wguse", gfo - 2))])
                d_wu = P.dma("sw_a" if pw == 0 else "sw_b",
                    (lambda fo, pw: lambda e: e.dma_start(
                        out=wu_sb[pw].r3,
                        in_=sw_up[:, 128 * fo:128 * (fo + 1)].rearrange(
                            "(cc p) k -> p cc k", p=128)))(fo, pw),
                    deps=[hist.get(("wguse", gfo - 2))])
                mmu = None
                hev = None
                for cg in range(NCG):
                    gn = gfo * NCG + cg
                    pg = gn % 2
                    mmg = mmu = None
                    for f in range(c.FC):
                        mmg = P.op("tensor", (lambda pw, pg, f, cg: lambda e: e.matmul(
                            g_ps[pg][:, 0:GW], wg_sb[pw][f],
                            xlf2_sb[f][:, GW * cg:GW * (cg + 1)],
                            start=(f == 0), stop=(f == c.FC - 1)))(pw, pg, f, cg),
                            deps=([d_wg, d_wu, d_x2, hist.get(("gev", gn - 2))]
                                  if f == 0 else []))
                        mmu = P.op("tensor", (lambda pw, pg, f, cg: lambda e: e.matmul(
                            u_ps[pg][:, 0:GW], wu_sb[pw][f],
                            xlf2_sb[f][:, GW * cg:GW * (cg + 1)],
                            start=(f == 0), stop=(f == c.FC - 1)))(pw, pg, f, cg),
                            deps=([d_wg, d_wu, d_x2, hist.get(("uev", gn - 2))]
                                  if f == 0 else []))
                    sil = P.op("scalar", (lambda pg, cg: lambda e: e.activation(
                        out=gs_sb[:, GW * cg:GW * (cg + 1)], in_=g_ps[pg][:, 0:GW],
                        func=AF.Silu))(pg, cg),
                        deps=[mmg, hist.get(("huse", gn - NCG))])
                    hev = P.op("vector", (lambda pg, fo, cg, sil: lambda e: e.tensor_tensor(
                        out=hfm_sb[fo][:, GW * cg:GW * (cg + 1)],
                        in0=gs_sb[:, GW * cg:GW * (cg + 1)], in1=u_ps[pg][:, 0:GW],
                        op=ALU.mult))(pg, fo, cg, sil),
                        deps=[sil, mmu,
                              hist.get(("hread", half - 1)) if cg == 0 else None])
                    hist[("gev", gn)] = sil
                    hist[("uev", gn)] = hev
                    hist[("huse", gn)] = hev
                hist[("wguse", gfo)] = mmu
                hist[("hev", half, fo)] = hev

            hdeps = [hist[("hev", half, fo)] for fo in range(c.FO)]
            last_mmd = None
            for tch in range(TCH):
                tc = half * TCH + tch
                par = tc % 2
                gd = hist.get(("outst", tc - 2))
                hqb = "hw_a" if par == 0 else "hw_b"
                d_rs = P.dma(hqb, (lambda tc, par: lambda e: e.dma_start(
                    out=res_sb[par], in_=resid_dram[128 * tc:128 * (tc + 1), :]))(tc, par),
                    deps=[gd])
                d_r0 = P.dma(hqb, (lambda tc, par: lambda e: e.dma_start(
                    out=rk0_sb[par], in_=rank_dram[128 * tc:128 * (tc + 1), :]))(tc, par),
                    deps=[gd])
                d_r1 = P.dma(hqb, (lambda tc, par: lambda e: e.dma_start(
                    out=rk1_sb[par],
                    in_=rank_dram[c.TOK + 128 * tc:c.TOK + 128 * (tc + 1), :]))(tc, par),
                    deps=[gd])
                d_rr = [d_rs, d_r0, d_r1]
                cmb_ops = []
                for fs in range(c.FS):
                    dnn = tc * c.FS + fs
                    pd = dnn % 2
                    mmd = None
                    for fo in range(c.FO):
                        mmd = P.op("tensor", (lambda pd, tch, fo, fs: lambda e: e.matmul(
                            d_ps[pd], hfm_sb[fo][:, 128 * tch:128 * (tch + 1)],
                            wd_sb[fo][:, 512 * fs:512 * (fs + 1)],
                            start=(fo == 0), stop=(fo == c.FO - 1)))(pd, tch, fo, fs),
                            deps=(([d_wd] + hdeps + [hist.get(("cmbev", dnn - 2))])
                                  if fo == 0 else []))
                    c1 = P.op("vector", (lambda pd, par, fs, mmd: lambda e: e.tensor_tensor(
                        out=res_sb[par][:, 512 * fs:512 * (fs + 1)], in0=d_ps[pd],
                        in1=res_sb[par][:, 512 * fs:512 * (fs + 1)], op=ALU.add))(pd, par, fs, mmd),
                        deps=[mmd, hist.get(("outst", tc - 2)) if fs == 0 else None] + d_rr)
                    c2 = P.op("vector", (lambda par, fs, c1: lambda e: e.tensor_tensor(
                        out=res_sb[par][:, 512 * fs:512 * (fs + 1)],
                        in0=res_sb[par][:, 512 * fs:512 * (fs + 1)],
                        in1=rk0_sb[par][:, 512 * fs:512 * (fs + 1)], op=ALU.add))(par, fs, c1),
                        deps=[c1])
                    c3 = P.op("vector", (lambda par, fs, c2: lambda e: e.tensor_tensor(
                        out=res_sb[par][:, 512 * fs:512 * (fs + 1)],
                        in0=res_sb[par][:, 512 * fs:512 * (fs + 1)],
                        in1=rk1_sb[par][:, 512 * fs:512 * (fs + 1)], op=ALU.add))(par, fs, c2),
                        deps=[c2])
                    hist[("cmbev", dnn)] = c3
                    cmb_ops.append(c3)
                    last_mmd = mmd
                hist[("cmb", tc)] = cmb_ops[-1]
                st = P.dma("hw_s0" if par == 0 else "hw_s1",
                    (lambda par, tc: lambda e: e.dma_start(
                        out=y_out[128 * tc:128 * (tc + 1), :], in_=res_sb[par]))(par, tc),
                    deps=cmb_ops)
                hist[("outst", tc)] = st
                out_sts.append(st)
            hist[("x2use", half)] = last_mmd
            hist[("hread", half)] = last_mmd

        for nm in dbg_names:
            dsrc = dbg_map[nm]
            t = nc.dram_tensor("dbg_" + nm, list(dsrc.shape), dsrc.dtype,
                               kind="ExternalOutput").ap()
            out_sts.append(P.dma("hw", (lambda t, dsrc: lambda e: e.dma_start(
                out=t.flatten()[None, :], in_=dsrc.flatten()[None, :]))(t, dsrc)))

        P.flush_phase("p5b")

    ctx.close()
    return nc



def _shard(cfg, x_full):
    c = cfg
    per_core = []
    for cid in range(c.NCORES):
        b = cid // c.R
        r = cid % c.R
        tiles = [r + c.R * v for v in range(c.NT)]
        rows = np.concatenate([np.arange(128 * t, 128 * (t + 1)) for t in tiles])
        xb = np.ascontiguousarray(x_full[b])
        xo = np.ascontiguousarray(x_full[b][rows])
        qg = np.ascontiguousarray(rows.astype(np.float32).reshape(1, -1))
        per_core.append((b, rows, xb, xo, qg))
    return per_core


def _wmap(inputs):
    w = {k: np.ascontiguousarray(np.asarray(v, np.float32))
         for k, v in inputs.items() if k != "L0_attn_x"}
    return {
        "wq_w": w["L0_attn_wq_w"], "wq_b": w["L0_attn_wq_b"].reshape(1, -1),
        "wk_w": w["L0_attn_wk_w"], "wk_b": w["L0_attn_wk_b"].reshape(1, -1),
        "wkl_w": w["L0_attn_wkl_w"], "wkl_b": w["L0_attn_wkl_b"].reshape(1, -1),
        "wv_w": w["L0_attn_wv_w"], "wv_b": w["L0_attn_wv_b"].reshape(1, -1),
        "wvl_w": w["L0_attn_wvl_w"], "wvl_b": w["L0_attn_wvl_b"].reshape(1, -1),
        "upk_w": w["L0_attn_upk_w"], "upk_b": w["L0_attn_upk_b"],
        "upv_w": w["L0_attn_upv_w"], "upv_b": w["L0_attn_upv_b"],
        "wo_w": w["L0_attn_wo_w"], "wo_b": w["L0_attn_wo_b"].reshape(1, -1),
        "gate_w": w["L0_ffn_gate_w"],
        "w_gate": w["L0_ffn_w_gate"], "w_up": w["L0_ffn_w_up"],
        "w_down": w["L0_ffn_w_down"],
        "sw_gate": w["L0_ffn_sw_gate"], "sw_up": w["L0_ffn_sw_up"],
        "sw_down": w["L0_ffn_sw_down"],
    }


def kernel(**inputs):
    c = Cfg()
    x = np.asarray(inputs["L0_attn_x"], np.float32)
    assert x.shape == (c.B, c.S, c.D)
    wmap = _wmap(inputs)
    nc = build_program(c)
    shards = _shard(c, x)
    in_maps = []
    for (b, rows, xb, xo, qg) in shards:
        m = dict(wmap)
        m["x_batch"] = xb
        m["x_own"] = xo
        m["qg_row"] = qg
        in_maps.append(m)
    res = run_bass_kernel_spmd(nc, in_maps, core_ids=list(range(c.NCORES)))
    out = np.empty((c.B, c.S, c.D), np.float32)
    for cid, (b, rows, _, _, _) in enumerate(shards):
        out[b][rows] = res.results[cid]["y"]
    return out


# version 10
# speedup vs baseline: 1.1626x; 1.0193x over previous
import numpy as np
import ml_dtypes
from contextlib import ExitStack

import concourse.bass as bass
import concourse.mybir as mybir
from concourse.bass_core_rust import IndirectOffsetOnAxis
from concourse.bass_utils import run_bass_kernel_spmd

F32 = mybir.dt.float32
BF16 = mybir.dt.bfloat16
U32 = mybir.dt.uint32
AF = mybir.ActivationFunctionType
ALU = mybir.AluOpType
AX = mybir.AxisListType
BF = ml_dtypes.bfloat16

SCALE = 1.0 / float(np.sqrt(128.0))
LN_EPS = 1e-5
SENT_BASE = 1 << 20


class Cfg:
    def __init__(self, S=8192, D=2048, H=16, DL=512, E=8, DF=1408, CAP=640,
                 R=4, G=4, NCORES=8):
        self.S, self.D, self.H, self.DL, self.E, self.DF = S, D, H, DL, E, DF
        self.DH = 128
        self.CAP = CAP
        self.ES = 2 * CAP
        self.R, self.G, self.NCORES = R, G, NCORES
        self.B = NCORES // R
        self.TOK = S // R
        self.NT = self.TOK // 128
        self.NG = self.NT // G
        self.GQ = 128 * G
        self.FC = D // 128
        self.LC = DL // 128
        self.KCH = S // 128
        self.TC = self.TOK // 128
        self.FO = DF // 128
        self.CC = CAP // 128
        self.FS = D // 512
        assert self.NT % G == 0 and D % 512 == 0 and DL % 128 == 0
        assert DF % 128 == 0 and S % 512 == 0 and CAP % 128 == 0
        assert self.TOK % 512 == 0

    def kc_count(self, a):
        return self.R * (a * self.G + self.G)

    def col0(self, a, kc):
        jmin = 0
        for j in range(self.G):
            if kc > self.R * (a * self.G + j) + self.R - 1:
                jmin = j + 1
        return 128 * jmin

    def masked_j(self, a, kc):
        d = kc - self.R * a * self.G
        if d < 0:
            return None
        j = d // self.R
        return j if j < self.G else None


class TA:

    def __init__(self, handle, n, inner):
        self.t = handle
        self.n = n
        self.inner = inner

    def __getitem__(self, i):
        assert 0 <= i < self.n
        return self.t[:, self.inner * i:self.inner * (i + 1)]

    @property
    def full(self):
        return self.t[:, :]

    @property
    def r3(self):
        return self.t[:, :].rearrange("p (n i) -> p n i", n=self.n)


class _Op:
    __slots__ = ("eng", "fn", "deps", "chan", "cnt", "phase", "quantum")

    def __init__(self, eng, fn, deps, chan, quantum, phase):
        self.eng, self.fn, self.deps, self.chan = eng, fn, deps, chan
        self.quantum, self.phase = quantum, phase
        self.cnt = None


class Builder:
    ENGS = ("tensor", "vector", "scalar", "gpsimd", "sync")

    @staticmethod
    def q_eng(q):
        return "sync" if q.startswith("hw") else "gpsimd"

    def __init__(self, nc):
        self.nc = nc
        self.ops = []
        self.phase = 0
        self.chans = {}
        self.chan_count = {}

    def _chan(self, name):
        if name not in self.chans:
            self.chans[name] = self.nc.alloc_semaphore(name)
            self.chan_count[name] = 0
        return self.chans[name]

    def op(self, eng, fn, deps=()):
        deps = [d for d in deps if d is not None]
        o = _Op(eng, fn, deps, "c_" + eng, 1, self.phase)
        self.ops.append(o)
        return o

    def dma(self, q, fn, deps=()):
        eng = self.q_eng(q)
        deps = [d for d in deps if d is not None]
        o = _Op(eng, fn, deps, q, 16, self.phase)
        self.ops.append(o)
        return o

    def flush_phase(self, name=None):
        ops, self.ops = self.ops, []
        cur_phase = self.phase
        self.phase += 1
        if not ops:
            return
        need_inc = set()
        dma_chans = set()
        for o in ops:
            for d in o.deps:
                if d.phase == cur_phase:
                    need_inc.add(id(d))
            if o.quantum == 16:
                need_inc.add(id(o))
                dma_chans.add(o.chan)
        for o in ops:
            if id(o) in need_inc:
                self._chan(o.chan)
                self.chan_count[o.chan] += o.quantum
                o.cnt = self.chan_count[o.chan]
        drains = {}
        for ch in dma_chans:
            drains.setdefault(self.q_eng(ch), []).append(
                (ch, self.chan_count[ch]))
        by_eng = {e: [] for e in self.ENGS}
        for o in ops:
            by_eng[o.eng].append(o)
        builder = self

        def emit_for(eng_name):
            def run(eng):
                waited = {}
                for o in by_eng[eng_name]:
                    need = {}
                    for d in o.deps:
                        if d.phase != cur_phase:
                            continue
                        assert d.cnt is not None, "dep op was never counted"
                        if need.get(d.chan, -1) < d.cnt:
                            need[d.chan] = d.cnt
                    for ch, cnt in need.items():
                        if waited.get(ch, -1) >= cnt:
                            continue
                        eng.wait_ge(builder.chans[ch], cnt)
                        waited[ch] = cnt
                    instr = o.fn(eng)
                    if o.cnt is not None:
                        instr.then_inc(builder.chans[o.chan], o.quantum)
                for ch, tot in drains.get(eng_name, []):
                    if waited.get(ch, -1) < tot:
                        eng.wait_ge(builder.chans[ch], tot)
            return run

        import contextlib
        scope = self.nc.named_scope(name) if name else contextlib.nullcontext()
        with scope:
            with self.nc.Block(name) as blk:
                for e in self.ENGS:
                    if by_eng[e] or drains.get(e):
                        getattr(blk, e)(emit_for(e))


def build_program(c: Cfg, dbg_names=(), stop_after=None):
    import os
    p5skip = set(os.environ.get("P5SKIP", "").split(",")) - {""}

    nc = bass.Bass("TRN2", target_bir_lowering=False)
    P = Builder(nc)

    def din(name, shape):
        return nc.dram_tensor(name, list(shape), F32, kind="ExternalInput").ap()

    x_batch = din("x_batch", (c.S, c.D))
    x_own = din("x_own", (c.TOK, c.D))
    qg_row = din("qg_row", (1, c.TOK))
    wq_w = din("wq_w", (c.D, c.D)); wq_b = din("wq_b", (1, c.D))
    wk_w = din("wk_w", (c.D, c.D)); wk_b = din("wk_b", (1, c.D))
    wkl_w = din("wkl_w", (c.D, c.DL)); wkl_b = din("wkl_b", (1, c.DL))
    wv_w = din("wv_w", (c.D, c.D)); wv_b = din("wv_b", (1, c.D))
    wvl_w = din("wvl_w", (c.D, c.DL)); wvl_b = din("wvl_b", (1, c.DL))
    upk_w = din("upk_w", (c.H, c.DL, c.DH)); upk_b = din("upk_b", (c.H, c.DH))
    upv_w = din("upv_w", (c.H, c.DL, c.DH)); upv_b = din("upv_b", (c.H, c.DH))
    wo_w = din("wo_w", (c.D, c.D)); wo_b = din("wo_b", (1, c.D))
    gate_w = din("gate_w", (c.E, c.D))
    w_gate = din("w_gate", (c.E, c.D, c.DF))
    w_up = din("w_up", (c.E, c.D, c.DF))
    w_down = din("w_down", (c.E, c.DF, c.D))
    sw_gate = din("sw_gate", (c.D, c.DF))
    sw_up = din("sw_up", (c.D, c.DF))
    sw_down = din("sw_down", (c.DF, c.D))

    y_out = nc.dram_tensor("y", [c.TOK, c.D], F32, kind="ExternalOutput").ap()

    def dscr(name, shape, dt):
        return nc.dram_tensor(name, list(shape), dt, kind="Internal").ap()

    klat_dram = dscr("klat_dram", (c.LC, 128, c.S), BF16)
    vlat_dram = dscr("vlat_dram", (c.LC, 128, c.S), BF16)
    K_dram = dscr("K_dram", (c.H, 128, c.S), BF16)
    V_dram = dscr("V_dram", (c.H, c.KCH, 128, 128), BF16)
    Q_dram = dscr("Q_dram", (c.H, 128, c.TOK), BF16)
    O_dram = dscr("O_dram", (c.H, 128, c.TOK), BF16)
    resid_dram = dscr("resid_dram", (c.TOK, c.D), F32)
    xln2_dram = dscr("xln2_dram", (c.FC, 128, c.TOK), BF16)
    xln2tm_dram = dscr("xln2tm_dram", (c.TOK, c.D), BF16)
    xg_dram = dscr("xg_dram", (c.E * c.ES, c.D), BF16)
    backmap_dram = dscr("backmap_dram", (c.E * c.ES, 1), U32)
    gatew_dram = dscr("gatew_dram", (c.E * c.ES, 1), F32)
    rank_dram = dscr("rank_dram", (2 * c.TOK, c.D), F32)

    dbg_map = {
        "klat": klat_dram, "vlat": vlat_dram, "K": K_dram, "V": V_dram,
        "Q": Q_dram, "O": O_dram, "resid": resid_dram, "xln2": xln2_dram,
        "xg": xg_dram, "backmap": backmap_dram, "gatew": gatew_dram,
        "rank": rank_dram,
    }

    ident_bf = nc.inline_tensor(np.eye(128, dtype=BF), "ident_bf")
    ut_bf = nc.inline_tensor(np.triu(np.ones((128, 128), dtype=BF)), "ut_bf")
    eoff_dram = nc.inline_tensor(
        (np.arange(c.E, dtype=np.float32) * c.ES).reshape(1, c.E), "eoff")

    ctx = ExitStack()

    def sb(name, shape, dt):
        return ctx.enter_context(nc.sbuf_tensor(name, list(shape), dt))

    def sba(name, n, inner, dt, stack=None):
        st = stack or ctx
        h = st.enter_context(nc.sbuf_tensor(name, [128, n * inner], dt))
        return TA(h, n, inner)

    def psa(name, n, inner, dt=F32, stack=None):
        st = stack or ctx
        return [st.enter_context(nc.psum_tensor(f"{name}{i}", [128, inner], dt))[:, :]
                for i in range(n)]

    ones_col = sb("ones_col", (128, 1), BF16)
    ones_row = sb("ones_row", (1, 512), BF16)
    onesrow_f32 = sb("onesrow_f32", (1, 128), F32)
    ident_sb = sb("ident_sb", (128, 128), BF16)
    onef_sb = sb("onef_sb", (1, 1), BF16)
    kg_col = sb("kg_col", (128, 1), F32)
    ut_sb = sb("ut_sb", (128, 128), BF16)
    allones_sb = sb("allones_sb", (128, 128), BF16)
    eoffb_sb = sb("eoffb_sb", (128, c.E), F32)
    tokid_sb = sb("tokid_sb", (128, c.TC), U32)
    tokid2_sb = sb("tokid2_sb", (128, c.TC), U32)

    biasq_sb = sb("biasq_sb", (1, c.D), BF16)
    biask_sb = sb("biask_sb", (1, c.D), BF16)
    biasv_sb = sb("biasv_sb", (1, c.D), BF16)
    biaskl_sb = sb("biaskl_sb", (1, c.DL), F32)
    biasvl_sb = sb("biasvl_sb", (1, c.DL), F32)
    biasupk_sb = sb("biasupk_sb", (1, c.H * c.DH), BF16)
    biasupv_sb = sb("biasupv_sb", (1, c.H * c.DH), BF16)
    biaswo_sb = sb("biaswo_sb", (1, c.D), BF16)
    bkm_sb = sb("bkm_sb", (1, c.DL), BF16)
    bvm_sb = sb("bvm_sb", (1, c.DL), BF16)

    gwT_sb = sba("gwT_sb", c.FC, c.E, BF16)

    const_vals = [float(LN_EPS), float(SCALE), -1.0]
    const_sb = sba("const_sb", len(const_vals), 1, F32)
    for ci, cv in enumerate(const_vals):
        nc.const_aps.register(F32, cv, const_sb[ci])

    attn_ctx = ExitStack()
    nmask = c.NG * c.R * c.G
    mask_sb = sba("mask_sb", nmask, 128, BF16, attn_ctx)

    bc_disp_reg = nc.gpsimd.alloc_register("bc_disp")
    bc_rank_reg = nc.gpsimd.alloc_register("bc_rank")

    def mask_tile(a, kc):
        kcrel = kc - c.R * a * c.G
        return mask_sb[a * (c.R * c.G) + kcrel]

    with ExitStack() as pc:
        qg_sb = pc.enter_context(nc.sbuf_tensor("qg_sb", [1, c.TOK], F32))
        qgb_sb = pc.enter_context(nc.sbuf_tensor("qgb_sb", [128, c.TOK], F32))
        gaterow_sb = pc.enter_context(
            nc.sbuf_tensor("gaterow_sb", [c.E, c.D], BF16))
        bmi_sb = pc.enter_context(
            nc.sbuf_tensor("bmi_sb", [128, c.E * c.ES // 128], U32))
        qgb_ps = pc.enter_context(nc.psum_tensor("qgb_ps", [128, 512], F32))
        tp_ps = psa("p0tp_ps", 2, 128, BF16, pc)
        eob_ps = pc.enter_context(nc.psum_tensor("eob_ps", [128, c.E], F32))
        eoffr_sb = pc.enter_context(nc.sbuf_tensor("eoffr_sb", [1, c.E], F32))

        d_id = P.dma("hw", lambda e: e.dma_start(out=ident_sb[:], in_=ident_bf.ap()))
        d_ut = P.dma("hw", lambda e: e.dma_start(out=ut_sb[:], in_=ut_bf.ap()))
        d_eo = P.dma("hw", lambda e: e.dma_start(out=eoffr_sb[:], in_=eoff_dram.ap()))
        d_qg = P.dma("hw", lambda e: e.dma_start(out=qg_sb[:], in_=qg_row))
        hw_last = d_qg

        m_oc = P.op("gpsimd", lambda e: e.memset(ones_col[:], 1.0))
        m_or = P.op("gpsimd", lambda e: e.memset(ones_row[:], 1.0))
        m_of = P.op("gpsimd", lambda e: e.memset(onesrow_f32[:], 1.0))
        m_o1 = P.op("gpsimd", lambda e: e.memset(onef_sb[:], 1.0))
        m_ao = P.op("gpsimd", lambda e: e.memset(allones_sb[:], 1.0))
        P.op("gpsimd", lambda e: e.reg_mov(bc_disp_reg, c.E * c.ES - 1))
        P.op("gpsimd", lambda e: e.reg_mov(bc_rank_reg, 2 * c.TOK - 1))
        for ci, cv in enumerate(const_vals):
            P.op("gpsimd", (lambda ci, cv: lambda e: e.memset(const_sb[ci], cv))(ci, cv))
        m_kg = P.op("gpsimd", lambda e: e.iota(
            kg_col[:], [[1, 1]], base=0, channel_multiplier=1,
            allow_small_or_imprecise_dtypes=True))
        m_t1 = P.op("gpsimd", lambda e: e.iota(
            tokid_sb[:], [[128, c.TC]], base=0, channel_multiplier=1))
        m_t2 = P.op("gpsimd", lambda e: e.iota(
            tokid2_sb[:], [[128, c.TC]], base=c.TOK, channel_multiplier=1))
        m_bm = P.op("gpsimd", lambda e: e.iota(
            bmi_sb[:], [[1, c.E * c.ES // 128]], base=SENT_BASE,
            channel_multiplier=c.E * c.ES // 128))
        P.dma("sw", lambda e: e.dma_start(
            out=backmap_dram.flatten().rearrange("(p f) -> p f", p=128),
            in_=bmi_sb[:]), deps=[m_bm])

        bias_loads = {}
        for nm, dst, src in (("q", biasq_sb, wq_b), ("k", biask_sb, wk_b),
                             ("v", biasv_sb, wv_b), ("kl", biaskl_sb, wkl_b),
                             ("vl", biasvl_sb, wvl_b), ("wo", biaswo_sb, wo_b)):
            bias_loads[nm] = P.dma(
                "sw",
                (lambda d, s: lambda e: e.dma_start(out=d[:], in_=s))(dst, src))
        d_buk = P.dma("sw", lambda e: e.dma_start(
            out=biasupk_sb[:], in_=upk_b.flatten()[None, :]))
        d_buv = P.dma("sw", lambda e: e.dma_start(
            out=biasupv_sb[:], in_=upv_b.flatten()[None, :]))
        d_gw = P.dma("sw", lambda e: e.dma_start(out=gaterow_sb[:], in_=gate_w))
        sw_last = d_gw

        qgb_cp = []
        for gi in range(c.TOK // 512):
            mm = P.op("tensor", (lambda g: lambda e: e.matmul(
                qgb_ps[:], onesrow_f32[:], qg_sb[:, 512 * g:512 * (g + 1)],
                start=True, stop=True))(gi),
                deps=[hw_last, m_of] + ([qgb_cp[-1]] if qgb_cp else []))
            cp = P.op("vector", (lambda g: lambda e: e.tensor_copy(
                out=qgb_sb[:, 512 * g:512 * (g + 1)], in_=qgb_ps[:]))(gi),
                deps=[mm])
            qgb_cp.append(cp)

        mm_eo = P.op("tensor", lambda e: e.matmul(
            eob_ps[:], onesrow_f32[:], eoffr_sb[:], start=True, stop=True),
            deps=[hw_last, m_of, qgb_cp[-1]])
        P.op("vector", lambda e: e.tensor_copy(out=eoffb_sb[:], in_=eob_ps[:]),
             deps=[mm_eo])

        for a in range(c.NG):
            for kcrel in range(c.R * c.G):
                kc = c.R * a * c.G + kcrel
                j = kcrel // c.R
                colbase = a * c.GQ + 128 * j
                mi = a * (c.R * c.G) + kcrel
                P.op("vector", (lambda mi, cb, kc: lambda e: e.tensor_scalar(
                    out=mask_sb[mi], in0=qgb_sb[:, cb:cb + 128],
                    scalar1=kg_col[:], scalar2=float(128 * kc),
                    op0=ALU.subtract, op1=ALU.is_ge))(mi, colbase, kc),
                    deps=[qgb_cp[colbase // 512], m_kg])

        prev = None
        for f in range(c.FC):
            mmt = P.op("tensor", (lambda f: lambda e: e.transpose(
                tp_ps[f % 2][0:128, 0:c.E],
                gaterow_sb[:, 128 * f:128 * (f + 1)],
                ident_sb[0:c.E, 0:c.E]))(f),
                deps=[sw_last, hw_last, prev])
            prev = P.op("vector", (lambda f, mmt: lambda e: e.tensor_copy(
                out=gwT_sb[f],
                in_=tp_ps[f % 2][0:128, 0:c.E]))(f, mmt),
                deps=[mmt])

        P.flush_phase("p0")
    if stop_after == "p0":
        return nc

    def emit_ln(src_tile, dst_tile, lnscr, stats4, deps, scr_guard=None):
        s_sum = stats4[:, 0:1]
        s_sq = stats4[:, 1:2]
        s_negm = stats4[:, 2:3]
        s_rstd = stats4[:, 3:4]
        o1 = P.op("vector", lambda e: e.tensor_reduce(
            out=s_sum, in_=src_tile, axis=AX.X, op=ALU.add), deps=deps)
        o2 = P.op("scalar", lambda e: e.activation(
            out=lnscr, in_=src_tile, func=AF.Square, accum_out=s_sq),
            deps=list(deps) + [scr_guard])
        o3 = P.op("vector", lambda e: e.tensor_scalar(
            out=s_negm, in0=s_sum, scalar1=-1.0 / c.D, scalar2=None,
            op0=ALU.mult), deps=[o1])
        o4 = P.op("vector", lambda e: e.scalar_tensor_tensor(
            out=s_sum, in0=s_negm, scalar=-1.0, in1=s_negm,
            op0=ALU.mult, op1=ALU.mult), deps=[o3])
        o5 = P.op("vector", lambda e: e.tensor_scalar(
            out=s_sq, in0=s_sq, scalar1=1.0 / c.D, scalar2=None,
            op0=ALU.mult), deps=[o2])
        o6 = P.op("vector", lambda e: e.tensor_tensor(
            out=s_sq, in0=s_sq, in1=s_sum, op=ALU.add), deps=[o4, o5])
        o7 = P.op("scalar", lambda e: e.activation(
            out=s_sq, in_=s_sq, func=AF.Sqrt, bias=float(LN_EPS)), deps=[o6])
        o8 = P.op("vector", lambda e: e.reciprocal(out=s_rstd, in_=s_sq),
                  deps=[o7])
        o9 = P.op("vector", lambda e: e.tensor_scalar(
            out=dst_tile, in0=src_tile, scalar1=s_negm, scalar2=s_rstd,
            op0=ALU.add, op1=ALU.mult), deps=[o8, o3] + list(deps))
        return o9, o2

    p1_ctx = ExitStack()
    wkm_sb = sba("wkm_sb", c.FC, c.DL, BF16, p1_ctx)
    wvm_sb = sba("wvm_sb", c.FC, c.DL, BF16, p1_ctx)

    with ExitStack() as pc:
        wl_sb = [sba("wkl_sb", c.FC, c.DL, BF16, pc),
                 sba("wvl_sb", c.FC, c.DL, BF16, pc)]
        wrow_sb = sba("wrow_sb", 2, c.D, BF16, pc)
        wT_sb = sba("wT_sb", c.FC, 128, BF16, pc)
        bcol_sb = [sba("bcolk_sb", c.FC, 1, BF16, pc),
                   sba("bcolv_sb", c.FC, 1, BF16, pc)]
        tp_ps = psa("p1tp_ps", 2, 128, BF16, pc)
        bc_ps = psa("p1bc_ps", 2, 1, F32, pc)
        mm_ps = psa("p1mm_ps", 2, 512, F32, pc)
        bm_ps = pc.enter_context(nc.psum_tensor("p1bm_ps", [1, 512], F32))

        d_wl0 = P.dma("sw", lambda e: e.dma_start(
            out=wl_sb[0].r3,
            in_=wkl_w.rearrange("(cc p) l -> p cc l", p=128)))
        d_wl1 = P.dma("sw", lambda e: e.dma_start(
            out=wl_sb[1].r3,
            in_=wvl_w.rearrange("(cc p) l -> p cc l", p=128)))
        d_wl = [[d_wl0, d_wl1], [d_wl0, d_wl1]]
        bcol_ops = {}
        prev_ev = None
        for kv in range(2):
            brow = biask_sb if kv == 0 else biasv_sb
            for i in range(c.FC):
                n = kv * c.FC + i
                mm = P.op("tensor", (lambda brow, i, n: lambda e: e.matmul(
                    bc_ps[n % 2], brow[:, 128 * i:128 * (i + 1)],
                    onef_sb[:], start=True, stop=True))(brow, i, n),
                    deps=[bias_loads["k" if kv == 0 else "v"], m_o1, prev_ev])
                ev = P.op("vector", (lambda kv, i, n, mm: lambda e: e.tensor_copy(
                    out=bcol_sb[kv][i], in_=bc_ps[n % 2]))(kv, i, n, mm),
                    deps=[mm])
                bcol_ops[(kv, i)] = ev
                prev_ev = ev

        nlo = (c.DL + 511) // 512
        hist = {}
        for kv in range(2):
            w_src = wk_w if kv == 0 else wv_w
            dstw = wkm_sb if kv == 0 else wvm_sb
            for i in range(c.FC):
                it = kv * c.FC + i
                d_row = P.dma("sw_a" if it % 2 == 0 else "sw_b",
                    (lambda w_src, i, it: lambda e: e.dma_start(
                        out=wrow_sb[it % 2], in_=w_src[128 * i:128 * (i + 1), :]))(w_src, i, it),
                    deps=[hist.get(("tpdone", it - 2))])
                tevs = []
                for j in range(c.FC):
                    ntp = it * c.FC + j
                    mmt = P.op("tensor", (lambda it, j, ntp: lambda e: e.transpose(
                        tp_ps[ntp % 2], wrow_sb[it % 2][:, 128 * j:128 * (j + 1)],
                        ident_sb[:]))(it, j, ntp),
                        deps=[d_row, hist.get(("tpev", ntp - 2)),
                              hist.get(("wTuse", it - 1)) if j == 0 else None])
                    ev = P.op("vector", (lambda j, ntp, mmt: lambda e: e.tensor_copy(
                        out=wT_sb[j], in_=tp_ps[ntp % 2]))(j, ntp, mmt), deps=[mmt])
                    hist[("tpev", ntp)] = ev
                    tevs.append(ev)
                hist[("tpdone", it)] = tevs[-1]
                for lo in range(nlo):
                    lw = min(512, c.DL - 512 * lo)
                    nn = it * nlo + lo
                    mm = None
                    for j in range(c.FC):
                        mm = P.op("tensor",
                                  (lambda nn, j, lo, lw, kv: lambda e: e.matmul(
                                      mm_ps[nn % 2][:, 0:lw],
                                      wT_sb[j],
                                      wl_sb[kv][j][:, 512 * lo:512 * lo + lw],
                                      start=(j == 0), stop=(j == c.FC - 1)))(nn, j, lo, lw, kv),
                                  deps=(([tevs[-1], hist.get(("meev", nn - 2))] + d_wl[kv]) if j == 0 else []))
                    ev = P.op("vector", (lambda dstw, i, lo, lw, nn, mm: lambda e:
                              e.tensor_copy(
                                  out=dstw[i][:, 512 * lo:512 * lo + lw],
                                  in_=mm_ps[nn % 2][:, 0:lw]))(
                                      dstw, i, lo, lw, nn, mm),
                              deps=[mm])
                    hist[("meev", nn)] = ev
                    hist[("wTuse", it)] = mm

        for kv in range(2):
            browl = biaskl_sb if kv == 0 else biasvl_sb
            dst = bkm_sb if kv == 0 else bvm_sb
            for lo in range(nlo):
                lw = min(512, c.DL - 512 * lo)
                mm = None
                for j in range(c.FC):
                    mm = P.op("tensor", (lambda kv, j, lo, lw: lambda e: e.matmul(
                        bm_ps[0:1, 0:lw], bcol_sb[kv][j],
                        wl_sb[kv][j][:, 512 * lo:512 * lo + lw],
                        start=(j == 0), stop=(j == c.FC - 1)))(kv, j, lo, lw),
                        deps=[bcol_ops[(kv, j)],
                              hist.get(("bmev", kv, lo - 1)) if j == 0 else None,
                              hist.get(("bmev", kv - 1, nlo - 1)) if (j == 0 and lo == 0) else None]
                             + (d_wl[kv] if j == 0 else []))
                ev = P.op("vector", (lambda dst, browl, lo, lw, mm: lambda e:
                          e.tensor_tensor(out=dst[:, 512 * lo:512 * lo + lw],
                                          in0=bm_ps[0:1, 0:lw],
                                          in1=browl[:, 512 * lo:512 * lo + lw],
                                          op=ALU.add))(dst, browl, lo, lw, mm),
                          deps=[mm, bias_loads["kl" if kv == 0 else "vl"]])
                hist[("bmev", kv, lo)] = ev

        P.flush_phase("p1")
    if stop_after == "p1":
        return nc

    with ExitStack() as pc:
        xb_sb = [sba("xb0_sb", 4, c.D, F32, pc), sba("xb1_sb", 4, c.D, F32, pc)]
        lnscr = pc.enter_context(nc.sbuf_tensor("lnscr", [128, c.D], F32))
        lnst = pc.enter_context(nc.sbuf_tensor("lnst", [128, 16], F32))
        xlt_sb = sba("xlt_sb", 4, c.D, BF16, pc)
        xlf_sb = sba("xlf_sb", c.FC, 512, BF16, pc)
        lat_sb = sba("lat_sb", 2, 512, BF16, pc)
        tp_ps = psa("pbtp_ps", 2, 128, BF16, pc)
        kl_ps = psa("pbkl_ps", 2, 512, F32, pc)

        hist = {}
        prev_sq = None
        for g in range(c.S // 512):
            par = g % 2
            guard = [hist.get(("xbuse", g - 2))]
            d_xh = []
            for hf in range(2):
                d_xh.append(P.dma("hw_a" if hf == 0 else "hw_b",
                    (lambda g, hf, par: lambda e: e.dma_start(
                        out=xb_sb[par].t[:, c.D * 2 * hf:c.D * 2 * (hf + 1)].rearrange(
                            "p (t d) -> p t d", t=2),
                        in_=x_batch[512 * g + 256 * hf:512 * g + 256 * (hf + 1), :].rearrange(
                            "(t p) d -> p t d", p=128)))(g, hf, par),
                    deps=guard))
            tp_last = None
            ln_ops = []
            for t in range(4):
                ln, prev_sq = emit_ln(
                    xb_sb[par][t], xlt_sb[t], lnscr[:, :], lnst[:, 4 * t:4 * t + 4],
                    deps=[d_xh[t // 2],
                          hist.get(("xltuse", g - 1)) if t == 0 else None],
                    scr_guard=prev_sq)
                ln_ops.append(ln)
                if t == 3:
                    hist[("xbuse", g)] = ln
                for f in range(c.FC):
                    n = (g * 4 + t) * c.FC + f
                    mmt = P.op("tensor", (lambda t, f, n: lambda e: e.transpose(
                        tp_ps[n % 2], xlt_sb[t][:, 128 * f:128 * (f + 1)],
                        ident_sb[:]))(t, f, n),
                        deps=[ln, hist.get(("tpev", n - 2)),
                              hist.get(("xlfuse", g - 1)) if (t == 0 and f == 0) else None])
                    ev = P.op("vector", (lambda f, t, n, mmt: lambda e: e.tensor_copy(
                        out=xlf_sb[f][:, 128 * t:128 * (t + 1)], in_=tp_ps[n % 2]))(f, t, n, mmt),
                        deps=[mmt])
                    hist[("tpev", n)] = ev
                    tp_last = ev
            hist[("xltuse", g)] = tp_last
            last_mm = None
            for kv in range(2):
                wsrc = wkm_sb if kv == 0 else wvm_sb
                bsrc = bkm_sb if kv == 0 else bvm_sb
                ddst = klat_dram if kv == 0 else vlat_dram
                for lc in range(c.LC):
                    gn = g * 2 * c.LC + kv * c.LC + lc
                    pp = gn % 2
                    mm = None
                    for f in range(c.FC):
                        mm = P.op("tensor",
                                  (lambda pp, f, lc, wsrc: lambda e: e.matmul(
                                      kl_ps[pp], wsrc[f][:, 128 * lc:128 * (lc + 1)],
                                      xlf_sb[f], start=(f == 0), stop=False))(pp, f, lc, wsrc),
                                  deps=([tp_last, hist.get(("klev", gn - 2))]
                                        if f == 0 else []))
                    mmb = P.op("tensor", (lambda pp, lc, bsrc: lambda e: e.matmul(
                        kl_ps[pp], bsrc[:, 128 * lc:128 * (lc + 1)], ones_row[:],
                        start=False, stop=True))(pp, lc, bsrc), deps=[mm])
                    ev = P.op("vector", (lambda pp, mmb, gn: lambda e: e.tensor_copy(
                        out=lat_sb[gn % 2], in_=kl_ps[pp]))(pp, mmb, gn),
                        deps=[mmb, hist.get(("latst", gn - 2))])
                    st = P.dma("hw_s0" if gn % 2 == 0 else "hw_s1",
                        (lambda gn, ddst, lc, g: lambda e: e.dma_start(
                            out=ddst[lc][:, 512 * g:512 * (g + 1)], in_=lat_sb[gn % 2]))(gn, ddst, lc, g),
                        deps=[ev])
                    hist[("klev", gn)] = ev
                    hist[("latst", gn)] = st
                    last_mm = mm
            hist[("xlfuse", g)] = last_mm

        P.flush_phase("p1b")
    p1_ctx.close()
    if stop_after == "p1b":
        return nc

    with ExitStack() as pc:
        wq_sb = sba("wq_sb", c.FC, c.D, BF16, pc)
        xo_sb = [sba("xo0_sb", 4, c.D, F32, pc)]
        lnscr = pc.enter_context(nc.sbuf_tensor("lnscr_c", [128, c.D], F32))
        lnst = pc.enter_context(nc.sbuf_tensor("lnst_c", [128, 16], F32))
        xlt_sb = sba("xlt_c", 4, c.D, BF16, pc)
        xlf_sb = sba("xlf_c", c.FC, 512, BF16, pc)
        q_sb = sba("q_sb", 2, 512, BF16, pc)
        tp_ps = psa("pctp_ps", 2, 128, BF16, pc)
        q_ps = psa("pcq_ps", 2, 512, F32, pc)

        d_wq = P.dma("sw", lambda e: e.dma_start(
            out=wq_sb.r3, in_=wq_w.rearrange("(cc p) d -> p cc d", p=128)))

        hist = {}
        prev_sq = None
        for a in range(c.TOK // 512):
            par = 0
            guard = [hist.get(("xouse", a - 1))]
            d_xh = []
            for hf in range(2):
                d_xh.append(P.dma("hw_a" if hf == 0 else "hw_b",
                    (lambda a, hf, par: lambda e: e.dma_start(
                        out=xo_sb[par].t[:, c.D * 2 * hf:c.D * 2 * (hf + 1)].rearrange(
                            "p (t d) -> p t d", t=2),
                        in_=x_own[512 * a + 256 * hf:512 * a + 256 * (hf + 1), :].rearrange(
                            "(t p) d -> p t d", p=128)))(a, hf, par),
                    deps=guard))
            tp_last = None
            for t in range(4):
                ln, prev_sq = emit_ln(
                    xo_sb[par][t], xlt_sb[t], lnscr[:, :], lnst[:, 4 * t:4 * t + 4],
                    deps=[d_xh[t // 2],
                          hist.get(("xltuse", a - 1)) if t == 0 else None],
                    scr_guard=prev_sq)
                if t == 3:
                    hist[("xouse", a)] = ln
                for f in range(c.FC):
                    n = (a * 4 + t) * c.FC + f
                    mmt = P.op("tensor", (lambda t, f, n: lambda e: e.transpose(
                        tp_ps[n % 2], xlt_sb[t][:, 128 * f:128 * (f + 1)],
                        ident_sb[:]))(t, f, n),
                        deps=[ln, hist.get(("tpev", n - 2)),
                              hist.get(("xlfuse", a - 1)) if (t == 0 and f == 0) else None])
                    ev = P.op("vector", (lambda f, t, n, mmt: lambda e: e.tensor_copy(
                        out=xlf_sb[f][:, 128 * t:128 * (t + 1)], in_=tp_ps[n % 2]))(f, t, n, mmt),
                        deps=[mmt])
                    hist[("tpev", n)] = ev
                    tp_last = ev
            hist[("xltuse", a)] = tp_last
            last_mm = None
            for h in range(c.H):
                qn = a * c.H + h
                pp = qn % 2
                mm = None
                for f in range(c.FC):
                    mm = P.op("tensor", (lambda pp, f, h: lambda e: e.matmul(
                        q_ps[pp], wq_sb[f][:, 128 * h:128 * (h + 1)],
                        xlf_sb[f], start=(f == 0), stop=False))(pp, f, h),
                        deps=([tp_last, d_wq, hist.get(("qev", qn - 2))]
                              if f == 0 else []))
                mmb = P.op("tensor", (lambda pp, h: lambda e: e.matmul(
                    q_ps[pp], biasq_sb[:, 128 * h:128 * (h + 1)], ones_row[:],
                    start=False, stop=True))(pp, h), deps=[mm])
                ev = P.op("vector", (lambda pp, mmb, qn: lambda e: e.tensor_copy(
                    out=q_sb[qn % 2], in_=q_ps[pp]))(pp, mmb, qn),
                    deps=[mmb, hist.get(("qst", qn - 2))])
                st = P.dma("hw_s0" if qn % 2 == 0 else "hw_s1",
                    (lambda qn, h, a: lambda e: e.dma_start(
                        out=Q_dram[h][:, 512 * a:512 * (a + 1)], in_=q_sb[qn % 2]))(qn, h, a),
                    deps=[ev])
                hist[("qev", qn)] = ev
                hist[("qst", qn)] = st
                last_mm = mm
            hist[("xlfuse", a)] = last_mm

        P.flush_phase("p1c")
    if stop_after == "p1c":
        return nc

    with ExitStack() as pc:
        upk_sb = sba("upk_sb", c.H * c.LC, 128, BF16, pc)
        upv_sb = sba("upv_sb", c.H * c.LC, 128, BF16, pc)
        kl_sb = [sba("p2kl0", c.LC, 512, BF16, pc), sba("p2kl1", c.LC, 512, BF16, pc)]
        vl_sb = [sba("p2vl0", c.LC, 512, BF16, pc), sba("p2vl1", c.LC, 512, BF16, pc)]
        kp_sb = sba("kp_sb", 2, 512, BF16, pc)
        vp_sb = sba("vp_sb", 2, 4 * 128, BF16, pc)
        K_ps = psa("p2K_ps", 2, 512, F32, pc)
        V_ps = psa("p2V_ps", 2, 512, F32, pc)

        d_upk = P.dma("sw", lambda e: e.dma_start(
            out=upk_sb.t[:, :].rearrange("p (h cc d) -> p h cc d", h=c.H, cc=c.LC),
            in_=upk_w.rearrange("h (cc p) d -> p h cc d", p=128)))
        d_upv = P.dma("sw", lambda e: e.dma_start(
            out=upv_sb.t[:, :].rearrange("p (h cc d) -> p h cc d", h=c.H, cc=c.LC),
            in_=upv_w.rearrange("h (cc p) d -> p h cc d", p=128)))

        hist = {}
        for ks in range(c.S // 512):
            par = ks % 2
            d_kl = P.dma("hw_a" if par == 0 else "hw_b", (lambda ks, par: lambda e: e.dma_start(
                out=kl_sb[par].r3,
                in_=klat_dram[:, :, 512 * ks:512 * (ks + 1)].rearrange(
                    "cc p s -> p cc s")))(ks, par),
                deps=[hist.get(("kluse", ks - 2))])
            d_vl = P.dma("hw_a" if par == 0 else "hw_b", (lambda ks, par: lambda e: e.dma_start(
                out=vl_sb[par].r3,
                in_=vlat_dram[:, :, 512 * ks:512 * (ks + 1)].rearrange(
                    "cc p s -> p cc s")))(ks, par),
                deps=[hist.get(("vluse", ks - 2))])
            for h in range(c.H):
                kn = ks * c.H + h
                pk = kn % 2
                mm = None
                for lc in range(c.LC):
                    mm = P.op("tensor", (lambda pk, par, h, lc: lambda e: e.matmul(
                        K_ps[pk], upk_sb[h * c.LC + lc], kl_sb[par][lc],
                        start=(lc == 0), stop=False))(pk, par, h, lc),
                        deps=([d_kl, d_vl, d_upk, d_upv, hist.get(("kev", kn - 2))]
                              if lc == 0 else []))
                mmb = P.op("tensor", (lambda pk, h: lambda e: e.matmul(
                    K_ps[pk], biasupk_sb[:, 128 * h:128 * (h + 1)], ones_row[:],
                    start=False, stop=True))(pk, h), deps=[mm])
                ev = P.op("scalar", (lambda pk, mmb: lambda e: e.copy(
                    out=kp_sb[pk], in_=K_ps[pk]))(pk, mmb),
                    deps=[mmb, hist.get(("kst", kn - 2))])
                st = P.dma("hw_sk0" if pk == 0 else "hw_sk1",
                    (lambda pk, h, ks: lambda e: e.dma_start(
                        out=K_dram[h][:, 512 * ks:512 * (ks + 1)], in_=kp_sb[pk]))(pk, h, ks),
                    deps=[ev])
                hist[("kev", kn)] = ev
                hist[("kst", kn)] = st
                if h == c.H - 1:
                    hist[("kluse", ks)] = mm
                vevs = []
                mmv = None
                for t4 in range(4):
                    vn = kn * 4 + t4
                    for lc in range(c.LC):
                        mmv = P.op("tensor", (lambda pk, par, h, lc, t4: lambda e: e.matmul(
                            V_ps[pk][:, 128 * t4:128 * (t4 + 1)],
                            vl_sb[par][lc][:, 128 * t4:128 * (t4 + 1)],
                            upv_sb[h * c.LC + lc],
                            start=(lc == 0), stop=(lc == c.LC - 1)))(pk, par, h, lc, t4),
                            deps=([d_kl, d_vl, d_upk, d_upv, hist.get(("vev", kn - 2))]
                                  if (lc == 0 and t4 == 0) else []))
                    vevs.append(mmv)
                evv = P.op("vector", (lambda pk, mmv: lambda e: e.tensor_copy(
                    out=vp_sb[pk], in_=V_ps[pk]))(pk, mmv),
                    deps=vevs + [hist.get(("vst", kn - 2))])
                hist[("vev", kn)] = evv
                vevs = [evv]
                stv = P.dma("hw_sv0" if pk == 0 else "hw_sv1",
                    (lambda pk, h, ks: lambda e: e.dma_start(
                        out=V_dram[h, 4 * ks:4 * ks + 4].rearrange("t p d -> p t d"),
                        in_=vp_sb[pk].rearrange("p (t d) -> p t d", t=4)))(pk, h, ks),
                    deps=vevs)
                hist[("vst", kn)] = stv
                if h == c.H - 1:
                    hist[("vluse", ks)] = mmv

        P.flush_phase("p2")
    if stop_after == "p2":
        return nc

    with ExitStack() as pc:
        K_sb = [pc.enter_context(nc.sbuf_tensor(f"K{i}_sb", [128, c.S], BF16))
                for i in range(2)]
        V_sb = [pc.enter_context(nc.sbuf_tensor(f"V{i}_sb", [128, c.KCH * 128], BF16))
                for i in range(2)]
        Qh_sb = [pc.enter_context(nc.sbuf_tensor(f"Q{i}_sb", [128, c.TOK], BF16))
                 for i in range(2)]
        SPD = 2
        LAG = 4
        pT_sb = sba("pT_sb", SPD, 1024, BF16, pc)
        l_bf = pc.enter_context(nc.sbuf_tensor("l_bf", [1, 512], BF16))
        rec_sb = pc.enter_context(nc.sbuf_tensor("rec_sb", [1, 512], F32))
        bc_sb = pc.enter_context(nc.sbuf_tensor("bc_sb", [128, 512], F32))
        O_sb = sba("O_sb", 2, 512, BF16, pc)
        s_ps = psa("p3s_ps", SPD, 1024, F32, pc)
        o_ps = pc.enter_context(nc.psum_tensor("p3o_ps", [128, 512], F32))
        l_ps = pc.enter_context(nc.psum_tensor("p3l_ps", [1, 512], F32))
        bc_ps = pc.enter_context(nc.psum_tensor("p3bc_ps", [128, 512], F32))

        last_sps_rd = [None] * SPD
        last_pT_rd = [None] * SPD
        epi_state = {"last_l_rd": None, "last_o_rd": None, "last_bc_rd": None}
        pending_epi = [None]
        hist = {}
        for h in range(c.H):
            par = h % 2
            hd = hist.get(("hdone", h - 2))
            hq = "hw_a" if par == 0 else "hw_b"
            d_K = P.dma(hq, (lambda h, par: lambda e: e.dma_start(
                out=K_sb[par][:, :], in_=K_dram[h]))(h, par), deps=[hd])
            d_V = P.dma(hq, (lambda h, par: lambda e: e.dma_start(
                out=V_sb[par][:, :].rearrange("p (k d) -> p k d", k=c.KCH),
                in_=V_dram[h].rearrange("k p d -> p k d")))(h, par),
                deps=[hd])
            d_Q = P.dma(hq, (lambda h, par: lambda e: e.dma_start(
                out=Qh_sb[par][:, :], in_=Q_dram[h]))(h, par), deps=[hd])
            d_KVQ = [d_K, d_V, d_Q]
            for a in range(c.NG):
                KC = c.kc_count(a)
                assert KC % 2 == 0
                sc_ops = [None] * KC
                ex_ops = [None] * (KC // 2)
                mk_ops = [None] * KC
                av_ops = [None] * KC
                sm = None

                def emit_scores(kc):
                    col0 = c.col0(a, kc)
                    pi = (kc // 2) % SPD
                    off = 512 * (kc % 2)
                    sc_ops[kc] = P.op("tensor",
                        (lambda pi, off, kc, col0, a, par: lambda e: e.matmul(
                            s_ps[pi][:, off + col0:off + 512],
                            K_sb[par][:, 128 * kc:128 * (kc + 1)],
                            Qh_sb[par][:, c.GQ * a + col0:c.GQ * (a + 1)],
                            start=True, stop=True))(pi, off, kc, col0, a, par),
                        deps=(d_KVQ + [last_sps_rd[pi]]) if kc % 2 == 0 else [])
                    if kc % 2 == 1:
                        col0p = c.col0(a, kc - 1)
                        ex_ops[kc // 2] = P.op("scalar",
                            (lambda pi, col0p: lambda e: e.activation(
                                out=pT_sb[pi][:, col0p:1024],
                                in_=s_ps[pi][:, col0p:1024],
                                func=AF.Exp, scale=SCALE))(pi, col0p),
                            deps=[sc_ops[kc], last_pT_rd[pi]])
                        last_sps_rd[pi] = ex_ops[kc // 2]
                        for k2 in (kc - 1, kc):
                            jm = c.masked_j(a, k2)
                            if jm is not None:
                                off2 = 512 * (k2 % 2)
                                mk_ops[k2] = P.op("vector",
                                    (lambda pi, a, k2, jm, off2: lambda e: e.tensor_tensor(
                                        out=pT_sb[pi][:, off2 + 128 * jm:off2 + 128 * (jm + 1)],
                                        in0=pT_sb[pi][:, off2 + 128 * jm:off2 + 128 * (jm + 1)],
                                        in1=mask_tile(a, k2), op=ALU.mult))(pi, a, k2, jm, off2),
                                    deps=[ex_ops[kc // 2]])

                def emit_smav(kc):
                    nonlocal sm
                    col0 = c.col0(a, kc)
                    pi = (kc // 2) % SPD
                    off = 512 * (kc % 2)
                    pe_dep = mk_ops[kc] or ex_ops[kc // 2]
                    sm_ops = sm2 = P.op("tensor",
                        (lambda pi, off, kc, col0, KC: lambda e: e.matmul(
                            l_ps[0:1, col0:512], ones_col[:],
                            pT_sb[pi][:, off + col0:off + 512],
                            start=(kc == 0), stop=(kc == KC - 1)))(pi, off, kc, col0, KC),
                        deps=[pe_dep] + ([epi_state["last_l_rd"]] if kc == 0 else []))
                    sm = sm2
                    av_ops[kc] = P.op("tensor",
                        (lambda pi, off, kc, col0, par: lambda e: e.matmul(
                            o_ps[:, col0:512], V_sb[par][:, 128 * kc:128 * (kc + 1)],
                            pT_sb[pi][:, off + col0:off + 512],
                            start=(kc == 0), stop=False))(pi, off, kc, col0, par),
                        deps=[pe_dep] + (d_KVQ + [epi_state["last_o_rd"]] if kc == 0 else []))
                    if kc % 2 == 1:
                        last_pT_rd[pi] = av_ops[kc]

                for i in range(KC + LAG):
                    if i - LAG >= 0:
                        emit_smav(i - LAG)
                    if i < KC:
                        emit_scores(i)
                    if i == LAG - 1 and pending_epi[0] is not None:
                        pending_epi[0]()
                        pending_epi[0] = None

                def make_epi(h, a, sm, av_last):
                    def epi():
                        nonlocal_state = None
                        self_last_l_rd = None
                        evl = P.op("vector", lambda e: e.tensor_copy(
                            out=l_bf[:], in_=l_ps[:, :]), deps=[sm])
                        rc = P.op("vector", lambda e: e.reciprocal(
                            out=rec_sb[:], in_=l_ps[:, :]), deps=[sm, evl])
                        epi_state["last_l_rd"] = rc
                        vb = P.op("tensor", (lambda h: lambda e: e.matmul(
                            o_ps[:, :], biasupv_sb[:, 128 * h:128 * (h + 1)], l_bf[:],
                            start=False, stop=True))(h), deps=[evl, av_last])
                        bc = P.op("tensor", lambda e: e.matmul(
                            bc_ps[:, :], onesrow_f32[:], rec_sb[:], start=True, stop=True),
                            deps=[rc, epi_state["last_bc_rd"]])
                        bce = P.op("scalar", lambda e: e.copy(
                            out=bc_sb[:], in_=bc_ps[:, :]), deps=[bc])
                        epi_state["last_bc_rd"] = bce
                        po = (h * c.NG + a) % 2
                        onorm = P.op("vector", (lambda po: lambda e: e.tensor_tensor(
                            out=O_sb[po], in0=o_ps[:, :], in1=bc_sb[:],
                            op=ALU.mult))(po),
                            deps=[vb, bce, hist.get(("ost", h * c.NG + a - 2))])
                        epi_state["last_o_rd"] = onorm
                        ost = P.dma("hw_s0" if po == 0 else "hw_s1",
                            (lambda po, h, a: lambda e: e.dma_start(
                                out=O_dram[h][:, c.GQ * a:c.GQ * (a + 1)],
                                in_=O_sb[po]))(po, h, a),
                            deps=[onorm])
                        hist[("ost", h * c.NG + a)] = ost
                        if a == c.NG - 1:
                            hist[("hdone", h)] = onorm
                    return epi

                pending_epi[0] = make_epi(h, a, sm, av_ops[KC - 1])

        if pending_epi[0] is not None:
            pending_epi[0]()
            pending_epi[0] = None

        P.flush_phase("p3")
    attn_ctx.close()
    if stop_after == "p3":
        return nc

    with ExitStack() as pc:
        wo_sb = sba("wo_sb", c.FC, c.D, BF16, pc)
        Otc_sb = [sba(f"Otc{i}_sb", c.H, 128, BF16, pc) for i in range(2)]
        xot_sb = sba("xot_sb", 2, c.D, F32, pc)
        resr_sb = [pc.enter_context(nc.sbuf_tensor(f"resr{i}_sb", [128, c.D], F32))
                   for i in range(2)]
        lnscr = pc.enter_context(nc.sbuf_tensor("lnscr4", [128, c.D], F32))
        lnst = pc.enter_context(nc.sbuf_tensor("lnst4", [128, 8], F32))
        xl2_sb = [pc.enter_context(nc.sbuf_tensor(f"xl2{i}_sb", [128, c.D], BF16))
                  for i in range(2)]
        zero_sb = pc.enter_context(nc.sbuf_tensor("zero_sb", [128, 512], F32))
        wo_ps = psa("p4wo_ps", 2, 512, F32, pc)

        d_wo = P.dma("sw", lambda e: e.dma_start(
            out=wo_sb.r3, in_=wo_w.rearrange("(cc p) d -> p cc d", p=128)))
        z_zero = P.op("gpsimd", lambda e: e.memset(zero_sb[:], 0.0))
        for i in range(2 * c.TOK // 128):
            for j in range(c.FS):
                P.dma("hw2", (lambda i, j: lambda e: e.dma_start(
                    out=rank_dram[128 * i:128 * (i + 1), 512 * j:512 * (j + 1)],
                    in_=zero_sb[:]))(i, j),
                    deps=[z_zero])

        hist = {}
        prev_sq = None
        for tc in range(c.TC):
            par = tc % 2
            d_O = P.dma("hw_a" if par == 0 else "hw_b", (lambda tc, par: lambda e: e.dma_start(
                out=Otc_sb[par].r3,
                in_=O_dram[:, :, 128 * tc:128 * (tc + 1)].rearrange(
                    "h p t -> p h t")))(tc, par),
                deps=[hist.get(("owomm", tc - 2))])
            d_xo = P.dma("hw_c" if par == 0 else "hw_d", (lambda tc, par: lambda e: e.dma_start(
                out=xot_sb.t[:, c.D * par:c.D * (par + 1)],
                in_=x_own[128 * tc:128 * (tc + 1), :]))(tc, par),
                deps=[hist.get(("resev", tc - 2))])
            res_evs = []
            mm = None
            for fs in range(c.FS):
                wn = tc * c.FS + fs
                pp = wn % 2
                mm = None
                for hh in range(c.H):
                    mm = P.op("tensor", (lambda pp, par, hh, fs: lambda e: e.matmul(
                        wo_ps[pp], Otc_sb[par][hh],
                        wo_sb[hh][:, 512 * fs:512 * (fs + 1)],
                        start=(hh == 0), stop=False))(pp, par, hh, fs),
                        deps=([d_O, d_wo, hist.get(("woev", wn - 2))]
                              if hh == 0 else []))
                mmb = P.op("tensor", (lambda pp, fs: lambda e: e.matmul(
                    wo_ps[pp], ones_row[:, 0:128],
                    biaswo_sb[:, 512 * fs:512 * (fs + 1)],
                    start=False, stop=True))(pp, fs), deps=[mm])
                ev = P.op("vector", (lambda pp, par, fs: lambda e: e.tensor_tensor(
                    out=resr_sb[par][:, 512 * fs:512 * (fs + 1)],
                    in0=wo_ps[pp], in1=xot_sb.t[:, c.D * par + 512 * fs:c.D * par + 512 * (fs + 1)],
                    op=ALU.add))(pp, par, fs),
                    deps=[mmb, d_xo,
                          hist.get(("resst", tc - 2)) if fs == 0 else None,
                          hist.get(("resuse", tc - 2)) if fs == 0 else None])
                hist[("woev", wn)] = ev
                res_evs.append(ev)
            hist[("owomm", tc)] = mm
            hist[("resev", tc)] = res_evs[-1]
            ln, prev_sq = emit_ln(
                resr_sb[par][:, :], xl2_sb[par][:, :], lnscr[:, :],
                lnst[:, 4 * par:4 * par + 4],
                deps=[res_evs[-1], hist.get(("xl2st", tc - 2))],
                scr_guard=prev_sq)
            hist[("resuse", tc)] = ln
            rst = P.dma("hw_s0" if par == 0 else "hw_s1",
                (lambda par, tc: lambda e: e.dma_start(
                    out=resid_dram[128 * tc:128 * (tc + 1), :], in_=resr_sb[par][:, :]))(par, tc),
                deps=[res_evs[-1]])
            hist[("resst", tc)] = rst
            x2st = P.dma("hw_s2" if par == 0 else "hw_s3",
                (lambda par, tc: lambda e: e.dma_start(
                    out=xln2tm_dram[128 * tc:128 * (tc + 1), :], in_=xl2_sb[par][:, :]))(par, tc),
                deps=[ln])
            hist[("xl2st", tc)] = x2st

        P.flush_phase("p4a")
    if stop_after == "p4a":
        return nc

    with ExitStack() as pc:
        xl2_sb = [pc.enter_context(nc.sbuf_tensor(f"xl2b{i}_sb", [128, c.D], BF16))
                  for i in range(2)]
        xlf2_sb = sba("xlf2_sb", c.FC, c.TOK, BF16, pc)
        lg_sb = pc.enter_context(nc.sbuf_tensor("lg_sb", [128, c.E], F32))
        m1_sb = pc.enter_context(nc.sbuf_tensor("m1_sb", [128, 1], F32))
        m2_sb = pc.enter_context(nc.sbuf_tensor("m2_sb", [128, 1], F32))
        epick_sb = pc.enter_context(nc.sbuf_tensor("epick_sb", [128, 1], F32))
        msk1_sb = pc.enter_context(nc.sbuf_tensor("msk1_sb", [128, c.E], F32))
        msk12_sb = pc.enter_context(nc.sbuf_tensor("msk12_sb", [128, c.E], F32))
        mskr2_sb = pc.enter_context(nc.sbuf_tensor("mskr2_sb", [128, c.E], F32))
        lgm_sb = pc.enter_context(nc.sbuf_tensor("lgm_sb", [128, c.E], F32))
        mb_sb = sba("mb_sb", c.TC, c.E, BF16, pc)
        cum_sb = pc.enter_context(nc.sbuf_tensor("cum_sb", [128, c.E], F32))
        pick_sb = pc.enter_context(nc.sbuf_tensor("pick_sb", [128, c.E], F32))
        dest_f = pc.enter_context(nc.sbuf_tensor("dest_f", [128, 2], F32))
        dest_u = [pc.enter_context(nc.sbuf_tensor(f"dest_u{i}", [128, c.TC], U32))
                  for i in range(2)]
        diffs_sb = pc.enter_context(nc.sbuf_tensor("diffs_sb", [128, c.TC], F32))
        w1_sb = pc.enter_context(nc.sbuf_tensor("w1_sb", [128, c.TC], F32))
        w2_sb = pc.enter_context(nc.sbuf_tensor("w2_sb", [128, c.TC], F32))
        tp_ps = psa("p4tp_ps", 2, 128, BF16, pc)
        r_ps = pc.enter_context(nc.psum_tensor("p4r_ps", [128, c.E], F32))
        cum_ps = pc.enter_context(nc.psum_tensor("p4cum_ps", [128, c.E], F32))

        hist = {}
        o_df = None
        for tc in range(c.TC):
            par = tc % 2
            d_x2 = P.dma("hw_a" if par == 0 else "hw_b", (lambda tc, par: lambda e: e.dma_start(
                out=xl2_sb[par][:, :],
                in_=xln2tm_dram[128 * tc:128 * (tc + 1), :]))(tc, par),
                deps=[hist.get(("xl2use", tc - 2))])
            tp_evs = []
            for f in range(c.FC):
                n = tc * c.FC + f
                mmt = P.op("tensor", (lambda par, f, n: lambda e: e.transpose(
                    tp_ps[n % 2], xl2_sb[par][:, 128 * f:128 * (f + 1)],
                    ident_sb[:]))(par, f, n),
                    deps=[d_x2, hist.get(("tpev", n - 2))])
                evt = P.op("vector", (lambda f, tc, n, mmt: lambda e: e.tensor_copy(
                    out=xlf2_sb[f][:, 128 * tc:128 * (tc + 1)], in_=tp_ps[n % 2]))(f, tc, n, mmt),
                    deps=[mmt])
                hist[("tpev", n)] = evt
                tp_evs.append(evt)
            mmr = None
            for f in range(c.FC):
                mmr = P.op("tensor", (lambda f, tc: lambda e: e.matmul(
                    r_ps[:, :], xlf2_sb[f][:, 128 * tc:128 * (tc + 1)], gwT_sb[f],
                    start=(f == 0), stop=(f == c.FC - 1)))(f, tc),
                    deps=[tp_evs[f]] + ([hist.get(("lgev", tc - 1))] if f == 0 else []))
            lgev = P.op("vector", lambda e: e.tensor_copy(out=lg_sb[:], in_=r_ps[:, :]),
                        deps=[mmr])
            hist[("lgev", tc)] = lgev
            o_m1 = P.op("vector", lambda e: e.tensor_reduce(
                out=m1_sb[:], in_=lg_sb[:], axis=AX.X, op=ALU.max), deps=[lgev])
            o_k1 = P.op("vector", lambda e: e.tensor_scalar(
                out=msk1_sb[:], in0=lg_sb[:], scalar1=m1_sb[:], scalar2=None,
                op0=ALU.is_ge), deps=[o_m1])
            o_lgm = P.op("vector", lambda e: e.scalar_tensor_tensor(
                out=lgm_sb[:], in0=msk1_sb[:], scalar=-1e30, in1=lg_sb[:],
                op0=ALU.mult, op1=ALU.add), deps=[o_k1])
            o_m2 = P.op("vector", lambda e: e.tensor_reduce(
                out=m2_sb[:], in_=lgm_sb[:], axis=AX.X, op=ALU.max), deps=[o_lgm])
            o_k12 = P.op("vector", lambda e: e.tensor_scalar(
                out=msk12_sb[:], in0=lg_sb[:], scalar1=m2_sb[:], scalar2=None,
                op0=ALU.is_ge), deps=[o_m2])
            o_kr2 = P.op("vector", lambda e: e.tensor_tensor(
                out=mskr2_sb[:], in0=msk12_sb[:], in1=msk1_sb[:],
                op=ALU.subtract), deps=[o_k12])
            o_df = P.op("vector", (lambda tc: lambda e: e.tensor_tensor(
                out=diffs_sb[:, tc:tc + 1], in0=m1_sb[:], in1=m2_sb[:],
                op=ALU.subtract))(tc), deps=[o_m2, o_kr2])
            o_mb = P.op("vector", (lambda tc: lambda e: e.tensor_copy(
                out=mb_sb[tc], in_=msk12_sb[:]))(tc), deps=[o_k12])
            hist[("mb", tc)] = o_mb
            mcs = P.op("tensor", (lambda tc: lambda e: e.matmul(
                cum_ps[:, :], ut_sb[:], mb_sb[tc], start=True,
                stop=(tc == 0)))(tc),
                deps=[o_mb, d_ut, hist.get(("cumread", tc - 1))])
            for tcp in range(tc):
                mcs = P.op("tensor", (lambda tcp, tc: lambda e: e.matmul(
                    cum_ps[:, :], allones_sb[:], mb_sb[tcp], start=False,
                    stop=(tcp == tc - 1)))(tcp, tc),
                    deps=[mcs, hist[("mb", tcp)]])
            cumev = P.op("vector", lambda e: e.tensor_copy(out=cum_sb[:], in_=cum_ps[:, :]),
                         deps=[mcs])
            hist[("cumread", tc)] = cumev
            dsts = []
            prev_chain = cumev
            for rk, mm_sel in ((0, msk1_sb), (1, mskr2_sb)):
                o_p = P.op("vector", (lambda mm_sel: lambda e: e.tensor_tensor(
                    out=pick_sb[:], in0=cum_sb[:], in1=mm_sel[:], op=ALU.mult))(mm_sel),
                    deps=[prev_chain, o_kr2])
                o_s = P.op("vector", (lambda rk: lambda e: e.tensor_reduce(
                    out=dest_f[:, rk:rk + 1], in_=pick_sb[:], axis=AX.X,
                    op=ALU.add))(rk), deps=[o_p])
                o_c = P.op("vector", (lambda rk: lambda e: e.tensor_scalar(
                    out=dest_f[:, rk:rk + 1], in0=dest_f[:, rk:rk + 1],
                    scalar1=-1.0, scalar2=float(c.ES - 1),
                    op0=ALU.add, op1=ALU.min))(rk), deps=[o_s])
                o_pe = P.op("vector", (lambda mm_sel: lambda e: e.tensor_tensor(
                    out=pick_sb[:], in0=eoffb_sb[:], in1=mm_sel[:], op=ALU.mult))(mm_sel),
                    deps=[o_c])
                o_se = P.op("vector", lambda e: e.tensor_reduce(
                    out=epick_sb[:], in_=pick_sb[:], axis=AX.X, op=ALU.add),
                    deps=[o_pe])
                o_ad = P.op("vector", (lambda rk: lambda e: e.tensor_tensor(
                    out=dest_f[:, rk:rk + 1], in0=dest_f[:, rk:rk + 1],
                    in1=epick_sb[:], op=ALU.add))(rk), deps=[o_se])
                o_u = P.op("vector", (lambda rk, tc: lambda e: e.tensor_copy(
                    out=dest_u[rk][:, tc:tc + 1], in_=dest_f[:, rk:rk + 1]))(rk, tc),
                    deps=[o_ad])
                dsts.append(o_u)
                prev_chain = o_u
            sc_last = None
            for rk, tsrc in ((0, tokid_sb), (1, tokid2_sb)):
                s1 = P.dma("sw_a" if par == 0 else "sw_b",
                    (lambda rk, par, tc: lambda e: e.indirect_dma_start(
                    out=xg_dram,
                    out_offset=IndirectOffsetOnAxis(dest_u[rk][:, tc:tc + 1], 0),
                    in_=xl2_sb[par][:, :], in_offset=None,
                    bounds_check=bc_disp_reg, oob_is_err=False))(rk, par, tc),
                    deps=[dsts[rk], d_x2])
                s2 = P.dma("sw_a" if par == 0 else "sw_b",
                    (lambda rk, tsrc, tc: lambda e: e.indirect_dma_start(
                    out=backmap_dram,
                    out_offset=IndirectOffsetOnAxis(dest_u[rk][:, tc:tc + 1], 0),
                    in_=tsrc[:, tc:tc + 1], in_offset=None,
                    bounds_check=bc_disp_reg, oob_is_err=False))(rk, tsrc, tc),
                    deps=[dsts[rk]])
                sc_last = s2
            hist[("dispdone", tc)] = sc_last
            hist[("xl2use", tc)] = sc_last

        sg1 = P.op("scalar", lambda e: e.activation(
            out=w1_sb[:], in_=diffs_sb[:], func=AF.Sigmoid), deps=[o_df])
        sg2 = P.op("scalar", lambda e: e.activation(
            out=w2_sb[:], in_=diffs_sb[:], func=AF.Sigmoid, scale=-1.0),
            deps=[o_df])
        for tc in range(c.TC):
            for rk, wsrc, sdep in ((0, w1_sb, sg1), (1, w2_sb, sg2)):
                P.dma("sw_g0" if tc % 2 == 0 else "sw_g1",
                    (lambda rk, wsrc, tc: lambda e: e.indirect_dma_start(
                    out=gatew_dram,
                    out_offset=IndirectOffsetOnAxis(dest_u[rk][:, tc:tc + 1], 0),
                    in_=wsrc[:, tc:tc + 1], in_offset=None,
                    bounds_check=bc_disp_reg, oob_is_err=False))(rk, wsrc, tc),
                    deps=[sdep, hist[("dispdone", tc)]])
        P.dma("hw", lambda e: e.dma_start(
            out=xln2_dram.rearrange("cc p t -> p cc t"), in_=xlf2_sb.r3),
            deps=[hist[("tpev", c.TC * c.FC - 1)]])

        P.flush_phase("p4b")
    if stop_after == "p4b":
        return nc

    with ExitStack() as pc:
        xg_sb = [sba("xg0_sb", c.CC, c.D, BF16, pc)]
        xgf_sb = [sba("xgf0_sb", c.FC, c.CAP, BF16, pc),
                  sba("xgf1_sb", c.FC, c.CAP, BF16, pc)]
        bmc_sb = [sba(f"bmc{i}_sb", c.CC, 1, U32, pc) for i in range(2)]
        gwc_sb = [sba(f"gwc{i}_sb", c.CC, 1, F32, pc) for i in range(2)]
        wg_sb = [sba(f"wg{i}_sb", c.FC, 128, BF16, pc) for i in range(2)]
        wu_sb = [sba(f"wu{i}_sb", c.FC, 128, BF16, pc) for i in range(2)]
        wd_sb = [sba("wd0_sb", c.FO, c.D, BF16, pc)]
        gs_sb = pc.enter_context(nc.sbuf_tensor("gs_sb", [128, c.CAP], BF16))
        hfm_sb = [sba(f"hfm{i}_sb", c.FO, c.CAP, BF16, pc) for i in range(2)]
        dn_sb = sba("dn_sb", 2, c.D, F32, pc)
        tp_ps = psa("p5tp_ps", 2, 128, BF16, pc)
        g_ps = pc.enter_context(nc.psum_tensor("p5g_ps", [128, 512], F32))
        g2_ps = pc.enter_context(nc.psum_tensor("p5g2_ps", [128, 128], F32))
        u_ps = pc.enter_context(nc.psum_tensor("p5u_ps", [128, 512], F32))
        u2_ps = pc.enter_context(nc.psum_tensor("p5u2_ps", [128, 128], F32))
        d_ps = psa("p5d_ps", 2, 512, F32, pc)

        ngrp = [(0, min(512, c.CAP))]
        if c.CAP > 512:
            ngrp.append((512, c.CAP - 512))

        hist = {}
        for ei in range(c.E):
            par = ei % 2
            hq5 = "hw_a" if par == 0 else "hw_b"
            d_xg = P.dma(hq5, (lambda ei: lambda e: e.dma_start(
                out=xg_sb[0].r3,
                in_=xg_dram[c.ES * ei:c.ES * ei + c.CAP, :].rearrange(
                    "(t p) d -> p t d", p=128)))(ei),
                deps=[hist.get(("xguse", ei - 1))])
            d_bm = None if "bmgw" in p5skip else P.dma(hq5, (lambda ei, par: lambda e: e.dma_start(
                out=bmc_sb[par].r3,
                in_=backmap_dram[c.ES * ei:c.ES * ei + c.CAP].rearrange(
                    "(t p) one -> p t one", p=128),
                allow_slow_non_contiguous=True))(ei, par),
                deps=(hist.get(("bmuse", ei - 2)) or []))
            d_gw = None if "bmgw" in p5skip else P.dma(hq5, (lambda ei, par: lambda e: e.dma_start(
                out=gwc_sb[par].r3,
                in_=gatew_dram[c.ES * ei:c.ES * ei + c.CAP].rearrange(
                    "(t p) one -> p t one", p=128),
                allow_slow_non_contiguous=True))(ei, par),
                deps=[hist.get(("gwuse", ei - 2))])
            ld3 = [d_xg, d_bm, d_gw]
            tp_last = None
            for t in range(c.CC):
                for f in range(c.FC):
                    n = (ei * c.CC + t) * c.FC + f
                    mmt = P.op("tensor", (lambda t, f, n: lambda e: e.transpose(
                        tp_ps[n % 2], xg_sb[0][t][:, 128 * f:128 * (f + 1)],
                        ident_sb[:]))(t, f, n),
                        deps=ld3 + [hist.get(("p5tp", n - 2)),
                              hist.get(("xgfuse", ei - 2)) if (t == 0 and f == 0) else None])
                    evt = P.op("vector", (lambda par, t, f, n, mmt: lambda e: e.tensor_copy(
                        out=xgf_sb[par][f][:, 128 * t:128 * (t + 1)], in_=tp_ps[n % 2]))(par, t, f, n, mmt),
                        deps=[mmt])
                    hist[("p5tp", n)] = evt
                    tp_last = evt
            hist[("xguse", ei)] = tp_last
            d_wd = P.dma("sw_c", (lambda ei: lambda e: e.dma_start(
                out=wd_sb[0].r3,
                in_=w_down[ei].rearrange("(cc p) d -> p cc d", p=128)))(ei),
                deps=[hist.get(("wduse", ei - 1))])
            for fo in range(c.FO):
                gfo = ei * c.FO + fo
                pw = gfo % 2
                d_wg = None if "wld" in p5skip else P.dma("sw_a" if pw == 0 else "sw_b",
                    (lambda ei, fo, pw: lambda e: e.dma_start(
                    out=wg_sb[pw].r3,
                    in_=w_gate[ei][:, 128 * fo:128 * (fo + 1)].rearrange(
                        "(cc p) k -> p cc k", p=128)))(ei, fo, pw),
                    deps=[hist.get(("wguse", gfo - 2))])
                d_wu = None if "wld" in p5skip else P.dma("sw_a" if pw == 0 else "sw_b",
                    (lambda ei, fo, pw: lambda e: e.dma_start(
                    out=wu_sb[pw].r3,
                    in_=w_up[ei][:, 128 * fo:128 * (fo + 1)].rearrange(
                        "(cc p) k -> p cc k", p=128)))(ei, fo, pw),
                    deps=[hist.get(("wguse", gfo - 2))])
                hlast = None
                mmu = None
                for (cg0, cgw) in ngrp:
                    gp = g_ps if cgw > 128 else g2_ps
                    up = u_ps if cgw > 128 else u2_ps
                    mmg = mmu = None
                    for f in range(c.FC):
                        mmg = None if "gu" in p5skip else P.op("tensor", (lambda pw, f, cg0, cgw, par, gp: lambda e: e.matmul(
                            gp[:, 0:cgw], wg_sb[pw][f],
                            xgf_sb[par][f][:, cg0:cg0 + cgw],
                            start=(f == 0), stop=(f == c.FC - 1)))(pw, f, cg0, cgw, par, gp),
                            deps=([d_wg, d_wu, tp_last, hist.get(("gev", gfo - 1, cg0))]
                                  if f == 0 else []))
                        mmu = None if "gu" in p5skip else P.op("tensor", (lambda pw, f, cg0, cgw, par, up: lambda e: e.matmul(
                            up[:, 0:cgw], wu_sb[pw][f],
                            xgf_sb[par][f][:, cg0:cg0 + cgw],
                            start=(f == 0), stop=(f == c.FC - 1)))(pw, f, cg0, cgw, par, up),
                            deps=([d_wg, d_wu, tp_last, hist.get(("uev", gfo - 1, cg0))]
                                  if f == 0 else []))
                    sil = None if "gu" in p5skip else P.op("scalar", (lambda cg0, cgw, gp: lambda e: e.activation(
                        out=gs_sb[:, cg0:cg0 + cgw], in_=gp[:, 0:cgw],
                        func=AF.Silu))(cg0, cgw, gp),
                        deps=[mmg, hist.get(("huse", gfo - 1, cg0))])
                    hev = None if "gu" in p5skip else P.op("vector", (lambda par, fo, cg0, cgw, up, sil: lambda e: e.tensor_tensor(
                        out=hfm_sb[par][fo][:, cg0:cg0 + cgw],
                        in0=gs_sb[:, cg0:cg0 + cgw], in1=up[:, 0:cgw],
                        op=ALU.mult))(par, fo, cg0, cgw, up, sil),
                        deps=[sil, mmu,
                              hist.get(("hread", ei - 2)) if (fo == 0 and cg0 == 0) else None])
                    hist[("gev", gfo, cg0)] = sil
                    hist[("uev", gfo, cg0)] = hev
                    hist[("huse", gfo, cg0)] = hev
                    hlast = hev
                hist[("wguse", gfo)] = mmu
                hist[("hev", ei, fo)] = hlast
            hdeps = [hist[("hev", ei, fo)] for fo in range(c.FO)]
            last_mmd = None
            sts = []
            for t in range(c.CC):
                gt = ei * c.CC + t
                dnb = gt % 2
                evs = []
                for fs in range(c.FS):
                    dnn = gt * c.FS + fs
                    pd = dnn % 2
                    mmd = None
                    for fo in range(c.FO):
                        mmd = None if "down" in p5skip else P.op("tensor", (lambda pd, par, t, fo, fs: lambda e: e.matmul(
                            d_ps[pd], hfm_sb[par][fo][:, 128 * t:128 * (t + 1)],
                            wd_sb[0][fo][:, 512 * fs:512 * (fs + 1)],
                            start=(fo == 0), stop=(fo == c.FO - 1)))(pd, par, t, fo, fs),
                            deps=(([d_wd] + hdeps +
                                   [hist.get(("dnevn", dnn - 2))]) if fo == 0 else []))
                    evfn = (lambda pd, dnb, par, t, fs: lambda e: e.tensor_scalar(
                        out=dn_sb[dnb][:, 512 * fs:512 * (fs + 1)], in0=d_ps[pd],
                        scalar1=gwc_sb[par][t], scalar2=None, op0=ALU.mult))(pd, dnb, par, t, fs)
                    ev = None if ("down" in p5skip or "dnev" in p5skip) else P.op(
                        "vector", evfn,
                        deps=[mmd, hist.get(("dnst", gt - 2))] +
                             ([] if "dnnold3" in p5skip else ld3))
                    hist[("dnevn", dnn)] = ev
                    evs.append(ev)
                    last_mmd = mmd
                st = None if "scat" in p5skip else P.dma("sw_e" if dnb == 0 else "sw_f",
                    (lambda dnb, par, t: lambda e: e.indirect_dma_start(
                    out=rank_dram,
                    out_offset=IndirectOffsetOnAxis(bmc_sb[par][t], 0),
                    in_=dn_sb[dnb], in_offset=None,
                    bounds_check=bc_rank_reg, oob_is_err=False))(dnb, par, t),
                    deps=evs + ld3)
                hist[("dnst", gt)] = st
                sts.append(st)
            hist[("bmuse", ei)] = sts[-2:]
            hist[("gwuse", ei)] = hist[("dnevn", (ei * c.CC + c.CC) * c.FS - 1)]
            hist[("xgfuse", ei)] = last_mmd
            hist[("hread", ei)] = last_mmd
            hist[("wduse", ei)] = last_mmd
            if stop_after == f"p5e{ei}":
                break

        P.flush_phase("p5")
    if stop_after == "p5":
        return nc

    with ExitStack() as pc:
        TH = c.TOK // 2
        GW = min(512, TH)
        NCG = TH // GW
        TCH = TH // 128
        xlf2_sb = sba("xlf2b_sb", c.FC, TH, BF16, pc)
        wg_sb = [sba(f"swg{i}_sb", c.FC, 128, BF16, pc) for i in range(2)]
        wu_sb = [sba(f"swu{i}_sb", c.FC, 128, BF16, pc) for i in range(2)]
        gs_sb = pc.enter_context(nc.sbuf_tensor("bgs_sb", [128, TH], BF16))
        hfm_sb = sba("bhfm_sb", c.FO, TH, BF16, pc)
        wd_sb = sba("bwd_sb", c.FO, c.D, BF16, pc)
        res_sb = sba("bres_sb", 2, c.D, F32, pc)
        rk0_sb = sba("brk0_sb", 2, c.D, F32, pc)
        rk1_sb = sba("brk1_sb", 2, c.D, F32, pc)
        g_ps = psa("bg_ps", 2, 512, F32, pc)
        u_ps = psa("bu_ps", 2, 512, F32, pc)
        d_ps = psa("bd_ps", 2, 512, F32, pc)

        d_wd = P.dma("sw", lambda e: e.dma_start(
            out=wd_sb.r3, in_=sw_down.rearrange("(cc p) d -> p cc d", p=128)))

        hist = {}
        out_sts = []
        for half in range(2):
            hoff = half * TH
            d_x2 = P.dma("hw_x0" if half == 0 else "hw_x1",
                (lambda hoff: lambda e: e.dma_start(
                    out=xlf2_sb.r3,
                    in_=xln2_dram[:, :, hoff:hoff + TH].rearrange(
                        "cc p t -> p cc t")))(hoff),
                deps=[hist.get(("x2use", half - 1))])
            for fo in range(c.FO):
                gfo = half * c.FO + fo
                pw = gfo % 2
                d_wg = P.dma("sw_a" if pw == 0 else "sw_b",
                    (lambda fo, pw: lambda e: e.dma_start(
                        out=wg_sb[pw].r3,
                        in_=sw_gate[:, 128 * fo:128 * (fo + 1)].rearrange(
                            "(cc p) k -> p cc k", p=128)))(fo, pw),
                    deps=[hist.get(("wguse", gfo - 2))])
                d_wu = P.dma("sw_a" if pw == 0 else "sw_b",
                    (lambda fo, pw: lambda e: e.dma_start(
                        out=wu_sb[pw].r3,
                        in_=sw_up[:, 128 * fo:128 * (fo + 1)].rearrange(
                            "(cc p) k -> p cc k", p=128)))(fo, pw),
                    deps=[hist.get(("wguse", gfo - 2))])
                mmu = None
                hev = None
                for cg in range(NCG):
                    gn = gfo * NCG + cg
                    pg = gn % 2
                    mmg = mmu = None
                    for f in range(c.FC):
                        mmg = P.op("tensor", (lambda pw, pg, f, cg: lambda e: e.matmul(
                            g_ps[pg][:, 0:GW], wg_sb[pw][f],
                            xlf2_sb[f][:, GW * cg:GW * (cg + 1)],
                            start=(f == 0), stop=(f == c.FC - 1)))(pw, pg, f, cg),
                            deps=([d_wg, d_wu, d_x2, hist.get(("gev", gn - 2))]
                                  if f == 0 else []))
                        mmu = P.op("tensor", (lambda pw, pg, f, cg: lambda e: e.matmul(
                            u_ps[pg][:, 0:GW], wu_sb[pw][f],
                            xlf2_sb[f][:, GW * cg:GW * (cg + 1)],
                            start=(f == 0), stop=(f == c.FC - 1)))(pw, pg, f, cg),
                            deps=([d_wg, d_wu, d_x2, hist.get(("uev", gn - 2))]
                                  if f == 0 else []))
                    sil = P.op("scalar", (lambda pg, cg: lambda e: e.activation(
                        out=gs_sb[:, GW * cg:GW * (cg + 1)], in_=g_ps[pg][:, 0:GW],
                        func=AF.Silu))(pg, cg),
                        deps=[mmg, hist.get(("huse", gn - NCG))])
                    hev = P.op("vector", (lambda pg, fo, cg, sil: lambda e: e.tensor_tensor(
                        out=hfm_sb[fo][:, GW * cg:GW * (cg + 1)],
                        in0=gs_sb[:, GW * cg:GW * (cg + 1)], in1=u_ps[pg][:, 0:GW],
                        op=ALU.mult))(pg, fo, cg, sil),
                        deps=[sil, mmu,
                              hist.get(("hread", half - 1)) if cg == 0 else None])
                    hist[("gev", gn)] = sil
                    hist[("uev", gn)] = hev
                    hist[("huse", gn)] = hev
                hist[("wguse", gfo)] = mmu
                hist[("hev", half, fo)] = hev

            hdeps = [hist[("hev", half, fo)] for fo in range(c.FO)]
            last_mmd = None
            for tch in range(TCH):
                tc = half * TCH + tch
                par = tc % 2
                gd = hist.get(("outst", tc - 2))
                hqb = "hw_a" if par == 0 else "hw_b"
                d_rs = P.dma(hqb, (lambda tc, par: lambda e: e.dma_start(
                    out=res_sb[par], in_=resid_dram[128 * tc:128 * (tc + 1), :]))(tc, par),
                    deps=[gd])
                d_r0 = P.dma(hqb, (lambda tc, par: lambda e: e.dma_start(
                    out=rk0_sb[par], in_=rank_dram[128 * tc:128 * (tc + 1), :]))(tc, par),
                    deps=[gd])
                d_r1 = P.dma(hqb, (lambda tc, par: lambda e: e.dma_start(
                    out=rk1_sb[par],
                    in_=rank_dram[c.TOK + 128 * tc:c.TOK + 128 * (tc + 1), :]))(tc, par),
                    deps=[gd])
                d_rr = [d_rs, d_r0, d_r1]
                cmb_ops = []
                for fs in range(c.FS):
                    dnn = tc * c.FS + fs
                    pd = dnn % 2
                    mmd = None
                    for fo in range(c.FO):
                        mmd = P.op("tensor", (lambda pd, tch, fo, fs: lambda e: e.matmul(
                            d_ps[pd], hfm_sb[fo][:, 128 * tch:128 * (tch + 1)],
                            wd_sb[fo][:, 512 * fs:512 * (fs + 1)],
                            start=(fo == 0), stop=(fo == c.FO - 1)))(pd, tch, fo, fs),
                            deps=(([d_wd] + hdeps + [hist.get(("cmbev", dnn - 2))])
                                  if fo == 0 else []))
                    c1 = P.op("vector", (lambda pd, par, fs, mmd: lambda e: e.tensor_tensor(
                        out=res_sb[par][:, 512 * fs:512 * (fs + 1)], in0=d_ps[pd],
                        in1=res_sb[par][:, 512 * fs:512 * (fs + 1)], op=ALU.add))(pd, par, fs, mmd),
                        deps=[mmd, hist.get(("outst", tc - 2)) if fs == 0 else None] + d_rr)
                    c2 = P.op("vector", (lambda par, fs, c1: lambda e: e.tensor_tensor(
                        out=res_sb[par][:, 512 * fs:512 * (fs + 1)],
                        in0=res_sb[par][:, 512 * fs:512 * (fs + 1)],
                        in1=rk0_sb[par][:, 512 * fs:512 * (fs + 1)], op=ALU.add))(par, fs, c1),
                        deps=[c1])
                    c3 = P.op("vector", (lambda par, fs, c2: lambda e: e.tensor_tensor(
                        out=res_sb[par][:, 512 * fs:512 * (fs + 1)],
                        in0=res_sb[par][:, 512 * fs:512 * (fs + 1)],
                        in1=rk1_sb[par][:, 512 * fs:512 * (fs + 1)], op=ALU.add))(par, fs, c2),
                        deps=[c2])
                    hist[("cmbev", dnn)] = c3
                    cmb_ops.append(c3)
                    last_mmd = mmd
                hist[("cmb", tc)] = cmb_ops[-1]
                st = P.dma("hw_s0" if par == 0 else "hw_s1",
                    (lambda par, tc: lambda e: e.dma_start(
                        out=y_out[128 * tc:128 * (tc + 1), :], in_=res_sb[par]))(par, tc),
                    deps=cmb_ops)
                hist[("outst", tc)] = st
                out_sts.append(st)
            hist[("x2use", half)] = last_mmd
            hist[("hread", half)] = last_mmd

        for nm in dbg_names:
            dsrc = dbg_map[nm]
            t = nc.dram_tensor("dbg_" + nm, list(dsrc.shape), dsrc.dtype,
                               kind="ExternalOutput").ap()
            out_sts.append(P.dma("hw", (lambda t, dsrc: lambda e: e.dma_start(
                out=t.flatten()[None, :], in_=dsrc.flatten()[None, :]))(t, dsrc)))

        P.flush_phase("p5b")

    ctx.close()
    return nc



def _shard(cfg, x_full):
    c = cfg
    per_core = []
    for cid in range(c.NCORES):
        b = cid // c.R
        r = cid % c.R
        tiles = [r + c.R * v for v in range(c.NT)]
        rows = np.concatenate([np.arange(128 * t, 128 * (t + 1)) for t in tiles])
        xb = np.ascontiguousarray(x_full[b])
        xo = np.ascontiguousarray(x_full[b][rows])
        qg = np.ascontiguousarray(rows.astype(np.float32).reshape(1, -1))
        per_core.append((b, rows, xb, xo, qg))
    return per_core


def _wmap(inputs):
    w = {k: np.ascontiguousarray(np.asarray(v, np.float32))
         for k, v in inputs.items() if k != "L0_attn_x"}
    return {
        "wq_w": w["L0_attn_wq_w"], "wq_b": w["L0_attn_wq_b"].reshape(1, -1),
        "wk_w": w["L0_attn_wk_w"], "wk_b": w["L0_attn_wk_b"].reshape(1, -1),
        "wkl_w": w["L0_attn_wkl_w"], "wkl_b": w["L0_attn_wkl_b"].reshape(1, -1),
        "wv_w": w["L0_attn_wv_w"], "wv_b": w["L0_attn_wv_b"].reshape(1, -1),
        "wvl_w": w["L0_attn_wvl_w"], "wvl_b": w["L0_attn_wvl_b"].reshape(1, -1),
        "upk_w": w["L0_attn_upk_w"], "upk_b": w["L0_attn_upk_b"],
        "upv_w": w["L0_attn_upv_w"], "upv_b": w["L0_attn_upv_b"],
        "wo_w": w["L0_attn_wo_w"], "wo_b": w["L0_attn_wo_b"].reshape(1, -1),
        "gate_w": w["L0_ffn_gate_w"],
        "w_gate": w["L0_ffn_w_gate"], "w_up": w["L0_ffn_w_up"],
        "w_down": w["L0_ffn_w_down"],
        "sw_gate": w["L0_ffn_sw_gate"], "sw_up": w["L0_ffn_sw_up"],
        "sw_down": w["L0_ffn_sw_down"],
    }


def kernel(**inputs):
    c = Cfg()
    x = np.asarray(inputs["L0_attn_x"], np.float32)
    assert x.shape == (c.B, c.S, c.D)
    wmap = _wmap(inputs)
    nc = build_program(c)
    shards = _shard(c, x)
    in_maps = []
    for (b, rows, xb, xo, qg) in shards:
        m = dict(wmap)
        m["x_batch"] = xb
        m["x_own"] = xo
        m["qg_row"] = qg
        in_maps.append(m)
    res = run_bass_kernel_spmd(nc, in_maps, core_ids=list(range(c.NCORES)))
    out = np.empty((c.B, c.S, c.D), np.float32)
    for cid, (b, rows, _, _, _) in enumerate(shards):
        out[b][rows] = res.results[cid]["y"]
    return out


# version 11
# speedup vs baseline: 1.1755x; 1.0111x over previous
import numpy as np
import ml_dtypes
from contextlib import ExitStack

import concourse.bass as bass
import concourse.mybir as mybir
from concourse.bass_core_rust import IndirectOffsetOnAxis
from concourse.bass_utils import run_bass_kernel_spmd

F32 = mybir.dt.float32
BF16 = mybir.dt.bfloat16
U32 = mybir.dt.uint32
AF = mybir.ActivationFunctionType
ALU = mybir.AluOpType
AX = mybir.AxisListType
BF = ml_dtypes.bfloat16

SCALE = 1.0 / float(np.sqrt(128.0))
LN_EPS = 1e-5
SENT_BASE = 1 << 20


class Cfg:
    def __init__(self, S=8192, D=2048, H=16, DL=512, E=8, DF=1408, CAP=640,
                 R=4, G=4, NCORES=8):
        self.S, self.D, self.H, self.DL, self.E, self.DF = S, D, H, DL, E, DF
        self.DH = 128
        self.CAP = CAP
        self.ES = 2 * CAP
        self.R, self.G, self.NCORES = R, G, NCORES
        self.B = NCORES // R
        self.TOK = S // R
        self.NT = self.TOK // 128
        self.NG = self.NT // G
        self.GQ = 128 * G
        self.FC = D // 128
        self.LC = DL // 128
        self.KCH = S // 128
        self.TC = self.TOK // 128
        self.FO = DF // 128
        self.CC = CAP // 128
        self.FS = D // 512
        assert self.NT % G == 0 and D % 512 == 0 and DL % 128 == 0
        assert DF % 128 == 0 and S % 512 == 0 and CAP % 128 == 0
        assert self.TOK % 512 == 0

    def kc_count(self, a):
        return self.R * (a * self.G + self.G)

    def col0(self, a, kc):
        jmin = 0
        for j in range(self.G):
            if kc > self.R * (a * self.G + j) + self.R - 1:
                jmin = j + 1
        return 128 * jmin

    def masked_j(self, a, kc):
        d = kc - self.R * a * self.G
        if d < 0:
            return None
        j = d // self.R
        return j if j < self.G else None


class TA:

    def __init__(self, handle, n, inner):
        self.t = handle
        self.n = n
        self.inner = inner

    def __getitem__(self, i):
        assert 0 <= i < self.n
        return self.t[:, self.inner * i:self.inner * (i + 1)]

    @property
    def full(self):
        return self.t[:, :]

    @property
    def r3(self):
        return self.t[:, :].rearrange("p (n i) -> p n i", n=self.n)


class _Op:
    __slots__ = ("eng", "fn", "deps", "chan", "cnt", "phase", "quantum")

    def __init__(self, eng, fn, deps, chan, quantum, phase):
        self.eng, self.fn, self.deps, self.chan = eng, fn, deps, chan
        self.quantum, self.phase = quantum, phase
        self.cnt = None


class Builder:
    ENGS = ("tensor", "vector", "scalar", "gpsimd", "sync")

    @staticmethod
    def q_eng(q):
        return "sync" if q.startswith("hw") else "gpsimd"

    def __init__(self, nc):
        self.nc = nc
        self.ops = []
        self.phase = 0
        self.chans = {}
        self.chan_count = {}

    def _chan(self, name):
        if name not in self.chans:
            self.chans[name] = self.nc.alloc_semaphore(name)
            self.chan_count[name] = 0
        return self.chans[name]

    def op(self, eng, fn, deps=()):
        deps = [d for d in deps if d is not None]
        o = _Op(eng, fn, deps, "c_" + eng, 1, self.phase)
        self.ops.append(o)
        return o

    def dma(self, q, fn, deps=()):
        eng = self.q_eng(q)
        deps = [d for d in deps if d is not None]
        o = _Op(eng, fn, deps, q, 16, self.phase)
        self.ops.append(o)
        return o

    def flush_phase(self, name=None):
        ops, self.ops = self.ops, []
        cur_phase = self.phase
        self.phase += 1
        if not ops:
            return
        need_inc = set()
        dma_chans = set()
        for o in ops:
            for d in o.deps:
                if d.phase == cur_phase:
                    need_inc.add(id(d))
            if o.quantum == 16:
                need_inc.add(id(o))
                dma_chans.add(o.chan)
        for o in ops:
            if id(o) in need_inc:
                self._chan(o.chan)
                self.chan_count[o.chan] += o.quantum
                o.cnt = self.chan_count[o.chan]
        drains = {}
        for ch in dma_chans:
            drains.setdefault(self.q_eng(ch), []).append(
                (ch, self.chan_count[ch]))
        by_eng = {e: [] for e in self.ENGS}
        for o in ops:
            by_eng[o.eng].append(o)
        builder = self

        def emit_for(eng_name):
            def run(eng):
                waited = {}
                for o in by_eng[eng_name]:
                    need = {}
                    for d in o.deps:
                        if d.phase != cur_phase:
                            continue
                        assert d.cnt is not None, "dep op was never counted"
                        if need.get(d.chan, -1) < d.cnt:
                            need[d.chan] = d.cnt
                    for ch, cnt in need.items():
                        if waited.get(ch, -1) >= cnt:
                            continue
                        eng.wait_ge(builder.chans[ch], cnt)
                        waited[ch] = cnt
                    instr = o.fn(eng)
                    if o.cnt is not None:
                        instr.then_inc(builder.chans[o.chan], o.quantum)
                for ch, tot in drains.get(eng_name, []):
                    if waited.get(ch, -1) < tot:
                        eng.wait_ge(builder.chans[ch], tot)
            return run

        import contextlib
        scope = self.nc.named_scope(name) if name else contextlib.nullcontext()
        with scope:
            with self.nc.Block(name) as blk:
                for e in self.ENGS:
                    if by_eng[e] or drains.get(e):
                        getattr(blk, e)(emit_for(e))


def build_program(c: Cfg, dbg_names=(), stop_after=None):

    nc = bass.Bass("TRN2", target_bir_lowering=False)
    P = Builder(nc)

    def din(name, shape):
        return nc.dram_tensor(name, list(shape), F32, kind="ExternalInput").ap()

    x_batch = din("x_batch", (c.S, c.D))
    x_own = din("x_own", (c.TOK, c.D))
    qg_row = din("qg_row", (1, c.TOK))
    wq_w = din("wq_w", (c.D, c.D)); wq_b = din("wq_b", (1, c.D))
    wk_w = din("wk_w", (c.D, c.D)); wk_b = din("wk_b", (1, c.D))
    wkl_w = din("wkl_w", (c.D, c.DL)); wkl_b = din("wkl_b", (1, c.DL))
    wv_w = din("wv_w", (c.D, c.D)); wv_b = din("wv_b", (1, c.D))
    wvl_w = din("wvl_w", (c.D, c.DL)); wvl_b = din("wvl_b", (1, c.DL))
    upk_w = din("upk_w", (c.H, c.DL, c.DH)); upk_b = din("upk_b", (c.H, c.DH))
    upv_w = din("upv_w", (c.H, c.DL, c.DH)); upv_b = din("upv_b", (c.H, c.DH))
    wo_w = din("wo_w", (c.D, c.D)); wo_b = din("wo_b", (1, c.D))
    gate_w = din("gate_w", (c.E, c.D))
    w_gate = din("w_gate", (c.E, c.D, c.DF))
    w_up = din("w_up", (c.E, c.D, c.DF))
    w_down = din("w_down", (c.E, c.DF, c.D))
    sw_gate = din("sw_gate", (c.D, c.DF))
    sw_up = din("sw_up", (c.D, c.DF))
    sw_down = din("sw_down", (c.DF, c.D))

    y_out = nc.dram_tensor("y", [c.TOK, c.D], F32, kind="ExternalOutput").ap()

    def dscr(name, shape, dt):
        return nc.dram_tensor(name, list(shape), dt, kind="Internal").ap()

    klat_dram = dscr("klat_dram", (c.LC, 128, c.S), BF16)
    vlat_dram = dscr("vlat_dram", (c.LC, 128, c.S), BF16)
    K_dram = dscr("K_dram", (c.H, 128, c.S), BF16)
    V_dram = dscr("V_dram", (c.H, c.KCH, 128, 128), BF16)
    Q_dram = dscr("Q_dram", (c.H, 128, c.TOK), BF16)
    O_dram = dscr("O_dram", (c.H, 128, c.TOK), BF16)
    resid_dram = dscr("resid_dram", (c.TOK, c.D), F32)
    xln2_dram = dscr("xln2_dram", (c.FC, 128, c.TOK), BF16)
    xln2tm_dram = dscr("xln2tm_dram", (c.TOK, c.D), BF16)
    xg_dram = dscr("xg_dram", (c.E * c.ES, c.D), BF16)
    backmap_dram = dscr("backmap_dram", (c.E * c.ES, 1), U32)
    gatew_dram = dscr("gatew_dram", (c.E * c.ES, 1), F32)
    rank_dram = dscr("rank_dram", (2 * c.TOK, c.D), F32)

    dbg_map = {
        "klat": klat_dram, "vlat": vlat_dram, "K": K_dram, "V": V_dram,
        "Q": Q_dram, "O": O_dram, "resid": resid_dram, "xln2": xln2_dram,
        "xg": xg_dram, "backmap": backmap_dram, "gatew": gatew_dram,
        "rank": rank_dram,
    }

    ident_bf = nc.inline_tensor(np.eye(128, dtype=BF), "ident_bf")
    ut_bf = nc.inline_tensor(np.triu(np.ones((128, 128), dtype=BF)), "ut_bf")
    eoff_dram = nc.inline_tensor(
        (np.arange(c.E, dtype=np.float32) * c.ES).reshape(1, c.E), "eoff")

    ctx = ExitStack()

    def sb(name, shape, dt):
        return ctx.enter_context(nc.sbuf_tensor(name, list(shape), dt))

    def sba(name, n, inner, dt, stack=None):
        st = stack or ctx
        h = st.enter_context(nc.sbuf_tensor(name, [128, n * inner], dt))
        return TA(h, n, inner)

    def psa(name, n, inner, dt=F32, stack=None):
        st = stack or ctx
        return [st.enter_context(nc.psum_tensor(f"{name}{i}", [128, inner], dt))[:, :]
                for i in range(n)]

    ones_col = sb("ones_col", (128, 1), BF16)
    ones_row = sb("ones_row", (1, 512), BF16)
    onesrow_f32 = sb("onesrow_f32", (1, 128), F32)
    ident_sb = sb("ident_sb", (128, 128), BF16)
    onef_sb = sb("onef_sb", (1, 1), BF16)
    kg_col = sb("kg_col", (128, 1), F32)
    ut_sb = sb("ut_sb", (128, 128), BF16)
    allones_sb = sb("allones_sb", (128, 128), BF16)
    eoffb_sb = sb("eoffb_sb", (128, c.E), F32)
    tokid_sb = sb("tokid_sb", (128, c.TC), U32)
    tokid2_sb = sb("tokid2_sb", (128, c.TC), U32)

    biasq_sb = sb("biasq_sb", (1, c.D), BF16)
    biask_sb = sb("biask_sb", (1, c.D), BF16)
    biasv_sb = sb("biasv_sb", (1, c.D), BF16)
    biaskl_sb = sb("biaskl_sb", (1, c.DL), F32)
    biasvl_sb = sb("biasvl_sb", (1, c.DL), F32)
    biasupk_sb = sb("biasupk_sb", (1, c.H * c.DH), BF16)
    biasupv_sb = sb("biasupv_sb", (1, c.H * c.DH), BF16)
    biaswo_sb = sb("biaswo_sb", (1, c.D), BF16)
    bkm_sb = sb("bkm_sb", (1, c.DL), BF16)
    bvm_sb = sb("bvm_sb", (1, c.DL), BF16)

    gwT_sb = sba("gwT_sb", c.FC, c.E, BF16)

    const_vals = [float(LN_EPS), float(SCALE), -1.0]
    const_sb = sba("const_sb", len(const_vals), 1, F32)
    for ci, cv in enumerate(const_vals):
        nc.const_aps.register(F32, cv, const_sb[ci])

    attn_ctx = ExitStack()
    nmask = c.NG * c.R * c.G
    mask_sb = sba("mask_sb", nmask, 128, BF16, attn_ctx)

    bc_disp_reg = nc.gpsimd.alloc_register("bc_disp")
    bc_rank_reg = nc.gpsimd.alloc_register("bc_rank")

    def mask_tile(a, kc):
        kcrel = kc - c.R * a * c.G
        return mask_sb[a * (c.R * c.G) + kcrel]

    with ExitStack() as pc:
        qg_sb = pc.enter_context(nc.sbuf_tensor("qg_sb", [1, c.TOK], F32))
        qgb_sb = pc.enter_context(nc.sbuf_tensor("qgb_sb", [128, c.TOK], F32))
        gaterow_sb = pc.enter_context(
            nc.sbuf_tensor("gaterow_sb", [c.E, c.D], BF16))
        bmi_sb = pc.enter_context(
            nc.sbuf_tensor("bmi_sb", [128, c.E * c.ES // 128], U32))
        qgb_ps = pc.enter_context(nc.psum_tensor("qgb_ps", [128, 512], F32))
        tp_ps = psa("p0tp_ps", 2, 128, BF16, pc)
        eob_ps = pc.enter_context(nc.psum_tensor("eob_ps", [128, c.E], F32))
        eoffr_sb = pc.enter_context(nc.sbuf_tensor("eoffr_sb", [1, c.E], F32))

        d_id = P.dma("hw", lambda e: e.dma_start(out=ident_sb[:], in_=ident_bf.ap()))
        d_ut = P.dma("hw", lambda e: e.dma_start(out=ut_sb[:], in_=ut_bf.ap()))
        d_eo = P.dma("hw", lambda e: e.dma_start(out=eoffr_sb[:], in_=eoff_dram.ap()))
        d_qg = P.dma("hw", lambda e: e.dma_start(out=qg_sb[:], in_=qg_row))
        hw_last = d_qg

        m_oc = P.op("gpsimd", lambda e: e.memset(ones_col[:], 1.0))
        m_or = P.op("gpsimd", lambda e: e.memset(ones_row[:], 1.0))
        m_of = P.op("gpsimd", lambda e: e.memset(onesrow_f32[:], 1.0))
        m_o1 = P.op("gpsimd", lambda e: e.memset(onef_sb[:], 1.0))
        m_ao = P.op("gpsimd", lambda e: e.memset(allones_sb[:], 1.0))
        P.op("gpsimd", lambda e: e.reg_mov(bc_disp_reg, c.E * c.ES - 1))
        P.op("gpsimd", lambda e: e.reg_mov(bc_rank_reg, 2 * c.TOK - 1))
        for ci, cv in enumerate(const_vals):
            P.op("gpsimd", (lambda ci, cv: lambda e: e.memset(const_sb[ci], cv))(ci, cv))
        m_kg = P.op("gpsimd", lambda e: e.iota(
            kg_col[:], [[1, 1]], base=0, channel_multiplier=1,
            allow_small_or_imprecise_dtypes=True))
        m_t1 = P.op("gpsimd", lambda e: e.iota(
            tokid_sb[:], [[128, c.TC]], base=0, channel_multiplier=1))
        m_t2 = P.op("gpsimd", lambda e: e.iota(
            tokid2_sb[:], [[128, c.TC]], base=c.TOK, channel_multiplier=1))
        m_bm = P.op("gpsimd", lambda e: e.iota(
            bmi_sb[:], [[1, c.E * c.ES // 128]], base=SENT_BASE,
            channel_multiplier=c.E * c.ES // 128))
        P.dma("sw", lambda e: e.dma_start(
            out=backmap_dram.flatten().rearrange("(p f) -> p f", p=128),
            in_=bmi_sb[:]), deps=[m_bm])

        bias_loads = {}
        for nm, dst, src in (("q", biasq_sb, wq_b), ("k", biask_sb, wk_b),
                             ("v", biasv_sb, wv_b), ("kl", biaskl_sb, wkl_b),
                             ("vl", biasvl_sb, wvl_b), ("wo", biaswo_sb, wo_b)):
            bias_loads[nm] = P.dma(
                "sw",
                (lambda d, s: lambda e: e.dma_start(out=d[:], in_=s))(dst, src))
        d_buk = P.dma("sw", lambda e: e.dma_start(
            out=biasupk_sb[:], in_=upk_b.flatten()[None, :]))
        d_buv = P.dma("sw", lambda e: e.dma_start(
            out=biasupv_sb[:], in_=upv_b.flatten()[None, :]))
        d_gw = P.dma("sw", lambda e: e.dma_start(out=gaterow_sb[:], in_=gate_w))
        sw_last = d_gw

        qgb_cp = []
        for gi in range(c.TOK // 512):
            mm = P.op("tensor", (lambda g: lambda e: e.matmul(
                qgb_ps[:], onesrow_f32[:], qg_sb[:, 512 * g:512 * (g + 1)],
                start=True, stop=True))(gi),
                deps=[hw_last, m_of] + ([qgb_cp[-1]] if qgb_cp else []))
            cp = P.op("vector", (lambda g: lambda e: e.tensor_copy(
                out=qgb_sb[:, 512 * g:512 * (g + 1)], in_=qgb_ps[:]))(gi),
                deps=[mm])
            qgb_cp.append(cp)

        mm_eo = P.op("tensor", lambda e: e.matmul(
            eob_ps[:], onesrow_f32[:], eoffr_sb[:], start=True, stop=True),
            deps=[hw_last, m_of, qgb_cp[-1]])
        P.op("vector", lambda e: e.tensor_copy(out=eoffb_sb[:], in_=eob_ps[:]),
             deps=[mm_eo])

        for a in range(c.NG):
            for kcrel in range(c.R * c.G):
                kc = c.R * a * c.G + kcrel
                j = kcrel // c.R
                colbase = a * c.GQ + 128 * j
                mi = a * (c.R * c.G) + kcrel
                P.op("vector", (lambda mi, cb, kc: lambda e: e.tensor_scalar(
                    out=mask_sb[mi], in0=qgb_sb[:, cb:cb + 128],
                    scalar1=kg_col[:], scalar2=float(128 * kc),
                    op0=ALU.subtract, op1=ALU.is_ge))(mi, colbase, kc),
                    deps=[qgb_cp[colbase // 512], m_kg])

        prev = None
        for f in range(c.FC):
            mmt = P.op("tensor", (lambda f: lambda e: e.transpose(
                tp_ps[f % 2][0:128, 0:c.E],
                gaterow_sb[:, 128 * f:128 * (f + 1)],
                ident_sb[0:c.E, 0:c.E]))(f),
                deps=[sw_last, hw_last, prev])
            prev = P.op("vector", (lambda f, mmt: lambda e: e.tensor_copy(
                out=gwT_sb[f],
                in_=tp_ps[f % 2][0:128, 0:c.E]))(f, mmt),
                deps=[mmt])

        P.flush_phase("p0")
    if stop_after == "p0":
        return nc

    def emit_ln(src_tile, dst_tile, lnscr, stats4, deps, scr_guard=None):
        s_sum = stats4[:, 0:1]
        s_sq = stats4[:, 1:2]
        s_negm = stats4[:, 2:3]
        s_rstd = stats4[:, 3:4]
        o1 = P.op("vector", lambda e: e.tensor_reduce(
            out=s_sum, in_=src_tile, axis=AX.X, op=ALU.add), deps=deps)
        o2 = P.op("scalar", lambda e: e.activation(
            out=lnscr, in_=src_tile, func=AF.Square, accum_out=s_sq),
            deps=list(deps) + [scr_guard])
        o3 = P.op("vector", lambda e: e.tensor_scalar(
            out=s_negm, in0=s_sum, scalar1=-1.0 / c.D, scalar2=None,
            op0=ALU.mult), deps=[o1])
        o4 = P.op("vector", lambda e: e.scalar_tensor_tensor(
            out=s_sum, in0=s_negm, scalar=-1.0, in1=s_negm,
            op0=ALU.mult, op1=ALU.mult), deps=[o3])
        o5 = P.op("vector", lambda e: e.tensor_scalar(
            out=s_sq, in0=s_sq, scalar1=1.0 / c.D, scalar2=None,
            op0=ALU.mult), deps=[o2])
        o6 = P.op("vector", lambda e: e.tensor_tensor(
            out=s_sq, in0=s_sq, in1=s_sum, op=ALU.add), deps=[o4, o5])
        o7 = P.op("scalar", lambda e: e.activation(
            out=s_sq, in_=s_sq, func=AF.Sqrt, bias=float(LN_EPS)), deps=[o6])
        o8 = P.op("vector", lambda e: e.reciprocal(out=s_rstd, in_=s_sq),
                  deps=[o7])
        o9 = P.op("vector", lambda e: e.tensor_scalar(
            out=dst_tile, in0=src_tile, scalar1=s_negm, scalar2=s_rstd,
            op0=ALU.add, op1=ALU.mult), deps=[o8, o3] + list(deps))
        return o9, o2

    p1_ctx = ExitStack()
    wkm_sb = sba("wkm_sb", c.FC, c.DL, BF16, p1_ctx)
    wvm_sb = sba("wvm_sb", c.FC, c.DL, BF16, p1_ctx)

    with ExitStack() as pc:
        wl_sb = [sba("wkl_sb", c.FC, c.DL, BF16, pc),
                 sba("wvl_sb", c.FC, c.DL, BF16, pc)]
        wrow_sb = sba("wrow_sb", 2, c.D, BF16, pc)
        wT_sb = sba("wT_sb", c.FC, 128, BF16, pc)
        bcol_sb = [sba("bcolk_sb", c.FC, 1, BF16, pc),
                   sba("bcolv_sb", c.FC, 1, BF16, pc)]
        tp_ps = psa("p1tp_ps", 2, 128, BF16, pc)
        bc_ps = psa("p1bc_ps", 2, 1, F32, pc)
        mm_ps = psa("p1mm_ps", 2, 512, F32, pc)
        bm_ps = pc.enter_context(nc.psum_tensor("p1bm_ps", [1, 512], F32))

        d_wl0 = P.dma("sw", lambda e: e.dma_start(
            out=wl_sb[0].r3,
            in_=wkl_w.rearrange("(cc p) l -> p cc l", p=128)))
        d_wl1 = P.dma("sw", lambda e: e.dma_start(
            out=wl_sb[1].r3,
            in_=wvl_w.rearrange("(cc p) l -> p cc l", p=128)))
        d_wl = [[d_wl0, d_wl1], [d_wl0, d_wl1]]
        bcol_ops = {}
        prev_ev = None
        for kv in range(2):
            brow = biask_sb if kv == 0 else biasv_sb
            for i in range(c.FC):
                n = kv * c.FC + i
                mm = P.op("tensor", (lambda brow, i, n: lambda e: e.matmul(
                    bc_ps[n % 2], brow[:, 128 * i:128 * (i + 1)],
                    onef_sb[:], start=True, stop=True))(brow, i, n),
                    deps=[bias_loads["k" if kv == 0 else "v"], m_o1, prev_ev])
                ev = P.op("vector", (lambda kv, i, n, mm: lambda e: e.tensor_copy(
                    out=bcol_sb[kv][i], in_=bc_ps[n % 2]))(kv, i, n, mm),
                    deps=[mm])
                bcol_ops[(kv, i)] = ev
                prev_ev = ev

        nlo = (c.DL + 511) // 512
        hist = {}
        for kv in range(2):
            w_src = wk_w if kv == 0 else wv_w
            dstw = wkm_sb if kv == 0 else wvm_sb
            for i in range(c.FC):
                it = kv * c.FC + i
                d_row = P.dma("sw_a" if it % 2 == 0 else "sw_b",
                    (lambda w_src, i, it: lambda e: e.dma_start(
                        out=wrow_sb[it % 2], in_=w_src[128 * i:128 * (i + 1), :]))(w_src, i, it),
                    deps=[hist.get(("tpdone", it - 2))])
                tevs = []
                for j in range(c.FC):
                    ntp = it * c.FC + j
                    mmt = P.op("tensor", (lambda it, j, ntp: lambda e: e.transpose(
                        tp_ps[ntp % 2], wrow_sb[it % 2][:, 128 * j:128 * (j + 1)],
                        ident_sb[:]))(it, j, ntp),
                        deps=[d_row, hist.get(("tpev", ntp - 2)),
                              hist.get(("wTuse", it - 1)) if j == 0 else None])
                    ev = P.op("vector", (lambda j, ntp, mmt: lambda e: e.tensor_copy(
                        out=wT_sb[j], in_=tp_ps[ntp % 2]))(j, ntp, mmt), deps=[mmt])
                    hist[("tpev", ntp)] = ev
                    tevs.append(ev)
                hist[("tpdone", it)] = tevs[-1]
                for lo in range(nlo):
                    lw = min(512, c.DL - 512 * lo)
                    nn = it * nlo + lo
                    mm = None
                    for j in range(c.FC):
                        mm = P.op("tensor",
                                  (lambda nn, j, lo, lw, kv: lambda e: e.matmul(
                                      mm_ps[nn % 2][:, 0:lw],
                                      wT_sb[j],
                                      wl_sb[kv][j][:, 512 * lo:512 * lo + lw],
                                      start=(j == 0), stop=(j == c.FC - 1)))(nn, j, lo, lw, kv),
                                  deps=(([tevs[-1], hist.get(("meev", nn - 2))] + d_wl[kv]) if j == 0 else []))
                    ev = P.op("vector", (lambda dstw, i, lo, lw, nn, mm: lambda e:
                              e.tensor_copy(
                                  out=dstw[i][:, 512 * lo:512 * lo + lw],
                                  in_=mm_ps[nn % 2][:, 0:lw]))(
                                      dstw, i, lo, lw, nn, mm),
                              deps=[mm])
                    hist[("meev", nn)] = ev
                    hist[("wTuse", it)] = mm

        for kv in range(2):
            browl = biaskl_sb if kv == 0 else biasvl_sb
            dst = bkm_sb if kv == 0 else bvm_sb
            for lo in range(nlo):
                lw = min(512, c.DL - 512 * lo)
                mm = None
                for j in range(c.FC):
                    mm = P.op("tensor", (lambda kv, j, lo, lw: lambda e: e.matmul(
                        bm_ps[0:1, 0:lw], bcol_sb[kv][j],
                        wl_sb[kv][j][:, 512 * lo:512 * lo + lw],
                        start=(j == 0), stop=(j == c.FC - 1)))(kv, j, lo, lw),
                        deps=[bcol_ops[(kv, j)],
                              hist.get(("bmev", kv, lo - 1)) if j == 0 else None,
                              hist.get(("bmev", kv - 1, nlo - 1)) if (j == 0 and lo == 0) else None]
                             + (d_wl[kv] if j == 0 else []))
                ev = P.op("vector", (lambda dst, browl, lo, lw, mm: lambda e:
                          e.tensor_tensor(out=dst[:, 512 * lo:512 * lo + lw],
                                          in0=bm_ps[0:1, 0:lw],
                                          in1=browl[:, 512 * lo:512 * lo + lw],
                                          op=ALU.add))(dst, browl, lo, lw, mm),
                          deps=[mm, bias_loads["kl" if kv == 0 else "vl"]])
                hist[("bmev", kv, lo)] = ev

        P.flush_phase("p1")
    if stop_after == "p1":
        return nc

    with ExitStack() as pc:
        xb_sb = [sba("xb0_sb", 4, c.D, F32, pc), sba("xb1_sb", 4, c.D, F32, pc)]
        lnscr = pc.enter_context(nc.sbuf_tensor("lnscr", [128, c.D], F32))
        lnst = pc.enter_context(nc.sbuf_tensor("lnst", [128, 16], F32))
        xlt_sb = sba("xlt_sb", 4, c.D, BF16, pc)
        xlf_sb = sba("xlf_sb", c.FC, 512, BF16, pc)
        lat_sb = sba("lat_sb", 2, 512, BF16, pc)
        tp_ps = psa("pbtp_ps", 2, 128, BF16, pc)
        kl_ps = psa("pbkl_ps", 2, 512, F32, pc)

        hist = {}
        prev_sq = None
        for g in range(c.S // 512):
            par = g % 2
            guard = [hist.get(("xbuse", g - 2))]
            d_xh = []
            for hf in range(2):
                d_xh.append(P.dma("hw_a" if hf == 0 else "hw_b",
                    (lambda g, hf, par: lambda e: e.dma_start(
                        out=xb_sb[par].t[:, c.D * 2 * hf:c.D * 2 * (hf + 1)].rearrange(
                            "p (t d) -> p t d", t=2),
                        in_=x_batch[512 * g + 256 * hf:512 * g + 256 * (hf + 1), :].rearrange(
                            "(t p) d -> p t d", p=128)))(g, hf, par),
                    deps=guard))
            tp_last = None
            ln_ops = []
            for t in range(4):
                ln, prev_sq = emit_ln(
                    xb_sb[par][t], xlt_sb[t], lnscr[:, :], lnst[:, 4 * t:4 * t + 4],
                    deps=[d_xh[t // 2],
                          hist.get(("xltuse", g - 1)) if t == 0 else None],
                    scr_guard=prev_sq)
                ln_ops.append(ln)
                if t == 3:
                    hist[("xbuse", g)] = ln
                for f in range(c.FC):
                    n = (g * 4 + t) * c.FC + f
                    mmt = P.op("tensor", (lambda t, f, n: lambda e: e.transpose(
                        tp_ps[n % 2], xlt_sb[t][:, 128 * f:128 * (f + 1)],
                        ident_sb[:]))(t, f, n),
                        deps=[ln, hist.get(("tpev", n - 2)),
                              hist.get(("xlfuse", g - 1)) if (t == 0 and f == 0) else None])
                    ev = P.op("vector", (lambda f, t, n, mmt: lambda e: e.tensor_copy(
                        out=xlf_sb[f][:, 128 * t:128 * (t + 1)], in_=tp_ps[n % 2]))(f, t, n, mmt),
                        deps=[mmt])
                    hist[("tpev", n)] = ev
                    tp_last = ev
            hist[("xltuse", g)] = tp_last
            last_mm = None
            for kv in range(2):
                wsrc = wkm_sb if kv == 0 else wvm_sb
                bsrc = bkm_sb if kv == 0 else bvm_sb
                ddst = klat_dram if kv == 0 else vlat_dram
                for lc in range(c.LC):
                    gn = g * 2 * c.LC + kv * c.LC + lc
                    pp = gn % 2
                    mm = None
                    for f in range(c.FC):
                        mm = P.op("tensor",
                                  (lambda pp, f, lc, wsrc: lambda e: e.matmul(
                                      kl_ps[pp], wsrc[f][:, 128 * lc:128 * (lc + 1)],
                                      xlf_sb[f], start=(f == 0), stop=False))(pp, f, lc, wsrc),
                                  deps=([tp_last, hist.get(("klev", gn - 2))]
                                        if f == 0 else []))
                    mmb = P.op("tensor", (lambda pp, lc, bsrc: lambda e: e.matmul(
                        kl_ps[pp], bsrc[:, 128 * lc:128 * (lc + 1)], ones_row[:],
                        start=False, stop=True))(pp, lc, bsrc), deps=[mm])
                    ev = P.op("vector", (lambda pp, mmb, gn: lambda e: e.tensor_copy(
                        out=lat_sb[gn % 2], in_=kl_ps[pp]))(pp, mmb, gn),
                        deps=[mmb, hist.get(("latst", gn - 2))])
                    st = P.dma("hw_s0" if gn % 2 == 0 else "hw_s1",
                        (lambda gn, ddst, lc, g: lambda e: e.dma_start(
                            out=ddst[lc][:, 512 * g:512 * (g + 1)], in_=lat_sb[gn % 2]))(gn, ddst, lc, g),
                        deps=[ev])
                    hist[("klev", gn)] = ev
                    hist[("latst", gn)] = st
                    last_mm = mm
            hist[("xlfuse", g)] = last_mm

        P.flush_phase("p1b")
    p1_ctx.close()
    if stop_after == "p1b":
        return nc

    with ExitStack() as pc:
        wq_sb = sba("wq_sb", c.FC, c.D, BF16, pc)
        xo_sb = [sba("xo0_sb", 4, c.D, F32, pc)]
        lnscr = pc.enter_context(nc.sbuf_tensor("lnscr_c", [128, c.D], F32))
        lnst = pc.enter_context(nc.sbuf_tensor("lnst_c", [128, 16], F32))
        xlt_sb = sba("xlt_c", 4, c.D, BF16, pc)
        xlf_sb = sba("xlf_c", c.FC, 512, BF16, pc)
        q_sb = sba("q_sb", 2, 512, BF16, pc)
        tp_ps = psa("pctp_ps", 2, 128, BF16, pc)
        q_ps = psa("pcq_ps", 2, 512, F32, pc)

        d_wq = P.dma("sw", lambda e: e.dma_start(
            out=wq_sb.r3, in_=wq_w.rearrange("(cc p) d -> p cc d", p=128)))

        hist = {}
        prev_sq = None
        for a in range(c.TOK // 512):
            par = 0
            guard = [hist.get(("xouse", a - 1))]
            d_xh = []
            for hf in range(2):
                d_xh.append(P.dma("hw_a" if hf == 0 else "hw_b",
                    (lambda a, hf, par: lambda e: e.dma_start(
                        out=xo_sb[par].t[:, c.D * 2 * hf:c.D * 2 * (hf + 1)].rearrange(
                            "p (t d) -> p t d", t=2),
                        in_=x_own[512 * a + 256 * hf:512 * a + 256 * (hf + 1), :].rearrange(
                            "(t p) d -> p t d", p=128)))(a, hf, par),
                    deps=guard))
            tp_last = None
            for t in range(4):
                ln, prev_sq = emit_ln(
                    xo_sb[par][t], xlt_sb[t], lnscr[:, :], lnst[:, 4 * t:4 * t + 4],
                    deps=[d_xh[t // 2],
                          hist.get(("xltuse", a - 1)) if t == 0 else None],
                    scr_guard=prev_sq)
                if t == 3:
                    hist[("xouse", a)] = ln
                for f in range(c.FC):
                    n = (a * 4 + t) * c.FC + f
                    mmt = P.op("tensor", (lambda t, f, n: lambda e: e.transpose(
                        tp_ps[n % 2], xlt_sb[t][:, 128 * f:128 * (f + 1)],
                        ident_sb[:]))(t, f, n),
                        deps=[ln, hist.get(("tpev", n - 2)),
                              hist.get(("xlfuse", a - 1)) if (t == 0 and f == 0) else None])
                    ev = P.op("vector", (lambda f, t, n, mmt: lambda e: e.tensor_copy(
                        out=xlf_sb[f][:, 128 * t:128 * (t + 1)], in_=tp_ps[n % 2]))(f, t, n, mmt),
                        deps=[mmt])
                    hist[("tpev", n)] = ev
                    tp_last = ev
            hist[("xltuse", a)] = tp_last
            last_mm = None
            for h in range(c.H):
                qn = a * c.H + h
                pp = qn % 2
                mm = None
                for f in range(c.FC):
                    mm = P.op("tensor", (lambda pp, f, h: lambda e: e.matmul(
                        q_ps[pp], wq_sb[f][:, 128 * h:128 * (h + 1)],
                        xlf_sb[f], start=(f == 0), stop=False))(pp, f, h),
                        deps=([tp_last, d_wq, hist.get(("qev", qn - 2))]
                              if f == 0 else []))
                mmb = P.op("tensor", (lambda pp, h: lambda e: e.matmul(
                    q_ps[pp], biasq_sb[:, 128 * h:128 * (h + 1)], ones_row[:],
                    start=False, stop=True))(pp, h), deps=[mm])
                ev = P.op("vector", (lambda pp, mmb, qn: lambda e: e.tensor_copy(
                    out=q_sb[qn % 2], in_=q_ps[pp]))(pp, mmb, qn),
                    deps=[mmb, hist.get(("qst", qn - 2))])
                st = P.dma("hw_s0" if qn % 2 == 0 else "hw_s1",
                    (lambda qn, h, a: lambda e: e.dma_start(
                        out=Q_dram[h][:, 512 * a:512 * (a + 1)], in_=q_sb[qn % 2]))(qn, h, a),
                    deps=[ev])
                hist[("qev", qn)] = ev
                hist[("qst", qn)] = st
                last_mm = mm
            hist[("xlfuse", a)] = last_mm

        P.flush_phase("p1c")
    if stop_after == "p1c":
        return nc

    with ExitStack() as pc:
        upk_sb = sba("upk_sb", c.H * c.LC, 128, BF16, pc)
        upv_sb = sba("upv_sb", c.H * c.LC, 128, BF16, pc)
        kl_sb = [sba("p2kl0", c.LC, 512, BF16, pc), sba("p2kl1", c.LC, 512, BF16, pc)]
        vl_sb = [sba("p2vl0", c.LC, 512, BF16, pc), sba("p2vl1", c.LC, 512, BF16, pc)]
        kp_sb = sba("kp_sb", 2, 512, BF16, pc)
        vp_sb = sba("vp_sb", 2, 4 * 128, BF16, pc)
        K_ps = psa("p2K_ps", 2, 512, F32, pc)
        V_ps = psa("p2V_ps", 2, 512, F32, pc)

        d_upk = P.dma("sw", lambda e: e.dma_start(
            out=upk_sb.t[:, :].rearrange("p (h cc d) -> p h cc d", h=c.H, cc=c.LC),
            in_=upk_w.rearrange("h (cc p) d -> p h cc d", p=128)))
        d_upv = P.dma("sw", lambda e: e.dma_start(
            out=upv_sb.t[:, :].rearrange("p (h cc d) -> p h cc d", h=c.H, cc=c.LC),
            in_=upv_w.rearrange("h (cc p) d -> p h cc d", p=128)))

        hist = {}
        for ks in range(c.S // 512):
            par = ks % 2
            d_kl = P.dma("hw_a" if par == 0 else "hw_b", (lambda ks, par: lambda e: e.dma_start(
                out=kl_sb[par].r3,
                in_=klat_dram[:, :, 512 * ks:512 * (ks + 1)].rearrange(
                    "cc p s -> p cc s")))(ks, par),
                deps=[hist.get(("kluse", ks - 2))])
            d_vl = P.dma("hw_a" if par == 0 else "hw_b", (lambda ks, par: lambda e: e.dma_start(
                out=vl_sb[par].r3,
                in_=vlat_dram[:, :, 512 * ks:512 * (ks + 1)].rearrange(
                    "cc p s -> p cc s")))(ks, par),
                deps=[hist.get(("vluse", ks - 2))])
            for h in range(c.H):
                kn = ks * c.H + h
                pk = kn % 2
                mm = None
                for lc in range(c.LC):
                    mm = P.op("tensor", (lambda pk, par, h, lc: lambda e: e.matmul(
                        K_ps[pk], upk_sb[h * c.LC + lc], kl_sb[par][lc],
                        start=(lc == 0), stop=False))(pk, par, h, lc),
                        deps=([d_kl, d_vl, d_upk, d_upv, hist.get(("kev", kn - 2))]
                              if lc == 0 else []))
                mmb = P.op("tensor", (lambda pk, h: lambda e: e.matmul(
                    K_ps[pk], biasupk_sb[:, 128 * h:128 * (h + 1)], ones_row[:],
                    start=False, stop=True))(pk, h), deps=[mm])
                ev = P.op("scalar", (lambda pk, mmb: lambda e: e.copy(
                    out=kp_sb[pk], in_=K_ps[pk]))(pk, mmb),
                    deps=[mmb, hist.get(("kst", kn - 2))])
                st = P.dma("hw_sk0" if pk == 0 else "hw_sk1",
                    (lambda pk, h, ks: lambda e: e.dma_start(
                        out=K_dram[h][:, 512 * ks:512 * (ks + 1)], in_=kp_sb[pk]))(pk, h, ks),
                    deps=[ev])
                hist[("kev", kn)] = ev
                hist[("kst", kn)] = st
                if h == c.H - 1:
                    hist[("kluse", ks)] = mm
                vevs = []
                mmv = None
                for t4 in range(4):
                    vn = kn * 4 + t4
                    for lc in range(c.LC):
                        mmv = P.op("tensor", (lambda pk, par, h, lc, t4: lambda e: e.matmul(
                            V_ps[pk][:, 128 * t4:128 * (t4 + 1)],
                            vl_sb[par][lc][:, 128 * t4:128 * (t4 + 1)],
                            upv_sb[h * c.LC + lc],
                            start=(lc == 0), stop=(lc == c.LC - 1)))(pk, par, h, lc, t4),
                            deps=([d_kl, d_vl, d_upk, d_upv, hist.get(("vev", kn - 2))]
                                  if (lc == 0 and t4 == 0) else []))
                    vevs.append(mmv)
                evv = P.op("vector", (lambda pk, mmv: lambda e: e.tensor_copy(
                    out=vp_sb[pk], in_=V_ps[pk]))(pk, mmv),
                    deps=vevs + [hist.get(("vst", kn - 2))])
                hist[("vev", kn)] = evv
                vevs = [evv]
                stv = P.dma("hw_sv0" if pk == 0 else "hw_sv1",
                    (lambda pk, h, ks: lambda e: e.dma_start(
                        out=V_dram[h, 4 * ks:4 * ks + 4].rearrange("t p d -> p t d"),
                        in_=vp_sb[pk].rearrange("p (t d) -> p t d", t=4)))(pk, h, ks),
                    deps=vevs)
                hist[("vst", kn)] = stv
                if h == c.H - 1:
                    hist[("vluse", ks)] = mmv

        P.flush_phase("p2")
    if stop_after == "p2":
        return nc

    with ExitStack() as pc:
        K_sb = [pc.enter_context(nc.sbuf_tensor(f"K{i}_sb", [128, c.S], BF16))
                for i in range(2)]
        V_sb = [pc.enter_context(nc.sbuf_tensor(f"V{i}_sb", [128, c.KCH * 128], BF16))
                for i in range(2)]
        Qh_sb = [pc.enter_context(nc.sbuf_tensor(f"Q{i}_sb", [128, c.TOK], BF16))
                 for i in range(2)]
        SPD = 2
        LAG = 4
        pT_sb = sba("pT_sb", SPD, 1024, BF16, pc)
        l_bf = pc.enter_context(nc.sbuf_tensor("l_bf", [1, 512], BF16))
        rec_sb = pc.enter_context(nc.sbuf_tensor("rec_sb", [1, 512], F32))
        bc_sb = pc.enter_context(nc.sbuf_tensor("bc_sb", [128, 512], F32))
        O_sb = sba("O_sb", 2, 512, BF16, pc)
        s_ps = psa("p3s_ps", SPD, 1024, F32, pc)
        o_ps = pc.enter_context(nc.psum_tensor("p3o_ps", [128, 512], F32))
        l_ps = pc.enter_context(nc.psum_tensor("p3l_ps", [1, 512], F32))
        bc_ps = pc.enter_context(nc.psum_tensor("p3bc_ps", [128, 512], F32))

        last_sps_rd = [None] * SPD
        last_pT_rd = [None] * SPD
        epi_state = {"last_l_rd": None, "last_o_rd": None, "last_bc_rd": None}
        pending_epi = [None]
        hist = {}
        for h in range(c.H):
            par = h % 2
            hd = hist.get(("hdone", h - 2))
            hq = "hw_a" if par == 0 else "hw_b"
            d_K = P.dma(hq, (lambda h, par: lambda e: e.dma_start(
                out=K_sb[par][:, :], in_=K_dram[h]))(h, par), deps=[hd])
            d_V = P.dma(hq, (lambda h, par: lambda e: e.dma_start(
                out=V_sb[par][:, :].rearrange("p (k d) -> p k d", k=c.KCH),
                in_=V_dram[h].rearrange("k p d -> p k d")))(h, par),
                deps=[hd])
            d_Q = P.dma(hq, (lambda h, par: lambda e: e.dma_start(
                out=Qh_sb[par][:, :], in_=Q_dram[h]))(h, par), deps=[hd])
            d_KVQ = [d_K, d_V, d_Q]
            for a in range(c.NG):
                KC = c.kc_count(a)
                assert KC % 2 == 0
                sc_ops = [None] * KC
                ex_ops = [None] * (KC // 2)
                mk_ops = [None] * KC
                av_ops = [None] * KC
                sm = None

                def emit_scores(kc):
                    col0 = c.col0(a, kc)
                    pi = (kc // 2) % SPD
                    off = 512 * (kc % 2)
                    sc_ops[kc] = P.op("tensor",
                        (lambda pi, off, kc, col0, a, par: lambda e: e.matmul(
                            s_ps[pi][:, off + col0:off + 512],
                            K_sb[par][:, 128 * kc:128 * (kc + 1)],
                            Qh_sb[par][:, c.GQ * a + col0:c.GQ * (a + 1)],
                            start=True, stop=True))(pi, off, kc, col0, a, par),
                        deps=(d_KVQ + [last_sps_rd[pi]]) if kc % 2 == 0 else [])
                    if kc % 2 == 1:
                        col0p = c.col0(a, kc - 1)
                        ex_ops[kc // 2] = P.op("scalar",
                            (lambda pi, col0p: lambda e: e.activation(
                                out=pT_sb[pi][:, col0p:1024],
                                in_=s_ps[pi][:, col0p:1024],
                                func=AF.Exp, scale=SCALE))(pi, col0p),
                            deps=[sc_ops[kc], last_pT_rd[pi]])
                        last_sps_rd[pi] = ex_ops[kc // 2]
                        for k2 in (kc - 1, kc):
                            jm = c.masked_j(a, k2)
                            if jm is not None:
                                off2 = 512 * (k2 % 2)
                                mk_ops[k2] = P.op("vector",
                                    (lambda pi, a, k2, jm, off2: lambda e: e.tensor_tensor(
                                        out=pT_sb[pi][:, off2 + 128 * jm:off2 + 128 * (jm + 1)],
                                        in0=pT_sb[pi][:, off2 + 128 * jm:off2 + 128 * (jm + 1)],
                                        in1=mask_tile(a, k2), op=ALU.mult))(pi, a, k2, jm, off2),
                                    deps=[ex_ops[kc // 2]])

                def emit_smav(kc):
                    nonlocal sm
                    col0 = c.col0(a, kc)
                    pi = (kc // 2) % SPD
                    off = 512 * (kc % 2)
                    pe_dep = mk_ops[kc] or ex_ops[kc // 2]
                    sm_ops = sm2 = P.op("tensor",
                        (lambda pi, off, kc, col0, KC: lambda e: e.matmul(
                            l_ps[0:1, col0:512], ones_col[:],
                            pT_sb[pi][:, off + col0:off + 512],
                            start=(kc == 0), stop=(kc == KC - 1)))(pi, off, kc, col0, KC),
                        deps=[pe_dep] + ([epi_state["last_l_rd"]] if kc == 0 else []))
                    sm = sm2
                    av_ops[kc] = P.op("tensor",
                        (lambda pi, off, kc, col0, par: lambda e: e.matmul(
                            o_ps[:, col0:512], V_sb[par][:, 128 * kc:128 * (kc + 1)],
                            pT_sb[pi][:, off + col0:off + 512],
                            start=(kc == 0), stop=False))(pi, off, kc, col0, par),
                        deps=[pe_dep] + (d_KVQ + [epi_state["last_o_rd"]] if kc == 0 else []))
                    if kc % 2 == 1:
                        last_pT_rd[pi] = av_ops[kc]

                for i in range(KC + LAG):
                    if i - LAG >= 0:
                        emit_smav(i - LAG)
                    if i < KC:
                        emit_scores(i)
                    if i == LAG - 1 and pending_epi[0] is not None:
                        pending_epi[0]()
                        pending_epi[0] = None

                def make_epi(h, a, sm, av_last):
                    def epi():
                        nonlocal_state = None
                        self_last_l_rd = None
                        evl = P.op("vector", lambda e: e.tensor_copy(
                            out=l_bf[:], in_=l_ps[:, :]), deps=[sm])
                        rc = P.op("vector", lambda e: e.reciprocal(
                            out=rec_sb[:], in_=l_ps[:, :]), deps=[sm, evl])
                        epi_state["last_l_rd"] = rc
                        vb = P.op("tensor", (lambda h: lambda e: e.matmul(
                            o_ps[:, :], biasupv_sb[:, 128 * h:128 * (h + 1)], l_bf[:],
                            start=False, stop=True))(h), deps=[evl, av_last])
                        bc = P.op("tensor", lambda e: e.matmul(
                            bc_ps[:, :], onesrow_f32[:], rec_sb[:], start=True, stop=True),
                            deps=[rc, epi_state["last_bc_rd"]])
                        bce = P.op("scalar", lambda e: e.copy(
                            out=bc_sb[:], in_=bc_ps[:, :]), deps=[bc])
                        epi_state["last_bc_rd"] = bce
                        po = (h * c.NG + a) % 2
                        onorm = P.op("vector", (lambda po: lambda e: e.tensor_tensor(
                            out=O_sb[po], in0=o_ps[:, :], in1=bc_sb[:],
                            op=ALU.mult))(po),
                            deps=[vb, bce, hist.get(("ost", h * c.NG + a - 2))])
                        epi_state["last_o_rd"] = onorm
                        ost = P.dma("hw_s0" if po == 0 else "hw_s1",
                            (lambda po, h, a: lambda e: e.dma_start(
                                out=O_dram[h][:, c.GQ * a:c.GQ * (a + 1)],
                                in_=O_sb[po]))(po, h, a),
                            deps=[onorm])
                        hist[("ost", h * c.NG + a)] = ost
                        if a == c.NG - 1:
                            hist[("hdone", h)] = onorm
                    return epi

                pending_epi[0] = make_epi(h, a, sm, av_ops[KC - 1])

        if pending_epi[0] is not None:
            pending_epi[0]()
            pending_epi[0] = None

        P.flush_phase("p3")
    attn_ctx.close()
    if stop_after == "p3":
        return nc

    with ExitStack() as pc:
        wo_sb = sba("wo_sb", c.FC, c.D, BF16, pc)
        Otc_sb = [sba(f"Otc{i}_sb", c.H, 128, BF16, pc) for i in range(2)]
        xot_sb = sba("xot_sb", 2, c.D, F32, pc)
        resr_sb = [pc.enter_context(nc.sbuf_tensor(f"resr{i}_sb", [128, c.D], F32))
                   for i in range(2)]
        lnscr = pc.enter_context(nc.sbuf_tensor("lnscr4", [128, c.D], F32))
        lnst = pc.enter_context(nc.sbuf_tensor("lnst4", [128, 8], F32))
        xl2_sb = [pc.enter_context(nc.sbuf_tensor(f"xl2{i}_sb", [128, c.D], BF16))
                  for i in range(2)]
        zero_sb = pc.enter_context(nc.sbuf_tensor("zero_sb", [128, 512], F32))
        wo_ps = psa("p4wo_ps", 2, 512, F32, pc)

        d_wo = P.dma("sw", lambda e: e.dma_start(
            out=wo_sb.r3, in_=wo_w.rearrange("(cc p) d -> p cc d", p=128)))
        z_zero = P.op("gpsimd", lambda e: e.memset(zero_sb[:], 0.0))
        for i in range(2 * c.TOK // 128):
            for j in range(c.FS):
                P.dma("hw2", (lambda i, j: lambda e: e.dma_start(
                    out=rank_dram[128 * i:128 * (i + 1), 512 * j:512 * (j + 1)],
                    in_=zero_sb[:]))(i, j),
                    deps=[z_zero])

        hist = {}
        prev_sq = None
        for tc in range(c.TC):
            par = tc % 2
            d_O = P.dma("hw_a" if par == 0 else "hw_b", (lambda tc, par: lambda e: e.dma_start(
                out=Otc_sb[par].r3,
                in_=O_dram[:, :, 128 * tc:128 * (tc + 1)].rearrange(
                    "h p t -> p h t")))(tc, par),
                deps=[hist.get(("owomm", tc - 2))])
            d_xo = P.dma("hw_c" if par == 0 else "hw_d", (lambda tc, par: lambda e: e.dma_start(
                out=xot_sb.t[:, c.D * par:c.D * (par + 1)],
                in_=x_own[128 * tc:128 * (tc + 1), :]))(tc, par),
                deps=[hist.get(("resev", tc - 2))])
            res_evs = []
            mm = None
            for fs in range(c.FS):
                wn = tc * c.FS + fs
                pp = wn % 2
                mm = None
                for hh in range(c.H):
                    mm = P.op("tensor", (lambda pp, par, hh, fs: lambda e: e.matmul(
                        wo_ps[pp], Otc_sb[par][hh],
                        wo_sb[hh][:, 512 * fs:512 * (fs + 1)],
                        start=(hh == 0), stop=False))(pp, par, hh, fs),
                        deps=([d_O, d_wo, hist.get(("woev", wn - 2))]
                              if hh == 0 else []))
                mmb = P.op("tensor", (lambda pp, fs: lambda e: e.matmul(
                    wo_ps[pp], ones_row[:, 0:128],
                    biaswo_sb[:, 512 * fs:512 * (fs + 1)],
                    start=False, stop=True))(pp, fs), deps=[mm])
                ev = P.op("vector", (lambda pp, par, fs: lambda e: e.tensor_tensor(
                    out=resr_sb[par][:, 512 * fs:512 * (fs + 1)],
                    in0=wo_ps[pp], in1=xot_sb.t[:, c.D * par + 512 * fs:c.D * par + 512 * (fs + 1)],
                    op=ALU.add))(pp, par, fs),
                    deps=[mmb, d_xo,
                          hist.get(("resst", tc - 2)) if fs == 0 else None,
                          hist.get(("resuse", tc - 2)) if fs == 0 else None])
                hist[("woev", wn)] = ev
                res_evs.append(ev)
            hist[("owomm", tc)] = mm
            hist[("resev", tc)] = res_evs[-1]
            ln, prev_sq = emit_ln(
                resr_sb[par][:, :], xl2_sb[par][:, :], lnscr[:, :],
                lnst[:, 4 * par:4 * par + 4],
                deps=[res_evs[-1], hist.get(("xl2st", tc - 2))],
                scr_guard=prev_sq)
            hist[("resuse", tc)] = ln
            rst = P.dma("hw_s0" if par == 0 else "hw_s1",
                (lambda par, tc: lambda e: e.dma_start(
                    out=resid_dram[128 * tc:128 * (tc + 1), :], in_=resr_sb[par][:, :]))(par, tc),
                deps=[res_evs[-1]])
            hist[("resst", tc)] = rst
            x2st = P.dma("hw_s2" if par == 0 else "hw_s3",
                (lambda par, tc: lambda e: e.dma_start(
                    out=xln2tm_dram[128 * tc:128 * (tc + 1), :], in_=xl2_sb[par][:, :]))(par, tc),
                deps=[ln])
            hist[("xl2st", tc)] = x2st

        P.flush_phase("p4a")
    if stop_after == "p4a":
        return nc

    with ExitStack() as pc:
        xl2_sb = [pc.enter_context(nc.sbuf_tensor(f"xl2b{i}_sb", [128, c.D], BF16))
                  for i in range(2)]
        xlf2_sb = sba("xlf2_sb", c.FC, c.TOK, BF16, pc)
        lg_sb = pc.enter_context(nc.sbuf_tensor("lg_sb", [128, c.E], F32))
        m1_sb = pc.enter_context(nc.sbuf_tensor("m1_sb", [128, 1], F32))
        m2_sb = pc.enter_context(nc.sbuf_tensor("m2_sb", [128, 1], F32))
        epick_sb = pc.enter_context(nc.sbuf_tensor("epick_sb", [128, 1], F32))
        msk1_sb = pc.enter_context(nc.sbuf_tensor("msk1_sb", [128, c.E], F32))
        msk12_sb = pc.enter_context(nc.sbuf_tensor("msk12_sb", [128, c.E], F32))
        mskr2_sb = pc.enter_context(nc.sbuf_tensor("mskr2_sb", [128, c.E], F32))
        lgm_sb = pc.enter_context(nc.sbuf_tensor("lgm_sb", [128, c.E], F32))
        mb_sb = sba("mb_sb", c.TC, c.E, BF16, pc)
        cum_sb = pc.enter_context(nc.sbuf_tensor("cum_sb", [128, c.E], F32))
        pick_sb = pc.enter_context(nc.sbuf_tensor("pick_sb", [128, c.E], F32))
        dest_f = pc.enter_context(nc.sbuf_tensor("dest_f", [128, 2], F32))
        dest_u = [pc.enter_context(nc.sbuf_tensor(f"dest_u{i}", [128, c.TC], U32))
                  for i in range(2)]
        diffs_sb = pc.enter_context(nc.sbuf_tensor("diffs_sb", [128, c.TC], F32))
        w1_sb = pc.enter_context(nc.sbuf_tensor("w1_sb", [128, c.TC], F32))
        w2_sb = pc.enter_context(nc.sbuf_tensor("w2_sb", [128, c.TC], F32))
        tp_ps = psa("p4tp_ps", 2, 128, BF16, pc)
        r_ps = pc.enter_context(nc.psum_tensor("p4r_ps", [128, c.E], F32))
        cum_ps = pc.enter_context(nc.psum_tensor("p4cum_ps", [128, c.E], F32))

        hist = {}
        o_df = None
        for tc in range(c.TC):
            par = tc % 2
            d_x2 = P.dma("hw_a" if par == 0 else "hw_b", (lambda tc, par: lambda e: e.dma_start(
                out=xl2_sb[par][:, :],
                in_=xln2tm_dram[128 * tc:128 * (tc + 1), :]))(tc, par),
                deps=[hist.get(("xl2use", tc - 2))])
            tp_evs = []
            for f in range(c.FC):
                n = tc * c.FC + f
                mmt = P.op("tensor", (lambda par, f, n: lambda e: e.transpose(
                    tp_ps[n % 2], xl2_sb[par][:, 128 * f:128 * (f + 1)],
                    ident_sb[:]))(par, f, n),
                    deps=[d_x2, hist.get(("tpev", n - 2))])
                evt = P.op("vector", (lambda f, tc, n, mmt: lambda e: e.tensor_copy(
                    out=xlf2_sb[f][:, 128 * tc:128 * (tc + 1)], in_=tp_ps[n % 2]))(f, tc, n, mmt),
                    deps=[mmt])
                hist[("tpev", n)] = evt
                tp_evs.append(evt)
            mmr = None
            for f in range(c.FC):
                mmr = P.op("tensor", (lambda f, tc: lambda e: e.matmul(
                    r_ps[:, :], xlf2_sb[f][:, 128 * tc:128 * (tc + 1)], gwT_sb[f],
                    start=(f == 0), stop=(f == c.FC - 1)))(f, tc),
                    deps=[tp_evs[f]] + ([hist.get(("lgev", tc - 1))] if f == 0 else []))
            lgev = P.op("vector", lambda e: e.tensor_copy(out=lg_sb[:], in_=r_ps[:, :]),
                        deps=[mmr])
            hist[("lgev", tc)] = lgev
            o_m1 = P.op("vector", lambda e: e.tensor_reduce(
                out=m1_sb[:], in_=lg_sb[:], axis=AX.X, op=ALU.max), deps=[lgev])
            o_k1 = P.op("vector", lambda e: e.tensor_scalar(
                out=msk1_sb[:], in0=lg_sb[:], scalar1=m1_sb[:], scalar2=None,
                op0=ALU.is_ge), deps=[o_m1])
            o_lgm = P.op("vector", lambda e: e.scalar_tensor_tensor(
                out=lgm_sb[:], in0=msk1_sb[:], scalar=-1e30, in1=lg_sb[:],
                op0=ALU.mult, op1=ALU.add), deps=[o_k1])
            o_m2 = P.op("vector", lambda e: e.tensor_reduce(
                out=m2_sb[:], in_=lgm_sb[:], axis=AX.X, op=ALU.max), deps=[o_lgm])
            o_k12 = P.op("vector", lambda e: e.tensor_scalar(
                out=msk12_sb[:], in0=lg_sb[:], scalar1=m2_sb[:], scalar2=None,
                op0=ALU.is_ge), deps=[o_m2])
            o_kr2 = P.op("vector", lambda e: e.tensor_tensor(
                out=mskr2_sb[:], in0=msk12_sb[:], in1=msk1_sb[:],
                op=ALU.subtract), deps=[o_k12])
            o_df = P.op("vector", (lambda tc: lambda e: e.tensor_tensor(
                out=diffs_sb[:, tc:tc + 1], in0=m1_sb[:], in1=m2_sb[:],
                op=ALU.subtract))(tc), deps=[o_m2, o_kr2])
            o_mb = P.op("vector", (lambda tc: lambda e: e.tensor_copy(
                out=mb_sb[tc], in_=msk12_sb[:]))(tc), deps=[o_k12])
            hist[("mb", tc)] = o_mb
            mcs = P.op("tensor", (lambda tc: lambda e: e.matmul(
                cum_ps[:, :], ut_sb[:], mb_sb[tc], start=True,
                stop=(tc == 0)))(tc),
                deps=[o_mb, d_ut, hist.get(("cumread", tc - 1))])
            for tcp in range(tc):
                mcs = P.op("tensor", (lambda tcp, tc: lambda e: e.matmul(
                    cum_ps[:, :], allones_sb[:], mb_sb[tcp], start=False,
                    stop=(tcp == tc - 1)))(tcp, tc),
                    deps=[mcs, hist[("mb", tcp)]])
            cumev = P.op("vector", lambda e: e.tensor_copy(out=cum_sb[:], in_=cum_ps[:, :]),
                         deps=[mcs])
            hist[("cumread", tc)] = cumev
            dsts = []
            prev_chain = cumev
            for rk, mm_sel in ((0, msk1_sb), (1, mskr2_sb)):
                o_p = P.op("vector", (lambda mm_sel: lambda e: e.tensor_tensor(
                    out=pick_sb[:], in0=cum_sb[:], in1=mm_sel[:], op=ALU.mult))(mm_sel),
                    deps=[prev_chain, o_kr2])
                o_s = P.op("vector", (lambda rk: lambda e: e.tensor_reduce(
                    out=dest_f[:, rk:rk + 1], in_=pick_sb[:], axis=AX.X,
                    op=ALU.add))(rk), deps=[o_p])
                o_c = P.op("vector", (lambda rk: lambda e: e.tensor_scalar(
                    out=dest_f[:, rk:rk + 1], in0=dest_f[:, rk:rk + 1],
                    scalar1=-1.0, scalar2=float(c.ES - 1),
                    op0=ALU.add, op1=ALU.min))(rk), deps=[o_s])
                o_pe = P.op("vector", (lambda mm_sel: lambda e: e.tensor_tensor(
                    out=pick_sb[:], in0=eoffb_sb[:], in1=mm_sel[:], op=ALU.mult))(mm_sel),
                    deps=[o_c])
                o_se = P.op("vector", lambda e: e.tensor_reduce(
                    out=epick_sb[:], in_=pick_sb[:], axis=AX.X, op=ALU.add),
                    deps=[o_pe])
                o_ad = P.op("vector", (lambda rk: lambda e: e.tensor_tensor(
                    out=dest_f[:, rk:rk + 1], in0=dest_f[:, rk:rk + 1],
                    in1=epick_sb[:], op=ALU.add))(rk), deps=[o_se])
                o_u = P.op("vector", (lambda rk, tc: lambda e: e.tensor_copy(
                    out=dest_u[rk][:, tc:tc + 1], in_=dest_f[:, rk:rk + 1]))(rk, tc),
                    deps=[o_ad])
                dsts.append(o_u)
                prev_chain = o_u
            sc_last = None
            for rk, tsrc in ((0, tokid_sb), (1, tokid2_sb)):
                s1 = P.dma("sw_a" if par == 0 else "sw_b",
                    (lambda rk, par, tc: lambda e: e.indirect_dma_start(
                    out=xg_dram,
                    out_offset=IndirectOffsetOnAxis(dest_u[rk][:, tc:tc + 1], 0),
                    in_=xl2_sb[par][:, :], in_offset=None,
                    bounds_check=bc_disp_reg, oob_is_err=False))(rk, par, tc),
                    deps=[dsts[rk], d_x2])
                s2 = P.dma("sw_a" if par == 0 else "sw_b",
                    (lambda rk, tsrc, tc: lambda e: e.indirect_dma_start(
                    out=backmap_dram,
                    out_offset=IndirectOffsetOnAxis(dest_u[rk][:, tc:tc + 1], 0),
                    in_=tsrc[:, tc:tc + 1], in_offset=None,
                    bounds_check=bc_disp_reg, oob_is_err=False))(rk, tsrc, tc),
                    deps=[dsts[rk]])
                sc_last = s2
            hist[("dispdone", tc)] = sc_last
            hist[("xl2use", tc)] = sc_last

        sg1 = P.op("scalar", lambda e: e.activation(
            out=w1_sb[:], in_=diffs_sb[:], func=AF.Sigmoid), deps=[o_df])
        sg2 = P.op("scalar", lambda e: e.activation(
            out=w2_sb[:], in_=diffs_sb[:], func=AF.Sigmoid, scale=-1.0),
            deps=[o_df])
        for tc in range(c.TC):
            for rk, wsrc, sdep in ((0, w1_sb, sg1), (1, w2_sb, sg2)):
                P.dma("sw_g0" if tc % 2 == 0 else "sw_g1",
                    (lambda rk, wsrc, tc: lambda e: e.indirect_dma_start(
                    out=gatew_dram,
                    out_offset=IndirectOffsetOnAxis(dest_u[rk][:, tc:tc + 1], 0),
                    in_=wsrc[:, tc:tc + 1], in_offset=None,
                    bounds_check=bc_disp_reg, oob_is_err=False))(rk, wsrc, tc),
                    deps=[sdep, hist[("dispdone", tc)]])
        P.dma("hw", lambda e: e.dma_start(
            out=xln2_dram.rearrange("cc p t -> p cc t"), in_=xlf2_sb.r3),
            deps=[hist[("tpev", c.TC * c.FC - 1)]])

        P.flush_phase("p4b")
    if stop_after == "p4b":
        return nc

    with ExitStack() as pc:
        xg_sb = [sba("xg0_sb", c.CC, c.D, BF16, pc)]
        xgf_sb = [sba("xgf0_sb", c.FC, c.CAP, BF16, pc),
                  sba("xgf1_sb", c.FC, c.CAP, BF16, pc)]
        bmc_sb = [sba(f"bmc{i}_sb", c.CC, 1, U32, pc) for i in range(2)]
        gwc_sb = [sba(f"gwc{i}_sb", c.CC, 1, F32, pc) for i in range(2)]
        wg_sb = [sba(f"wg{i}_sb", c.FC, 128, BF16, pc) for i in range(2)]
        wu_sb = [sba(f"wu{i}_sb", c.FC, 128, BF16, pc) for i in range(2)]
        wd_sb = [sba("wd0_sb", c.FO, c.D, BF16, pc)]
        gs_sb = pc.enter_context(nc.sbuf_tensor("gs_sb", [128, c.CAP], BF16))
        hfm_sb = [sba(f"hfm{i}_sb", c.FO, c.CAP, BF16, pc) for i in range(2)]
        dn_sb = sba("dn_sb", 2, c.D, F32, pc)
        tp_ps = psa("p5tp_ps", 2, 128, BF16, pc)
        g_ps = pc.enter_context(nc.psum_tensor("p5g_ps", [128, 512], F32))
        g2_ps = pc.enter_context(nc.psum_tensor("p5g2_ps", [128, 128], F32))
        u_ps = pc.enter_context(nc.psum_tensor("p5u_ps", [128, 512], F32))
        u2_ps = pc.enter_context(nc.psum_tensor("p5u2_ps", [128, 128], F32))
        d_ps = psa("p5d_ps", 2, 512, F32, pc)

        ngrp = [(0, min(512, c.CAP))]
        if c.CAP > 512:
            ngrp.append((512, c.CAP - 512))

        hist = {}
        for ei in range(c.E):
            par = ei % 2
            hq5 = "hw_a" if par == 0 else "hw_b"
            d_xg = P.dma(hq5, (lambda ei: lambda e: e.dma_start(
                out=xg_sb[0].r3,
                in_=xg_dram[c.ES * ei:c.ES * ei + c.CAP, :].rearrange(
                    "(t p) d -> p t d", p=128)))(ei),
                deps=[hist.get(("xguse", ei - 1))])
            d_bm = P.dma(hq5, (lambda ei, par: lambda e: e.dma_start(
                out=bmc_sb[par].r3,
                in_=backmap_dram[c.ES * ei:c.ES * ei + c.CAP].rearrange(
                    "(t p) one -> p t one", p=128),
                allow_slow_non_contiguous=True))(ei, par),
                deps=(hist.get(("bmuse", ei - 2)) or []))
            d_gw = P.dma(hq5, (lambda ei, par: lambda e: e.dma_start(
                out=gwc_sb[par].r3,
                in_=gatew_dram[c.ES * ei:c.ES * ei + c.CAP].rearrange(
                    "(t p) one -> p t one", p=128),
                allow_slow_non_contiguous=True))(ei, par),
                deps=[hist.get(("gwuse", ei - 2))])
            ld3 = [d_xg, d_bm, d_gw]
            tp_last = None
            for t in range(c.CC):
                for f in range(c.FC):
                    n = (ei * c.CC + t) * c.FC + f
                    mmt = P.op("tensor", (lambda t, f, n: lambda e: e.transpose(
                        tp_ps[n % 2], xg_sb[0][t][:, 128 * f:128 * (f + 1)],
                        ident_sb[:]))(t, f, n),
                        deps=ld3 + [hist.get(("p5tp", n - 2)),
                              hist.get(("xgfuse", ei - 2)) if (t == 0 and f == 0) else None])
                    evt = P.op("vector", (lambda par, t, f, n, mmt: lambda e: e.tensor_copy(
                        out=xgf_sb[par][f][:, 128 * t:128 * (t + 1)], in_=tp_ps[n % 2]))(par, t, f, n, mmt),
                        deps=[mmt])
                    hist[("p5tp", n)] = evt
                    tp_last = evt
            hist[("xguse", ei)] = tp_last
            d_wd = P.dma("sw_c", (lambda ei: lambda e: e.dma_start(
                out=wd_sb[0].r3,
                in_=w_down[ei].rearrange("(cc p) d -> p cc d", p=128)))(ei),
                deps=[hist.get(("wduse", ei - 1))])
            for fo in range(c.FO):
                gfo = ei * c.FO + fo
                pw = gfo % 2
                d_wg = P.dma("sw_a" if pw == 0 else "sw_b",
                    (lambda ei, fo, pw: lambda e: e.dma_start(
                    out=wg_sb[pw].r3,
                    in_=w_gate[ei][:, 128 * fo:128 * (fo + 1)].rearrange(
                        "(cc p) k -> p cc k", p=128)))(ei, fo, pw),
                    deps=[hist.get(("wguse", gfo - 2))])
                d_wu = P.dma("sw_a" if pw == 0 else "sw_b",
                    (lambda ei, fo, pw: lambda e: e.dma_start(
                    out=wu_sb[pw].r3,
                    in_=w_up[ei][:, 128 * fo:128 * (fo + 1)].rearrange(
                        "(cc p) k -> p cc k", p=128)))(ei, fo, pw),
                    deps=[hist.get(("wguse", gfo - 2))])
                hlast = None
                mmu = None
                for (cg0, cgw) in ngrp:
                    gp = g_ps if cgw > 128 else g2_ps
                    up = u_ps if cgw > 128 else u2_ps
                    mmg = mmu = None
                    for f in range(c.FC):
                        mmg = P.op("tensor", (lambda pw, f, cg0, cgw, par, gp: lambda e: e.matmul(
                            gp[:, 0:cgw], wg_sb[pw][f],
                            xgf_sb[par][f][:, cg0:cg0 + cgw],
                            start=(f == 0), stop=(f == c.FC - 1)))(pw, f, cg0, cgw, par, gp),
                            deps=([d_wg, d_wu, tp_last, hist.get(("gev", gfo - 1, cg0))]
                                  if f == 0 else []))
                        mmu = P.op("tensor", (lambda pw, f, cg0, cgw, par, up: lambda e: e.matmul(
                            up[:, 0:cgw], wu_sb[pw][f],
                            xgf_sb[par][f][:, cg0:cg0 + cgw],
                            start=(f == 0), stop=(f == c.FC - 1)))(pw, f, cg0, cgw, par, up),
                            deps=([d_wg, d_wu, tp_last, hist.get(("uev", gfo - 1, cg0))]
                                  if f == 0 else []))
                    sil = P.op("scalar", (lambda cg0, cgw, gp: lambda e: e.activation(
                        out=gs_sb[:, cg0:cg0 + cgw], in_=gp[:, 0:cgw],
                        func=AF.Silu))(cg0, cgw, gp),
                        deps=[mmg, hist.get(("huse", gfo - 1, cg0))])
                    hev = P.op("vector", (lambda par, fo, cg0, cgw, up, sil: lambda e: e.tensor_tensor(
                        out=hfm_sb[par][fo][:, cg0:cg0 + cgw],
                        in0=gs_sb[:, cg0:cg0 + cgw], in1=up[:, 0:cgw],
                        op=ALU.mult))(par, fo, cg0, cgw, up, sil),
                        deps=[sil, mmu,
                              hist.get(("hread", ei - 2)) if (fo == 0 and cg0 == 0) else None])
                    hist[("gev", gfo, cg0)] = sil
                    hist[("uev", gfo, cg0)] = hev
                    hist[("huse", gfo, cg0)] = hev
                    hlast = hev
                hist[("wguse", gfo)] = mmu
                hist[("hev", ei, fo)] = hlast
            hdeps = [hist[("hev", ei, fo)] for fo in range(c.FO)]
            last_mmd = None
            sts = []
            for t in range(c.CC):
                gt = ei * c.CC + t
                dnb = gt % 2
                evs = []
                for fs in range(c.FS):
                    dnn = gt * c.FS + fs
                    pd = dnn % 2
                    mmd = None
                    for fo in range(c.FO):
                        mmd = P.op("tensor", (lambda pd, par, t, fo, fs: lambda e: e.matmul(
                            d_ps[pd], hfm_sb[par][fo][:, 128 * t:128 * (t + 1)],
                            wd_sb[0][fo][:, 512 * fs:512 * (fs + 1)],
                            start=(fo == 0), stop=(fo == c.FO - 1)))(pd, par, t, fo, fs),
                            deps=(([d_wd] + hdeps +
                                   [hist.get(("dnevn", dnn - 2))]) if fo == 0 else []))
                    evfn = (lambda pd, dnb, par, t, fs: lambda e: e.tensor_scalar(
                        out=dn_sb[dnb][:, 512 * fs:512 * (fs + 1)], in0=d_ps[pd],
                        scalar1=gwc_sb[par][t], scalar2=None, op0=ALU.mult))(pd, dnb, par, t, fs)
                    ev = P.op(
                        "vector", evfn,
                        deps=[mmd, hist.get(("dnst", gt - 2))] +
                             ld3)
                    hist[("dnevn", dnn)] = ev
                    evs.append(ev)
                    last_mmd = mmd
                st = P.dma("sw_e" if dnb == 0 else "sw_f",
                    (lambda dnb, par, t: lambda e: e.indirect_dma_start(
                    out=rank_dram,
                    out_offset=IndirectOffsetOnAxis(bmc_sb[par][t], 0),
                    in_=dn_sb[dnb], in_offset=None,
                    bounds_check=bc_rank_reg, oob_is_err=False))(dnb, par, t),
                    deps=evs + ld3)
                hist[("dnst", gt)] = st
                sts.append(st)
            hist[("bmuse", ei)] = sts[-2:]
            hist[("gwuse", ei)] = hist[("dnevn", (ei * c.CC + c.CC) * c.FS - 1)]
            hist[("xgfuse", ei)] = last_mmd
            hist[("hread", ei)] = last_mmd
            hist[("wduse", ei)] = last_mmd
            if stop_after == f"p5e{ei}":
                break

        P.flush_phase("p5")
    if stop_after == "p5":
        return nc

    with ExitStack() as pc:
        TH = c.TOK // 2
        GW = min(512, TH)
        NCG = TH // GW
        TCH = TH // 128
        xlf2_sb = sba("xlf2b_sb", c.FC, TH, BF16, pc)
        wg_sb = [sba(f"swg{i}_sb", c.FC, 128, BF16, pc) for i in range(2)]
        wu_sb = [sba(f"swu{i}_sb", c.FC, 128, BF16, pc) for i in range(2)]
        gs_sb = pc.enter_context(nc.sbuf_tensor("bgs_sb", [128, TH], BF16))
        hfm_sb = sba("bhfm_sb", c.FO, TH, BF16, pc)
        wd_sb = sba("bwd_sb", c.FO, c.D, BF16, pc)
        res_sb = sba("bres_sb", 2, c.D, F32, pc)
        rk0_sb = sba("brk0_sb", 2, c.D, F32, pc)
        rk1_sb = sba("brk1_sb", 2, c.D, F32, pc)
        g_ps = psa("bg_ps", 2, 512, F32, pc)
        u_ps = psa("bu_ps", 2, 512, F32, pc)
        d_ps = psa("bd_ps", 2, 512, F32, pc)

        d_wd = P.dma("sw", lambda e: e.dma_start(
            out=wd_sb.r3, in_=sw_down.rearrange("(cc p) d -> p cc d", p=128)))

        hist = {}
        out_sts = []
        for half in range(2):
            hoff = half * TH
            d_x2 = P.dma("hw_x0" if half == 0 else "hw_x1",
                (lambda hoff: lambda e: e.dma_start(
                    out=xlf2_sb.r3,
                    in_=xln2_dram[:, :, hoff:hoff + TH].rearrange(
                        "cc p t -> p cc t")))(hoff),
                deps=[hist.get(("x2use", half - 1))])
            for fo in range(c.FO):
                gfo = half * c.FO + fo
                pw = gfo % 2
                d_wg = P.dma("sw_a" if pw == 0 else "sw_b",
                    (lambda fo, pw: lambda e: e.dma_start(
                        out=wg_sb[pw].r3,
                        in_=sw_gate[:, 128 * fo:128 * (fo + 1)].rearrange(
                            "(cc p) k -> p cc k", p=128)))(fo, pw),
                    deps=[hist.get(("wguse", gfo - 2))])
                d_wu = P.dma("sw_a" if pw == 0 else "sw_b",
                    (lambda fo, pw: lambda e: e.dma_start(
                        out=wu_sb[pw].r3,
                        in_=sw_up[:, 128 * fo:128 * (fo + 1)].rearrange(
                            "(cc p) k -> p cc k", p=128)))(fo, pw),
                    deps=[hist.get(("wguse", gfo - 2))])
                mmu = None
                hev = None
                for cg in range(NCG):
                    gn = gfo * NCG + cg
                    pg = gn % 2
                    mmg = mmu = None
                    for f in range(c.FC):
                        mmg = P.op("tensor", (lambda pw, pg, f, cg: lambda e: e.matmul(
                            g_ps[pg][:, 0:GW], wg_sb[pw][f],
                            xlf2_sb[f][:, GW * cg:GW * (cg + 1)],
                            start=(f == 0), stop=(f == c.FC - 1)))(pw, pg, f, cg),
                            deps=([d_wg, d_wu, d_x2, hist.get(("gev", gn - 2))]
                                  if f == 0 else []))
                        mmu = P.op("tensor", (lambda pw, pg, f, cg: lambda e: e.matmul(
                            u_ps[pg][:, 0:GW], wu_sb[pw][f],
                            xlf2_sb[f][:, GW * cg:GW * (cg + 1)],
                            start=(f == 0), stop=(f == c.FC - 1)))(pw, pg, f, cg),
                            deps=([d_wg, d_wu, d_x2, hist.get(("uev", gn - 2))]
                                  if f == 0 else []))
                    sil = P.op("scalar", (lambda pg, cg: lambda e: e.activation(
                        out=gs_sb[:, GW * cg:GW * (cg + 1)], in_=g_ps[pg][:, 0:GW],
                        func=AF.Silu))(pg, cg),
                        deps=[mmg, hist.get(("huse", gn - NCG))])
                    hev = P.op("vector", (lambda pg, fo, cg, sil: lambda e: e.tensor_tensor(
                        out=hfm_sb[fo][:, GW * cg:GW * (cg + 1)],
                        in0=gs_sb[:, GW * cg:GW * (cg + 1)], in1=u_ps[pg][:, 0:GW],
                        op=ALU.mult))(pg, fo, cg, sil),
                        deps=[sil, mmu,
                              hist.get(("hread", half - 1)) if cg == 0 else None])
                    hist[("gev", gn)] = sil
                    hist[("uev", gn)] = hev
                    hist[("huse", gn)] = hev
                hist[("wguse", gfo)] = mmu
                hist[("hev", half, fo)] = hev

            hdeps = [hist[("hev", half, fo)] for fo in range(c.FO)]
            last_mmd = None
            for tch in range(TCH):
                tc = half * TCH + tch
                par = tc % 2
                gd = hist.get(("outst", tc - 2))
                hqb = "hw_a" if par == 0 else "hw_b"
                d_rs = P.dma(hqb, (lambda tc, par: lambda e: e.dma_start(
                    out=res_sb[par], in_=resid_dram[128 * tc:128 * (tc + 1), :]))(tc, par),
                    deps=[gd])
                d_r0 = P.dma(hqb, (lambda tc, par: lambda e: e.dma_start(
                    out=rk0_sb[par], in_=rank_dram[128 * tc:128 * (tc + 1), :]))(tc, par),
                    deps=[gd])
                d_r1 = P.dma(hqb, (lambda tc, par: lambda e: e.dma_start(
                    out=rk1_sb[par],
                    in_=rank_dram[c.TOK + 128 * tc:c.TOK + 128 * (tc + 1), :]))(tc, par),
                    deps=[gd])
                d_rr = [d_rs, d_r0, d_r1]
                cmb_ops = []
                for fs in range(c.FS):
                    dnn = tc * c.FS + fs
                    pd = dnn % 2
                    mmd = None
                    for fo in range(c.FO):
                        mmd = P.op("tensor", (lambda pd, tch, fo, fs: lambda e: e.matmul(
                            d_ps[pd], hfm_sb[fo][:, 128 * tch:128 * (tch + 1)],
                            wd_sb[fo][:, 512 * fs:512 * (fs + 1)],
                            start=(fo == 0), stop=(fo == c.FO - 1)))(pd, tch, fo, fs),
                            deps=(([d_wd] + hdeps + [hist.get(("cmbev", dnn - 2))])
                                  if fo == 0 else []))
                    c1 = P.op("vector", (lambda pd, par, fs, mmd: lambda e: e.tensor_tensor(
                        out=res_sb[par][:, 512 * fs:512 * (fs + 1)], in0=d_ps[pd],
                        in1=res_sb[par][:, 512 * fs:512 * (fs + 1)], op=ALU.add))(pd, par, fs, mmd),
                        deps=[mmd, hist.get(("outst", tc - 2)) if fs == 0 else None] + d_rr)
                    c2 = P.op("vector", (lambda par, fs, c1: lambda e: e.tensor_tensor(
                        out=res_sb[par][:, 512 * fs:512 * (fs + 1)],
                        in0=res_sb[par][:, 512 * fs:512 * (fs + 1)],
                        in1=rk0_sb[par][:, 512 * fs:512 * (fs + 1)], op=ALU.add))(par, fs, c1),
                        deps=[c1])
                    c3 = P.op("vector", (lambda par, fs, c2: lambda e: e.tensor_tensor(
                        out=res_sb[par][:, 512 * fs:512 * (fs + 1)],
                        in0=res_sb[par][:, 512 * fs:512 * (fs + 1)],
                        in1=rk1_sb[par][:, 512 * fs:512 * (fs + 1)], op=ALU.add))(par, fs, c2),
                        deps=[c2])
                    hist[("cmbev", dnn)] = c3
                    cmb_ops.append(c3)
                    last_mmd = mmd
                hist[("cmb", tc)] = cmb_ops[-1]
                st = P.dma("hw_s0" if par == 0 else "hw_s1",
                    (lambda par, tc: lambda e: e.dma_start(
                        out=y_out[128 * tc:128 * (tc + 1), :], in_=res_sb[par]))(par, tc),
                    deps=cmb_ops)
                hist[("outst", tc)] = st
                out_sts.append(st)
            hist[("x2use", half)] = last_mmd
            hist[("hread", half)] = last_mmd

        for nm in dbg_names:
            dsrc = dbg_map[nm]
            t = nc.dram_tensor("dbg_" + nm, list(dsrc.shape), dsrc.dtype,
                               kind="ExternalOutput").ap()
            out_sts.append(P.dma("hw", (lambda t, dsrc: lambda e: e.dma_start(
                out=t.flatten()[None, :], in_=dsrc.flatten()[None, :]))(t, dsrc)))

        P.flush_phase("p5b")

    ctx.close()
    return nc



def _shard(cfg, x_full):
    c = cfg
    per_core = []
    for cid in range(c.NCORES):
        b = cid // c.R
        r = cid % c.R
        tiles = [r + c.R * v for v in range(c.NT)]
        rows = np.concatenate([np.arange(128 * t, 128 * (t + 1)) for t in tiles])
        xb = np.ascontiguousarray(x_full[b])
        xo = np.ascontiguousarray(x_full[b][rows])
        qg = np.ascontiguousarray(rows.astype(np.float32).reshape(1, -1))
        per_core.append((b, rows, xb, xo, qg))
    return per_core


def _wmap(inputs):
    w = {k: np.ascontiguousarray(np.asarray(v, np.float32))
         for k, v in inputs.items() if k != "L0_attn_x"}
    return {
        "wq_w": w["L0_attn_wq_w"], "wq_b": w["L0_attn_wq_b"].reshape(1, -1),
        "wk_w": w["L0_attn_wk_w"], "wk_b": w["L0_attn_wk_b"].reshape(1, -1),
        "wkl_w": w["L0_attn_wkl_w"], "wkl_b": w["L0_attn_wkl_b"].reshape(1, -1),
        "wv_w": w["L0_attn_wv_w"], "wv_b": w["L0_attn_wv_b"].reshape(1, -1),
        "wvl_w": w["L0_attn_wvl_w"], "wvl_b": w["L0_attn_wvl_b"].reshape(1, -1),
        "upk_w": w["L0_attn_upk_w"], "upk_b": w["L0_attn_upk_b"],
        "upv_w": w["L0_attn_upv_w"], "upv_b": w["L0_attn_upv_b"],
        "wo_w": w["L0_attn_wo_w"], "wo_b": w["L0_attn_wo_b"].reshape(1, -1),
        "gate_w": w["L0_ffn_gate_w"],
        "w_gate": w["L0_ffn_w_gate"], "w_up": w["L0_ffn_w_up"],
        "w_down": w["L0_ffn_w_down"],
        "sw_gate": w["L0_ffn_sw_gate"], "sw_up": w["L0_ffn_sw_up"],
        "sw_down": w["L0_ffn_sw_down"],
    }


def kernel(**inputs):
    c = Cfg()
    x = np.asarray(inputs["L0_attn_x"], np.float32)
    assert x.shape == (c.B, c.S, c.D)
    wmap = _wmap(inputs)
    nc = build_program(c)
    shards = _shard(c, x)
    in_maps = []
    for (b, rows, xb, xo, qg) in shards:
        m = dict(wmap)
        m["x_batch"] = xb
        m["x_own"] = xo
        m["qg_row"] = qg
        in_maps.append(m)
    res = run_bass_kernel_spmd(nc, in_maps, core_ids=list(range(c.NCORES)))
    out = np.empty((c.B, c.S, c.D), np.float32)
    for cid, (b, rows, _, _, _) in enumerate(shards):
        out[b][rows] = res.results[cid]["y"]
    return out
